# version 16
# baseline (speedup 1.0000x reference)
import os
import numpy as np
import concourse.bass as bass
import concourse.mybir as mybir
from concourse.bass_utils import run_bass_kernel_spmd

F32 = mybir.dt.float32
BF16 = mybir.dt.bfloat16
I16 = mybir.dt.int16
I8 = mybir.dt.int8
AF = mybir.ActivationFunctionType
ALU = mybir.AluOpType
AXX = mybir.AxisListType.X

_SIM = os.environ.get("BASS_SIM_FP64") == "1"
RC = 127.5 if _SIM else 128.0
RD = 127.0 if _SIM else 128.0

T = 2048
D = 1024
NT = T // 128
NC = D // 128
H = 16
NJ = 16
DH = 4096
HCH = 32
TS = 256
NS = T // TS
EPS = 1e-5

ENGS = ("sync", "tensor", "vector", "scalar", "gpsimd")


class Sem:
    def __init__(self, nc, name):
        self._cm = nc.semaphore(name)
        self.h = self._cm.__enter__()
        self.n = 0
        self.name = name


class Plan:
    def __init__(self):
        self.items = {e: [] for e in ENGS}
        self.incs = {e: {} for e in ENGS}

    AUTO = {}

    def op(self, eng, emit, waits=(), inc=None, amt=1):
        w = [(s, v) for (s, v) in waits if v and v > 0]
        if inc is None and eng in self.AUTO:
            inc = self.AUTO[eng]
            amt = 1
        if inc is not None:
            inc.n += amt
            self.incs[eng][inc.name] = inc
        self.items[eng].append((w, emit, inc, amt))
        return inc.n if inc is not None else None

    def replay(self, eng, e, hwm):
        for w, emit, inc, amt in self.items[eng]:
            for s, v in w:
                if hwm[eng].get(s.name, 0) < v:
                    e.wait_ge(s.h, v)
                    hwm[eng][s.name] = v
            ins = emit(e)
            if inc is not None:
                ins.then_inc(inc.h, amt)
        for s in self.incs[eng].values():
            if hwm[eng].get(s.name, 0) < s.n:
                e.wait_ge(s.h, s.n)
                hwm[eng][s.name] = s.n


PHASES = []
_PHTOT = {e: 0 for e in ENGS}


def run_block(nc, plan, hwm, phase=""):
    for e in ENGS:
        _PHTOT[e] += len(plan.items[e])
    PHASES.append((phase, dict(_PHTOT)))
    with nc.Block() as blk:
        @blk.sync
        def _(e):
            plan.replay("sync", e, hwm)

        @blk.tensor
        def _(e):
            plan.replay("tensor", e, hwm)

        @blk.vector
        def _(e):
            plan.replay("vector", e, hwm)

        @blk.scalar
        def _(e):
            plan.replay("scalar", e, hwm)

        @blk.gpsimd
        def _(e):
            plan.replay("gpsimd", e, hwm)


def build_nc():
    PHASES.clear()
    for e in ENGS:
        _PHTOT[e] = 0
    nc = bass.Bass("TRN2", target_bir_lowering=False, debug=False,
                   detect_race_conditions=False)

    dx = nc.dram_tensor("x", [T, D], F32, kind="ExternalInput").ap()
    dw = {k: nc.dram_tensor(k, [D, D], F32, kind="ExternalInput").ap()
          for k in ("wq", "wk", "wv", "wo")}
    dbq = nc.dram_tensor("bq", [128, NC], F32, kind="ExternalInput").ap()
    dbk = nc.dram_tensor("bk", [128, NC], F32, kind="ExternalInput").ap()
    dbv = nc.dram_tensor("bv", [1, D], F32, kind="ExternalInput").ap()
    dbo = nc.dram_tensor("bo", [1, D], F32, kind="ExternalInput").ap()
    dw1t = nc.dram_tensor("w1t", [D, DH], F32, kind="ExternalInput").ap()
    dw2t = nc.dram_tensor("w2t", [DH, D], F32, kind="ExternalInput").ap()
    ddiag = nc.dram_tensor("diag3", [128, 3 * HCH * 128], F32, kind="ExternalInput").ap()
    dcb = nc.dram_tensor("convb", [128, HCH], F32, kind="ExternalInput").ap()
    dmask = nc.dram_tensor("mask01", [128, 384], F32, kind="ExternalInput").ap()
    diden = nc.dram_tensor("iden", [128, 128], F32, kind="ExternalInput").ap()
    dhalves = nc.dram_tensor("halves", [128, 2], F32, kind="ExternalInput").ap()
    dones128 = nc.dram_tensor("ones128", [1, 128], F32, kind="ExternalInput").ap()
    donescol = nc.dram_tensor("onescol", [128, 1], F32, kind="ExternalInput").ap()
    dout = nc.dram_tensor("out", [T, D], F32, kind="ExternalOutput").ap()
    DBG = os.environ.get("KDEBUG") == "1"
    iknd = "ExternalOutput" if DBG else "Internal"
    dx2 = nc.dram_tensor("x2sp", [T, D], BF16, kind=iknd).ap()
    dw1b = nc.dram_tensor("w1b16", [4, 128, NC * 1024], BF16, kind=iknd).ap()
    dw2b = nc.dram_tensor("w2b16", [128, HCH * 1024], BF16, kind=iknd).ap()
    dx3t = nc.dram_tensor("x3tsp", [NC, 128, T + 2], BF16, kind=iknd).ap()
    dbgO = (nc.dram_tensor("dbgO", [T, D], BF16, kind="ExternalOutput").ap()
            if DBG else None)
    dbgS = (nc.dram_tensor("dbgS", [128, 80], F32, kind="ExternalOutput").ap()
            if DBG else None)

    sD = Sem(nc, "sD")
    sG = Sem(nc, "sG")
    sX = [Sem(nc, "sX0"), Sem(nc, "sX1")]
    sX2 = [Sem(nc, "sX2_0"), Sem(nc, "sX2_1")]
    sW = [Sem(nc, f"sW{i}") for i in range(4)]
    sSp = [Sem(nc, f"sSp{i}") for i in range(4)]
    sOut = [Sem(nc, "sOut0"), Sem(nc, "sOut1")]
    sGq = Sem(nc, "sGq")
    sGk = Sem(nc, "sGk")
    sGs = [Sem(nc, "sGs0"), Sem(nc, "sGs1")]
    sGw = [Sem(nc, "sGw0"), Sem(nc, "sGw1")]
    sW2 = [Sem(nc, "sW2_0"), Sem(nc, "sW2_1")]
    sT = Sem(nc, "sT")
    sV = Sem(nc, "sV")
    sS = Sem(nc, "sS")
    sP = Sem(nc, "sP")
    Plan.AUTO = {"vector": sV, "scalar": sS, "gpsimd": sP}
    hwm = {e: {} for e in ENGS}
    cms = {}
    aseq = {"n": 0}

    def alloc(name, shape, dt, space="sbuf", side=None):
        cm = (nc.sbuf_tensor(name, shape, dt, side=side) if space == "sbuf"
              else nc.psum_tensor(name, shape, dt))
        h = cm.__enter__()
        aseq["n"] += 1
        cms[name] = (aseq["n"], cm)
        return h

    def free(*names):
        for nm in sorted(names, key=lambda nm: -cms[nm][0]):
            cms.pop(nm)[1].__exit__(None, None, None)

    bqs = alloc("bqs", [128, NC], F32)
    bks = alloc("bks", [128, NC], F32)
    bvB = alloc("bvB", [128, D], F32, side="right")
    boB = alloc("boB", [128, D], F32, side="right")
    bvrow = alloc("bvrow", [1, D], F32, side="right")
    borow = alloc("borow", [1, D], F32, side="right")
    cbs = alloc("cbs", [128, HCH], F32)
    mask01 = alloc("sb_mask01", [128, 384], BF16)
    iden = alloc("sb_iden", [128, 128], BF16)
    idenf = alloc("sb_idenf", [128, 128], F32)
    halves = alloc("sb_halves", [128, 2], F32)
    ones128 = alloc("sb_ones128", [1, 128], F32)
    onescol = alloc("sb_onescol", [128, 1], F32)
    epsc = alloc("epsc", [128, 1], F32)
    rcc = alloc("rcc", [128, 1], F32)
    negC = alloc("negC", [128, H], F32)
    eC = alloc("eC", [128, H], F32)
    mwc = alloc("mwc", [128, 2], F32)
    swc = alloc("swc", [128, 2], F32)
    stat = alloc("stat", [128, 12], F32)
    aggr = alloc("aggr", [128, 2], F32)
    scol = alloc("scol", [128, 1], F32)
    rcol = alloc("rcol", [128, 1], F32)
    wsum = alloc("wsum", [128, 64], F32)
    mrow = alloc("mrow", [1, 64], F32)
    mtot = alloc("mtot", [1, 2], F32)
    zrow = alloc("zrow", [128, 1], BF16)
    qmx = alloc("qmx", [128, NC], F32)
    kmx = alloc("kmx", [128, NC], F32)
    qkp = alloc("qkp", [128, NC], F32)
    c2x8 = alloc("c2x8", [2, NC], F32)
    crow = alloc("crow", [1, 16], F32)
    dbgs = alloc("dbgs", [128, 80], F32)

    p = Plan()
    for (dst, src) in ((bqs, dbq), (bks, dbk), (bvrow, dbv), (borow, dbo),
                       (cbs, dcb), (halves, dhalves), (ones128, dones128),
                       (onescol, donescol), (idenf, diden)):
        p.op("sync", lambda e, d=dst, s=src: e.dma_start(out=d[:], in_=s[:, :]),
             inc=sD, amt=16)
    p.op("gpsimd", lambda e: e.dma_start(out=mask01[:], in_=dmask[:, :]), inc=sG, amt=16)
    p.op("gpsimd", lambda e: e.dma_start(out=iden[:], in_=diden[:, :]), inc=sG, amt=16)
    p.op("vector", lambda e: e.memset(epsc[:], EPS), inc=sV)
    p.op("vector", lambda e: e.memset(rcc[:], RC), inc=sV)
    nzr = p.op("vector", lambda e: e.memset(zrow[:], 0.0), inc=sV)
    for c in range(NC):
        p.op("sync", lambda e, c=c: e.dma_start(out=dx3t[c, :, 0:1], in_=zrow[:, 0:1],
             allow_slow_non_contiguous=True), waits=[(sV, nzr)], inc=sD, amt=16)
        p.op("sync", lambda e, c=c: e.dma_start(out=dx3t[c, :, T + 1:T + 2], in_=zrow[:, 0:1],
             allow_slow_non_contiguous=True), inc=sD, amt=16)
    p.op("sync", lambda e: e.nop(), waits=[(sD, sD.n)])
    p.op("gpsimd", lambda e: e.nop(), waits=[(sG, sG.n)])
    run_block(nc, p, hwm, "b0_const")

    B1 = [alloc(f"B1_{c}", [128, T], BF16) for c in range(NC)]
    QT = [alloc(f"QT{c}", [128, T], BF16) for c in range(NC)]
    KT = [alloc(f"KT{c}", [128, T], BF16) for c in range(NC)]
    VA = [alloc(f"VA{j}", [128, H * 65], BF16) for j in range(NJ + 2)]

    def vaug(j):
        return VA[j + 1]

    xin = [alloc(f"xin_{i}", [128, D], F32) for i in range(2)]
    x1b = [alloc(f"x1b_{i}", [128, D], BF16) for i in range(2)]
    ptr = [alloc(f"ptr_{i}", [128, 1024], BF16, "psum") for i in range(2)]
    pbc = [alloc(f"pbc_{i}", [128, 512], F32, "psum") for i in range(2)]

    p = Plan()
    for j in (-1, NJ):
        p.op("vector", lambda e, j=j: e.memset(
            vaug(j).ap().rearrange("p (h c) -> p h c", c=65)[:, :, 0:64], 0.0))
    vinit = 0
    for j in range(-1, NJ + 1):
        vinit = p.op("vector", lambda e, j=j: e.memset(
            vaug(j).ap().rearrange("p (h c) -> p h c", c=65)[:, :, 64:65], 1.0), inc=sV)

    bc_ev = 0
    for i, (dst, src) in enumerate(((bvB, bvrow), (bvB, bvrow), (boB, borow), (boB, borow))):
        off = (i % 2) * 512
        wl = [(sV, bc_ev)] if i >= 2 else []
        n = p.op("tensor", lambda e, s=src, off=off, i=i: e.matmul(
            pbc[i % 2][:, :], ones128[:, :], s[0:1, off:off + 512], start=True, stop=True),
            waits=wl, inc=sT)
        bc_ev = p.op("vector", lambda e, d=dst, off=off, i=i: e.tensor_copy(
            d[:, off:off + 512], pbc[i % 2][:, :]), waits=[(sT, n)], inc=sV)

    ln_apply = [0] * NT
    tr_done = [[0] * NC for _ in range(NT)]
    ev_done = [[0] * NC for _ in range(NT)]
    sq_done = [0] * NT
    for t in range(NT):
        wl = [(sV, ln_apply[t - 2])] if t >= 2 else []
        xload = p.op("sync", lambda e, t=t: e.dma_start(
            out=xin[t % 2][:], in_=dx[t * 128:(t + 1) * 128, :]), waits=wl,
            inc=sX[t % 2], amt=16)
        wl = [(sX[t % 2], xload)]
        if t >= 1:
            wl.append((sS, sq_done[t - 1]))
        p.op("vector", lambda e, t=t: e.bn_stats(stat[:, 0:6], xin[t % 2][:, 0:512]),
             waits=wl)
        nsb = p.op("vector", lambda e, t=t: e.bn_stats(stat[:, 6:12], xin[t % 2][:, 512:1024]))
        nag = p.op("vector", lambda e: e.bn_aggr(aggr[:], stat[:]), waits=[(sV, nsb)], inc=sV)
        sq_done[t] = p.op("scalar", lambda e: e.activation(
            out=scol[:], in_=aggr[:, 1:2], func=AF.Sqrt, bias=epsc[:], scale=1.0),
            waits=[(sV, nag)], inc=sS)
        nrc1 = p.op("vector", lambda e: e.reciprocal(rcol[:], scol[:]),
                    waits=[(sS, sq_done[t])])
        wl = [(sV, nrc1)]
        if t >= 2:
            wl.append((sT, tr_done[t - 2][NC - 1]))
        ln_apply[t] = p.op("vector", lambda e, t=t: e.tensor_scalar(
            out=x1b[t % 2][:], in0=xin[t % 2][:], scalar1=aggr[:, 0:1], scalar2=rcol[:],
            op0=ALU.subtract, op1=ALU.mult), waits=wl, inc=sV)
        for c in range(NC):
            k = t * NC + c
            wl = [(sV, ln_apply[t])]
            if k >= 2:
                wl.append((sS, ev_done[(k - 2) // NC][(k - 2) % NC]))
            tr_done[t][c] = p.op("tensor", lambda e, t=t, c=c, k=k: e.transpose(
                ptr[k % 2][:, 0:128], x1b[t % 2][:, c * 128:(c + 1) * 128], iden[:]),
                waits=wl, inc=sT)
            ev_done[t][c] = p.op("scalar", lambda e, t=t, c=c, k=k: e.copy(
                B1[c][:, t * 128:(t + 1) * 128], ptr[k % 2][:, 0:128]),
                waits=[(sT, tr_done[t][c])], inc=sS)
    run_block(nc, p, hwm, "b1_ln1")
    free("xin_0", "xin_1", "x1b_0", "x1b_1", "ptr_0", "ptr_1", "pbc_0", "pbc_1")
    free("bvrow", "borow")

    wsb = [[alloc(f"wsb{s}_{c}", [128, D], BF16) for c in range(NC)] for s in range(2)]
    wstage = [alloc(f"wstage_{i}", [128, 1024], F32) for i in range(2)]
    pq = [alloc(f"pq_{i}", [128, 512], F32, "psum") for i in range(4)]
    pms = alloc("pms", [128, 512], F32, "psum")

    p = Plan()
    wq_ld = [0] * NC
    wk_ld = [0] * NC
    wv_ld = [0] * NC
    for c in range(NC):
        wq_ld[c] = p.op("gpsimd", lambda e, c=c: e.dma_start(
            out=wsb[0][c][:], in_=dw["wq"][c * 128:(c + 1) * 128, :]), inc=sGq, amt=16)
    for c in range(NC):
        wk_ld[c] = p.op("gpsimd", lambda e, c=c: e.dma_start(
            out=wsb[1][c][:], in_=dw["wk"][c * 128:(c + 1) * 128, :]), inc=sGk, amt=16)

    red_done = []
    piece_srcs = []
    for ch in range(NC):
        for h4 in range(4):
            piece_srcs.append((dw1t[ch * 128:(ch + 1) * 128, h4 * 1024:(h4 + 1) * 1024], 1024))
    for ch in range(HCH):
        piece_srcs.append((dw2t[ch * 128:(ch + 1) * 128, :], 1024))
    for idx, (src, width) in enumerate(piece_srcs):
        slot = idx % 2
        wl = [(sV, red_done[idx - 2])] if idx >= 2 else []
        ld = p.op("sync", lambda e, src=src, slot=slot, width=width: e.dma_start(
            out=wstage[slot][:, 0:width], in_=src), waits=wl, inc=sW[slot], amt=16)
        red_done.append(p.op("vector", lambda e, slot=slot, idx=idx, width=width: e.reduce_sum(
            wsum[:, idx:idx + 1], wstage[slot][:, 0:width], axis=AXX,
            apply_absolute_value=True), waits=[(sW[slot], ld)], inc=sV))

    def proj_qk(wset, wld, bias_s, OUT, slot_ev, evac_eng, wsem):
        for oc in range(NC):
            for st in range(4):
                slot = (oc * 4 + st) % 4
                wl = [(wsem, wld[NC - 1])]
                if slot_ev[slot]:
                    wl.append((evac_eng, slot_ev[slot]))
                last = 0
                for ic in range(NC):
                    last = p.op("tensor", lambda e, oc=oc, st=st, ic=ic, slot=slot, wset=wset: e.matmul(
                        pq[slot][:, :], wsb[wset][ic][:, oc * 128:(oc + 1) * 128],
                        B1[ic][:, st * 512:(st + 1) * 512],
                        start=(ic == 0), stop=(ic == NC - 1)),
                        waits=wl if ic == 0 else [],
                        inc=sT if ic == NC - 1 else None)
                slot_ev[slot] = p.op("scalar", lambda e, oc=oc, st=st, slot=slot, OUT=OUT, bias_s=bias_s: e.activation(
                    out=OUT[oc][:, st * 512:(st + 1) * 512], in_=pq[slot][:, :],
                    func=AF.Identity, bias=bias_s[:, oc:oc + 1], scale=1.0),
                    waits=[(sT, last)], inc=sS)
        return slot_ev

    slot_ev = [0, 0, 0, 0]
    slot_ev = proj_qk(0, wq_ld, bqs, QT, slot_ev, sS, sGq)
    for c in range(NC):
        wv_ld[c] = p.op("gpsimd", lambda e, c=c: e.dma_start(
            out=wsb[0][c][:], in_=dw["wv"][c * 128:(c + 1) * 128, :]),
            waits=[(sS, max(slot_ev))], inc=sGq, amt=16)
    slot_ev = proj_qk(1, wk_ld, bks, KT, slot_ev, sS, sGk)

    n1 = p.op("tensor", lambda e: e.matmul(pms[0:1, 0:64], onescol[:, :], wsum[:, 0:64],
                                           start=True, stop=True),
              waits=[(sV, red_done[-1])], inc=sT)
    n2 = p.op("scalar", lambda e: e.copy(mrow[0:1, 0:64], pms[0:1, 0:64]),
              waits=[(sT, n1)], inc=sS)
    p.op("vector", lambda e: e.reduce_sum(mtot[0:1, 0:1], mrow[0:1, 0:32], axis=AXX),
         waits=[(sS, n2)])
    nr2 = p.op("vector", lambda e: e.reduce_sum(mtot[0:1, 1:2], mrow[0:1, 32:64], axis=AXX))
    n3 = p.op("vector", lambda e: e.tensor_scalar(out=mtot[:], in0=mtot[:],
                                             scalar1=1.0 / (D * DH), scalar2=1e-5,
                                             op0=ALU.mult, op1=ALU.max),
              waits=[(sV, nr2)], inc=sV)
    n4 = p.op("tensor", lambda e: e.matmul(pms[:, 0:2], ones128[:, :], mtot[0:1, :],
                                           start=True, stop=True),
              waits=[(sV, n3)], inc=sT)
    n5 = p.op("scalar", lambda e: e.copy(mwc[:], pms[0:128, 0:2]), waits=[(sT, n4)], inc=sS)
    p.op("vector", lambda e: e.reciprocal(swc[:, 0:1], mwc[:, 0:1]), waits=[(sS, n5)])
    n6 = p.op("vector", lambda e: e.reciprocal(swc[:, 1:2], mwc[:, 1:2]), inc=sV)
    if DBG:
        p.op("vector", lambda e: e.tensor_copy(dbgs[:, 32:34], mwc[:, :]),
             waits=[(sV, n6)])
        p.op("vector", lambda e: e.tensor_copy(dbgs[:, 34:36], swc[:, :]))
        p.op("vector", lambda e: e.tensor_copy(dbgs[0:1, 36:60], mrow[0:1, 24:48]))
        p.op("vector", lambda e: e.tensor_copy(dbgs[0:1, 60:62], mtot[0:1, :]))

    prev_s_ev = list(slot_ev)
    slot_vv = [0, 0, 0, 0]
    for t in range(NT):
        for ofs in range(2):
            slot = (t * 2 + ofs) % 4
            wl = [(sGq, wv_ld[NC - 1]), (sV, bc_ev), (sV, vinit)]
            if prev_s_ev[slot]:
                wl.append((sS, prev_s_ev[slot]))
                prev_s_ev[slot] = 0
            if slot_vv[slot]:
                wl.append((sV, slot_vv[slot]))
            last = 0
            for ic in range(NC):
                last = p.op("tensor", lambda e, t=t, ofs=ofs, ic=ic, slot=slot: e.matmul(
                    pq[slot][:, :], B1[ic][:, t * 128:(t + 1) * 128],
                    wsb[0][ic][:, ofs * 512:(ofs + 1) * 512],
                    start=(ic == 0), stop=(ic == NC - 1)),
                    waits=wl if ic == 0 else [],
                    inc=sT if ic == NC - 1 else None)
            slot_vv[slot] = p.op("vector", lambda e, t=t, ofs=ofs, slot=slot: e.scalar_tensor_tensor(
                out=vaug(t).ap().rearrange("p (h c) -> p h c", c=65)[:, ofs * 8:(ofs + 1) * 8, 0:64],
                in0=pq[slot][:, :], scalar=1.0,
                in1=bvB[:, ofs * 512:(ofs + 1) * 512],
                op0=ALU.mult, op1=ALU.add),
                waits=[(sT, last)], inc=sV)

    nred = 0
    kev = max(slot_ev)
    for c in range(NC):
        p.op("vector", lambda e, c=c: e.reduce_max(qmx[:, c:c + 1], QT[c][:, :],
                                                   axis=AXX, apply_absolute_value=True),
             waits=[(sS, kev)] if c == 0 else [])
        nred = p.op("vector", lambda e, c=c: e.reduce_max(kmx[:, c:c + 1], KT[c][:, :],
                                                          axis=AXX, apply_absolute_value=True))
    nq = p.op("vector", lambda e: e.tensor_mul(qkp[:], qmx[:], kmx[:]),
              waits=[(sV, nred)], inc=sV)
    nm = p.op("tensor", lambda e: e.matmul(pms[0:2, 0:8], halves[:, :], qkp[:, :],
                                           start=True, stop=True),
              waits=[(sV, nq), (sS, n5)], inc=sT)
    ne = p.op("scalar", lambda e: e.copy(c2x8[:], pms[0:2, 0:8]), waits=[(sT, nm)], inc=sS)
    nts = p.op("vector", lambda e: e.tensor_scalar(out=c2x8[:], in0=c2x8[:], scalar1=0.125,
                                                   scalar2=-40.0, op0=ALU.mult, op1=ALU.add),
               waits=[(sS, ne)])
    nrl = p.op("vector", lambda e: e.tensor_relu(c2x8[:], c2x8[:]), waits=[(sV, nts)])
    nr = p.op("vector", lambda e: e.tensor_scalar(out=c2x8[:], in0=c2x8[:], scalar1=-1.0,
                                                  scalar2=None, op0=ALU.mult),
              waits=[(sV, nrl)], inc=sV)
    nc1 = p.op("vector", lambda e: e.tensor_copy(crow[0:1, 0:8], c2x8[0:1, :]),
               waits=[(sV, nr)], inc=sV)
    nd = p.op("sync", lambda e: e.dma_start(out=crow[0:1, 8:16], in_=c2x8[1:2, :]),
              waits=[(sV, nr)], inc=sD, amt=16)
    nb1 = p.op("tensor", lambda e: e.matmul(pms[:, 0:8], ones128[:, :], crow[0:1, 0:8],
                                            start=True, stop=True),
               waits=[(sV, nc1), (sD, nd), (sS, ne)], inc=sT)
    nb1e = p.op("scalar", lambda e: e.copy(
        negC.ap().rearrange("p (a b) -> p a b", b=2)[:, :, 0:1], pms[0:128, 0:8]),
        waits=[(sT, nb1)], inc=sS)
    nb2 = p.op("tensor", lambda e: e.matmul(pms[:, 0:8], ones128[:, :], crow[0:1, 8:16],
                                            start=True, stop=True),
               waits=[(sS, nb1e)], inc=sT)
    nb2e = p.op("scalar", lambda e: e.copy(
        negC.ap().rearrange("p (a b) -> p a b", b=2)[:, :, 1:2], pms[0:128, 0:8]),
        waits=[(sT, nb2)], inc=sS)
    nec = p.op("scalar", lambda e: e.activation(out=eC[:], in_=negC[:], func=AF.Exp,
                                                bias=0.0, scale=1.0),
               waits=[(sS, nb2e)], inc=sS)


    run_block(nc, p, hwm, "b2_qkv")
    free(*[f"wsb{s}_{c}" for s in range(2) for c in range(NC)],
         "wstage_0", "wstage_1", "pq_0", "pq_1", "pq_2", "pq_3", "pms")

    B2 = [alloc(f"B2_{t}", [128, D], BF16, side="right") for t in range(NT)]
    expraw = [alloc(f"expraw_{i}", [128, 384], BF16) for i in range(3)]
    expm = [alloc(f"expm_{i}", [128, 384], BF16) for i in range(3)]
    rodc = [alloc(f"rodc_{i}", [128, 1], F32) for i in range(2)]
    psc = [alloc(f"psc_{i}", [128, 512], F32, "psum") for i in range(3)]
    pod = [alloc(f"pod_{i}", [128, 512], F32, "psum") for i in range(4)]

    p = Plan()
    wstage3 = [alloc(f"wstage3_{i}", [128, 1024], F32) for i in range(4)]
    tern_u16 = [alloc(f"tern_u16_{i}", [128, 1024], I16) for i in range(2)]
    tern2b = [alloc(f"tern2b_{i}", [128, 1024], BF16) for i in range(2)]
    tern_jobs = []
    for ch in range(NC):
        for q in range(4):
            tern_jobs.append((dw1t[ch * 128:(ch + 1) * 128, q * 1024:(q + 1) * 1024],
                              dw1b[q, :, ch * 1024:(ch + 1) * 1024], 0))
    for ch in range(HCH):
        tern_jobs.append((dw2t[ch * 128:(ch + 1) * 128, :],
                          dw2b[:, ch * 1024:(ch + 1) * 1024], 1))
    tstate = {"sp": [0, 0, 0, 0]}

    def tern_steps():
        for i0 in range(0, len(tern_jobs), 4):
            picks = [k for k in range(i0, min(i0 + 4, len(tern_jobs)))]
            lds, n1s, n2s = {}, {}, {}
            for k in picks:
                srcA, dst, wi = tern_jobs[k]
                slot = k % 4
                wl = [(sV, n6)]
                if tstate["sp"][slot]:
                    wl.append((sSp[slot], tstate["sp"][slot]))
                lds[k] = p.op("sync", lambda e, srcA=srcA, slot=slot: e.dma_start(
                    out=wstage3[slot][:, :], in_=srcA), waits=wl, inc=sW[slot], amt=16)
                yield
            for j0 in range(0, len(picks), 2):
                pair = picks[j0:j0 + 2]
                for k in pair:
                    wi = tern_jobs[k][2]
                    slot = k % 4
                    wl = [(sW[slot], lds[k])]
                    if tstate.get("u16", [0, 0])[k % 2]:
                        wl.append((sV, tstate["u16"][k % 2]))
                    n1s[k] = p.op("vector", lambda e, slot=slot, wi=wi, k=k: e.tensor_scalar(
                        out=tern_u16[k % 2][:, :], in0=wstage3[slot][:, :],
                        scalar1=swc[:, wi:wi + 1], scalar2=RC, op0=ALU.mult, op1=ALU.add),
                        waits=wl)
                    yield
                for k in pair:
                    slot = k % 4
                    n2s[k] = p.op("vector", lambda e, slot=slot, k=k: e.tensor_scalar(
                        out=wstage3[slot][:, :], in0=tern_u16[k % 2][:, :],
                        scalar1=RD, scalar2=1.0, op0=ALU.subtract, op1=ALU.min),
                        waits=[(sV, n1s[k])])
                    tstate.setdefault("u16", [0, 0])[k % 2] = n2s[k]
                    yield
                for k in pair:
                    slot = k % 4
                    wl = [(sV, n2s[k])]
                    if tstate.get("t2b", [0, 0])[k % 2]:
                        wl.append((sSp[(k - 2) % 4], tstate["t2b"][k % 2]))
                    n2s[k] = p.op("vector", lambda e, slot=slot, k=k: e.tensor_scalar(
                        out=tern2b[k % 2][:, :], in0=wstage3[slot][:, :],
                        scalar1=-1.0, scalar2=None, op0=ALU.max),
                        waits=wl, inc=sV)
                    yield
                for k in pair:
                    slot = k % 4
                    dst = tern_jobs[k][1]
                    tstate["sp"][slot] = p.op("sync", lambda e, dst=dst, slot=slot, k=k: e.dma_start(
                        out=dst, in_=tern2b[k % 2][:, :]), waits=[(sV, n2s[k])],
                        inc=sSp[slot], amt=16)
                    tstate.setdefault("t2b", [0, 0])[k % 2] = tstate["sp"][slot]
                    yield

    tern_gen = tern_steps()

    def win(j):
        q0 = (j - 1) * 128
        q1 = (j + 2) * 128
        c0 = max(0, -q0)
        q0r = max(q0, 0)
        q1r = min(q1, T)
        return q0r, q1r, c0, c0 + (q1r - q0r)

    pod_ev = [0, 0, 0, 0]
    psc_ev = [0, 0, 0]
    expm_use = [0, 0, 0]
    mask_done = [0, 0, 0]
    rodc_use = [0, 0]
    mask_info = {}

    def emit_av(h, j):
        sl = (j + 1) % 3
        msem, nmask = mask_info.pop((h, j))
        last_use = 0
        for s in range(3):
            qb = j - 1 + s
            if qb < 0 or qb >= NJ:
                continue
            wc = qb * 128 - (j - 1) * 128
            first = (j == qb - 1)
            lastc = (j == qb + 1)
            wl = [(msem, nmask)]
            if first and pod_ev[qb % 4]:
                wl.append((sV, pod_ev[qb % 4]))
            last_use = p.op("tensor", lambda e, qb=qb, j=j, h=h, sl=sl, wc=wc, first=first, lastc=lastc: e.matmul(
                pod[qb % 4][:, 0:65], expm[sl][:, wc:wc + 128],
                vaug(j)[:, h * 65:h * 65 + 65], start=first, stop=lastc),
                waits=wl, inc=sT)
            if lastc:
                wl = [(sT, last_use)]
                if rodc_use[qb % 2]:
                    wl.append((sV, rodc_use[qb % 2]))
                nrc = p.op("vector", lambda e, qb=qb: e.reciprocal(
                    rodc[qb % 2][:, :], pod[qb % 4][:, 64:65]), waits=wl, inc=sV)
                pod_ev[qb % 4] = p.op("vector", lambda e, qb=qb, h=h: e.tensor_scalar(
                    out=B2[qb][:, h * 64:(h + 1) * 64], in0=pod[qb % 4][:, 0:64],
                    scalar1=rodc[qb % 2][:, :], scalar2=None, op0=ALU.mult),
                    waits=[(sV, nrc)], inc=sV)
                rodc_use[qb % 2] = pod_ev[qb % 4]
        if last_use:
            expm_use[sl] = last_use

    seq = [(h, j) for h in range(H) for j in range(-1, NJ + 1)]
    for idx, (h, j) in enumerate(seq):
        hc = h // 2
        hr = (h % 2) * 64
        sl = (j + 1) % 3
        q0r, q1r, c0, c1 = win(j)
        pad = (j < 0) or (j >= NJ)
        if not pad:
            wl = [(sS, nec)]
            if psc_ev[sl]:
                wl.append((sS, psc_ev[sl]))
            nmm = p.op("tensor", lambda e, hc=hc, hr=hr, j=j, sl=sl, q0r=q0r, q1r=q1r, c0=c0, c1=c1: e.matmul(
                psc[sl][:, c0:c1], KT[hc][hr:hr + 64, j * 128:(j + 1) * 128],
                QT[hc][hr:hr + 64, q0r:q1r], start=True, stop=True),
                waits=wl, inc=sT)
            wl = [(sT, nmm)]
            if mask_done[sl]:
                wl.append((sP, mask_done[sl]))
            psc_ev[sl] = p.op("scalar", lambda e, h=h, sl=sl, c0=c0, c1=c1: e.activation(
                out=expraw[sl][:, c0:c1], in_=psc[sl][:, c0:c1], func=AF.Exp,
                bias=negC[:, h:h + 1], scale=0.125), waits=wl, inc=sS)
            wl = [(sS, psc_ev[sl])]
            if expm_use[sl]:
                wl.append((sT, expm_use[sl]))
            mask_done[sl] = p.op("gpsimd", lambda e, sl=sl, c0=c0, c1=c1: e.tensor_tensor(
                expm[sl][:, c0:c1], expraw[sl][:, c0:c1], mask01[:, c0:c1], ALU.mult),
                waits=wl, inc=sP)
            mask_info[(h, j)] = (sP, mask_done[sl])
        else:
            wl = [(sS, nec)]
            if expm_use[sl]:
                wl.append((sT, expm_use[sl]))
            nmask = p.op("vector", lambda e, h=h, sl=sl, c0=c0, c1=c1: e.tensor_scalar(
                out=expm[sl][:, c0:c1], in0=mask01[:, c0:c1],
                scalar1=eC[:, h:h + 1], scalar2=None, op0=ALU.mult),
                waits=wl, inc=sV)
            mask_info[(h, j)] = (sV, nmask)
        if idx >= 2:
            emit_av(*seq[idx - 2])
        for _ in range(2):
            next(tern_gen, None)
    emit_av(*seq[-2])
    emit_av(*seq[-1])
    for _ in tern_gen:
        pass
    p.op("sync", lambda e: e.nop(), waits=[(sD, sD.n)] +
         [(s_, s_.n) for s_ in sW] + [(s_, s_.n) for s_ in sSp])
    run_block(nc, p, hwm, "b3_attn")
    free(*[f"QT{c}" for c in range(NC)], *[f"KT{c}" for c in range(NC)],
         *[f"VA{j}" for j in range(NJ + 2)],
         *[f"wstage3_{i}" for i in range(4)], "tern_u16_0", "tern_u16_1",
         "tern2b_0", "tern2b_1",
         "expraw_0", "expraw_1", "expraw_2", "expm_0", "expm_1", "expm_2",
         "rodc_0", "rodc_1",
         "psc_0", "psc_1", "psc_2", "pod_0", "pod_1", "pod_2", "pod_3")

    wosb = [alloc(f"wo_{c}", [128, D], BF16) for c in range(NC)]
    attn = [alloc(f"attn_{i}", [128, D], F32) for i in range(2)]
    x2f = [alloc(f"x2f_{i}", [128, D], F32) for i in range(2)]
    x3f = [alloc(f"x3f_{i}", [128, D], F32) for i in range(2)]
    xre = [alloc(f"xre_{i}", [128, D], F32) for i in range(2)]
    q16 = alloc("q16", [128, D], I16)
    x3q = [alloc(f"x3q_{i}", [128, D], BF16) for i in range(2)]
    x3st = [alloc(f"x3st_{i}", [128, D], BF16) for i in range(2)]
    sxc = alloc("sxc", [128, 1], F32)
    sxic = alloc("sxic", [128, 1], F32)
    sxcf = alloc("sxcf", [128, 1], F32)
    bmq = alloc("bmq", [128, 1], F32)
    bmc = alloc("bmc", [128, 1], F32)
    ptrO = [alloc(f"ptrO_{i}", [128, 1024], BF16, "psum") for i in range(2)]
    ptrX = [alloc(f"ptrX_{i}", [128, 1024], BF16, "psum") for i in range(2)]
    po4 = [alloc(f"po4_{i}", [128, 512], F32, "psum") for i in range(2)]

    p = Plan()
    if DBG:
        for t in range(NT):
            p.op("sync", lambda e, t=t: e.dma_start(
                out=dbgO[t * 128:(t + 1) * 128, :], in_=B2[t][:, :]), inc=sD, amt=16)
    wo_ld = [0] * NC
    for c in range(NC):
        wo_ld[c] = p.op("gpsimd", lambda e, c=c: e.dma_start(
            out=wosb[c][:], in_=dw["wo"][c * 128:(c + 1) * 128, :]), inc=sGq, amt=16)
    otr = [[0] * NC for _ in range(NT)]
    oev = [[0] * NC for _ in range(NT)]
    po_ev = [0, 0]
    x2sp = [0, 0]
    x3sp = [0, 0]
    apply2 = [0] * NT
    apply3 = [0] * NT
    red_prev = [0, 0]
    q_done = [0] * NT
    x3tr = [[0] * NC for _ in range(NT)]
    x3ev = [[0] * NC for _ in range(NT)]
    for t in range(NT):
        for c in range(NC):
            k = t * NC + c
            wl = []
            if k >= 2:
                wl.append((sS, oev[(k - 2) // NC][(k - 2) % NC]))
            otr[t][c] = p.op("tensor", lambda e, t=t, c=c, k=k: e.transpose(
                ptrO[k % 2][:, 0:128], B2[t][:, c * 128:(c + 1) * 128], iden[:]),
                waits=wl, inc=sT)
            oev[t][c] = p.op("scalar", lambda e, t=t, c=c, k=k: e.copy(
                B1[c][:, t * 128:(t + 1) * 128], ptrO[k % 2][:, 0:128]),
                waits=[(sT, otr[t][c])], inc=sS)
        xre_ld = p.op("sync", lambda e, t=t: e.dma_start(
            out=xre[t % 2][:], in_=dx[t * 128:(t + 1) * 128, :]),
            waits=[(sV, apply2[t - 2])] if t >= 2 else [], inc=sX[t % 2], amt=16)
        for ofs in range(2):
            slot = ofs
            wl = [(sS, oev[t][NC - 1]), (sGq, wo_ld[NC - 1])]
            if po_ev[slot]:
                wl.append((sV, po_ev[slot]))
            last = 0
            for c in range(NC):
                last = p.op("tensor", lambda e, t=t, ofs=ofs, c=c, slot=slot: e.matmul(
                    po4[slot][:, :], B1[c][:, t * 128:(t + 1) * 128],
                    wosb[c][:, ofs * 512:(ofs + 1) * 512],
                    start=(c == 0), stop=(c == NC - 1)),
                    waits=wl if c == 0 else [],
                    inc=sT if c == NC - 1 else None)
            wl2 = [(sT, last)]
            if t >= 2:
                wl2.append((sV, apply2[t - 2]))
            po_ev[slot] = p.op("vector", lambda e, t=t, ofs=ofs, slot=slot: e.scalar_tensor_tensor(
                out=attn[t % 2][:, ofs * 512:(ofs + 1) * 512], in0=po4[slot][:, :],
                scalar=1.0, in1=boB[:, ofs * 512:(ofs + 1) * 512],
                op0=ALU.mult, op1=ALU.add), waits=wl2, inc=sV)
        p.op("vector", lambda e, t=t: e.bn_stats(stat[:, 0:6], attn[t % 2][:, 0:512]),
             waits=[(sV, po_ev[1])])
        nsb = p.op("vector", lambda e, t=t: e.bn_stats(stat[:, 6:12], attn[t % 2][:, 512:1024]))
        nag = p.op("vector", lambda e: e.bn_aggr(aggr[:], stat[:]), waits=[(sV, nsb)], inc=sV)
        nsq = p.op("scalar", lambda e: e.activation(out=scol[:], in_=aggr[:, 1:2],
                                                    func=AF.Sqrt, bias=epsc[:], scale=1.0),
                   waits=[(sV, nag)], inc=sS)
        nrc = p.op("vector", lambda e: e.reciprocal(rcol[:], scol[:]),
                   waits=[(sS, nsq)], inc=sV)
        nbm = p.op("vector", lambda e: e.tensor_scalar(
            out=bmc[:], in0=aggr[:, 0:1], scalar1=rcol[:], scalar2=-1.0,
            op0=ALU.mult, op1=ALU.mult), waits=[(sV, nrc)], inc=sV)
        napl2 = p.op("scalar", lambda e, t=t: e.activation(
            out=attn[t % 2][:], in_=attn[t % 2][:], func=AF.Identity,
            bias=bmc[:], scale=rcol[:]), waits=[(sV, nbm)], inc=sS)
        wl = [(sX[t % 2], xre_ld), (sS, napl2)]
        if x2sp[t % 2]:
            wl.append((sGs[t % 2], x2sp[t % 2]))
        apply2[t] = p.op("vector", lambda e, t=t: e.tensor_add(
            x2f[t % 2][:], attn[t % 2][:], xre[t % 2][:]), waits=wl, inc=sV)
        x2sp[t % 2] = p.op("gpsimd", lambda e, t=t: e.dma_start(
            out=dx2[t * 128:(t + 1) * 128, :], in_=x2f[t % 2][:, :]),
            waits=[(sV, apply2[t])], inc=sGs[t % 2], amt=16)
        p.op("vector", lambda e, t=t: e.bn_stats(stat[:, 0:6], x2f[t % 2][:, 0:512]),
             waits=[(sV, apply2[t])])
        nsb = p.op("vector", lambda e, t=t: e.bn_stats(stat[:, 6:12], x2f[t % 2][:, 512:1024]))
        nag = p.op("vector", lambda e: e.bn_aggr(aggr[:], stat[:]), waits=[(sV, nsb)], inc=sV)
        nsq2 = p.op("scalar", lambda e: e.activation(out=scol[:], in_=aggr[:, 1:2],
                                                     func=AF.Sqrt, bias=epsc[:], scale=1.0),
                    waits=[(sV, nag)], inc=sS)
        nrc2 = p.op("vector", lambda e: e.reciprocal(rcol[:], scol[:]),
                    waits=[(sS, nsq2)], inc=sV)
        nbm2 = p.op("vector", lambda e: e.tensor_scalar(
            out=bmc[:], in0=aggr[:, 0:1], scalar1=rcol[:], scalar2=-1.0,
            op0=ALU.mult, op1=ALU.mult), waits=[(sV, nrc2)], inc=sV)
        wl = [(sV, nbm2)]
        if t >= 2:
            wl.append((sV, red_prev[t % 2]))
        apply3[t] = p.op("scalar", lambda e, t=t: e.activation(
            out=x3f[t % 2][:], in_=x2f[t % 2][:], func=AF.Identity,
            bias=bmc[:], scale=rcol[:]), waits=wl, inc=sS)
        wl = [(sS, apply3[t])]
        nred4 = p.op("vector", lambda e, t=t: e.reduce_max(
            sxc[:, :], x3f[t % 2][:, :], axis=AXX, apply_absolute_value=True),
            waits=wl, inc=sV)
        red_prev[t % 2] = nred4
        nclp = p.op("vector", lambda e: e.tensor_scalar(out=sxc[:], in0=sxc[:], scalar1=1e-5,
                                                        scalar2=None, op0=ALU.max),
                    waits=[(sV, nred4)], inc=sV)
        nrcq = p.op("vector", lambda e: e.reciprocal(sxic[:], sxc[:]),
                    waits=[(sV, nclp)], inc=sV)
        nm127 = p.op("vector", lambda e: e.tensor_scalar(out=sxic[:], in0=sxic[:],
                                                         scalar1=127.0, scalar2=None,
                                                         op0=ALU.mult),
                     waits=[(sV, nrcq)], inc=sV)
        p.op("scalar", lambda e, t=t: e.activation(
            out=q16[:, :], in_=x3f[t % 2][:], func=AF.Identity,
            bias=rcc[:], scale=sxic[:]), waits=[(sV, nm127)], inc=sS)
        nsxf = p.op("vector", lambda e: e.tensor_scalar(out=sxcf[:], in0=sxc[:],
                                                        scalar1=1.0 / 127.0,
                                                        scalar2=None, op0=ALU.mult),
                    waits=[(sV, nclp)], inc=sV)
        nbmq = p.op("vector", lambda e: e.tensor_scalar(out=bmq[:], in0=sxcf[:],
                                                        scalar1=-float(RD),
                                                        scalar2=None, op0=ALU.mult),
                    waits=[(sV, nsxf)], inc=sV)
        q_done[t] = p.op("scalar", lambda e, t=t: e.activation(
            out=x3q[t % 2][:], in_=q16[:, :], func=AF.Identity,
            bias=bmq[:], scale=sxcf[:]), waits=[(sV, nbmq)], inc=sS)
        def emit_x3t(tp):
            for c in range(NC):
                k = tp * NC + c
                wl = [(sS, q_done[tp])]
                if k >= 2:
                    wl.append((sS, x3ev[(k - 2) // NC][(k - 2) % NC]))
                x3tr[tp][c] = p.op("tensor", lambda e, tp=tp, c=c, k=k: e.transpose(
                    ptrX[k % 2][:, 0:128], x3q[tp % 2][:, c * 128:(c + 1) * 128], iden[:]),
                    waits=wl, inc=sT)
                wl = [(sT, x3tr[tp][c])]
                if c == 0 and x3sp[tp % 2]:
                    wl.append((sSp[tp % 2], x3sp[tp % 2]))
                x3ev[tp][c] = p.op("scalar", lambda e, tp=tp, c=c, k=k: e.copy(
                    x3st[tp % 2][:, c * 128:(c + 1) * 128], ptrX[k % 2][:, 0:128]),
                    waits=wl, inc=sS)
            x3sp[tp % 2] = p.op("sync", lambda e, tp=tp: e.dma_start(
                out=dx3t[:, :, 1 + tp * 128:1 + (tp + 1) * 128].transpose([1, 0, 2]),
                in_=x3st[tp % 2][:, :]), waits=[(sS, x3ev[tp][NC - 1])],
                inc=sSp[tp % 2], amt=16)

        if t >= 1:
            emit_x3t(t - 1)
        if t == NT - 1:
            emit_x3t(t)
    p.op("sync", lambda e: e.nop(), waits=[(sSp[0], sSp[0].n), (sSp[1], sSp[1].n),
                                           (sX[0], sX[0].n), (sX[1], sX[1].n)])
    p.op("gpsimd", lambda e: e.nop(), waits=[(sGs[0], sGs[0].n), (sGs[1], sGs[1].n),
                                             (sGq, sGq.n)])
    run_block(nc, p, hwm, "b4_oproj")
    free(*[f"wo_{c}" for c in range(NC)], *[f"B1_{c}" for c in range(NC)],
         *[f"B2_{t}" for t in range(NT)],
         "bvB", "boB",
         "attn_0", "attn_1", "x2f_0", "x2f_1", "x3f_0", "x3f_1", "xre_0", "xre_1",
         "q16", "x3q_0", "x3q_1", "x3st_0", "x3st_1", "sxc", "sxic", "sxcf", "bmq", "bmc",
         "ptrO_0", "ptrO_1", "ptrX_0", "ptrX_1", "po4_0", "po4_1")

    diag = alloc("diag", [128, 3 * HCH * 128], BF16)
    w1h = [alloc(f"w1h_{i}", [128, 4 * 1024], BF16) for i in range(2)]
    w2b = [alloc(f"w2b_{i}", [128, 4 * 1024], BF16) for i in range(2)]
    x3w = [alloc(f"x3w_{i}", [128, 8 * 256], BF16) for i in range(3)]
    Ab = [alloc(f"Ab_{i}", [128, HCH * 258], BF16) for i in range(2)]
    H2 = alloc("H2", [128, HCH * 256], F32)
    H2q = [alloc(f"H2q_{i}", [128, HCH * 256], BF16) for i in range(2)]
    tland = [alloc(f"tland_{i}", [128, 256], F32) for i in range(2)]
    q16b = [alloc(f"q16b_{i}", [128, 256], I16) for i in range(2)]
    apar = alloc("apar", [128, 4 * 256], F32)
    amaxC = alloc("amaxC", [128, 256], F32)
    v2 = alloc("v2", [128, 4], F32)
    sxr = alloc("sxr", [1, 256], F32)
    sxB = alloc("sxB", [128, 256], F32)
    sc2b = [alloc(f"sc2b_{i}", [128, 2], F32) for i in range(2)]
    out2 = [alloc(f"out2_{i}", [128, D], F32) for i in range(2)]
    x4b = [alloc(f"x4b_{i}", [128, D], F32) for i in range(2)]
    x2re = [alloc(f"x2re_{i}", [128, D], BF16) for i in range(2)]
    pA = [alloc(f"pA_{i}", [128, 512], F32, "psum") for i in range(2)]
    pC = alloc("pC", [128, 512], F32, "psum")
    pM = [alloc(f"pM_{i}", [128, 512], F32, "psum") for i in range(4)]
    pX = alloc("pX", [128, 512], F32, "psum")

    p = Plan()
    dg_ld = p.op("gpsimd", lambda e: e.dma_start(
        out=diag[:, :], in_=ddiag[:, :]), inc=sG, amt=16)

    def dg(k, c):
        return diag[:, (k * HCH + c) * 128:(k * HCH + c) * 128 + 128]

    def ab(s, c):
        return Ab[s % 2][:, c * 258:(c + 1) * 258]

    x3_ld = [0] * (NS + 2)
    a_mm = [[0] * HCH for _ in range(NS + 2)]
    a_ev = [[0] * HCH for _ in range(NS + 2)]
    halo_d = [[0] * HCH for _ in range(NS + 2)]
    conv_mm = [[0] * HCH for _ in range(NS)]
    silu_d = [[0] * HCH for _ in range(NS)]
    ntp_d = [[0] * HCH for _ in range(NS)]
    h2q_d = [[0] * HCH for _ in range(NS)]
    apar_d = [[0] * 4 for _ in range(NS)]
    mm2_last = [0] * NS
    w1_ld = {}
    w1_used = {}
    pa_ev = [0, 0]
    pc_ev = [0]
    pm_ev = [0, 0, 0, 0]
    w2_used = [0, 0]
    x2re_used = [0, 0]
    out_dma = [0, 0]
    vcomb = [0]
    sxB_ev = [0]
    sc2_d = [0, 0]
    trans_last = [0]
    ktrans_last = [0]

    def emit_x3ld(s2):
        par = s2 % 3
        if s2 == NS:
            src = dx3t[:, :, s2 * TS:s2 * TS + 2].transpose([1, 0, 2])
            dst = x3w[par].ap().rearrange("p (c n) -> p c n", n=256)[:, :, 0:2]
        else:
            src = dx3t[:, :, s2 * TS:s2 * TS + 256].transpose([1, 0, 2])
            dst = x3w[par][:, :]
        wl = [(sT, a_mm[s2 - 3][HCH - 1])] if s2 >= 3 else []
        if s2 >= 2:
            wl.append((sX[s2 % 2], x3_ld[s2 - 2]))
        x3_ld[s2] = p.op("sync", lambda e, src=src, dst=dst: e.dma_start(out=dst, in_=src),
                         waits=wl, inc=sX[s2 % 2], amt=16)

    def emit_w1ld(s2, q):
        j = s2 * 8 + q
        wl = [(sT, w1_used[j - 2])] if (j - 2) in w1_used else []
        g = q // 2
        half = q % 2
        src = dw1b[g, :, :].rearrange("p (c h) -> p c h", h=1024)[
            :, :, half * 512:(half + 1) * 512]
        w1_ld[j] = p.op("gpsimd", lambda e, src=src, j=j: e.dma_start(
            out=w1h[j % 2][:, :], in_=src), waits=wl, inc=sGw[j % 2], amt=16)

    def emit_A_chunk(s2, hcx):
        par = s2 % 3
        width = 2 if s2 == NS else 256
        slotp = hcx % 2
        j = s2 * 8 + hcx // 4
        hs2 = hcx % 4
        wl = [(sX[s2 % 2], x3_ld[s2]), (sGw[j % 2], w1_ld[j])]
        if pa_ev[slotp]:
            wl.append((sS, pa_ev[slotp]))
        last = 0
        for d in range(NC):
            if s2 == NS:
                mv = x3w[par].ap().rearrange("p (c n) -> p c n", n=256)[:, d, 0:2]
            else:
                mv = x3w[par][:, d * 256:(d + 1) * 256]
            last = p.op("tensor", lambda e, j=j, hs2=hs2, d=d, slotp=slotp, mv=mv, width=width: e.matmul(
                pA[slotp][:, 0:width],
                w1h[j % 2][:, d * 512 + hs2 * 128:d * 512 + hs2 * 128 + 128],
                mv, start=(d == 0), stop=(d == NC - 1)),
                waits=wl if d == 0 else [],
                inc=sT if d == NC - 1 else None)
        a_mm[s2][hcx] = last
        w1_used[j] = last
        wl = [(sT, last)]
        if s2 >= 2:
            wl.append((sT, conv_mm[s2 - 2][hcx]))
        pa_ev[slotp] = p.op("scalar", lambda e, s2=s2, hcx=hcx, slotp=slotp, width=width: e.activation(
            out=ab(s2, hcx)[:, 0:width], in_=pA[slotp][:, 0:width],
            func=AF.Identity, bias=0.0, scale=mwc[:, 0:1]), waits=wl, inc=sS)
        a_ev[s2][hcx] = pa_ev[slotp]
        if s2 >= 1:
            halo_d[s2 - 1][hcx] = p.op("scalar", lambda e, s2=s2, hcx=hcx: e.copy(
                ab(s2 - 1, hcx)[:, 256:258], ab(s2, hcx)[:, 0:2]),
                waits=[(sS, a_ev[s2][hcx])], inc=sS)

    def emit_conv_chunk(s, hcx):
        wl = [(sS, halo_d[s][hcx]), (sG, dg_ld)]
        if pc_ev[0]:
            wl.append((sS, pc_ev[0]))
        if s >= 1:
            wl.append((sV, ntp_d[s - 1][hcx]))
            wl.append((sV, apar_d[s - 1][hcx // 8]))
        last = 0
        for k in range(3):
            last = p.op("tensor", lambda e, s=s, hcx=hcx, k=k: e.matmul(
                pC[:, 0:256], dg(k, hcx), ab(s, hcx)[:, k:k + 256],
                start=(k == 0), stop=(k == 2)),
                waits=wl if k == 0 else [],
                inc=sT if k == 2 else None)
        conv_mm[s][hcx] = last
        silu_d[s][hcx] = p.op("scalar", lambda e, s=s, hcx=hcx: e.activation(
            out=H2[:, hcx * 256:(hcx + 1) * 256], in_=pC[:, 0:256],
            func=AF.Silu, bias=cbs[:, hcx:hcx + 1], scale=1.0),
            waits=[(sT, last)], inc=sS)
        pc_ev[0] = silu_d[s][hcx]

    def emit_mm2(sm):
        par = sm % 2
        x2_ld = [0, 0]
        for tsl in range(2):
            t = sm * 2 + tsl
            wl = [(sV, x2re_used[tsl])] if x2re_used[tsl] else []
            x2_ld[tsl] = p.op("sync", lambda e, t=t, tsl=tsl: e.dma_start(
                out=x2re[tsl][:, :], in_=dx2[t * 128:(t + 1) * 128, :]),
                waits=wl, inc=sX2[tsl], amt=16)
        last = 0
        for cc in range(0, HCH, 4):
            b = (cc // 4) % 2
            wl = [(sT, w2_used[b])] if w2_used[b] else []
            w2_ld = p.op("sync", lambda e, cc=cc, b=b: e.dma_start(
                out=w2b[b][:, :], in_=dw2b[:, cc * 1024:(cc + 4) * 1024]),
                waits=wl, inc=sW2[b], amt=16)
            for hcx in range(cc, cc + 4):
                qq = hcx % 4
                for tsl in range(2):
                    for ofs in range(2):
                        slot = tsl * 2 + ofs
                        wl = []
                        if hcx == cc and tsl == 0 and ofs == 0:
                            wl.append((sW2[b], w2_ld))
                            wl.append((sV, h2q_d[sm][cc + 3]))
                        if hcx == 0 and pm_ev[slot]:
                            wl.append((sV, pm_ev[slot]))
                        fin = (hcx == cc + 3 and tsl == 1 and ofs == 1)
                        last = p.op("tensor", lambda e, sm=sm, hcx=hcx, tsl=tsl, ofs=ofs, b=b, qq=qq, slot=slot: e.matmul(
                            pM[slot][:, :],
                            H2q[sm % 2][:, hcx * 256 + tsl * 128:hcx * 256 + tsl * 128 + 128],
                            w2b[b][:, qq * 1024 + ofs * 512:qq * 1024 + ofs * 512 + 512],
                            start=(hcx == 0), stop=(hcx == HCH - 1)),
                            waits=wl, inc=sT if fin else None)
            w2_used[b] = last
        mm2_last[sm] = last
        for tsl in range(2):
            for ofs in range(2):
                slot = tsl * 2 + ofs
                wl = [(sT, mm2_last[sm]), (sV, sc2_d[par])]
                if x2re_used[tsl]:
                    wl.append((sV, x2re_used[tsl]))
                pm_ev[slot] = p.op("vector", lambda e, tsl=tsl, ofs=ofs, slot=slot, par=par: e.tensor_scalar(
                    out=out2[tsl][:, ofs * 512:(ofs + 1) * 512], in0=pM[slot][:, :],
                    scalar1=sc2b[par][:, tsl:tsl + 1], scalar2=None, op0=ALU.mult),
                    waits=wl, inc=sV)
            p.op("vector", lambda e, tsl=tsl: e.bn_stats(stat[:, 0:6], out2[tsl][:, 0:512]),
                 waits=[(sV, pm_ev[tsl * 2 + 1])])
            nsb = p.op("vector", lambda e, tsl=tsl: e.bn_stats(stat[:, 6:12],
                                                               out2[tsl][:, 512:1024]))
            nag = p.op("vector", lambda e: e.bn_aggr(aggr[:], stat[:]),
                       waits=[(sV, nsb)], inc=sV)
            nsq = p.op("scalar", lambda e: e.activation(out=scol[:], in_=aggr[:, 1:2],
                                                        func=AF.Sqrt, bias=epsc[:], scale=1.0),
                       waits=[(sV, nag)], inc=sS)
            nrcp5 = p.op("vector", lambda e: e.reciprocal(rcol[:], scol[:]),
                         waits=[(sS, nsq)])
            napl4 = p.op("vector", lambda e, tsl=tsl: e.tensor_scalar(
                out=out2[tsl][:], in0=out2[tsl][:], scalar1=aggr[:, 0:1], scalar2=rcol[:],
                op0=ALU.subtract, op1=ALU.mult), waits=[(sV, nrcp5)])
            wl = [(sX2[tsl], x2_ld[tsl]), (sV, napl4)]
            if out_dma[tsl]:
                wl.append((sOut[tsl], out_dma[tsl]))
            x2re_used[tsl] = p.op("vector", lambda e, tsl=tsl: e.tensor_add(
                x4b[tsl][:], out2[tsl][:], x2re[tsl][:]), waits=wl, inc=sV)
            t = sm * 2 + tsl
            out_dma[tsl] = p.op("sync", lambda e, t=t, tsl=tsl: e.dma_start(
                out=dout[t * 128:(t + 1) * 128, :], in_=x4b[tsl][:, :]),
                waits=[(sV, x2re_used[tsl])], inc=sOut[tsl], amt=16)

    emit_x3ld(0)
    emit_x3ld(1)
    emit_x3ld(2)
    for s2 in (0, 1):
        for hcx in range(HCH):
            if hcx % 4 == 0:
                emit_w1ld(s2, hcx // 4)
            emit_A_chunk(s2, hcx)

    for s in range(NS):
        s2 = s + 2
        if s + 3 <= NS:
            emit_x3ld(s + 3)
        for hcx in range(HCH):
            emit_conv_chunk(s, hcx)
            if s2 <= NS:
                if hcx % 4 == 0:
                    emit_w1ld(s2, hcx // 4)
                emit_A_chunk(s2, hcx)
            if hcx % 8 == 7:
                g3 = hcx // 8
                wl = [(sS, silu_d[s][hcx])]
                if g3 == 0 and vcomb[0]:
                    wl.append((sV, vcomb[0]))
                apar_d[s][g3] = p.op("vector", lambda e, g3=g3: e.reduce_max(
                    apar[:, g3 * 256:(g3 + 1) * 256],
                    H2.ap().rearrange("p (c n) -> p n c", c=HCH)[:, :, g3 * 8:(g3 + 1) * 8],
                    axis=AXX, apply_absolute_value=True), waits=wl, inc=sV)
        cA = p.op("vector", lambda e: e.tensor_tensor(
            amaxC[:, :], apar[:, 0:256], apar[:, 256:512], ALU.max),
            waits=[(sV, apar_d[s][1]), (sT, trans_last[0])], inc=sV)
        cB = p.op("vector", lambda e: e.tensor_tensor(
            apar[:, 768:1024], apar[:, 512:768], apar[:, 768:1024], ALU.max),
            waits=[(sV, apar_d[s][3])], inc=sV)
        cF = p.op("vector", lambda e: e.tensor_tensor(
            amaxC[:, :], amaxC[:, :], apar[:, 768:1024], ALU.max),
            waits=[(sV, cA), (sV, cB)], inc=sV)
        vcomb[0] = cF
        red = 0
        for tt in range(2):
            wl = [(sV, cF), (sV, sxB_ev[0])] if tt == 0 else [(sV, red)]
            ntr = p.op("tensor", lambda e, tt=tt: e.transpose(
                pX[:, 0:128], amaxC[:, tt * 128:(tt + 1) * 128], idenf[:]),
                waits=wl, inc=sT)
            red = p.op("vector", lambda e, tt=tt: e.reduce_max(
                v2[:, tt:tt + 1], pX[0:128, 0:128], axis=AXX),
                waits=[(sT, ntr)], inc=sV)
            trans_last[0] = ntr
        ncl5 = p.op("vector", lambda e: e.tensor_scalar(out=v2[:, 0:2], in0=v2[:, 0:2],
                                                        scalar1=1e-5, scalar2=None,
                                                        op0=ALU.max),
                    waits=[(sV, red), (sT, ktrans_last[0])], inc=sV)
        p.op("vector", lambda e: e.reciprocal(v2[:, 2:3], v2[:, 0:1]), waits=[(sV, ncl5)])
        nrB = p.op("vector", lambda e: e.reciprocal(v2[:, 3:4], v2[:, 1:2]))
        nm7 = p.op("vector", lambda e: e.tensor_scalar(out=v2[:, 2:4], in0=v2[:, 2:4],
                                                       scalar1=127.0, scalar2=None,
                                                       op0=ALU.mult),
                   waits=[(sV, nrB)], inc=sV)
        nv2 = p.op("vector", lambda e: e.tensor_scalar(out=v2[:, 0:2], in0=v2[:, 0:2],
                                                       scalar1=1.0 / 127.0, scalar2=None,
                                                       op0=ALU.mult), inc=sV)
        for tsl in range(2):
            sc2_d[s % 2] = p.op("vector", lambda e, tsl=tsl, s=s: e.tensor_tensor(
                sc2b[s % 2][:, tsl:tsl + 1], v2[:, tsl:tsl + 1], mwc[:, 1:2], ALU.mult),
                waits=[(sV, nv2)], inc=sV)
        prev_ev = 0
        for k in range(2):
            wl = [(sV, nm7)]
            if prev_ev:
                wl.append((sV, prev_ev))
            ntr = p.op("tensor", lambda e, k=k: e.transpose(
                pX[0:1, 128:256], v2[:, 2 + k:3 + k], idenf[:]), waits=wl, inc=sT)
            prev_ev = p.op("vector", lambda e, k=k: e.tensor_copy(
                sxr[0:1, k * 128:(k + 1) * 128], pX[0:1, 128:256]),
                waits=[(sT, ntr)], inc=sV)
            ktrans_last[0] = ntr
        nb = p.op("tensor", lambda e: e.matmul(pX[:, 256:512], ones128[:, :],
                                               sxr[0:1, :], start=True, stop=True),
                  waits=[(sV, prev_ev)], inc=sT)
        wl = [(sT, nb)]
        if s >= 1:
            wl.append((sV, ntp_d[s - 1][HCH - 1]))
        sxB_ev[0] = p.op("vector", lambda e: e.tensor_copy(sxB[:, :], pX[0:128, 256:512]),
                         waits=wl, inc=sV)
        for h0 in range(0, HCH, 2):
            ntl = [0, 0]
            nuu = [0, 0]
            for i in range(2):
                hcx = h0 + i
                wl = [(sV, sxB_ev[0])]
                if hcx >= 2:
                    wl.append((sV, h2q_d[s][hcx - 2]))
                ntl[i] = p.op("vector", lambda e, hcx=hcx, i=i: e.tensor_tensor(
                    tland[i][:, :], H2[:, hcx * 256:(hcx + 1) * 256], sxB[:, :], ALU.mult),
                    waits=wl, inc=sV)
                ntp_d[s][hcx] = ntl[i]
            for i in range(2):
                nuu[i] = p.op("vector", lambda e, i=i: e.tensor_scalar(
                    out=q16b[i][:, :], in0=tland[i][:, :],
                    scalar1=RC, scalar2=None, op0=ALU.add),
                    waits=[(sV, ntl[i])], inc=sV)
            for i in range(2):
                hcx = h0 + i
                wl = [(sV, nuu[i])]
                if s >= 2:
                    wl.append((sT, mm2_last[s - 2]))
                h2q_d[s][hcx] = p.op("vector", lambda e, s=s, hcx=hcx, i=i: e.tensor_scalar(
                    out=H2q[s % 2][:, hcx * 256:(hcx + 1) * 256], in0=q16b[i][:, :],
                    scalar1=RD, scalar2=None, op0=ALU.subtract), waits=wl, inc=sV)
        if s >= 1:
            emit_mm2(s - 1)
    emit_mm2(NS - 1)
    if DBG:
        p.op("sync", lambda e: e.dma_start(out=dbgS[:, :], in_=dbgs[:, :]), inc=sD, amt=16)
    p.op("sync", lambda e: e.nop(), waits=[
        (sOut[0], sOut[0].n), (sOut[1], sOut[1].n),
        (sX[0], sX[0].n), (sX[1], sX[1].n), (sX2[0], sX2[0].n), (sX2[1], sX2[1].n)])
    p.op("gpsimd", lambda e: e.nop(), waits=[
        (sG, sG.n), (sGw[0], sGw[0].n), (sGw[1], sGw[1].n),
        (sW2[0], sW2[0].n), (sW2[1], sW2[1].n)])
    run_block(nc, p, hwm, "b5_ffn")

    free(*list(cms))
    for s in (sP, sS, sV, sT, *sW2[::-1], *sGw[::-1], *sGs[::-1], sGk, sGq,
              *sOut[::-1], *sSp[::-1], *sW[::-1], *sX2[::-1], *sX[::-1], sG, sD):
        s._cm.__exit__(None, None, None)
    return nc


_NC_CACHE = {}


def _get_nc():
    key = "nc"
    if key not in _NC_CACHE:
        _NC_CACHE[key] = build_nc()
    return _NC_CACHE[key]


def kernel(L0_attn_x, L0_attn_wq, L0_attn_bq, L0_attn_wk, L0_attn_bk,
           L0_attn_wv, L0_attn_bv, L0_attn_wo, L0_attn_bo,
           L0_ffn_w1, L0_ffn_conv_w, L0_ffn_conv_b, L0_ffn_w2):
    x = np.ascontiguousarray(np.asarray(L0_attn_x, dtype=np.float32))
    B = x.shape[0]
    f32 = lambda a: np.ascontiguousarray(np.asarray(a, dtype=np.float32))
    wq, wk, wv, wo = f32(L0_attn_wq), f32(L0_attn_wk), f32(L0_attn_wv), f32(L0_attn_wo)
    bq = np.ascontiguousarray(f32(L0_attn_bq).reshape(NC, 128).T)
    bk = np.ascontiguousarray(f32(L0_attn_bk).reshape(NC, 128).T)
    bv = f32(L0_attn_bv).reshape(1, D)
    bo = f32(L0_attn_bo).reshape(1, D)
    w1t = np.ascontiguousarray(f32(L0_ffn_w1).T)
    w2t = np.ascontiguousarray(f32(L0_ffn_w2).T)
    cw = f32(L0_ffn_conv_w).reshape(DH, 3)
    diag3 = np.zeros((128, 3, HCH, 128), dtype=np.float32)
    idx = np.arange(128)
    for k in range(3):
        for c in range(HCH):
            diag3[idx, k, c, idx] = cw[c * 128 + idx, k]
    diag3 = diag3.reshape(128, 3 * HCH * 128)
    convb = np.ascontiguousarray(f32(L0_ffn_conv_b).reshape(HCH, 128).T)
    pp = np.arange(128)[:, None]
    ff = np.arange(384)[None, :]
    mask = ((pp <= ff) & (ff <= pp + 256)).astype(np.float32)
    iden = np.eye(128, dtype=np.float32)
    hv = np.zeros((128, 2), dtype=np.float32)
    hv[:64, 0] = 1.0
    hv[64:, 1] = 1.0
    ones128 = np.ones((1, 128), dtype=np.float32)
    onescol = np.ones((128, 1), dtype=np.float32)

    nc = _get_nc()
    common = dict(wq=wq, wk=wk, wv=wv, wo=wo, bq=bq, bk=bk, bv=bv, bo=bo,
                  w1t=w1t, w2t=w2t, diag3=diag3, convb=convb, mask01=mask,
                  iden=iden, halves=hv, ones128=ones128, onescol=onescol)
    in_maps = [dict(common, x=np.ascontiguousarray(x[i])) for i in range(B)]
    res = run_bass_kernel_spmd(nc, in_maps, core_ids=list(range(B)))
    out = np.stack([np.asarray(r["out"], dtype=np.float32) for r in res.results], axis=0)
    return out


# version 17
# speedup vs baseline: 1.0485x; 1.0485x over previous
import os
import numpy as np
import concourse.bass as bass
import concourse.mybir as mybir
from concourse.bass_utils import run_bass_kernel_spmd

F32 = mybir.dt.float32
BF16 = mybir.dt.bfloat16
I16 = mybir.dt.int16
I8 = mybir.dt.int8
AF = mybir.ActivationFunctionType
ALU = mybir.AluOpType
AXX = mybir.AxisListType.X

_SIM = os.environ.get("BASS_SIM_FP64") == "1"
RC = 127.5 if _SIM else 128.0
RD = 127.0 if _SIM else 128.0

T = 2048
D = 1024
NT = T // 128
NC = D // 128
H = 16
NJ = 16
DH = 4096
HCH = 32
TS = 256
NS = T // TS
EPS = 1e-5

ENGS = ("sync", "tensor", "vector", "scalar", "gpsimd")


class Sem:
    def __init__(self, nc, name):
        self._cm = nc.semaphore(name)
        self.h = self._cm.__enter__()
        self.n = 0
        self.name = name


class Plan:
    def __init__(self):
        self.items = {e: [] for e in ENGS}
        self.incs = {e: {} for e in ENGS}

    AUTO = {}

    def op(self, eng, emit, waits=(), inc=None, amt=1):
        w = [(s, v) for (s, v) in waits if v and v > 0]
        if inc is None and eng in self.AUTO:
            inc = self.AUTO[eng]
            amt = 1
        if inc is not None:
            inc.n += amt
            self.incs[eng][inc.name] = inc
        self.items[eng].append((w, emit, inc, amt))
        return inc.n if inc is not None else None

    def replay(self, eng, e, hwm):
        for w, emit, inc, amt in self.items[eng]:
            for s, v in w:
                if hwm[eng].get(s.name, 0) < v:
                    e.wait_ge(s.h, v)
                    hwm[eng][s.name] = v
            ins = emit(e)
            if inc is not None:
                ins.then_inc(inc.h, amt)
        for s in self.incs[eng].values():
            if hwm[eng].get(s.name, 0) < s.n:
                e.wait_ge(s.h, s.n)
                hwm[eng][s.name] = s.n


PHASES = []
_PHTOT = {e: 0 for e in ENGS}


def run_block(nc, plan, hwm, phase=""):
    for e in ENGS:
        _PHTOT[e] += len(plan.items[e])
    PHASES.append((phase, dict(_PHTOT)))
    with nc.Block() as blk:
        @blk.sync
        def _(e):
            plan.replay("sync", e, hwm)

        @blk.tensor
        def _(e):
            plan.replay("tensor", e, hwm)

        @blk.vector
        def _(e):
            plan.replay("vector", e, hwm)

        @blk.scalar
        def _(e):
            plan.replay("scalar", e, hwm)

        @blk.gpsimd
        def _(e):
            plan.replay("gpsimd", e, hwm)


def build_nc():
    PHASES.clear()
    for e in ENGS:
        _PHTOT[e] = 0
    nc = bass.Bass("TRN2", target_bir_lowering=False, debug=False,
                   detect_race_conditions=False)

    dx = nc.dram_tensor("x", [T, D], F32, kind="ExternalInput").ap()
    dw = {k: nc.dram_tensor(k, [D, D], F32, kind="ExternalInput").ap()
          for k in ("wq", "wk", "wv", "wo")}
    dbq = nc.dram_tensor("bq", [128, NC], F32, kind="ExternalInput").ap()
    dbk = nc.dram_tensor("bk", [128, NC], F32, kind="ExternalInput").ap()
    dbv = nc.dram_tensor("bv", [1, D], F32, kind="ExternalInput").ap()
    dbo = nc.dram_tensor("bo", [1, D], F32, kind="ExternalInput").ap()
    dw1t = nc.dram_tensor("w1t", [D, DH], F32, kind="ExternalInput").ap()
    dw2t = nc.dram_tensor("w2t", [DH, D], F32, kind="ExternalInput").ap()
    ddiag = nc.dram_tensor("diag3", [128, 3 * HCH * 128], F32, kind="ExternalInput").ap()
    dcb = nc.dram_tensor("convb", [128, HCH], F32, kind="ExternalInput").ap()
    dmask = nc.dram_tensor("mask01", [128, 384], F32, kind="ExternalInput").ap()
    diden = nc.dram_tensor("iden", [128, 128], F32, kind="ExternalInput").ap()
    dhalves = nc.dram_tensor("halves", [128, 2], F32, kind="ExternalInput").ap()
    dones128 = nc.dram_tensor("ones128", [1, 128], F32, kind="ExternalInput").ap()
    donescol = nc.dram_tensor("onescol", [128, 1], F32, kind="ExternalInput").ap()
    dout = nc.dram_tensor("out", [T, D], F32, kind="ExternalOutput").ap()
    DBG = os.environ.get("KDEBUG") == "1"
    iknd = "ExternalOutput" if DBG else "Internal"
    dx2 = nc.dram_tensor("x2sp", [T, D], BF16, kind=iknd).ap()
    dw1b = nc.dram_tensor("w1b16", [4, 128, NC * 1024], BF16, kind=iknd).ap()
    dw2b = nc.dram_tensor("w2b16", [128, HCH * 1024], BF16, kind=iknd).ap()
    dx3t = nc.dram_tensor("x3tsp", [NC, 128, T + 2], BF16, kind=iknd).ap()
    dbgO = (nc.dram_tensor("dbgO", [T, D], BF16, kind="ExternalOutput").ap()
            if DBG else None)
    dbgS = (nc.dram_tensor("dbgS", [128, 80], F32, kind="ExternalOutput").ap()
            if DBG else None)

    sD = Sem(nc, "sD")
    sG = Sem(nc, "sG")
    sX = [Sem(nc, "sX0"), Sem(nc, "sX1")]
    sX2 = [Sem(nc, "sX2_0"), Sem(nc, "sX2_1")]
    sW = [Sem(nc, f"sW{i}") for i in range(4)]
    sSp = [Sem(nc, f"sSp{i}") for i in range(4)]
    sOut = [Sem(nc, "sOut0"), Sem(nc, "sOut1")]
    sGq = Sem(nc, "sGq")
    sGk = Sem(nc, "sGk")
    sGs = [Sem(nc, "sGs0"), Sem(nc, "sGs1")]
    sGw = [Sem(nc, "sGw0"), Sem(nc, "sGw1")]
    sW2 = [Sem(nc, "sW2_0"), Sem(nc, "sW2_1")]
    sT = Sem(nc, "sT")
    sV = Sem(nc, "sV")
    sS = Sem(nc, "sS")
    sP = Sem(nc, "sP")
    Plan.AUTO = {"vector": sV, "scalar": sS, "gpsimd": sP}
    hwm = {e: {} for e in ENGS}
    cms = {}
    aseq = {"n": 0}

    def alloc(name, shape, dt, space="sbuf", side=None):
        cm = (nc.sbuf_tensor(name, shape, dt, side=side) if space == "sbuf"
              else nc.psum_tensor(name, shape, dt))
        h = cm.__enter__()
        aseq["n"] += 1
        cms[name] = (aseq["n"], cm)
        return h

    def free(*names):
        for nm in sorted(names, key=lambda nm: -cms[nm][0]):
            cms.pop(nm)[1].__exit__(None, None, None)

    bqs = alloc("bqs", [128, NC], F32)
    bks = alloc("bks", [128, NC], F32)
    bvB = alloc("bvB", [128, D], F32, side="right")
    boB = alloc("boB", [128, D], F32, side="right")
    bvrow = alloc("bvrow", [1, D], F32, side="right")
    borow = alloc("borow", [1, D], F32, side="right")
    cbs = alloc("cbs", [128, HCH], F32)
    mask01 = alloc("sb_mask01", [128, 384], BF16)
    iden = alloc("sb_iden", [128, 128], BF16)
    idenf = alloc("sb_idenf", [128, 128], F32)
    halves = alloc("sb_halves", [128, 2], F32)
    ones128 = alloc("sb_ones128", [1, 128], F32)
    onescol = alloc("sb_onescol", [128, 1], F32)
    epsc = alloc("epsc", [128, 1], F32)
    negC = alloc("negC", [128, H], F32)
    eC = alloc("eC", [128, H], F32)
    mwc = alloc("mwc", [128, 2], F32)
    swc = alloc("swc", [128, 2], F32)
    stat = alloc("stat", [128, 12], F32)
    aggr = alloc("aggr", [128, 2], F32)
    scol = alloc("scol", [128, 1], F32)
    rcol = alloc("rcol", [128, 1], F32)
    wsum = alloc("wsum", [128, 64], F32)
    mrow = alloc("mrow", [1, 64], F32)
    mtot = alloc("mtot", [1, 2], F32)
    zrow = alloc("zrow", [128, 1], BF16)
    qmx = alloc("qmx", [128, NC], F32)
    kmx = alloc("kmx", [128, NC], F32)
    qkp = alloc("qkp", [128, NC], F32)
    c2x8 = alloc("c2x8", [2, NC], F32)
    crow = alloc("crow", [1, 16], F32)
    dbgs = alloc("dbgs", [128, 80], F32)

    p = Plan()
    for (dst, src) in ((bqs, dbq), (bks, dbk), (bvrow, dbv), (borow, dbo),
                       (cbs, dcb), (halves, dhalves), (ones128, dones128),
                       (onescol, donescol), (idenf, diden)):
        p.op("sync", lambda e, d=dst, s=src: e.dma_start(out=d[:], in_=s[:, :]),
             inc=sD, amt=16)
    p.op("gpsimd", lambda e: e.dma_start(out=mask01[:], in_=dmask[:, :]), inc=sG, amt=16)
    p.op("gpsimd", lambda e: e.dma_start(out=iden[:], in_=diden[:, :]), inc=sG, amt=16)
    p.op("vector", lambda e: e.memset(epsc[:], EPS), inc=sV)
    nzr = p.op("vector", lambda e: e.memset(zrow[:], 0.0), inc=sV)
    for c in range(NC):
        p.op("sync", lambda e, c=c: e.dma_start(out=dx3t[c, :, 0:1], in_=zrow[:, 0:1],
             allow_slow_non_contiguous=True), waits=[(sV, nzr)], inc=sD, amt=16)
        p.op("sync", lambda e, c=c: e.dma_start(out=dx3t[c, :, T + 1:T + 2], in_=zrow[:, 0:1],
             allow_slow_non_contiguous=True), inc=sD, amt=16)
    p.op("sync", lambda e: e.nop(), waits=[(sD, sD.n)])
    p.op("gpsimd", lambda e: e.nop(), waits=[(sG, sG.n)])
    run_block(nc, p, hwm, "b0_const")

    B1 = [alloc(f"B1_{c}", [128, T], BF16) for c in range(NC)]
    QT = [alloc(f"QT{c}", [128, T], BF16) for c in range(NC)]
    KT = [alloc(f"KT{c}", [128, T], BF16) for c in range(NC)]
    VA = [alloc(f"VA{j}", [128, H * 65], BF16) for j in range(NJ + 2)]

    def vaug(j):
        return VA[j + 1]

    xin = [alloc(f"xin_{i}", [128, D], F32) for i in range(2)]
    x1b = [alloc(f"x1b_{i}", [128, D], BF16) for i in range(2)]
    ptr = [alloc(f"ptr_{i}", [128, 1024], BF16, "psum") for i in range(2)]
    pbc = [alloc(f"pbc_{i}", [128, 512], F32, "psum") for i in range(2)]

    p = Plan()
    for j in (-1, NJ):
        p.op("vector", lambda e, j=j: e.memset(
            vaug(j).ap().rearrange("p (h c) -> p h c", c=65)[:, :, 0:64], 0.0))
    vinit = 0
    for j in range(-1, NJ + 1):
        vinit = p.op("vector", lambda e, j=j: e.memset(
            vaug(j).ap().rearrange("p (h c) -> p h c", c=65)[:, :, 64:65], 1.0), inc=sV)

    bc_ev = 0
    for i, (dst, src) in enumerate(((bvB, bvrow), (bvB, bvrow), (boB, borow), (boB, borow))):
        off = (i % 2) * 512
        wl = [(sV, bc_ev)] if i >= 2 else []
        n = p.op("tensor", lambda e, s=src, off=off, i=i: e.matmul(
            pbc[i % 2][:, :], ones128[:, :], s[0:1, off:off + 512], start=True, stop=True),
            waits=wl, inc=sT)
        bc_ev = p.op("vector", lambda e, d=dst, off=off, i=i: e.tensor_copy(
            d[:, off:off + 512], pbc[i % 2][:, :]), waits=[(sT, n)], inc=sV)

    ln_apply = [0] * NT
    tr_done = [[0] * NC for _ in range(NT)]
    ev_done = [[0] * NC for _ in range(NT)]
    sq_done = [0] * NT
    for t in range(NT):
        wl = [(sV, ln_apply[t - 2])] if t >= 2 else []
        xload = p.op("sync", lambda e, t=t: e.dma_start(
            out=xin[t % 2][:], in_=dx[t * 128:(t + 1) * 128, :]), waits=wl,
            inc=sX[t % 2], amt=16)
        wl = [(sX[t % 2], xload)]
        if t >= 1:
            wl.append((sS, sq_done[t - 1]))
        p.op("vector", lambda e, t=t: e.bn_stats(stat[:, 0:6], xin[t % 2][:, 0:512]),
             waits=wl)
        nsb = p.op("vector", lambda e, t=t: e.bn_stats(stat[:, 6:12], xin[t % 2][:, 512:1024]))
        nag = p.op("vector", lambda e: e.bn_aggr(aggr[:], stat[:]), waits=[(sV, nsb)], inc=sV)
        sq_done[t] = p.op("scalar", lambda e: e.activation(
            out=scol[:], in_=aggr[:, 1:2], func=AF.Sqrt, bias=epsc[:], scale=1.0),
            waits=[(sV, nag)], inc=sS)
        nrc1 = p.op("vector", lambda e: e.reciprocal(rcol[:], scol[:]),
                    waits=[(sS, sq_done[t])])
        wl = [(sV, nrc1)]
        if t >= 2:
            wl.append((sT, tr_done[t - 2][NC - 1]))
        ln_apply[t] = p.op("vector", lambda e, t=t: e.tensor_scalar(
            out=x1b[t % 2][:], in0=xin[t % 2][:], scalar1=aggr[:, 0:1], scalar2=rcol[:],
            op0=ALU.subtract, op1=ALU.mult), waits=wl, inc=sV)
        for c in range(NC):
            k = t * NC + c
            wl = [(sV, ln_apply[t])]
            if k >= 2:
                wl.append((sS, ev_done[(k - 2) // NC][(k - 2) % NC]))
            tr_done[t][c] = p.op("tensor", lambda e, t=t, c=c, k=k: e.transpose(
                ptr[k % 2][:, 0:128], x1b[t % 2][:, c * 128:(c + 1) * 128], iden[:]),
                waits=wl, inc=sT)
            ev_done[t][c] = p.op("scalar", lambda e, t=t, c=c, k=k: e.copy(
                B1[c][:, t * 128:(t + 1) * 128], ptr[k % 2][:, 0:128]),
                waits=[(sT, tr_done[t][c])], inc=sS)
    run_block(nc, p, hwm, "b1_ln1")
    free("xin_0", "xin_1", "x1b_0", "x1b_1", "ptr_0", "ptr_1", "pbc_0", "pbc_1")
    free("bvrow", "borow")

    wsb = [[alloc(f"wsb{s}_{c}", [128, D], BF16) for c in range(NC)] for s in range(2)]
    wstage = [alloc(f"wstage_{i}", [128, 1024], F32) for i in range(2)]
    pq = [alloc(f"pq_{i}", [128, 512], F32, "psum") for i in range(4)]
    pms = alloc("pms", [128, 512], F32, "psum")

    p = Plan()
    wq_ld = [0] * NC
    wk_ld = [0] * NC
    wv_ld = [0] * NC
    for c in range(NC):
        wq_ld[c] = p.op("gpsimd", lambda e, c=c: e.dma_start(
            out=wsb[0][c][:], in_=dw["wq"][c * 128:(c + 1) * 128, :]), inc=sGq, amt=16)
    for c in range(NC):
        wk_ld[c] = p.op("gpsimd", lambda e, c=c: e.dma_start(
            out=wsb[1][c][:], in_=dw["wk"][c * 128:(c + 1) * 128, :]), inc=sGk, amt=16)

    red_done = []
    piece_srcs = []
    for ch in range(NC):
        for h4 in range(4):
            piece_srcs.append((dw1t[ch * 128:(ch + 1) * 128, h4 * 1024:(h4 + 1) * 1024], 1024))
    for ch in range(HCH):
        piece_srcs.append((dw2t[ch * 128:(ch + 1) * 128, :], 1024))
    for idx, (src, width) in enumerate(piece_srcs):
        slot = idx % 2
        wl = [(sV, red_done[idx - 2])] if idx >= 2 else []
        ld = p.op("sync", lambda e, src=src, slot=slot, width=width: e.dma_start(
            out=wstage[slot][:, 0:width], in_=src), waits=wl, inc=sW[slot], amt=16)
        red_done.append(p.op("vector", lambda e, slot=slot, idx=idx, width=width: e.reduce_sum(
            wsum[:, idx:idx + 1], wstage[slot][:, 0:width], axis=AXX,
            apply_absolute_value=True), waits=[(sW[slot], ld)], inc=sV))

    def proj_qk(wset, wld, bias_s, OUT, slot_ev, evac_eng, wsem):
        for oc in range(NC):
            for st in range(4):
                slot = (oc * 4 + st) % 4
                wl = [(wsem, wld[NC - 1])]
                if slot_ev[slot]:
                    wl.append((evac_eng, slot_ev[slot]))
                last = 0
                for ic in range(NC):
                    last = p.op("tensor", lambda e, oc=oc, st=st, ic=ic, slot=slot, wset=wset: e.matmul(
                        pq[slot][:, :], wsb[wset][ic][:, oc * 128:(oc + 1) * 128],
                        B1[ic][:, st * 512:(st + 1) * 512],
                        start=(ic == 0), stop=(ic == NC - 1)),
                        waits=wl if ic == 0 else [],
                        inc=sT if ic == NC - 1 else None)
                slot_ev[slot] = p.op("scalar", lambda e, oc=oc, st=st, slot=slot, OUT=OUT, bias_s=bias_s: e.activation(
                    out=OUT[oc][:, st * 512:(st + 1) * 512], in_=pq[slot][:, :],
                    func=AF.Identity, bias=bias_s[:, oc:oc + 1], scale=1.0),
                    waits=[(sT, last)], inc=sS)
        return slot_ev

    slot_ev = [0, 0, 0, 0]
    slot_ev = proj_qk(0, wq_ld, bqs, QT, slot_ev, sS, sGq)
    for c in range(NC):
        wv_ld[c] = p.op("gpsimd", lambda e, c=c: e.dma_start(
            out=wsb[0][c][:], in_=dw["wv"][c * 128:(c + 1) * 128, :]),
            waits=[(sS, max(slot_ev))], inc=sGq, amt=16)
    slot_ev = proj_qk(1, wk_ld, bks, KT, slot_ev, sS, sGk)

    n1 = p.op("tensor", lambda e: e.matmul(pms[0:1, 0:64], onescol[:, :], wsum[:, 0:64],
                                           start=True, stop=True),
              waits=[(sV, red_done[-1])], inc=sT)
    n2 = p.op("scalar", lambda e: e.copy(mrow[0:1, 0:64], pms[0:1, 0:64]),
              waits=[(sT, n1)], inc=sS)
    p.op("vector", lambda e: e.reduce_sum(mtot[0:1, 0:1], mrow[0:1, 0:32], axis=AXX),
         waits=[(sS, n2)])
    nr2 = p.op("vector", lambda e: e.reduce_sum(mtot[0:1, 1:2], mrow[0:1, 32:64], axis=AXX))
    n3 = p.op("vector", lambda e: e.tensor_scalar(out=mtot[:], in0=mtot[:],
                                             scalar1=1.0 / (D * DH), scalar2=1e-5,
                                             op0=ALU.mult, op1=ALU.max),
              waits=[(sV, nr2)], inc=sV)
    n4 = p.op("tensor", lambda e: e.matmul(pms[:, 0:2], ones128[:, :], mtot[0:1, :],
                                           start=True, stop=True),
              waits=[(sV, n3)], inc=sT)
    n5 = p.op("scalar", lambda e: e.copy(mwc[:], pms[0:128, 0:2]), waits=[(sT, n4)], inc=sS)
    p.op("vector", lambda e: e.reciprocal(swc[:, 0:1], mwc[:, 0:1]), waits=[(sS, n5)])
    n6 = p.op("vector", lambda e: e.reciprocal(swc[:, 1:2], mwc[:, 1:2]), inc=sV)
    if DBG:
        p.op("vector", lambda e: e.tensor_copy(dbgs[:, 32:34], mwc[:, :]),
             waits=[(sV, n6)])
        p.op("vector", lambda e: e.tensor_copy(dbgs[:, 34:36], swc[:, :]))
        p.op("vector", lambda e: e.tensor_copy(dbgs[0:1, 36:60], mrow[0:1, 24:48]))
        p.op("vector", lambda e: e.tensor_copy(dbgs[0:1, 60:62], mtot[0:1, :]))

    prev_s_ev = list(slot_ev)
    slot_vv = [0, 0, 0, 0]
    for t in range(NT):
        for ofs in range(2):
            slot = (t * 2 + ofs) % 4
            wl = [(sGq, wv_ld[NC - 1]), (sV, bc_ev), (sV, vinit)]
            if prev_s_ev[slot]:
                wl.append((sS, prev_s_ev[slot]))
                prev_s_ev[slot] = 0
            if slot_vv[slot]:
                wl.append((sV, slot_vv[slot]))
            last = 0
            for ic in range(NC):
                last = p.op("tensor", lambda e, t=t, ofs=ofs, ic=ic, slot=slot: e.matmul(
                    pq[slot][:, :], B1[ic][:, t * 128:(t + 1) * 128],
                    wsb[0][ic][:, ofs * 512:(ofs + 1) * 512],
                    start=(ic == 0), stop=(ic == NC - 1)),
                    waits=wl if ic == 0 else [],
                    inc=sT if ic == NC - 1 else None)
            slot_vv[slot] = p.op("vector", lambda e, t=t, ofs=ofs, slot=slot: e.scalar_tensor_tensor(
                out=vaug(t).ap().rearrange("p (h c) -> p h c", c=65)[:, ofs * 8:(ofs + 1) * 8, 0:64],
                in0=pq[slot][:, :], scalar=1.0,
                in1=bvB[:, ofs * 512:(ofs + 1) * 512],
                op0=ALU.mult, op1=ALU.add),
                waits=[(sT, last)], inc=sV)

    nred = 0
    kev = max(slot_ev)
    for c in range(NC):
        p.op("vector", lambda e, c=c: e.reduce_max(qmx[:, c:c + 1], QT[c][:, :],
                                                   axis=AXX, apply_absolute_value=True),
             waits=[(sS, kev)] if c == 0 else [])
        nred = p.op("vector", lambda e, c=c: e.reduce_max(kmx[:, c:c + 1], KT[c][:, :],
                                                          axis=AXX, apply_absolute_value=True))
    nq = p.op("vector", lambda e: e.tensor_mul(qkp[:], qmx[:], kmx[:]),
              waits=[(sV, nred)], inc=sV)
    nm = p.op("tensor", lambda e: e.matmul(pms[0:2, 0:8], halves[:, :], qkp[:, :],
                                           start=True, stop=True),
              waits=[(sV, nq), (sS, n5)], inc=sT)
    ne = p.op("scalar", lambda e: e.copy(c2x8[:], pms[0:2, 0:8]), waits=[(sT, nm)], inc=sS)
    nts = p.op("vector", lambda e: e.tensor_scalar(out=c2x8[:], in0=c2x8[:], scalar1=0.125,
                                                   scalar2=-40.0, op0=ALU.mult, op1=ALU.add),
               waits=[(sS, ne)])
    nrl = p.op("vector", lambda e: e.tensor_relu(c2x8[:], c2x8[:]), waits=[(sV, nts)])
    nr = p.op("vector", lambda e: e.tensor_scalar(out=c2x8[:], in0=c2x8[:], scalar1=-1.0,
                                                  scalar2=None, op0=ALU.mult),
              waits=[(sV, nrl)], inc=sV)
    nc1 = p.op("vector", lambda e: e.tensor_copy(crow[0:1, 0:8], c2x8[0:1, :]),
               waits=[(sV, nr)], inc=sV)
    nd = p.op("sync", lambda e: e.dma_start(out=crow[0:1, 8:16], in_=c2x8[1:2, :]),
              waits=[(sV, nr)], inc=sD, amt=16)
    nb1 = p.op("tensor", lambda e: e.matmul(pms[:, 0:8], ones128[:, :], crow[0:1, 0:8],
                                            start=True, stop=True),
               waits=[(sV, nc1), (sD, nd), (sS, ne)], inc=sT)
    nb1e = p.op("scalar", lambda e: e.copy(
        negC.ap().rearrange("p (a b) -> p a b", b=2)[:, :, 0:1], pms[0:128, 0:8]),
        waits=[(sT, nb1)], inc=sS)
    nb2 = p.op("tensor", lambda e: e.matmul(pms[:, 0:8], ones128[:, :], crow[0:1, 8:16],
                                            start=True, stop=True),
               waits=[(sS, nb1e)], inc=sT)
    nb2e = p.op("scalar", lambda e: e.copy(
        negC.ap().rearrange("p (a b) -> p a b", b=2)[:, :, 1:2], pms[0:128, 0:8]),
        waits=[(sT, nb2)], inc=sS)
    nec = p.op("scalar", lambda e: e.activation(out=eC[:], in_=negC[:], func=AF.Exp,
                                                bias=0.0, scale=1.0),
               waits=[(sS, nb2e)], inc=sS)


    run_block(nc, p, hwm, "b2_qkv")
    free(*[f"wsb{s}_{c}" for s in range(2) for c in range(NC)],
         "wstage_0", "wstage_1", "pq_0", "pq_1", "pq_2", "pq_3", "pms")

    B2 = [alloc(f"B2_{t}", [128, D], BF16, side="right") for t in range(NT)]
    expraw = [alloc(f"expraw_{i}", [128, 384], BF16) for i in range(3)]
    expm = [alloc(f"expm_{i}", [128, 384], BF16) for i in range(3)]
    rodc = [alloc(f"rodc_{i}", [128, 1], F32) for i in range(2)]
    psc = [alloc(f"psc_{i}", [128, 512], F32, "psum") for i in range(3)]
    pod = [alloc(f"pod_{i}", [128, 512], F32, "psum") for i in range(4)]

    p = Plan()
    wstage3 = [alloc(f"wstage3_{i}", [128, 1024], F32) for i in range(4)]
    tern_u16 = [alloc(f"tern_u16_{i}", [128, 1024], I16) for i in range(2)]
    tern2b = [alloc(f"tern2b_{i}", [128, 1024], BF16) for i in range(2)]
    tern_jobs = []
    for ch in range(NC):
        for q in range(4):
            tern_jobs.append((dw1t[ch * 128:(ch + 1) * 128, q * 1024:(q + 1) * 1024],
                              dw1b[q, :, ch * 1024:(ch + 1) * 1024], 0))
    for ch in range(HCH):
        tern_jobs.append((dw2t[ch * 128:(ch + 1) * 128, :],
                          dw2b[:, ch * 1024:(ch + 1) * 1024], 1))
    tstate = {"sp": [0, 0, 0, 0]}

    def tern_steps():
        for i0 in range(0, len(tern_jobs), 4):
            picks = [k for k in range(i0, min(i0 + 4, len(tern_jobs)))]
            lds, n1s, n2s = {}, {}, {}
            for k in picks:
                srcA, dst, wi = tern_jobs[k]
                slot = k % 4
                wl = [(sV, n6)]
                if tstate["sp"][slot]:
                    wl.append((sSp[slot], tstate["sp"][slot]))
                lds[k] = p.op("sync", lambda e, srcA=srcA, slot=slot: e.dma_start(
                    out=wstage3[slot][:, :], in_=srcA), waits=wl, inc=sW[slot], amt=16)
                yield
            for j0 in range(0, len(picks), 2):
                pair = picks[j0:j0 + 2]
                for k in pair:
                    wi = tern_jobs[k][2]
                    slot = k % 4
                    wl = [(sW[slot], lds[k])]
                    if tstate.get("u16", [0, 0])[k % 2]:
                        wl.append((sV, tstate["u16"][k % 2]))
                    n1s[k] = p.op("vector", lambda e, slot=slot, wi=wi, k=k: e.tensor_scalar(
                        out=tern_u16[k % 2][:, :], in0=wstage3[slot][:, :],
                        scalar1=swc[:, wi:wi + 1], scalar2=RC, op0=ALU.mult, op1=ALU.add),
                        waits=wl)
                    yield
                for k in pair:
                    slot = k % 4
                    n2s[k] = p.op("vector", lambda e, slot=slot, k=k: e.tensor_scalar(
                        out=wstage3[slot][:, :], in0=tern_u16[k % 2][:, :],
                        scalar1=RD, scalar2=1.0, op0=ALU.subtract, op1=ALU.min),
                        waits=[(sV, n1s[k])])
                    tstate.setdefault("u16", [0, 0])[k % 2] = n2s[k]
                    yield
                for k in pair:
                    slot = k % 4
                    wl = [(sV, n2s[k])]
                    if tstate.get("t2b", [0, 0])[k % 2]:
                        wl.append((sSp[(k - 2) % 4], tstate["t2b"][k % 2]))
                    n2s[k] = p.op("vector", lambda e, slot=slot, k=k: e.tensor_scalar(
                        out=tern2b[k % 2][:, :], in0=wstage3[slot][:, :],
                        scalar1=-1.0, scalar2=None, op0=ALU.max),
                        waits=wl, inc=sV)
                    yield
                for k in pair:
                    slot = k % 4
                    dst = tern_jobs[k][1]
                    tstate["sp"][slot] = p.op("sync", lambda e, dst=dst, slot=slot, k=k: e.dma_start(
                        out=dst, in_=tern2b[k % 2][:, :]), waits=[(sV, n2s[k])],
                        inc=sSp[slot], amt=16)
                    tstate.setdefault("t2b", [0, 0])[k % 2] = tstate["sp"][slot]
                    yield

    tern_gen = tern_steps()

    def win(j):
        q0 = (j - 1) * 128
        q1 = (j + 2) * 128
        c0 = max(0, -q0)
        q0r = max(q0, 0)
        q1r = min(q1, T)
        return q0r, q1r, c0, c0 + (q1r - q0r)

    pod_ev = [0, 0, 0, 0]
    psc_ev = [0, 0, 0]
    expm_use = [0, 0, 0]
    mask_done = [0, 0, 0]
    rodc_use = [0, 0]
    mask_info = {}

    def emit_av(h, j):
        sl = (j + 1) % 3
        msem, nmask = mask_info.pop((h, j))
        last_use = 0
        for s in range(3):
            qb = j - 1 + s
            if qb < 0 or qb >= NJ:
                continue
            wc = qb * 128 - (j - 1) * 128
            first = (j == qb - 1)
            lastc = (j == qb + 1)
            wl = [(msem, nmask)]
            if first and pod_ev[qb % 4]:
                wl.append((sV, pod_ev[qb % 4]))
            last_use = p.op("tensor", lambda e, qb=qb, j=j, h=h, sl=sl, wc=wc, first=first, lastc=lastc: e.matmul(
                pod[qb % 4][:, 0:65], expm[sl][:, wc:wc + 128],
                vaug(j)[:, h * 65:h * 65 + 65], start=first, stop=lastc),
                waits=wl, inc=sT)
            if lastc:
                wl = [(sT, last_use)]
                if rodc_use[qb % 2]:
                    wl.append((sV, rodc_use[qb % 2]))
                nrc = p.op("vector", lambda e, qb=qb: e.reciprocal(
                    rodc[qb % 2][:, :], pod[qb % 4][:, 64:65]), waits=wl, inc=sV)
                pod_ev[qb % 4] = p.op("vector", lambda e, qb=qb, h=h: e.tensor_scalar(
                    out=B2[qb][:, h * 64:(h + 1) * 64], in0=pod[qb % 4][:, 0:64],
                    scalar1=rodc[qb % 2][:, :], scalar2=None, op0=ALU.mult),
                    waits=[(sV, nrc)], inc=sV)
                rodc_use[qb % 2] = pod_ev[qb % 4]
        if last_use:
            expm_use[sl] = last_use

    seq = [(h, j) for h in range(H) for j in range(-1, NJ + 1)]
    for idx, (h, j) in enumerate(seq):
        hc = h // 2
        hr = (h % 2) * 64
        sl = (j + 1) % 3
        q0r, q1r, c0, c1 = win(j)
        pad = (j < 0) or (j >= NJ)
        if not pad:
            wl = [(sS, nec)]
            if psc_ev[sl]:
                wl.append((sS, psc_ev[sl]))
            nmm = p.op("tensor", lambda e, hc=hc, hr=hr, j=j, sl=sl, q0r=q0r, q1r=q1r, c0=c0, c1=c1: e.matmul(
                psc[sl][:, c0:c1], KT[hc][hr:hr + 64, j * 128:(j + 1) * 128],
                QT[hc][hr:hr + 64, q0r:q1r], start=True, stop=True),
                waits=wl, inc=sT)
            wl = [(sT, nmm)]
            if mask_done[sl]:
                wl.append((sP, mask_done[sl]))
            psc_ev[sl] = p.op("scalar", lambda e, h=h, sl=sl, c0=c0, c1=c1: e.activation(
                out=expraw[sl][:, c0:c1], in_=psc[sl][:, c0:c1], func=AF.Exp,
                bias=negC[:, h:h + 1], scale=0.125), waits=wl, inc=sS)
            wl = [(sS, psc_ev[sl])]
            if expm_use[sl]:
                wl.append((sT, expm_use[sl]))
            mask_done[sl] = p.op("gpsimd", lambda e, sl=sl, c0=c0, c1=c1: e.tensor_tensor(
                expm[sl][:, c0:c1], expraw[sl][:, c0:c1], mask01[:, c0:c1], ALU.mult),
                waits=wl, inc=sP)
            mask_info[(h, j)] = (sP, mask_done[sl])
        else:
            wl = [(sS, nec)]
            if expm_use[sl]:
                wl.append((sT, expm_use[sl]))
            nmask = p.op("vector", lambda e, h=h, sl=sl, c0=c0, c1=c1: e.tensor_scalar(
                out=expm[sl][:, c0:c1], in0=mask01[:, c0:c1],
                scalar1=eC[:, h:h + 1], scalar2=None, op0=ALU.mult),
                waits=wl, inc=sV)
            mask_info[(h, j)] = (sV, nmask)
        if idx >= 2:
            emit_av(*seq[idx - 2])
        for _ in range(2):
            next(tern_gen, None)
    emit_av(*seq[-2])
    emit_av(*seq[-1])
    for _ in tern_gen:
        pass
    p.op("sync", lambda e: e.nop(), waits=[(sD, sD.n)] +
         [(s_, s_.n) for s_ in sW] + [(s_, s_.n) for s_ in sSp])
    run_block(nc, p, hwm, "b3_attn")
    free(*[f"QT{c}" for c in range(NC)], *[f"KT{c}" for c in range(NC)],
         *[f"VA{j}" for j in range(NJ + 2)],
         *[f"wstage3_{i}" for i in range(4)], "tern_u16_0", "tern_u16_1",
         "tern2b_0", "tern2b_1",
         "expraw_0", "expraw_1", "expraw_2", "expm_0", "expm_1", "expm_2",
         "rodc_0", "rodc_1",
         "psc_0", "psc_1", "psc_2", "pod_0", "pod_1", "pod_2", "pod_3")

    wosb = [alloc(f"wo_{c}", [128, D], BF16) for c in range(NC)]
    attn = [alloc(f"attn_{i}", [128, D], F32) for i in range(2)]
    x2f = [alloc(f"x2f_{i}", [128, D], F32) for i in range(2)]
    x3f = [alloc(f"x3f_{i}", [128, D], F32) for i in range(2)]
    xre = [alloc(f"xre_{i}", [128, D], F32) for i in range(2)]
    q16 = alloc("q16", [128, D], I16)
    x3q = [alloc(f"x3q_{i}", [128, D], BF16) for i in range(2)]
    x3st = [alloc(f"x3st_{i}", [128, D], BF16) for i in range(2)]
    sxc = alloc("sxc", [128, 1], F32)
    sxic = alloc("sxic", [128, 1], F32)
    ptrO = [alloc(f"ptrO_{i}", [128, 1024], BF16, "psum") for i in range(2)]
    ptrX = [alloc(f"ptrX_{i}", [128, 1024], BF16, "psum") for i in range(2)]
    po4 = [alloc(f"po4_{i}", [128, 512], F32, "psum") for i in range(2)]

    p = Plan()
    if DBG:
        for t in range(NT):
            p.op("sync", lambda e, t=t: e.dma_start(
                out=dbgO[t * 128:(t + 1) * 128, :], in_=B2[t][:, :]), inc=sD, amt=16)
    wo_ld = [0] * NC
    for c in range(NC):
        wo_ld[c] = p.op("gpsimd", lambda e, c=c: e.dma_start(
            out=wosb[c][:], in_=dw["wo"][c * 128:(c + 1) * 128, :]), inc=sGq, amt=16)
    otr = [[0] * NC for _ in range(NT)]
    oev = [[0] * NC for _ in range(NT)]
    po_ev = [0, 0]
    x2sp = [0, 0]
    x3sp = [0, 0]
    apply2 = [0] * NT
    apply3 = [0] * NT
    q_done = [0] * NT
    x3tr = [[0] * NC for _ in range(NT)]
    x3ev = [[0] * NC for _ in range(NT)]
    for t in range(NT):
        for c in range(NC):
            k = t * NC + c
            wl = []
            if k >= 2:
                wl.append((sS, oev[(k - 2) // NC][(k - 2) % NC]))
            otr[t][c] = p.op("tensor", lambda e, t=t, c=c, k=k: e.transpose(
                ptrO[k % 2][:, 0:128], B2[t][:, c * 128:(c + 1) * 128], iden[:]),
                waits=wl, inc=sT)
            oev[t][c] = p.op("scalar", lambda e, t=t, c=c, k=k: e.copy(
                B1[c][:, t * 128:(t + 1) * 128], ptrO[k % 2][:, 0:128]),
                waits=[(sT, otr[t][c])], inc=sS)
        xre_ld = p.op("sync", lambda e, t=t: e.dma_start(
            out=xre[t % 2][:], in_=dx[t * 128:(t + 1) * 128, :]),
            waits=[(sV, apply2[t - 2])] if t >= 2 else [], inc=sX[t % 2], amt=16)
        for ofs in range(2):
            slot = ofs
            wl = [(sS, oev[t][NC - 1]), (sGq, wo_ld[NC - 1])]
            if po_ev[slot]:
                wl.append((sV, po_ev[slot]))
            last = 0
            for c in range(NC):
                last = p.op("tensor", lambda e, t=t, ofs=ofs, c=c, slot=slot: e.matmul(
                    po4[slot][:, :], B1[c][:, t * 128:(t + 1) * 128],
                    wosb[c][:, ofs * 512:(ofs + 1) * 512],
                    start=(c == 0), stop=(c == NC - 1)),
                    waits=wl if c == 0 else [],
                    inc=sT if c == NC - 1 else None)
            wl2 = [(sT, last)]
            if t >= 2:
                wl2.append((sV, apply2[t - 2]))
            po_ev[slot] = p.op("vector", lambda e, t=t, ofs=ofs, slot=slot: e.scalar_tensor_tensor(
                out=attn[t % 2][:, ofs * 512:(ofs + 1) * 512], in0=po4[slot][:, :],
                scalar=1.0, in1=boB[:, ofs * 512:(ofs + 1) * 512],
                op0=ALU.mult, op1=ALU.add), waits=wl2, inc=sV)
        if t >= 1:
            emit_x3t(t - 1)
        p.op("vector", lambda e, t=t: e.bn_stats(stat[:, 0:6], attn[t % 2][:, 0:512]),
             waits=[(sV, po_ev[1])])
        nsb = p.op("vector", lambda e, t=t: e.bn_stats(stat[:, 6:12], attn[t % 2][:, 512:1024]))
        nag = p.op("vector", lambda e: e.bn_aggr(aggr[:], stat[:]), waits=[(sV, nsb)], inc=sV)
        nsq = p.op("scalar", lambda e: e.activation(out=scol[:], in_=aggr[:, 1:2],
                                                    func=AF.Sqrt, bias=epsc[:], scale=1.0),
                   waits=[(sV, nag)], inc=sS)
        nrcp = p.op("vector", lambda e: e.reciprocal(rcol[:], scol[:]), waits=[(sS, nsq)])
        if DBG and t == 0:
            p.op("vector", lambda e: e.tensor_copy(dbgs[:, 64:66], aggr[:, :]))
            p.op("vector", lambda e: e.tensor_copy(dbgs[:, 66:67], scol[:, :]))
            p.op("vector", lambda e: e.tensor_copy(dbgs[:, 67:68], rcol[:, :]),
                 waits=[(sV, nrcp)])
            p.op("vector", lambda e: e.tensor_copy(dbgs[:, 72:76], attn[0][:, 0:4]))
        napl2 = p.op("vector", lambda e, t=t: e.tensor_scalar(
            out=attn[t % 2][:], in0=attn[t % 2][:], scalar1=aggr[:, 0:1], scalar2=rcol[:],
            op0=ALU.subtract, op1=ALU.mult), waits=[(sV, nrcp)])
        wl = [(sX[t % 2], xre_ld), (sV, napl2)]
        if x2sp[t % 2]:
            wl.append((sGs[t % 2], x2sp[t % 2]))
        apply2[t] = p.op("vector", lambda e, t=t: e.tensor_add(
            x2f[t % 2][:], attn[t % 2][:], xre[t % 2][:]), waits=wl, inc=sV)
        x2sp[t % 2] = p.op("gpsimd", lambda e, t=t: e.dma_start(
            out=dx2[t * 128:(t + 1) * 128, :], in_=x2f[t % 2][:, :]),
            waits=[(sV, apply2[t])], inc=sGs[t % 2], amt=16)
        p.op("vector", lambda e, t=t: e.bn_stats(stat[:, 0:6], x2f[t % 2][:, 0:512]),
             waits=[(sV, apply2[t])])
        nsb = p.op("vector", lambda e, t=t: e.bn_stats(stat[:, 6:12], x2f[t % 2][:, 512:1024]))
        nag = p.op("vector", lambda e: e.bn_aggr(aggr[:], stat[:]), waits=[(sV, nsb)], inc=sV)
        nsq = p.op("scalar", lambda e: e.activation(out=scol[:], in_=aggr[:, 1:2],
                                                    func=AF.Sqrt, bias=epsc[:], scale=1.0),
                   waits=[(sV, nag)], inc=sS)
        nrcp2 = p.op("vector", lambda e: e.reciprocal(rcol[:], scol[:]), waits=[(sS, nsq)])
        if DBG and t == 0:
            p.op("vector", lambda e: e.tensor_copy(dbgs[:, 68:70], aggr[:, :]))
            p.op("vector", lambda e: e.tensor_copy(dbgs[:, 70:71], scol[:, :]))
            p.op("vector", lambda e: e.tensor_copy(dbgs[:, 71:72], rcol[:, :]),
                 waits=[(sV, nrcp2)])
        wl = [(sV, nrcp2)]
        if t >= 2:
            wl.append((sV, q_done[t - 2]))
        apply3[t] = p.op("vector", lambda e, t=t: e.tensor_scalar(
            out=x3f[t % 2][:], in0=x2f[t % 2][:], scalar1=aggr[:, 0:1], scalar2=rcol[:],
            op0=ALU.subtract, op1=ALU.mult), waits=wl, inc=sV)
        wl = [(sV, apply3[t])]
        if t >= 1:
            wl.append((sV, q_done[t - 1]))
        nred4 = p.op("vector", lambda e, t=t: e.reduce_max(
            sxc[:, :], x3f[t % 2][:, :], axis=AXX, apply_absolute_value=True), waits=wl)
        nclp = p.op("vector", lambda e: e.tensor_scalar(out=sxc[:], in0=sxc[:], scalar1=1e-5,
                                                        scalar2=None, op0=ALU.max),
                    waits=[(sV, nred4)])
        nrs = p.op("vector", lambda e: e.reciprocal(sxic[:], sxc[:]), waits=[(sV, nclp)])
        nm127 = p.op("vector", lambda e: e.tensor_scalar(out=sxic[:], in0=sxic[:],
                                                         scalar1=127.0, scalar2=None,
                                                         op0=ALU.mult), waits=[(sV, nrs)])
        nq16t = p.op("vector", lambda e, t=t: e.tensor_scalar(
            out=q16[:, :], in0=x3f[t % 2][:], scalar1=sxic[:], scalar2=RC,
            op0=ALU.mult, op1=ALU.add), waits=[(sV, nm127)])
        nsxf = p.op("vector", lambda e: e.tensor_scalar(out=sxc[:], in0=sxc[:],
                                                        scalar1=1.0 / 127.0,
                                                        scalar2=None, op0=ALU.mult))
        if DBG:
            p.op("vector", lambda e, t=t: e.tensor_copy(dbgs[:, t:t + 1], sxc[:, :]),
                 waits=[(sV, nsxf)])
            p.op("vector", lambda e, t=t: e.tensor_copy(dbgs[:, 16 + t:17 + t], sxic[:, :]))
        wl = [(sV, nq16t), (sV, nsxf)]
        if t >= 2:
            wl.append((sS, x3ev[t - 2][NC - 1]))
        q_done[t] = p.op("vector", lambda e, t=t: e.tensor_scalar(
            out=x3q[t % 2][:], in0=q16[:, :], scalar1=RD, scalar2=sxc[:],
            op0=ALU.subtract, op1=ALU.mult), waits=wl, inc=sV)

        def emit_x3t(tp):
            for c in range(NC):
                k = tp * NC + c
                wl = [(sV, q_done[tp])]
                if k >= 2:
                    wl.append((sS, x3ev[(k - 2) // NC][(k - 2) % NC]))
                x3tr[tp][c] = p.op("tensor", lambda e, tp=tp, c=c, k=k: e.transpose(
                    ptrX[k % 2][:, 0:128], x3q[tp % 2][:, c * 128:(c + 1) * 128], iden[:]),
                    waits=wl, inc=sT)
                wl = [(sT, x3tr[tp][c])]
                if c == 0 and x3sp[tp % 2]:
                    wl.append((sSp[tp % 2], x3sp[tp % 2]))
                x3ev[tp][c] = p.op("scalar", lambda e, tp=tp, c=c, k=k: e.copy(
                    x3st[tp % 2][:, c * 128:(c + 1) * 128], ptrX[k % 2][:, 0:128]),
                    waits=wl, inc=sS)
            x3sp[tp % 2] = p.op("sync", lambda e, tp=tp: e.dma_start(
                out=dx3t[:, :, 1 + tp * 128:1 + (tp + 1) * 128].transpose([1, 0, 2]),
                in_=x3st[tp % 2][:, :]), waits=[(sS, x3ev[tp][NC - 1])],
                inc=sSp[tp % 2], amt=16)

        if t == NT - 1:
            emit_x3t(t)
    p.op("sync", lambda e: e.nop(), waits=[(sSp[0], sSp[0].n), (sSp[1], sSp[1].n),
                                           (sX[0], sX[0].n), (sX[1], sX[1].n)])
    p.op("gpsimd", lambda e: e.nop(), waits=[(sGs[0], sGs[0].n), (sGs[1], sGs[1].n),
                                             (sGq, sGq.n)])
    run_block(nc, p, hwm, "b4_oproj")
    free(*[f"wo_{c}" for c in range(NC)], *[f"B1_{c}" for c in range(NC)],
         *[f"B2_{t}" for t in range(NT)],
         "bvB", "boB",
         "attn_0", "attn_1", "x2f_0", "x2f_1", "x3f_0", "x3f_1", "xre_0", "xre_1",
         "q16", "x3q_0", "x3q_1", "x3st_0", "x3st_1", "sxc", "sxic",
         "ptrO_0", "ptrO_1", "ptrX_0", "ptrX_1", "po4_0", "po4_1")

    diag = alloc("diag", [128, 3 * HCH * 128], BF16)
    w1h = [alloc(f"w1h_{i}", [128, 4 * 1024], BF16) for i in range(2)]
    w2b = [alloc(f"w2b_{i}", [128, 4 * 1024], BF16) for i in range(2)]
    x3w = [alloc(f"x3w_{i}", [128, 8 * 256], BF16) for i in range(3)]
    Ab = [alloc(f"Ab_{i}", [128, HCH * 258], BF16) for i in range(2)]
    H2 = alloc("H2", [128, HCH * 256], F32)
    H2q = [alloc(f"H2q_{i}", [128, HCH * 256], BF16) for i in range(2)]
    tland = [alloc(f"tland_{i}", [128, 256], F32) for i in range(2)]
    q16b = [alloc(f"q16b_{i}", [128, 256], I16) for i in range(2)]
    apar = alloc("apar", [128, 4 * 256], F32)
    amaxC = alloc("amaxC", [128, 256], F32)
    v2 = alloc("v2", [128, 4], F32)
    sxr = alloc("sxr", [1, 256], F32)
    sxB = alloc("sxB", [128, 256], F32)
    sc2b = [alloc(f"sc2b_{i}", [128, 2], F32) for i in range(2)]
    out2 = [alloc(f"out2_{i}", [128, D], F32) for i in range(2)]
    x4b = [alloc(f"x4b_{i}", [128, D], F32) for i in range(2)]
    x2re = [alloc(f"x2re_{i}", [128, D], BF16) for i in range(2)]
    pA = [alloc(f"pA_{i}", [128, 512], F32, "psum") for i in range(2)]
    pC = alloc("pC", [128, 512], F32, "psum")
    pM = [alloc(f"pM_{i}", [128, 512], F32, "psum") for i in range(4)]
    pX = alloc("pX", [128, 512], F32, "psum")

    p = Plan()
    dg_ld = p.op("gpsimd", lambda e: e.dma_start(
        out=diag[:, :], in_=ddiag[:, :]), inc=sG, amt=16)

    def dg(k, c):
        return diag[:, (k * HCH + c) * 128:(k * HCH + c) * 128 + 128]

    def ab(s, c):
        return Ab[s % 2][:, c * 258:(c + 1) * 258]

    x3_ld = [0] * (NS + 2)
    a_mm = [[0] * HCH for _ in range(NS + 2)]
    a_ev = [[0] * HCH for _ in range(NS + 2)]
    halo_d = [[0] * HCH for _ in range(NS + 2)]
    conv_mm = [[0] * HCH for _ in range(NS)]
    silu_d = [[0] * HCH for _ in range(NS)]
    ntp_d = [[0] * HCH for _ in range(NS)]
    h2q_d = [[0] * HCH for _ in range(NS)]
    apar_d = [[0] * 4 for _ in range(NS)]
    mm2_last = [0] * NS
    w1_ld = {}
    w1_used = {}
    pa_ev = [0, 0]
    pc_ev = [0]
    pm_ev = [0, 0, 0, 0]
    w2_used = [0, 0]
    x2re_used = [0, 0]
    out_dma = [0, 0]
    vcomb = [0]
    sxB_ev = [0]
    sc2_d = [0, 0]
    trans_last = [0]
    ktrans_last = [0]

    def emit_x3ld(s2):
        par = s2 % 3
        if s2 == NS:
            src = dx3t[:, :, s2 * TS:s2 * TS + 2].transpose([1, 0, 2])
            dst = x3w[par].ap().rearrange("p (c n) -> p c n", n=256)[:, :, 0:2]
        else:
            src = dx3t[:, :, s2 * TS:s2 * TS + 256].transpose([1, 0, 2])
            dst = x3w[par][:, :]
        wl = [(sT, a_mm[s2 - 3][HCH - 1])] if s2 >= 3 else []
        if s2 >= 2:
            wl.append((sX[s2 % 2], x3_ld[s2 - 2]))
        x3_ld[s2] = p.op("sync", lambda e, src=src, dst=dst: e.dma_start(out=dst, in_=src),
                         waits=wl, inc=sX[s2 % 2], amt=16)

    def emit_w1ld(s2, q):
        j = s2 * 8 + q
        wl = [(sT, w1_used[j - 2])] if (j - 2) in w1_used else []
        g = q // 2
        half = q % 2
        src = dw1b[g, :, :].rearrange("p (c h) -> p c h", h=1024)[
            :, :, half * 512:(half + 1) * 512]
        w1_ld[j] = p.op("gpsimd", lambda e, src=src, j=j: e.dma_start(
            out=w1h[j % 2][:, :], in_=src), waits=wl, inc=sGw[j % 2], amt=16)

    def emit_A_chunk(s2, hcx):
        par = s2 % 3
        width = 2 if s2 == NS else 256
        slotp = hcx % 2
        j = s2 * 8 + hcx // 4
        hs2 = hcx % 4
        wl = [(sX[s2 % 2], x3_ld[s2]), (sGw[j % 2], w1_ld[j])]
        if pa_ev[slotp]:
            wl.append((sS, pa_ev[slotp]))
        last = 0
        for d in range(NC):
            if s2 == NS:
                mv = x3w[par].ap().rearrange("p (c n) -> p c n", n=256)[:, d, 0:2]
            else:
                mv = x3w[par][:, d * 256:(d + 1) * 256]
            last = p.op("tensor", lambda e, j=j, hs2=hs2, d=d, slotp=slotp, mv=mv, width=width: e.matmul(
                pA[slotp][:, 0:width],
                w1h[j % 2][:, d * 512 + hs2 * 128:d * 512 + hs2 * 128 + 128],
                mv, start=(d == 0), stop=(d == NC - 1)),
                waits=wl if d == 0 else [],
                inc=sT if d == NC - 1 else None)
        a_mm[s2][hcx] = last
        w1_used[j] = last
        wl = [(sT, last)]
        if s2 >= 2:
            wl.append((sT, conv_mm[s2 - 2][hcx]))
        pa_ev[slotp] = p.op("scalar", lambda e, s2=s2, hcx=hcx, slotp=slotp, width=width: e.activation(
            out=ab(s2, hcx)[:, 0:width], in_=pA[slotp][:, 0:width],
            func=AF.Identity, bias=0.0, scale=mwc[:, 0:1]), waits=wl, inc=sS)
        a_ev[s2][hcx] = pa_ev[slotp]
        if s2 >= 1:
            halo_d[s2 - 1][hcx] = p.op("scalar", lambda e, s2=s2, hcx=hcx: e.copy(
                ab(s2 - 1, hcx)[:, 256:258], ab(s2, hcx)[:, 0:2]),
                waits=[(sS, a_ev[s2][hcx])], inc=sS)

    def emit_conv_chunk(s, hcx):
        wl = [(sS, halo_d[s][hcx]), (sG, dg_ld)]
        if pc_ev[0]:
            wl.append((sS, pc_ev[0]))
        if s >= 1:
            wl.append((sV, ntp_d[s - 1][hcx]))
            wl.append((sV, apar_d[s - 1][hcx // 8]))
        last = 0
        for k in range(3):
            last = p.op("tensor", lambda e, s=s, hcx=hcx, k=k: e.matmul(
                pC[:, 0:256], dg(k, hcx), ab(s, hcx)[:, k:k + 256],
                start=(k == 0), stop=(k == 2)),
                waits=wl if k == 0 else [],
                inc=sT if k == 2 else None)
        conv_mm[s][hcx] = last
        silu_d[s][hcx] = p.op("scalar", lambda e, s=s, hcx=hcx: e.activation(
            out=H2[:, hcx * 256:(hcx + 1) * 256], in_=pC[:, 0:256],
            func=AF.Silu, bias=cbs[:, hcx:hcx + 1], scale=1.0),
            waits=[(sT, last)], inc=sS)
        pc_ev[0] = silu_d[s][hcx]

    def emit_mm2(sm):
        par = sm % 2
        x2_ld = [0, 0]
        for tsl in range(2):
            t = sm * 2 + tsl
            wl = [(sV, x2re_used[tsl])] if x2re_used[tsl] else []
            x2_ld[tsl] = p.op("sync", lambda e, t=t, tsl=tsl: e.dma_start(
                out=x2re[tsl][:, :], in_=dx2[t * 128:(t + 1) * 128, :]),
                waits=wl, inc=sX2[tsl], amt=16)
        last = 0
        for cc in range(0, HCH, 4):
            b = (cc // 4) % 2
            wl = [(sT, w2_used[b])] if w2_used[b] else []
            w2_ld = p.op("sync", lambda e, cc=cc, b=b: e.dma_start(
                out=w2b[b][:, :], in_=dw2b[:, cc * 1024:(cc + 4) * 1024]),
                waits=wl, inc=sW2[b], amt=16)
            for hcx in range(cc, cc + 4):
                qq = hcx % 4
                for tsl in range(2):
                    for ofs in range(2):
                        slot = tsl * 2 + ofs
                        wl = []
                        if hcx == cc and tsl == 0 and ofs == 0:
                            wl.append((sW2[b], w2_ld))
                            wl.append((sV, h2q_d[sm][cc + 3]))
                        if hcx == 0 and pm_ev[slot]:
                            wl.append((sV, pm_ev[slot]))
                        fin = (hcx == cc + 3 and tsl == 1 and ofs == 1)
                        last = p.op("tensor", lambda e, sm=sm, hcx=hcx, tsl=tsl, ofs=ofs, b=b, qq=qq, slot=slot: e.matmul(
                            pM[slot][:, :],
                            H2q[sm % 2][:, hcx * 256 + tsl * 128:hcx * 256 + tsl * 128 + 128],
                            w2b[b][:, qq * 1024 + ofs * 512:qq * 1024 + ofs * 512 + 512],
                            start=(hcx == 0), stop=(hcx == HCH - 1)),
                            waits=wl, inc=sT if fin else None)
            w2_used[b] = last
        mm2_last[sm] = last
        for tsl in range(2):
            for ofs in range(2):
                slot = tsl * 2 + ofs
                wl = [(sT, mm2_last[sm]), (sV, sc2_d[par])]
                if x2re_used[tsl]:
                    wl.append((sV, x2re_used[tsl]))
                pm_ev[slot] = p.op("vector", lambda e, tsl=tsl, ofs=ofs, slot=slot, par=par: e.tensor_scalar(
                    out=out2[tsl][:, ofs * 512:(ofs + 1) * 512], in0=pM[slot][:, :],
                    scalar1=sc2b[par][:, tsl:tsl + 1], scalar2=None, op0=ALU.mult),
                    waits=wl, inc=sV)
            p.op("vector", lambda e, tsl=tsl: e.bn_stats(stat[:, 0:6], out2[tsl][:, 0:512]),
                 waits=[(sV, pm_ev[tsl * 2 + 1])])
            nsb = p.op("vector", lambda e, tsl=tsl: e.bn_stats(stat[:, 6:12],
                                                               out2[tsl][:, 512:1024]))
            nag = p.op("vector", lambda e: e.bn_aggr(aggr[:], stat[:]),
                       waits=[(sV, nsb)], inc=sV)
            nsq = p.op("scalar", lambda e: e.activation(out=scol[:], in_=aggr[:, 1:2],
                                                        func=AF.Sqrt, bias=epsc[:], scale=1.0),
                       waits=[(sV, nag)], inc=sS)
            nrcp5 = p.op("vector", lambda e: e.reciprocal(rcol[:], scol[:]),
                         waits=[(sS, nsq)])
            napl4 = p.op("vector", lambda e, tsl=tsl: e.tensor_scalar(
                out=out2[tsl][:], in0=out2[tsl][:], scalar1=aggr[:, 0:1], scalar2=rcol[:],
                op0=ALU.subtract, op1=ALU.mult), waits=[(sV, nrcp5)])
            wl = [(sX2[tsl], x2_ld[tsl]), (sV, napl4)]
            if out_dma[tsl]:
                wl.append((sOut[tsl], out_dma[tsl]))
            x2re_used[tsl] = p.op("vector", lambda e, tsl=tsl: e.tensor_add(
                x4b[tsl][:], out2[tsl][:], x2re[tsl][:]), waits=wl, inc=sV)
            t = sm * 2 + tsl
            out_dma[tsl] = p.op("sync", lambda e, t=t, tsl=tsl: e.dma_start(
                out=dout[t * 128:(t + 1) * 128, :], in_=x4b[tsl][:, :]),
                waits=[(sV, x2re_used[tsl])], inc=sOut[tsl], amt=16)

    emit_x3ld(0)
    emit_x3ld(1)
    emit_x3ld(2)
    for s2 in (0, 1):
        for hcx in range(HCH):
            if hcx % 4 == 0:
                emit_w1ld(s2, hcx // 4)
            emit_A_chunk(s2, hcx)

    for s in range(NS):
        s2 = s + 2
        if s + 3 <= NS:
            emit_x3ld(s + 3)
        for hcx in range(HCH):
            emit_conv_chunk(s, hcx)
            if s2 <= NS:
                if hcx % 4 == 0:
                    emit_w1ld(s2, hcx // 4)
                emit_A_chunk(s2, hcx)
            if hcx % 8 == 7:
                g3 = hcx // 8
                wl = [(sS, silu_d[s][hcx])]
                if g3 == 0 and vcomb[0]:
                    wl.append((sV, vcomb[0]))
                apar_d[s][g3] = p.op("vector", lambda e, g3=g3: e.reduce_max(
                    apar[:, g3 * 256:(g3 + 1) * 256],
                    H2.ap().rearrange("p (c n) -> p n c", c=HCH)[:, :, g3 * 8:(g3 + 1) * 8],
                    axis=AXX, apply_absolute_value=True), waits=wl, inc=sV)
        cA = p.op("vector", lambda e: e.tensor_tensor(
            amaxC[:, :], apar[:, 0:256], apar[:, 256:512], ALU.max),
            waits=[(sV, apar_d[s][1]), (sT, trans_last[0])], inc=sV)
        cB = p.op("vector", lambda e: e.tensor_tensor(
            apar[:, 768:1024], apar[:, 512:768], apar[:, 768:1024], ALU.max),
            waits=[(sV, apar_d[s][3])], inc=sV)
        cF = p.op("vector", lambda e: e.tensor_tensor(
            amaxC[:, :], amaxC[:, :], apar[:, 768:1024], ALU.max),
            waits=[(sV, cA), (sV, cB)], inc=sV)
        vcomb[0] = cF
        red = 0
        for tt in range(2):
            wl = [(sV, cF), (sV, sxB_ev[0])] if tt == 0 else [(sV, red)]
            ntr = p.op("tensor", lambda e, tt=tt: e.transpose(
                pX[:, 0:128], amaxC[:, tt * 128:(tt + 1) * 128], idenf[:]),
                waits=wl, inc=sT)
            red = p.op("vector", lambda e, tt=tt: e.reduce_max(
                v2[:, tt:tt + 1], pX[0:128, 0:128], axis=AXX),
                waits=[(sT, ntr)], inc=sV)
            trans_last[0] = ntr
        ncl5 = p.op("vector", lambda e: e.tensor_scalar(out=v2[:, 0:2], in0=v2[:, 0:2],
                                                        scalar1=1e-5, scalar2=None,
                                                        op0=ALU.max),
                    waits=[(sV, red), (sT, ktrans_last[0])], inc=sV)
        p.op("vector", lambda e: e.reciprocal(v2[:, 2:3], v2[:, 0:1]), waits=[(sV, ncl5)])
        nrB = p.op("vector", lambda e: e.reciprocal(v2[:, 3:4], v2[:, 1:2]))
        nm7 = p.op("vector", lambda e: e.tensor_scalar(out=v2[:, 2:4], in0=v2[:, 2:4],
                                                       scalar1=127.0, scalar2=None,
                                                       op0=ALU.mult),
                   waits=[(sV, nrB)], inc=sV)
        nv2 = p.op("vector", lambda e: e.tensor_scalar(out=v2[:, 0:2], in0=v2[:, 0:2],
                                                       scalar1=1.0 / 127.0, scalar2=None,
                                                       op0=ALU.mult), inc=sV)
        for tsl in range(2):
            sc2_d[s % 2] = p.op("vector", lambda e, tsl=tsl, s=s: e.tensor_tensor(
                sc2b[s % 2][:, tsl:tsl + 1], v2[:, tsl:tsl + 1], mwc[:, 1:2], ALU.mult),
                waits=[(sV, nv2)], inc=sV)
        prev_ev = 0
        for k in range(2):
            wl = [(sV, nm7)]
            if prev_ev:
                wl.append((sV, prev_ev))
            ntr = p.op("tensor", lambda e, k=k: e.transpose(
                pX[0:1, 128:256], v2[:, 2 + k:3 + k], idenf[:]), waits=wl, inc=sT)
            prev_ev = p.op("vector", lambda e, k=k: e.tensor_copy(
                sxr[0:1, k * 128:(k + 1) * 128], pX[0:1, 128:256]),
                waits=[(sT, ntr)], inc=sV)
            ktrans_last[0] = ntr
        nb = p.op("tensor", lambda e: e.matmul(pX[:, 256:512], ones128[:, :],
                                               sxr[0:1, :], start=True, stop=True),
                  waits=[(sV, prev_ev)], inc=sT)
        wl = [(sT, nb)]
        if s >= 1:
            wl.append((sV, ntp_d[s - 1][HCH - 1]))
        sxB_ev[0] = p.op("vector", lambda e: e.tensor_copy(sxB[:, :], pX[0:128, 256:512]),
                         waits=wl, inc=sV)
        for h0 in range(0, HCH, 2):
            ntl = [0, 0]
            nuu = [0, 0]
            for i in range(2):
                hcx = h0 + i
                wl = [(sV, sxB_ev[0])]
                if hcx >= 2:
                    wl.append((sV, h2q_d[s][hcx - 2]))
                ntl[i] = p.op("vector", lambda e, hcx=hcx, i=i: e.tensor_tensor(
                    tland[i][:, :], H2[:, hcx * 256:(hcx + 1) * 256], sxB[:, :], ALU.mult),
                    waits=wl, inc=sV)
                ntp_d[s][hcx] = ntl[i]
            for i in range(2):
                nuu[i] = p.op("vector", lambda e, i=i: e.tensor_scalar(
                    out=q16b[i][:, :], in0=tland[i][:, :],
                    scalar1=RC, scalar2=None, op0=ALU.add),
                    waits=[(sV, ntl[i])], inc=sV)
            for i in range(2):
                hcx = h0 + i
                wl = [(sV, nuu[i])]
                if s >= 2:
                    wl.append((sT, mm2_last[s - 2]))
                h2q_d[s][hcx] = p.op("vector", lambda e, s=s, hcx=hcx, i=i: e.tensor_scalar(
                    out=H2q[s % 2][:, hcx * 256:(hcx + 1) * 256], in0=q16b[i][:, :],
                    scalar1=RD, scalar2=None, op0=ALU.subtract), waits=wl, inc=sV)
        if s >= 1:
            emit_mm2(s - 1)
    emit_mm2(NS - 1)
    if DBG:
        p.op("sync", lambda e: e.dma_start(out=dbgS[:, :], in_=dbgs[:, :]), inc=sD, amt=16)
    p.op("sync", lambda e: e.nop(), waits=[
        (sOut[0], sOut[0].n), (sOut[1], sOut[1].n),
        (sX[0], sX[0].n), (sX[1], sX[1].n), (sX2[0], sX2[0].n), (sX2[1], sX2[1].n)])
    p.op("gpsimd", lambda e: e.nop(), waits=[
        (sG, sG.n), (sGw[0], sGw[0].n), (sGw[1], sGw[1].n),
        (sW2[0], sW2[0].n), (sW2[1], sW2[1].n)])
    run_block(nc, p, hwm, "b5_ffn")

    free(*list(cms))
    for s in (sP, sS, sV, sT, *sW2[::-1], *sGw[::-1], *sGs[::-1], sGk, sGq,
              *sOut[::-1], *sSp[::-1], *sW[::-1], *sX2[::-1], *sX[::-1], sG, sD):
        s._cm.__exit__(None, None, None)
    return nc


_NC_CACHE = {}


def _get_nc():
    key = "nc"
    if key not in _NC_CACHE:
        _NC_CACHE[key] = build_nc()
    return _NC_CACHE[key]


def kernel(L0_attn_x, L0_attn_wq, L0_attn_bq, L0_attn_wk, L0_attn_bk,
           L0_attn_wv, L0_attn_bv, L0_attn_wo, L0_attn_bo,
           L0_ffn_w1, L0_ffn_conv_w, L0_ffn_conv_b, L0_ffn_w2):
    x = np.ascontiguousarray(np.asarray(L0_attn_x, dtype=np.float32))
    B = x.shape[0]
    f32 = lambda a: np.ascontiguousarray(np.asarray(a, dtype=np.float32))
    wq, wk, wv, wo = f32(L0_attn_wq), f32(L0_attn_wk), f32(L0_attn_wv), f32(L0_attn_wo)
    bq = np.ascontiguousarray(f32(L0_attn_bq).reshape(NC, 128).T)
    bk = np.ascontiguousarray(f32(L0_attn_bk).reshape(NC, 128).T)
    bv = f32(L0_attn_bv).reshape(1, D)
    bo = f32(L0_attn_bo).reshape(1, D)
    w1t = np.ascontiguousarray(f32(L0_ffn_w1).T)
    w2t = np.ascontiguousarray(f32(L0_ffn_w2).T)
    cw = f32(L0_ffn_conv_w).reshape(DH, 3)
    diag3 = np.zeros((128, 3, HCH, 128), dtype=np.float32)
    idx = np.arange(128)
    for k in range(3):
        for c in range(HCH):
            diag3[idx, k, c, idx] = cw[c * 128 + idx, k]
    diag3 = diag3.reshape(128, 3 * HCH * 128)
    convb = np.ascontiguousarray(f32(L0_ffn_conv_b).reshape(HCH, 128).T)
    pp = np.arange(128)[:, None]
    ff = np.arange(384)[None, :]
    mask = ((pp <= ff) & (ff <= pp + 256)).astype(np.float32)
    iden = np.eye(128, dtype=np.float32)
    hv = np.zeros((128, 2), dtype=np.float32)
    hv[:64, 0] = 1.0
    hv[64:, 1] = 1.0
    ones128 = np.ones((1, 128), dtype=np.float32)
    onescol = np.ones((128, 1), dtype=np.float32)

    nc = _get_nc()
    common = dict(wq=wq, wk=wk, wv=wv, wo=wo, bq=bq, bk=bk, bv=bv, bo=bo,
                  w1t=w1t, w2t=w2t, diag3=diag3, convb=convb, mask01=mask,
                  iden=iden, halves=hv, ones128=ones128, onescol=onescol)
    in_maps = [dict(common, x=np.ascontiguousarray(x[i])) for i in range(B)]
    res = run_bass_kernel_spmd(nc, in_maps, core_ids=list(range(B)))
    out = np.stack([np.asarray(r["out"], dtype=np.float32) for r in res.results], axis=0)
    return out


# version 18
# speedup vs baseline: 1.1299x; 1.0776x over previous
import os
import numpy as np
import concourse.bass as bass
import concourse.mybir as mybir
from concourse.bass_utils import run_bass_kernel_spmd

F32 = mybir.dt.float32
BF16 = mybir.dt.bfloat16
I16 = mybir.dt.int16
I8 = mybir.dt.int8
AF = mybir.ActivationFunctionType
ALU = mybir.AluOpType
AXX = mybir.AxisListType.X

_SIM = os.environ.get("BASS_SIM_FP64") == "1"
RC = 127.5 if _SIM else 128.0
RD = 127.0 if _SIM else 128.0

T = 2048
D = 1024
NT = T // 128
NC = D // 128
H = 16
NJ = 16
DH = 4096
HCH = 32
TS = 256
NS = T // TS
EPS = 1e-5

ENGS = ("sync", "tensor", "vector", "scalar", "gpsimd")


class Sem:
    def __init__(self, nc, name):
        self._cm = nc.semaphore(name)
        self.h = self._cm.__enter__()
        self.n = 0
        self.name = name


class Plan:
    def __init__(self):
        self.items = {e: [] for e in ENGS}
        self.incs = {e: {} for e in ENGS}

    AUTO = {}

    def op(self, eng, emit, waits=(), inc=None, amt=1):
        w = [(s, v) for (s, v) in waits if v and v > 0]
        if inc is None and eng in self.AUTO:
            inc = self.AUTO[eng]
            amt = 1
        if inc is not None:
            inc.n += amt
            self.incs[eng][inc.name] = inc
        self.items[eng].append((w, emit, inc, amt))
        return inc.n if inc is not None else None

    def replay(self, eng, e, hwm):
        for w, emit, inc, amt in self.items[eng]:
            for s, v in w:
                if hwm[eng].get(s.name, 0) < v:
                    e.wait_ge(s.h, v)
                    hwm[eng][s.name] = v
            ins = emit(e)
            if inc is not None:
                ins.then_inc(inc.h, amt)
        for s in self.incs[eng].values():
            if hwm[eng].get(s.name, 0) < s.n:
                e.wait_ge(s.h, s.n)
                hwm[eng][s.name] = s.n


PHASES = []
_PHTOT = {e: 0 for e in ENGS}


def run_block(nc, plan, hwm, phase=""):
    for e in ENGS:
        _PHTOT[e] += len(plan.items[e])
    PHASES.append((phase, dict(_PHTOT)))
    with nc.Block() as blk:
        @blk.sync
        def _(e):
            plan.replay("sync", e, hwm)

        @blk.tensor
        def _(e):
            plan.replay("tensor", e, hwm)

        @blk.vector
        def _(e):
            plan.replay("vector", e, hwm)

        @blk.scalar
        def _(e):
            plan.replay("scalar", e, hwm)

        @blk.gpsimd
        def _(e):
            plan.replay("gpsimd", e, hwm)


def build_nc():
    PHASES.clear()
    for e in ENGS:
        _PHTOT[e] = 0
    nc = bass.Bass("TRN2", target_bir_lowering=False, debug=False,
                   detect_race_conditions=False)

    dx = nc.dram_tensor("x", [T, D], F32, kind="ExternalInput").ap()
    dw = {k: nc.dram_tensor(k, [D, D], F32, kind="ExternalInput").ap()
          for k in ("wq", "wk", "wv", "wo")}
    dbq = nc.dram_tensor("bq", [128, NC], F32, kind="ExternalInput").ap()
    dbk = nc.dram_tensor("bk", [128, NC], F32, kind="ExternalInput").ap()
    dbv = nc.dram_tensor("bv", [1, D], F32, kind="ExternalInput").ap()
    dbo = nc.dram_tensor("bo", [1, D], F32, kind="ExternalInput").ap()
    dw1t = nc.dram_tensor("w1t", [D, DH], F32, kind="ExternalInput").ap()
    dw2t = nc.dram_tensor("w2t", [DH, D], F32, kind="ExternalInput").ap()
    ddiag = nc.dram_tensor("diag3", [128, 3 * HCH * 128], F32, kind="ExternalInput").ap()
    dcb = nc.dram_tensor("convb", [128, HCH], F32, kind="ExternalInput").ap()
    dmask = nc.dram_tensor("mask01", [128, 384], F32, kind="ExternalInput").ap()
    diden = nc.dram_tensor("iden", [128, 128], F32, kind="ExternalInput").ap()
    dhalves = nc.dram_tensor("halves", [128, 2], F32, kind="ExternalInput").ap()
    dones128 = nc.dram_tensor("ones128", [1, 128], F32, kind="ExternalInput").ap()
    donescol = nc.dram_tensor("onescol", [128, 1], F32, kind="ExternalInput").ap()
    dout = nc.dram_tensor("out", [T, D], F32, kind="ExternalOutput").ap()
    DBG = os.environ.get("KDEBUG") == "1"
    iknd = "ExternalOutput" if DBG else "Internal"
    dx2 = nc.dram_tensor("x2sp", [T, D], BF16, kind=iknd).ap()
    dw1b = nc.dram_tensor("w1b16", [4, 128, NC * 1024], BF16, kind=iknd).ap()
    dw2b = nc.dram_tensor("w2b16", [128, HCH * 1024], BF16, kind=iknd).ap()
    dx3t = nc.dram_tensor("x3tsp", [NC, 128, T + 2], BF16, kind=iknd).ap()
    dbgO = (nc.dram_tensor("dbgO", [T, D], BF16, kind="ExternalOutput").ap()
            if DBG else None)
    dbgS = (nc.dram_tensor("dbgS", [128, 80], F32, kind="ExternalOutput").ap()
            if DBG else None)

    sD = Sem(nc, "sD")
    sG = Sem(nc, "sG")
    sX = [Sem(nc, "sX0"), Sem(nc, "sX1")]
    sX2 = [Sem(nc, "sX2_0"), Sem(nc, "sX2_1")]
    sW = [Sem(nc, f"sW{i}") for i in range(4)]
    sSp = [Sem(nc, f"sSp{i}") for i in range(4)]
    sOut = [Sem(nc, "sOut0"), Sem(nc, "sOut1")]
    sGq = Sem(nc, "sGq")
    sGk = Sem(nc, "sGk")
    sGs = [Sem(nc, "sGs0"), Sem(nc, "sGs1")]
    sGw = [Sem(nc, "sGw0"), Sem(nc, "sGw1")]
    sW2 = [Sem(nc, "sW2_0"), Sem(nc, "sW2_1")]
    sT = Sem(nc, "sT")
    sV = Sem(nc, "sV")
    sS = Sem(nc, "sS")
    sP = Sem(nc, "sP")
    Plan.AUTO = {"vector": sV, "scalar": sS, "gpsimd": sP}
    hwm = {e: {} for e in ENGS}
    cms = {}
    aseq = {"n": 0}

    def alloc(name, shape, dt, space="sbuf", side=None):
        cm = (nc.sbuf_tensor(name, shape, dt, side=side) if space == "sbuf"
              else nc.psum_tensor(name, shape, dt))
        h = cm.__enter__()
        aseq["n"] += 1
        cms[name] = (aseq["n"], cm)
        return h

    def free(*names):
        for nm in sorted(names, key=lambda nm: -cms[nm][0]):
            cms.pop(nm)[1].__exit__(None, None, None)

    bqs = alloc("bqs", [128, NC], F32)
    bks = alloc("bks", [128, NC], F32)
    bvB = alloc("bvB", [128, D], F32, side="right")
    boB = alloc("boB", [128, D], F32, side="right")
    bvrow = alloc("bvrow", [1, D], F32, side="right")
    borow = alloc("borow", [1, D], F32, side="right")
    cbs = alloc("cbs", [128, HCH], F32)
    mask01 = alloc("sb_mask01", [128, 384], BF16)
    iden = alloc("sb_iden", [128, 128], BF16)
    idenf = alloc("sb_idenf", [128, 128], F32)
    halves = alloc("sb_halves", [128, 2], F32)
    ones128 = alloc("sb_ones128", [1, 128], F32)
    onescol = alloc("sb_onescol", [128, 1], F32)
    epsc = alloc("epsc", [128, 1], F32)
    negC = alloc("negC", [128, H], F32)
    eC = alloc("eC", [128, H], F32)
    mwc = alloc("mwc", [128, 2], F32)
    swc = alloc("swc", [128, 2], F32)
    stat = alloc("stat", [128, 12], F32)
    aggr = alloc("aggr", [128, 2], F32)
    scol = alloc("scol", [128, 1], F32)
    rcol = alloc("rcol", [128, 1], F32)
    wsum = alloc("wsum", [128, 64], F32)
    mrow = alloc("mrow", [1, 64], F32)
    mtot = alloc("mtot", [1, 2], F32)
    zrow = alloc("zrow", [128, 1], BF16)
    qmx = alloc("qmx", [128, NC], F32)
    kmx = alloc("kmx", [128, NC], F32)
    qkp = alloc("qkp", [128, NC], F32)
    c2x8 = alloc("c2x8", [2, NC], F32)
    crow = alloc("crow", [1, 16], F32)
    dbgs = alloc("dbgs", [128, 80], F32)

    p = Plan()
    for (dst, src) in ((bqs, dbq), (bks, dbk), (bvrow, dbv), (borow, dbo),
                       (cbs, dcb), (halves, dhalves), (ones128, dones128),
                       (onescol, donescol), (idenf, diden)):
        p.op("sync", lambda e, d=dst, s=src: e.dma_start(out=d[:], in_=s[:, :]),
             inc=sD, amt=16)
    p.op("gpsimd", lambda e: e.dma_start(out=mask01[:], in_=dmask[:, :]), inc=sG, amt=16)
    p.op("gpsimd", lambda e: e.dma_start(out=iden[:], in_=diden[:, :]), inc=sG, amt=16)
    p.op("vector", lambda e: e.memset(epsc[:], EPS), inc=sV)
    nzr = p.op("vector", lambda e: e.memset(zrow[:], 0.0), inc=sV)
    for c in range(NC):
        p.op("sync", lambda e, c=c: e.dma_start(out=dx3t[c, :, 0:1], in_=zrow[:, 0:1],
             allow_slow_non_contiguous=True), waits=[(sV, nzr)], inc=sD, amt=16)
        p.op("sync", lambda e, c=c: e.dma_start(out=dx3t[c, :, T + 1:T + 2], in_=zrow[:, 0:1],
             allow_slow_non_contiguous=True), inc=sD, amt=16)
    p.op("sync", lambda e: e.nop(), waits=[(sD, sD.n)])
    p.op("gpsimd", lambda e: e.nop(), waits=[(sG, sG.n)])
    run_block(nc, p, hwm, "b0_const")

    B1all = alloc("B1all", [128, NC * T], BF16)
    B1 = [B1all[:, c * T:(c + 1) * T] for c in range(NC)]
    B1r3 = B1all.ap().rearrange("p (c t) -> p c t", t=T)
    QT = [alloc(f"QT{c}", [128, T], BF16) for c in range(NC)]
    KT = [alloc(f"KT{c}", [128, T], BF16) for c in range(NC)]
    VA = [alloc(f"VA{j}", [128, H * 65], BF16) for j in range(NJ + 2)]

    def vaug(j):
        return VA[j + 1]

    xin = [alloc(f"xin_{i}", [128, D], F32) for i in range(2)]
    x1b = [alloc(f"x1b_{i}", [128, D], BF16) for i in range(2)]
    ptr = [alloc(f"ptr_{i}", [128, 1024], BF16, "psum") for i in range(2)]
    pbc = [alloc(f"pbc_{i}", [128, 512], F32, "psum") for i in range(2)]

    p = Plan()
    for j in (-1, NJ):
        p.op("vector", lambda e, j=j: e.memset(
            vaug(j).ap().rearrange("p (h c) -> p h c", c=65)[:, :, 0:64], 0.0))
    vinit = 0
    for j in range(-1, NJ + 1):
        vinit = p.op("vector", lambda e, j=j: e.memset(
            vaug(j).ap().rearrange("p (h c) -> p h c", c=65)[:, :, 64:65], 1.0), inc=sV)

    bc_ev = 0
    for i, (dst, src) in enumerate(((bvB, bvrow), (bvB, bvrow), (boB, borow), (boB, borow))):
        off = (i % 2) * 512
        wl = [(sV, bc_ev)] if i >= 2 else []
        n = p.op("tensor", lambda e, s=src, off=off, i=i: e.matmul(
            pbc[i % 2][:, :], ones128[:, :], s[0:1, off:off + 512], start=True, stop=True),
            waits=wl, inc=sT)
        bc_ev = p.op("vector", lambda e, d=dst, off=off, i=i: e.tensor_copy(
            d[:, off:off + 512], pbc[i % 2][:, :]), waits=[(sT, n)], inc=sV)

    ln_apply = [0] * NT
    tr_done = [0] * NT
    ev_done = [0] * NT
    sq_done = [0] * NT
    for t in range(NT):
        wl = [(sV, ln_apply[t - 2])] if t >= 2 else []
        xload = p.op("sync", lambda e, t=t: e.dma_start(
            out=xin[t % 2][:], in_=dx[t * 128:(t + 1) * 128, :]), waits=wl,
            inc=sX[t % 2], amt=16)
        wl = [(sX[t % 2], xload)]
        if t >= 1:
            wl.append((sS, sq_done[t - 1]))
        p.op("vector", lambda e, t=t: e.bn_stats(stat[:, 0:6], xin[t % 2][:, 0:512]),
             waits=wl)
        nsb = p.op("vector", lambda e, t=t: e.bn_stats(stat[:, 6:12], xin[t % 2][:, 512:1024]))
        nag = p.op("vector", lambda e: e.bn_aggr(aggr[:], stat[:]), waits=[(sV, nsb)], inc=sV)
        sq_done[t] = p.op("scalar", lambda e: e.activation(
            out=scol[:], in_=aggr[:, 1:2], func=AF.Sqrt, bias=epsc[:], scale=1.0),
            waits=[(sV, nag)], inc=sS)
        nrc1 = p.op("vector", lambda e: e.reciprocal(rcol[:], scol[:]),
                    waits=[(sS, sq_done[t])])
        wl = [(sV, nrc1)]
        if t >= 2:
            wl.append((sT, tr_done[t - 2]))
        ln_apply[t] = p.op("vector", lambda e, t=t: e.tensor_scalar(
            out=x1b[t % 2][:], in0=xin[t % 2][:], scalar1=aggr[:, 0:1], scalar2=rcol[:],
            op0=ALU.subtract, op1=ALU.mult), waits=wl, inc=sV)
        for c in range(NC):
            wl = []
            if c == 0:
                wl.append((sV, ln_apply[t]))
                if t >= 2:
                    wl.append((sS, ev_done[t - 2]))
            tr_done[t] = p.op("tensor", lambda e, t=t, c=c: e.transpose(
                ptr[t % 2][:, c * 128:(c + 1) * 128],
                x1b[t % 2][:, c * 128:(c + 1) * 128], iden[:]),
                waits=wl, inc=sT if c == NC - 1 else None)
        ev_done[t] = p.op("scalar", lambda e, t=t: e.copy(
            B1r3[:, :, t * 128:(t + 1) * 128],
            ptr[t % 2].ap().rearrange("p (c q) -> p c q", q=128)[:, :, :]),
            waits=[(sT, tr_done[t])], inc=sS)
    run_block(nc, p, hwm, "b1_ln1")
    free("xin_0", "xin_1", "x1b_0", "x1b_1", "ptr_0", "ptr_1", "pbc_0", "pbc_1")
    free("bvrow", "borow")

    wsb = [[alloc(f"wsb{s}_{c}", [128, D], BF16) for c in range(NC)] for s in range(2)]
    wstage = [alloc(f"wstage_{i}", [128, 1024], F32) for i in range(2)]
    pq = [alloc(f"pq_{i}", [128, 512], F32, "psum") for i in range(4)]
    pms = alloc("pms", [128, 512], F32, "psum")

    p = Plan()
    wq_ld = [0] * NC
    wk_ld = [0] * NC
    wv_ld = [0] * NC
    for c in range(NC):
        wq_ld[c] = p.op("gpsimd", lambda e, c=c: e.dma_start(
            out=wsb[0][c][:], in_=dw["wq"][c * 128:(c + 1) * 128, :]), inc=sGq, amt=16)
    for c in range(NC):
        wk_ld[c] = p.op("gpsimd", lambda e, c=c: e.dma_start(
            out=wsb[1][c][:], in_=dw["wk"][c * 128:(c + 1) * 128, :]), inc=sGk, amt=16)

    red_done = []
    piece_srcs = []
    for ch in range(NC):
        for h4 in range(4):
            piece_srcs.append((dw1t[ch * 128:(ch + 1) * 128, h4 * 1024:(h4 + 1) * 1024], 1024))
    for ch in range(HCH):
        piece_srcs.append((dw2t[ch * 128:(ch + 1) * 128, :], 1024))

    def emit_piece(idx):
        src, width = piece_srcs[idx]
        slot = idx % 2
        wl = [(sV, red_done[idx - 2])] if idx >= 2 else []
        ld = p.op("sync", lambda e, src=src, slot=slot, width=width: e.dma_start(
            out=wstage[slot][:, 0:width], in_=src), waits=wl, inc=sW[slot], amt=16)
        red_done.append(p.op("vector", lambda e, slot=slot, idx=idx, width=width: e.reduce_sum(
            wsum[:, idx:idx + 1], wstage[slot][:, 0:width], axis=AXX,
            apply_absolute_value=True), waits=[(sW[slot], ld)], inc=sV))

    NPC1 = 40
    for idx in range(NPC1):
        emit_piece(idx)

    def proj_qk(wset, wld, bias_s, OUT, slot_ev, evac_eng, wsem):
        for oc in range(NC):
            for st in range(4):
                slot = (oc * 4 + st) % 4
                wl = [(wsem, wld[NC - 1])]
                if slot_ev[slot]:
                    wl.append((evac_eng, slot_ev[slot]))
                last = 0
                for ic in range(NC):
                    last = p.op("tensor", lambda e, oc=oc, st=st, ic=ic, slot=slot, wset=wset: e.matmul(
                        pq[slot][:, :], wsb[wset][ic][:, oc * 128:(oc + 1) * 128],
                        B1[ic][:, st * 512:(st + 1) * 512],
                        start=(ic == 0), stop=(ic == NC - 1)),
                        waits=wl if ic == 0 else [],
                        inc=sT if ic == NC - 1 else None)
                slot_ev[slot] = p.op("scalar", lambda e, oc=oc, st=st, slot=slot, OUT=OUT, bias_s=bias_s: e.activation(
                    out=OUT[oc][:, st * 512:(st + 1) * 512], in_=pq[slot][:, :],
                    func=AF.Identity, bias=bias_s[:, oc:oc + 1], scale=1.0),
                    waits=[(sT, last)], inc=sS)
        return slot_ev

    slot_ev = [0, 0, 0, 0]
    slot_ev = proj_qk(0, wq_ld, bqs, QT, slot_ev, sS, sGq)
    for c in range(NC):
        wv_ld[c] = p.op("gpsimd", lambda e, c=c: e.dma_start(
            out=wsb[0][c][:], in_=dw["wv"][c * 128:(c + 1) * 128, :]),
            waits=[(sS, max(slot_ev))], inc=sGq, amt=16)
    slot_ev = proj_qk(1, wk_ld, bks, KT, slot_ev, sS, sGk)

    prev_s_ev = list(slot_ev)
    slot_vv = [0, 0, 0, 0]
    nred = 0
    kev = max(slot_ev)

    def emit_qkmax(i):
        src = QT[i] if i < NC else KT[i - NC]
        dst = qmx if i < NC else kmx
        c = i % NC
        return p.op("vector", lambda e, src=src, dst=dst, c=c: e.reduce_max(
            dst[:, c:c + 1], src[:, :], axis=AXX, apply_absolute_value=True),
            waits=[(sS, kev)] if i == 0 else [], inc=sV)

    for t in range(NT):
        for ofs in range(2):
            g = t * 2 + ofs
            slot = g % 4
            wl = [(sGq, wv_ld[NC - 1]), (sV, bc_ev), (sV, vinit)]
            if prev_s_ev[slot]:
                wl.append((sS, prev_s_ev[slot]))
                prev_s_ev[slot] = 0
            if slot_vv[slot]:
                wl.append((sV, slot_vv[slot]))
            last = 0
            for ic in range(NC):
                last = p.op("tensor", lambda e, t=t, ofs=ofs, ic=ic, slot=slot: e.matmul(
                    pq[slot][:, :], B1[ic][:, t * 128:(t + 1) * 128],
                    wsb[0][ic][:, ofs * 512:(ofs + 1) * 512],
                    start=(ic == 0), stop=(ic == NC - 1)),
                    waits=wl if ic == 0 else [],
                    inc=sT if ic == NC - 1 else None)
            slot_vv[slot] = p.op("vector", lambda e, t=t, ofs=ofs, slot=slot: e.scalar_tensor_tensor(
                out=vaug(t).ap().rearrange("p (h c) -> p h c", c=65)[:, ofs * 8:(ofs + 1) * 8, 0:64],
                in0=pq[slot][:, :], scalar=1.0,
                in1=bvB[:, ofs * 512:(ofs + 1) * 512],
                op0=ALU.mult, op1=ALU.add),
                waits=[(sT, last)], inc=sV)
            if g % 2 == 1 and g // 2 < 2 * NC:
                nred = emit_qkmax(g // 2)

    for idx in range(NPC1, len(piece_srcs)):
        emit_piece(idx)

    n1 = p.op("tensor", lambda e: e.matmul(pms[0:1, 0:64], onescol[:, :], wsum[:, 0:64],
                                           start=True, stop=True),
              waits=[(sV, red_done[-1])], inc=sT)
    n2 = p.op("scalar", lambda e: e.copy(mrow[0:1, 0:64], pms[0:1, 0:64]),
              waits=[(sT, n1)], inc=sS)
    p.op("vector", lambda e: e.reduce_sum(mtot[0:1, 0:1], mrow[0:1, 0:32], axis=AXX),
         waits=[(sS, n2)])
    nr2 = p.op("vector", lambda e: e.reduce_sum(mtot[0:1, 1:2], mrow[0:1, 32:64], axis=AXX))
    n3 = p.op("vector", lambda e: e.tensor_scalar(out=mtot[:], in0=mtot[:],
                                             scalar1=1.0 / (D * DH), scalar2=1e-5,
                                             op0=ALU.mult, op1=ALU.max),
              waits=[(sV, nr2)], inc=sV)
    n4 = p.op("tensor", lambda e: e.matmul(pms[:, 0:2], ones128[:, :], mtot[0:1, :],
                                           start=True, stop=True),
              waits=[(sV, n3)], inc=sT)
    n5 = p.op("scalar", lambda e: e.copy(mwc[:], pms[0:128, 0:2]), waits=[(sT, n4)], inc=sS)
    p.op("vector", lambda e: e.reciprocal(swc[:, 0:1], mwc[:, 0:1]), waits=[(sS, n5)])
    n6 = p.op("vector", lambda e: e.reciprocal(swc[:, 1:2], mwc[:, 1:2]), inc=sV)
    if DBG:
        p.op("vector", lambda e: e.tensor_copy(dbgs[:, 32:34], mwc[:, :]),
             waits=[(sV, n6)])
        p.op("vector", lambda e: e.tensor_copy(dbgs[:, 34:36], swc[:, :]))
        p.op("vector", lambda e: e.tensor_copy(dbgs[0:1, 36:60], mrow[0:1, 24:48]))
        p.op("vector", lambda e: e.tensor_copy(dbgs[0:1, 60:62], mtot[0:1, :]))

    nq = p.op("vector", lambda e: e.tensor_mul(qkp[:], qmx[:], kmx[:]),
              waits=[(sV, nred)], inc=sV)
    nm = p.op("tensor", lambda e: e.matmul(pms[0:2, 0:8], halves[:, :], qkp[:, :],
                                           start=True, stop=True),
              waits=[(sV, nq), (sS, n5)], inc=sT)
    ne = p.op("scalar", lambda e: e.copy(c2x8[:], pms[0:2, 0:8]), waits=[(sT, nm)], inc=sS)
    nts = p.op("vector", lambda e: e.tensor_scalar(out=c2x8[:], in0=c2x8[:], scalar1=0.125,
                                                   scalar2=-40.0, op0=ALU.mult, op1=ALU.add),
               waits=[(sS, ne)])
    nrl = p.op("vector", lambda e: e.tensor_relu(c2x8[:], c2x8[:]), waits=[(sV, nts)])
    nr = p.op("vector", lambda e: e.tensor_scalar(out=c2x8[:], in0=c2x8[:], scalar1=-1.0,
                                                  scalar2=None, op0=ALU.mult),
              waits=[(sV, nrl)], inc=sV)
    nc1 = p.op("vector", lambda e: e.tensor_copy(crow[0:1, 0:8], c2x8[0:1, :]),
               waits=[(sV, nr)], inc=sV)
    nd = p.op("sync", lambda e: e.dma_start(out=crow[0:1, 8:16], in_=c2x8[1:2, :]),
              waits=[(sV, nr)], inc=sD, amt=16)
    nb1 = p.op("tensor", lambda e: e.matmul(pms[:, 0:8], ones128[:, :], crow[0:1, 0:8],
                                            start=True, stop=True),
               waits=[(sV, nc1), (sD, nd), (sS, ne)], inc=sT)
    nb1e = p.op("scalar", lambda e: e.copy(
        negC.ap().rearrange("p (a b) -> p a b", b=2)[:, :, 0:1], pms[0:128, 0:8]),
        waits=[(sT, nb1)], inc=sS)
    nb2 = p.op("tensor", lambda e: e.matmul(pms[:, 0:8], ones128[:, :], crow[0:1, 8:16],
                                            start=True, stop=True),
               waits=[(sS, nb1e)], inc=sT)
    nb2e = p.op("scalar", lambda e: e.copy(
        negC.ap().rearrange("p (a b) -> p a b", b=2)[:, :, 1:2], pms[0:128, 0:8]),
        waits=[(sT, nb2)], inc=sS)
    nec = p.op("scalar", lambda e: e.activation(out=eC[:], in_=negC[:], func=AF.Exp,
                                                bias=0.0, scale=1.0),
               waits=[(sS, nb2e)], inc=sS)


    run_block(nc, p, hwm, "b2_qkv")
    free(*[f"wsb{s}_{c}" for s in range(2) for c in range(NC)],
         "wstage_0", "wstage_1", "pq_0", "pq_1", "pq_2", "pq_3", "pms")

    B2 = [alloc(f"B2_{t}", [128, D], BF16, side="right") for t in range(NT)]
    expraw = [alloc(f"expraw_{i}", [128, 384], BF16) for i in range(3)]
    expm = [alloc(f"expm_{i}", [128, 384], BF16) for i in range(3)]
    rodc = [alloc(f"rodc_{i}", [128, 1], F32) for i in range(2)]
    psc = [alloc(f"psc_{i}", [128, 512], F32, "psum") for i in range(3)]
    pod = [alloc(f"pod_{i}", [128, 512], F32, "psum") for i in range(4)]

    p = Plan()
    wstage3 = [alloc(f"wstage3_{i}", [128, 1024], F32) for i in range(4)]
    tern_u16 = [alloc(f"tern_u16_{i}", [128, 1024], I16) for i in range(2)]
    tern2b = [alloc(f"tern2b_{i}", [128, 1024], BF16) for i in range(2)]
    tern_jobs = []
    for ch in range(NC):
        for q in range(4):
            tern_jobs.append((dw1t[ch * 128:(ch + 1) * 128, q * 1024:(q + 1) * 1024],
                              dw1b[q, :, ch * 1024:(ch + 1) * 1024], 0))
    for ch in range(HCH):
        tern_jobs.append((dw2t[ch * 128:(ch + 1) * 128, :],
                          dw2b[:, ch * 1024:(ch + 1) * 1024], 1))
    tstate = {"sp": [0, 0, 0, 0]}

    def tern_steps():
        for i0 in range(0, len(tern_jobs), 4):
            picks = [k for k in range(i0, min(i0 + 4, len(tern_jobs)))]
            lds, n1s, n2s = {}, {}, {}
            for k in picks:
                srcA, dst, wi = tern_jobs[k]
                slot = k % 4
                wl = [(sV, n6)]
                if tstate["sp"][slot]:
                    wl.append((sSp[slot], tstate["sp"][slot]))
                lds[k] = p.op("sync", lambda e, srcA=srcA, slot=slot: e.dma_start(
                    out=wstage3[slot][:, :], in_=srcA), waits=wl, inc=sW[slot], amt=16)
                yield
            for j0 in range(0, len(picks), 2):
                pair = picks[j0:j0 + 2]
                for k in pair:
                    wi = tern_jobs[k][2]
                    slot = k % 4
                    wl = [(sW[slot], lds[k])]
                    if tstate.get("u16", [0, 0])[k % 2]:
                        wl.append((sV, tstate["u16"][k % 2]))
                    n1s[k] = p.op("vector", lambda e, slot=slot, wi=wi, k=k: e.tensor_scalar(
                        out=tern_u16[k % 2][:, :], in0=wstage3[slot][:, :],
                        scalar1=swc[:, wi:wi + 1], scalar2=RC, op0=ALU.mult, op1=ALU.add),
                        waits=wl)
                    yield
                for k in pair:
                    slot = k % 4
                    n2s[k] = p.op("vector", lambda e, slot=slot, k=k: e.tensor_scalar(
                        out=wstage3[slot][:, :], in0=tern_u16[k % 2][:, :],
                        scalar1=RD, scalar2=1.0, op0=ALU.subtract, op1=ALU.min),
                        waits=[(sV, n1s[k])])
                    tstate.setdefault("u16", [0, 0])[k % 2] = n2s[k]
                    yield
                for k in pair:
                    slot = k % 4
                    wl = [(sV, n2s[k])]
                    if tstate.get("t2b", [0, 0])[k % 2]:
                        wl.append((sSp[(k - 2) % 4], tstate["t2b"][k % 2]))
                    n2s[k] = p.op("vector", lambda e, slot=slot, k=k: e.tensor_scalar(
                        out=tern2b[k % 2][:, :], in0=wstage3[slot][:, :],
                        scalar1=-1.0, scalar2=None, op0=ALU.max),
                        waits=wl, inc=sV)
                    yield
                for k in pair:
                    slot = k % 4
                    dst = tern_jobs[k][1]
                    tstate["sp"][slot] = p.op("sync", lambda e, dst=dst, slot=slot, k=k: e.dma_start(
                        out=dst, in_=tern2b[k % 2][:, :]), waits=[(sV, n2s[k])],
                        inc=sSp[slot], amt=16)
                    tstate.setdefault("t2b", [0, 0])[k % 2] = tstate["sp"][slot]
                    yield

    tern_gen = tern_steps()

    def win(j):
        q0 = (j - 1) * 128
        q1 = (j + 2) * 128
        c0 = max(0, -q0)
        q0r = max(q0, 0)
        q1r = min(q1, T)
        return q0r, q1r, c0, c0 + (q1r - q0r)

    pod_ev = [0, 0, 0, 0]
    psc_ev = [0, 0, 0]
    expm_use = [0, 0, 0]
    mask_done = [0, 0, 0]
    rodc_use = [0, 0]
    mask_info = {}

    def emit_av(h, j):
        sl = (j + 1) % 3
        msem, nmask = mask_info.pop((h, j))
        last_use = 0
        for s in range(3):
            qb = j - 1 + s
            if qb < 0 or qb >= NJ:
                continue
            wc = qb * 128 - (j - 1) * 128
            first = (j == qb - 1)
            lastc = (j == qb + 1)
            wl = [(msem, nmask)]
            if first and pod_ev[qb % 4]:
                wl.append((sV, pod_ev[qb % 4]))
            last_use = p.op("tensor", lambda e, qb=qb, j=j, h=h, sl=sl, wc=wc, first=first, lastc=lastc: e.matmul(
                pod[qb % 4][:, 0:65], expm[sl][:, wc:wc + 128],
                vaug(j)[:, h * 65:h * 65 + 65], start=first, stop=lastc),
                waits=wl, inc=sT)
            if lastc:
                wl = [(sT, last_use)]
                if rodc_use[qb % 2]:
                    wl.append((sV, rodc_use[qb % 2]))
                nrc = p.op("vector", lambda e, qb=qb: e.reciprocal(
                    rodc[qb % 2][:, :], pod[qb % 4][:, 64:65]), waits=wl, inc=sV)
                pod_ev[qb % 4] = p.op("vector", lambda e, qb=qb, h=h: e.tensor_scalar(
                    out=B2[qb][:, h * 64:(h + 1) * 64], in0=pod[qb % 4][:, 0:64],
                    scalar1=rodc[qb % 2][:, :], scalar2=None, op0=ALU.mult),
                    waits=[(sV, nrc)], inc=sV)
                rodc_use[qb % 2] = pod_ev[qb % 4]
        if last_use:
            expm_use[sl] = last_use

    seq = [(h, j) for h in range(H) for j in range(-1, NJ + 1)]
    for idx, (h, j) in enumerate(seq):
        hc = h // 2
        hr = (h % 2) * 64
        sl = (j + 1) % 3
        q0r, q1r, c0, c1 = win(j)
        pad = (j < 0) or (j >= NJ)
        if not pad:
            wl = [(sS, nec)]
            if psc_ev[sl]:
                wl.append((sS, psc_ev[sl]))
            nmm = p.op("tensor", lambda e, hc=hc, hr=hr, j=j, sl=sl, q0r=q0r, q1r=q1r, c0=c0, c1=c1: e.matmul(
                psc[sl][:, c0:c1], KT[hc][hr:hr + 64, j * 128:(j + 1) * 128],
                QT[hc][hr:hr + 64, q0r:q1r], start=True, stop=True),
                waits=wl, inc=sT)
            wl = [(sT, nmm)]
            if mask_done[sl]:
                wl.append((sP, mask_done[sl]))
            psc_ev[sl] = p.op("scalar", lambda e, h=h, sl=sl, c0=c0, c1=c1: e.activation(
                out=expraw[sl][:, c0:c1], in_=psc[sl][:, c0:c1], func=AF.Exp,
                bias=negC[:, h:h + 1], scale=0.125), waits=wl, inc=sS)
            wl = [(sS, psc_ev[sl])]
            if expm_use[sl]:
                wl.append((sT, expm_use[sl]))
            mask_done[sl] = p.op("gpsimd", lambda e, sl=sl, c0=c0, c1=c1: e.tensor_tensor(
                expm[sl][:, c0:c1], expraw[sl][:, c0:c1], mask01[:, c0:c1], ALU.mult),
                waits=wl, inc=sP)
            mask_info[(h, j)] = (sP, mask_done[sl])
        else:
            wl = [(sS, nec)]
            if expm_use[sl]:
                wl.append((sT, expm_use[sl]))
            nmask = p.op("vector", lambda e, h=h, sl=sl, c0=c0, c1=c1: e.tensor_scalar(
                out=expm[sl][:, c0:c1], in0=mask01[:, c0:c1],
                scalar1=eC[:, h:h + 1], scalar2=None, op0=ALU.mult),
                waits=wl, inc=sV)
            mask_info[(h, j)] = (sV, nmask)
        if idx >= 2:
            emit_av(*seq[idx - 2])
        for _ in range(2):
            next(tern_gen, None)
    emit_av(*seq[-2])
    emit_av(*seq[-1])
    for _ in tern_gen:
        pass
    p.op("sync", lambda e: e.nop(), waits=[(sD, sD.n)] +
         [(s_, s_.n) for s_ in sW] + [(s_, s_.n) for s_ in sSp])
    run_block(nc, p, hwm, "b3_attn")
    free(*[f"QT{c}" for c in range(NC)], *[f"KT{c}" for c in range(NC)],
         *[f"VA{j}" for j in range(NJ + 2)],
         *[f"wstage3_{i}" for i in range(4)], "tern_u16_0", "tern_u16_1",
         "tern2b_0", "tern2b_1",
         "expraw_0", "expraw_1", "expraw_2", "expm_0", "expm_1", "expm_2",
         "rodc_0", "rodc_1",
         "psc_0", "psc_1", "psc_2", "pod_0", "pod_1", "pod_2", "pod_3")

    wosb = [alloc(f"wo_{c}", [128, D], BF16) for c in range(NC)]
    attn = [alloc(f"attn_{i}", [128, D], F32) for i in range(2)]
    x2f = [alloc(f"x2f_{i}", [128, D], F32) for i in range(2)]
    x3f = [alloc(f"x3f_{i}", [128, D], F32) for i in range(2)]
    xre = [alloc(f"xre_{i}", [128, D], F32) for i in range(2)]
    q16 = alloc("q16", [128, D], I16)
    x3q = [alloc(f"x3q_{i}", [128, D], BF16) for i in range(2)]
    x3st = [alloc(f"x3st_{i}", [128, D], BF16) for i in range(2)]
    sxc = alloc("sxc", [128, 1], F32)
    sxic = alloc("sxic", [128, 1], F32)
    ptrO = [alloc(f"ptrO_{i}", [128, 1024], BF16, "psum") for i in range(2)]
    ptrX = [alloc(f"ptrX_{i}", [128, 1024], BF16, "psum") for i in range(2)]
    po4 = [alloc(f"po4_{i}", [128, 512], F32, "psum") for i in range(2)]

    p = Plan()
    if DBG:
        for t in range(NT):
            p.op("sync", lambda e, t=t: e.dma_start(
                out=dbgO[t * 128:(t + 1) * 128, :], in_=B2[t][:, :]), inc=sD, amt=16)
    wo_ld = [0] * NC
    for c in range(NC):
        wo_ld[c] = p.op("gpsimd", lambda e, c=c: e.dma_start(
            out=wosb[c][:], in_=dw["wo"][c * 128:(c + 1) * 128, :]), inc=sGq, amt=16)
    otr = [0] * NT
    oev = [0] * NT
    po_ev = [0, 0]
    x2sp = [0, 0]
    x3sp = [0, 0]
    apply2 = [0] * NT
    apply3 = [0] * NT
    q_done = [0] * NT
    x3tr = [0] * NT
    x3ev = [0] * NT
    for t in range(NT):
        for c in range(NC):
            wl = []
            if c == 0 and t >= 2:
                wl.append((sS, oev[t - 2]))
            otr[t] = p.op("tensor", lambda e, t=t, c=c: e.transpose(
                ptrO[t % 2][:, c * 128:(c + 1) * 128],
                B2[t][:, c * 128:(c + 1) * 128], iden[:]),
                waits=wl, inc=sT if c == NC - 1 else None)
        oev[t] = p.op("scalar", lambda e, t=t: e.copy(
            B1r3[:, :, t * 128:(t + 1) * 128],
            ptrO[t % 2].ap().rearrange("p (c q) -> p c q", q=128)[:, :, :]),
            waits=[(sT, otr[t])], inc=sS)
        xre_ld = p.op("sync", lambda e, t=t: e.dma_start(
            out=xre[t % 2][:], in_=dx[t * 128:(t + 1) * 128, :]),
            waits=[(sV, apply2[t - 2])] if t >= 2 else [], inc=sX[t % 2], amt=16)
        for ofs in range(2):
            slot = ofs
            wl = [(sS, oev[t]), (sGq, wo_ld[NC - 1])]
            if po_ev[slot]:
                wl.append((sV, po_ev[slot]))
            last = 0
            for c in range(NC):
                last = p.op("tensor", lambda e, t=t, ofs=ofs, c=c, slot=slot: e.matmul(
                    po4[slot][:, :], B1[c][:, t * 128:(t + 1) * 128],
                    wosb[c][:, ofs * 512:(ofs + 1) * 512],
                    start=(c == 0), stop=(c == NC - 1)),
                    waits=wl if c == 0 else [],
                    inc=sT if c == NC - 1 else None)
            wl2 = [(sT, last)]
            if t >= 2:
                wl2.append((sV, apply2[t - 2]))
            po_ev[slot] = p.op("vector", lambda e, t=t, ofs=ofs, slot=slot: e.scalar_tensor_tensor(
                out=attn[t % 2][:, ofs * 512:(ofs + 1) * 512], in0=po4[slot][:, :],
                scalar=1.0, in1=boB[:, ofs * 512:(ofs + 1) * 512],
                op0=ALU.mult, op1=ALU.add), waits=wl2, inc=sV)
        if t >= 1:
            emit_x3t(t - 1)
        p.op("vector", lambda e, t=t: e.bn_stats(stat[:, 0:6], attn[t % 2][:, 0:512]),
             waits=[(sV, po_ev[1])])
        nsb = p.op("vector", lambda e, t=t: e.bn_stats(stat[:, 6:12], attn[t % 2][:, 512:1024]))
        nag = p.op("vector", lambda e: e.bn_aggr(aggr[:], stat[:]), waits=[(sV, nsb)], inc=sV)
        nsq = p.op("scalar", lambda e: e.activation(out=scol[:], in_=aggr[:, 1:2],
                                                    func=AF.Sqrt, bias=epsc[:], scale=1.0),
                   waits=[(sV, nag)], inc=sS)
        nrcp = p.op("vector", lambda e: e.reciprocal(rcol[:], scol[:]), waits=[(sS, nsq)])
        if DBG and t == 0:
            p.op("vector", lambda e: e.tensor_copy(dbgs[:, 64:66], aggr[:, :]))
            p.op("vector", lambda e: e.tensor_copy(dbgs[:, 66:67], scol[:, :]))
            p.op("vector", lambda e: e.tensor_copy(dbgs[:, 67:68], rcol[:, :]),
                 waits=[(sV, nrcp)])
            p.op("vector", lambda e: e.tensor_copy(dbgs[:, 72:76], attn[0][:, 0:4]))
        napl2 = p.op("vector", lambda e, t=t: e.tensor_scalar(
            out=attn[t % 2][:], in0=attn[t % 2][:], scalar1=aggr[:, 0:1], scalar2=rcol[:],
            op0=ALU.subtract, op1=ALU.mult), waits=[(sV, nrcp)])
        wl = [(sX[t % 2], xre_ld), (sV, napl2)]
        if x2sp[t % 2]:
            wl.append((sGs[t % 2], x2sp[t % 2]))
        apply2[t] = p.op("vector", lambda e, t=t: e.tensor_add(
            x2f[t % 2][:], attn[t % 2][:], xre[t % 2][:]), waits=wl, inc=sV)
        x2sp[t % 2] = p.op("gpsimd", lambda e, t=t: e.dma_start(
            out=dx2[t * 128:(t + 1) * 128, :], in_=x2f[t % 2][:, :]),
            waits=[(sV, apply2[t])], inc=sGs[t % 2], amt=16)
        p.op("vector", lambda e, t=t: e.bn_stats(stat[:, 0:6], x2f[t % 2][:, 0:512]),
             waits=[(sV, apply2[t])])
        nsb = p.op("vector", lambda e, t=t: e.bn_stats(stat[:, 6:12], x2f[t % 2][:, 512:1024]))
        nag = p.op("vector", lambda e: e.bn_aggr(aggr[:], stat[:]), waits=[(sV, nsb)], inc=sV)
        nsq = p.op("scalar", lambda e: e.activation(out=scol[:], in_=aggr[:, 1:2],
                                                    func=AF.Sqrt, bias=epsc[:], scale=1.0),
                   waits=[(sV, nag)], inc=sS)
        nrcp2 = p.op("vector", lambda e: e.reciprocal(rcol[:], scol[:]), waits=[(sS, nsq)])
        if DBG and t == 0:
            p.op("vector", lambda e: e.tensor_copy(dbgs[:, 68:70], aggr[:, :]))
            p.op("vector", lambda e: e.tensor_copy(dbgs[:, 70:71], scol[:, :]))
            p.op("vector", lambda e: e.tensor_copy(dbgs[:, 71:72], rcol[:, :]),
                 waits=[(sV, nrcp2)])
        wl = [(sV, nrcp2)]
        if t >= 2:
            wl.append((sV, q_done[t - 2]))
        apply3[t] = p.op("vector", lambda e, t=t: e.tensor_scalar(
            out=x3f[t % 2][:], in0=x2f[t % 2][:], scalar1=aggr[:, 0:1], scalar2=rcol[:],
            op0=ALU.subtract, op1=ALU.mult), waits=wl, inc=sV)
        wl = [(sV, apply3[t])]
        if t >= 1:
            wl.append((sV, q_done[t - 1]))
        nred4 = p.op("vector", lambda e, t=t: e.reduce_max(
            sxc[:, :], x3f[t % 2][:, :], axis=AXX, apply_absolute_value=True), waits=wl)
        nclp = p.op("vector", lambda e: e.tensor_scalar(out=sxc[:], in0=sxc[:], scalar1=1e-5,
                                                        scalar2=None, op0=ALU.max),
                    waits=[(sV, nred4)])
        nrs = p.op("vector", lambda e: e.reciprocal(sxic[:], sxc[:]), waits=[(sV, nclp)])
        nm127 = p.op("vector", lambda e: e.tensor_scalar(out=sxic[:], in0=sxic[:],
                                                         scalar1=127.0, scalar2=None,
                                                         op0=ALU.mult), waits=[(sV, nrs)])
        nq16t = p.op("vector", lambda e, t=t: e.tensor_scalar(
            out=q16[:, :], in0=x3f[t % 2][:], scalar1=sxic[:], scalar2=RC,
            op0=ALU.mult, op1=ALU.add), waits=[(sV, nm127)])
        nsxf = p.op("vector", lambda e: e.tensor_scalar(out=sxc[:], in0=sxc[:],
                                                        scalar1=1.0 / 127.0,
                                                        scalar2=None, op0=ALU.mult))
        if DBG:
            p.op("vector", lambda e, t=t: e.tensor_copy(dbgs[:, t:t + 1], sxc[:, :]),
                 waits=[(sV, nsxf)])
            p.op("vector", lambda e, t=t: e.tensor_copy(dbgs[:, 16 + t:17 + t], sxic[:, :]))
        wl = [(sV, nq16t), (sV, nsxf)]
        if t >= 2:
            wl.append((sS, x3ev[t - 2]))
        q_done[t] = p.op("vector", lambda e, t=t: e.tensor_scalar(
            out=x3q[t % 2][:], in0=q16[:, :], scalar1=RD, scalar2=sxc[:],
            op0=ALU.subtract, op1=ALU.mult), waits=wl, inc=sV)

        def emit_x3t(tp):
            for c in range(NC):
                wl = []
                if c == 0:
                    wl.append((sV, q_done[tp]))
                    if tp >= 2:
                        wl.append((sS, x3ev[tp - 2]))
                x3tr[tp] = p.op("tensor", lambda e, tp=tp, c=c: e.transpose(
                    ptrX[tp % 2][:, c * 128:(c + 1) * 128],
                    x3q[tp % 2][:, c * 128:(c + 1) * 128], iden[:]),
                    waits=wl, inc=sT if c == NC - 1 else None)
            wl = [(sT, x3tr[tp])]
            if x3sp[tp % 2]:
                wl.append((sSp[tp % 2], x3sp[tp % 2]))
            x3ev[tp] = p.op("scalar", lambda e, tp=tp: e.copy(
                x3st[tp % 2][:, :], ptrX[tp % 2][:, :]), waits=wl, inc=sS)
            x3sp[tp % 2] = p.op("sync", lambda e, tp=tp: e.dma_start(
                out=dx3t[:, :, 1 + tp * 128:1 + (tp + 1) * 128].transpose([1, 0, 2]),
                in_=x3st[tp % 2][:, :]), waits=[(sS, x3ev[tp])],
                inc=sSp[tp % 2], amt=16)

        if t == NT - 1:
            emit_x3t(t)
    p.op("sync", lambda e: e.nop(), waits=[(sSp[0], sSp[0].n), (sSp[1], sSp[1].n),
                                           (sX[0], sX[0].n), (sX[1], sX[1].n)])
    p.op("gpsimd", lambda e: e.nop(), waits=[(sGs[0], sGs[0].n), (sGs[1], sGs[1].n),
                                             (sGq, sGq.n)])
    run_block(nc, p, hwm, "b4_oproj")
    free(*[f"wo_{c}" for c in range(NC)], "B1all",
         *[f"B2_{t}" for t in range(NT)],
         "bvB", "boB",
         "attn_0", "attn_1", "x2f_0", "x2f_1", "x3f_0", "x3f_1", "xre_0", "xre_1",
         "q16", "x3q_0", "x3q_1", "x3st_0", "x3st_1", "sxc", "sxic",
         "ptrO_0", "ptrO_1", "ptrX_0", "ptrX_1", "po4_0", "po4_1")

    diag = alloc("diag", [128, 3 * HCH * 128], BF16)
    w1h = [alloc(f"w1h_{i}", [128, 4 * 1024], BF16) for i in range(2)]
    w2b = [alloc(f"w2b_{i}", [128, 4 * 1024], BF16) for i in range(2)]
    x3w = [alloc(f"x3w_{i}", [128, 8 * 256], BF16) for i in range(3)]
    Ab = [alloc(f"Ab_{i}", [128, HCH * 258], BF16) for i in range(2)]
    H2 = alloc("H2", [128, HCH * 256], F32)
    H2q = [alloc(f"H2q_{i}", [128, HCH * 256], BF16) for i in range(2)]
    tland = [alloc(f"tland_{i}", [128, 256], F32) for i in range(2)]
    q16b = [alloc(f"q16b_{i}", [128, 256], I16) for i in range(2)]
    apar = alloc("apar", [128, 4 * 256], F32)
    amaxC = alloc("amaxC", [128, 256], F32)
    v2 = alloc("v2", [128, 4], F32)
    sxr = alloc("sxr", [1, 256], F32)
    sxB = alloc("sxB", [128, 256], F32)
    sc2b = [alloc(f"sc2b_{i}", [128, 2], F32) for i in range(2)]
    out2 = [alloc(f"out2_{i}", [128, D], F32) for i in range(2)]
    x4b = [alloc(f"x4b_{i}", [128, D], F32) for i in range(2)]
    x2re = [alloc(f"x2re_{i}", [128, D], BF16) for i in range(2)]
    pA = [alloc(f"pA_{i}", [128, 512], F32, "psum") for i in range(2)]
    pC = alloc("pC", [128, 512], F32, "psum")
    pM = [alloc(f"pM_{i}", [128, 512], F32, "psum") for i in range(4)]
    pX = alloc("pX", [128, 512], F32, "psum")

    p = Plan()
    dg_ld = p.op("gpsimd", lambda e: e.dma_start(
        out=diag[:, :], in_=ddiag[:, :]), inc=sG, amt=16)

    def dg(k, c):
        return diag[:, (k * HCH + c) * 128:(k * HCH + c) * 128 + 128]

    def ab(s, c):
        return Ab[s % 2][:, c * 258:(c + 1) * 258]

    x3_ld = [0] * (NS + 2)
    a_mm = [[0] * HCH for _ in range(NS + 2)]
    a_ev = [[0] * HCH for _ in range(NS + 2)]
    halo_d = [[0] * HCH for _ in range(NS + 2)]
    conv_mm = [[0] * HCH for _ in range(NS)]
    silu_d = [[0] * HCH for _ in range(NS)]
    ntp_d = [[0] * HCH for _ in range(NS)]
    h2q_d = [[0] * HCH for _ in range(NS)]
    apar_d = [[0] * 4 for _ in range(NS)]
    mm2_last = [0] * NS
    w1_ld = {}
    w1_used = {}
    pa_ev = [0, 0]
    pc_ev = [0]
    pm_ev = [0, 0, 0, 0]
    w2_used = [0, 0]
    x2re_used = [0, 0]
    out_dma = [0, 0]
    vcomb = [0]
    sxB_ev = [0]
    sc2_d = [0, 0]
    trans_last = [0]
    ktrans_last = [0]

    def emit_x3ld(s2):
        par = s2 % 3
        if s2 == NS:
            src = dx3t[:, :, s2 * TS:s2 * TS + 2].transpose([1, 0, 2])
            dst = x3w[par].ap().rearrange("p (c n) -> p c n", n=256)[:, :, 0:2]
        else:
            src = dx3t[:, :, s2 * TS:s2 * TS + 256].transpose([1, 0, 2])
            dst = x3w[par][:, :]
        wl = [(sT, a_mm[s2 - 3][HCH - 1])] if s2 >= 3 else []
        if s2 >= 2:
            wl.append((sX[s2 % 2], x3_ld[s2 - 2]))
        x3_ld[s2] = p.op("sync", lambda e, src=src, dst=dst: e.dma_start(out=dst, in_=src),
                         waits=wl, inc=sX[s2 % 2], amt=16)

    def emit_w1ld(s2, q):
        j = s2 * 8 + q
        wl = [(sT, w1_used[j - 2])] if (j - 2) in w1_used else []
        g = q // 2
        half = q % 2
        src = dw1b[g, :, :].rearrange("p (c h) -> p c h", h=1024)[
            :, :, half * 512:(half + 1) * 512]
        w1_ld[j] = p.op("gpsimd", lambda e, src=src, j=j: e.dma_start(
            out=w1h[j % 2][:, :], in_=src), waits=wl, inc=sGw[j % 2], amt=16)

    def emit_A_chunk(s2, hcx):
        par = s2 % 3
        width = 2 if s2 == NS else 256
        slotp = hcx % 2
        j = s2 * 8 + hcx // 4
        hs2 = hcx % 4
        wl = [(sX[s2 % 2], x3_ld[s2]), (sGw[j % 2], w1_ld[j])]
        if pa_ev[slotp]:
            wl.append((sS, pa_ev[slotp]))
        last = 0
        for d in range(NC):
            if s2 == NS:
                mv = x3w[par].ap().rearrange("p (c n) -> p c n", n=256)[:, d, 0:2]
            else:
                mv = x3w[par][:, d * 256:(d + 1) * 256]
            last = p.op("tensor", lambda e, j=j, hs2=hs2, d=d, slotp=slotp, mv=mv, width=width: e.matmul(
                pA[slotp][:, 0:width],
                w1h[j % 2][:, d * 512 + hs2 * 128:d * 512 + hs2 * 128 + 128],
                mv, start=(d == 0), stop=(d == NC - 1)),
                waits=wl if d == 0 else [],
                inc=sT if d == NC - 1 else None)
        a_mm[s2][hcx] = last
        w1_used[j] = last
        wl = [(sT, last)]
        if s2 >= 2:
            wl.append((sT, conv_mm[s2 - 2][hcx]))
        pa_ev[slotp] = p.op("scalar", lambda e, s2=s2, hcx=hcx, slotp=slotp, width=width: e.activation(
            out=ab(s2, hcx)[:, 0:width], in_=pA[slotp][:, 0:width],
            func=AF.Identity, bias=0.0, scale=mwc[:, 0:1]), waits=wl, inc=sS)
        a_ev[s2][hcx] = pa_ev[slotp]
        if s2 >= 1:
            halo_d[s2 - 1][hcx] = p.op("scalar", lambda e, s2=s2, hcx=hcx: e.copy(
                ab(s2 - 1, hcx)[:, 256:258], ab(s2, hcx)[:, 0:2]),
                waits=[(sS, a_ev[s2][hcx])], inc=sS)

    def emit_conv_chunk(s, hcx):
        wl = [(sS, halo_d[s][hcx]), (sG, dg_ld)]
        if pc_ev[0]:
            wl.append((sS, pc_ev[0]))
        if s >= 1:
            wl.append((sV, ntp_d[s - 1][hcx]))
            wl.append((sV, apar_d[s - 1][hcx // 8]))
        last = 0
        for k in range(3):
            last = p.op("tensor", lambda e, s=s, hcx=hcx, k=k: e.matmul(
                pC[:, 0:256], dg(k, hcx), ab(s, hcx)[:, k:k + 256],
                start=(k == 0), stop=(k == 2)),
                waits=wl if k == 0 else [],
                inc=sT if k == 2 else None)
        conv_mm[s][hcx] = last
        silu_d[s][hcx] = p.op("scalar", lambda e, s=s, hcx=hcx: e.activation(
            out=H2[:, hcx * 256:(hcx + 1) * 256], in_=pC[:, 0:256],
            func=AF.Silu, bias=cbs[:, hcx:hcx + 1], scale=1.0),
            waits=[(sT, last)], inc=sS)
        pc_ev[0] = silu_d[s][hcx]

    def emit_mm2(sm):
        par = sm % 2
        x2_ld = [0, 0]
        for tsl in range(2):
            t = sm * 2 + tsl
            wl = [(sV, x2re_used[tsl])] if x2re_used[tsl] else []
            x2_ld[tsl] = p.op("sync", lambda e, t=t, tsl=tsl: e.dma_start(
                out=x2re[tsl][:, :], in_=dx2[t * 128:(t + 1) * 128, :]),
                waits=wl, inc=sX2[tsl], amt=16)
        last = 0
        for cc in range(0, HCH, 4):
            b = (cc // 4) % 2
            wl = [(sT, w2_used[b])] if w2_used[b] else []
            w2_ld = p.op("sync", lambda e, cc=cc, b=b: e.dma_start(
                out=w2b[b][:, :], in_=dw2b[:, cc * 1024:(cc + 4) * 1024]),
                waits=wl, inc=sW2[b], amt=16)
            for hcx in range(cc, cc + 4):
                qq = hcx % 4
                for tsl in range(2):
                    for ofs in range(2):
                        slot = tsl * 2 + ofs
                        wl = []
                        if hcx == cc and tsl == 0 and ofs == 0:
                            wl.append((sW2[b], w2_ld))
                            wl.append((sV, h2q_d[sm][cc + 3]))
                        if hcx == 0 and pm_ev[slot]:
                            wl.append((sV, pm_ev[slot]))
                        fin = (hcx == cc + 3 and tsl == 1 and ofs == 1)
                        last = p.op("tensor", lambda e, sm=sm, hcx=hcx, tsl=tsl, ofs=ofs, b=b, qq=qq, slot=slot: e.matmul(
                            pM[slot][:, :],
                            H2q[sm % 2][:, hcx * 256 + tsl * 128:hcx * 256 + tsl * 128 + 128],
                            w2b[b][:, qq * 1024 + ofs * 512:qq * 1024 + ofs * 512 + 512],
                            start=(hcx == 0), stop=(hcx == HCH - 1)),
                            waits=wl, inc=sT if fin else None)
            w2_used[b] = last
        mm2_last[sm] = last
        for tsl in range(2):
            for ofs in range(2):
                slot = tsl * 2 + ofs
                wl = [(sT, mm2_last[sm]), (sV, sc2_d[par])]
                if x2re_used[tsl]:
                    wl.append((sV, x2re_used[tsl]))
                pm_ev[slot] = p.op("vector", lambda e, tsl=tsl, ofs=ofs, slot=slot, par=par: e.tensor_scalar(
                    out=out2[tsl][:, ofs * 512:(ofs + 1) * 512], in0=pM[slot][:, :],
                    scalar1=sc2b[par][:, tsl:tsl + 1], scalar2=None, op0=ALU.mult),
                    waits=wl, inc=sV)
            p.op("vector", lambda e, tsl=tsl: e.bn_stats(stat[:, 0:6], out2[tsl][:, 0:512]),
                 waits=[(sV, pm_ev[tsl * 2 + 1])])
            nsb = p.op("vector", lambda e, tsl=tsl: e.bn_stats(stat[:, 6:12],
                                                               out2[tsl][:, 512:1024]))
            nag = p.op("vector", lambda e: e.bn_aggr(aggr[:], stat[:]),
                       waits=[(sV, nsb)], inc=sV)
            nsq = p.op("scalar", lambda e: e.activation(out=scol[:], in_=aggr[:, 1:2],
                                                        func=AF.Sqrt, bias=epsc[:], scale=1.0),
                       waits=[(sV, nag)], inc=sS)
            nrcp5 = p.op("vector", lambda e: e.reciprocal(rcol[:], scol[:]),
                         waits=[(sS, nsq)])
            napl4 = p.op("vector", lambda e, tsl=tsl: e.tensor_scalar(
                out=out2[tsl][:], in0=out2[tsl][:], scalar1=aggr[:, 0:1], scalar2=rcol[:],
                op0=ALU.subtract, op1=ALU.mult), waits=[(sV, nrcp5)])
            wl = [(sX2[tsl], x2_ld[tsl]), (sV, napl4)]
            if out_dma[tsl]:
                wl.append((sOut[tsl], out_dma[tsl]))
            x2re_used[tsl] = p.op("vector", lambda e, tsl=tsl: e.tensor_add(
                x4b[tsl][:], out2[tsl][:], x2re[tsl][:]), waits=wl, inc=sV)
            t = sm * 2 + tsl
            out_dma[tsl] = p.op("sync", lambda e, t=t, tsl=tsl: e.dma_start(
                out=dout[t * 128:(t + 1) * 128, :], in_=x4b[tsl][:, :]),
                waits=[(sV, x2re_used[tsl])], inc=sOut[tsl], amt=16)

    emit_x3ld(0)
    emit_x3ld(1)
    emit_x3ld(2)
    for s2 in (0, 1):
        for hcx in range(HCH):
            if hcx % 4 == 0:
                emit_w1ld(s2, hcx // 4)
            emit_A_chunk(s2, hcx)

    for s in range(NS):
        s2 = s + 2
        if s + 3 <= NS:
            emit_x3ld(s + 3)
        for hcx in range(HCH):
            emit_conv_chunk(s, hcx)
            if s2 <= NS:
                if hcx % 4 == 0:
                    emit_w1ld(s2, hcx // 4)
                emit_A_chunk(s2, hcx)
            if hcx % 8 == 7:
                g3 = hcx // 8
                wl = [(sS, silu_d[s][hcx])]
                if g3 == 0 and vcomb[0]:
                    wl.append((sV, vcomb[0]))
                apar_d[s][g3] = p.op("vector", lambda e, g3=g3: e.reduce_max(
                    apar[:, g3 * 256:(g3 + 1) * 256],
                    H2.ap().rearrange("p (c n) -> p n c", c=HCH)[:, :, g3 * 8:(g3 + 1) * 8],
                    axis=AXX, apply_absolute_value=True), waits=wl, inc=sV)
        cA = p.op("vector", lambda e: e.tensor_tensor(
            amaxC[:, :], apar[:, 0:256], apar[:, 256:512], ALU.max),
            waits=[(sV, apar_d[s][1]), (sT, trans_last[0])], inc=sV)
        cB = p.op("vector", lambda e: e.tensor_tensor(
            apar[:, 768:1024], apar[:, 512:768], apar[:, 768:1024], ALU.max),
            waits=[(sV, apar_d[s][3])], inc=sV)
        cF = p.op("vector", lambda e: e.tensor_tensor(
            amaxC[:, :], amaxC[:, :], apar[:, 768:1024], ALU.max),
            waits=[(sV, cA), (sV, cB)], inc=sV)
        vcomb[0] = cF
        red = 0
        for tt in range(2):
            wl = [(sV, cF), (sV, sxB_ev[0])] if tt == 0 else [(sV, red)]
            ntr = p.op("tensor", lambda e, tt=tt: e.transpose(
                pX[:, 0:128], amaxC[:, tt * 128:(tt + 1) * 128], idenf[:]),
                waits=wl, inc=sT)
            red = p.op("vector", lambda e, tt=tt: e.reduce_max(
                v2[:, tt:tt + 1], pX[0:128, 0:128], axis=AXX),
                waits=[(sT, ntr)], inc=sV)
            trans_last[0] = ntr
        ncl5 = p.op("vector", lambda e: e.tensor_scalar(out=v2[:, 0:2], in0=v2[:, 0:2],
                                                        scalar1=1e-5, scalar2=None,
                                                        op0=ALU.max),
                    waits=[(sV, red), (sT, ktrans_last[0])], inc=sV)
        p.op("vector", lambda e: e.reciprocal(v2[:, 2:3], v2[:, 0:1]), waits=[(sV, ncl5)])
        nrB = p.op("vector", lambda e: e.reciprocal(v2[:, 3:4], v2[:, 1:2]))
        nm7 = p.op("vector", lambda e: e.tensor_scalar(out=v2[:, 2:4], in0=v2[:, 2:4],
                                                       scalar1=127.0, scalar2=None,
                                                       op0=ALU.mult),
                   waits=[(sV, nrB)], inc=sV)
        nv2 = p.op("vector", lambda e: e.tensor_scalar(out=v2[:, 0:2], in0=v2[:, 0:2],
                                                       scalar1=1.0 / 127.0, scalar2=None,
                                                       op0=ALU.mult), inc=sV)
        for tsl in range(2):
            sc2_d[s % 2] = p.op("vector", lambda e, tsl=tsl, s=s: e.tensor_tensor(
                sc2b[s % 2][:, tsl:tsl + 1], v2[:, tsl:tsl + 1], mwc[:, 1:2], ALU.mult),
                waits=[(sV, nv2)], inc=sV)
        prev_ev = 0
        for k in range(2):
            wl = [(sV, nm7)]
            if prev_ev:
                wl.append((sV, prev_ev))
            ntr = p.op("tensor", lambda e, k=k: e.transpose(
                pX[0:1, 128:256], v2[:, 2 + k:3 + k], idenf[:]), waits=wl, inc=sT)
            prev_ev = p.op("vector", lambda e, k=k: e.tensor_copy(
                sxr[0:1, k * 128:(k + 1) * 128], pX[0:1, 128:256]),
                waits=[(sT, ntr)], inc=sV)
            ktrans_last[0] = ntr
        nb = p.op("tensor", lambda e: e.matmul(pX[:, 256:512], ones128[:, :],
                                               sxr[0:1, :], start=True, stop=True),
                  waits=[(sV, prev_ev)], inc=sT)
        wl = [(sT, nb)]
        if s >= 1:
            wl.append((sV, ntp_d[s - 1][HCH - 1]))
        sxB_ev[0] = p.op("vector", lambda e: e.tensor_copy(sxB[:, :], pX[0:128, 256:512]),
                         waits=wl, inc=sV)
        for h0 in range(0, HCH, 2):
            ntl = [0, 0]
            nuu = [0, 0]
            for i in range(2):
                hcx = h0 + i
                wl = [(sV, sxB_ev[0])]
                if hcx >= 2:
                    wl.append((sV, h2q_d[s][hcx - 2]))
                ntl[i] = p.op("vector", lambda e, hcx=hcx, i=i: e.tensor_tensor(
                    tland[i][:, :], H2[:, hcx * 256:(hcx + 1) * 256], sxB[:, :], ALU.mult),
                    waits=wl, inc=sV)
                ntp_d[s][hcx] = ntl[i]
            for i in range(2):
                nuu[i] = p.op("vector", lambda e, i=i: e.tensor_scalar(
                    out=q16b[i][:, :], in0=tland[i][:, :],
                    scalar1=RC, scalar2=None, op0=ALU.add),
                    waits=[(sV, ntl[i])], inc=sV)
            for i in range(2):
                hcx = h0 + i
                wl = [(sV, nuu[i])]
                if s >= 2:
                    wl.append((sT, mm2_last[s - 2]))
                h2q_d[s][hcx] = p.op("vector", lambda e, s=s, hcx=hcx, i=i: e.tensor_scalar(
                    out=H2q[s % 2][:, hcx * 256:(hcx + 1) * 256], in0=q16b[i][:, :],
                    scalar1=RD, scalar2=None, op0=ALU.subtract), waits=wl, inc=sV)
        if s >= 1:
            emit_mm2(s - 1)
    emit_mm2(NS - 1)
    if DBG:
        p.op("sync", lambda e: e.dma_start(out=dbgS[:, :], in_=dbgs[:, :]), inc=sD, amt=16)
    p.op("sync", lambda e: e.nop(), waits=[
        (sOut[0], sOut[0].n), (sOut[1], sOut[1].n),
        (sX[0], sX[0].n), (sX[1], sX[1].n), (sX2[0], sX2[0].n), (sX2[1], sX2[1].n)])
    p.op("gpsimd", lambda e: e.nop(), waits=[
        (sG, sG.n), (sGw[0], sGw[0].n), (sGw[1], sGw[1].n),
        (sW2[0], sW2[0].n), (sW2[1], sW2[1].n)])
    run_block(nc, p, hwm, "b5_ffn")

    free(*list(cms))
    for s in (sP, sS, sV, sT, *sW2[::-1], *sGw[::-1], *sGs[::-1], sGk, sGq,
              *sOut[::-1], *sSp[::-1], *sW[::-1], *sX2[::-1], *sX[::-1], sG, sD):
        s._cm.__exit__(None, None, None)
    return nc


_NC_CACHE = {}


def _get_nc():
    key = "nc"
    if key not in _NC_CACHE:
        _NC_CACHE[key] = build_nc()
    return _NC_CACHE[key]


def kernel(L0_attn_x, L0_attn_wq, L0_attn_bq, L0_attn_wk, L0_attn_bk,
           L0_attn_wv, L0_attn_bv, L0_attn_wo, L0_attn_bo,
           L0_ffn_w1, L0_ffn_conv_w, L0_ffn_conv_b, L0_ffn_w2):
    x = np.ascontiguousarray(np.asarray(L0_attn_x, dtype=np.float32))
    B = x.shape[0]
    f32 = lambda a: np.ascontiguousarray(np.asarray(a, dtype=np.float32))
    wq, wk, wv, wo = f32(L0_attn_wq), f32(L0_attn_wk), f32(L0_attn_wv), f32(L0_attn_wo)
    bq = np.ascontiguousarray(f32(L0_attn_bq).reshape(NC, 128).T)
    bk = np.ascontiguousarray(f32(L0_attn_bk).reshape(NC, 128).T)
    bv = f32(L0_attn_bv).reshape(1, D)
    bo = f32(L0_attn_bo).reshape(1, D)
    w1t = np.ascontiguousarray(f32(L0_ffn_w1).T)
    w2t = np.ascontiguousarray(f32(L0_ffn_w2).T)
    cw = f32(L0_ffn_conv_w).reshape(DH, 3)
    diag3 = np.zeros((128, 3, HCH, 128), dtype=np.float32)
    idx = np.arange(128)
    for k in range(3):
        for c in range(HCH):
            diag3[idx, k, c, idx] = cw[c * 128 + idx, k]
    diag3 = diag3.reshape(128, 3 * HCH * 128)
    convb = np.ascontiguousarray(f32(L0_ffn_conv_b).reshape(HCH, 128).T)
    pp = np.arange(128)[:, None]
    ff = np.arange(384)[None, :]
    mask = ((pp <= ff) & (ff <= pp + 256)).astype(np.float32)
    iden = np.eye(128, dtype=np.float32)
    hv = np.zeros((128, 2), dtype=np.float32)
    hv[:64, 0] = 1.0
    hv[64:, 1] = 1.0
    ones128 = np.ones((1, 128), dtype=np.float32)
    onescol = np.ones((128, 1), dtype=np.float32)

    nc = _get_nc()
    common = dict(wq=wq, wk=wk, wv=wv, wo=wo, bq=bq, bk=bk, bv=bv, bo=bo,
                  w1t=w1t, w2t=w2t, diag3=diag3, convb=convb, mask01=mask,
                  iden=iden, halves=hv, ones128=ones128, onescol=onescol)
    in_maps = [dict(common, x=np.ascontiguousarray(x[i])) for i in range(B)]
    res = run_bass_kernel_spmd(nc, in_maps, core_ids=list(range(B)))
    out = np.stack([np.asarray(r["out"], dtype=np.float32) for r in res.results], axis=0)
    return out


# version 19
# speedup vs baseline: 1.1476x; 1.0156x over previous
import os
import numpy as np
import concourse.bass as bass
import concourse.mybir as mybir
from concourse.bass_utils import run_bass_kernel_spmd

F32 = mybir.dt.float32
BF16 = mybir.dt.bfloat16
I16 = mybir.dt.int16
I8 = mybir.dt.int8
AF = mybir.ActivationFunctionType
ALU = mybir.AluOpType
AXX = mybir.AxisListType.X

_SIM = os.environ.get("BASS_SIM_FP64") == "1"
RC = 127.5 if _SIM else 128.0
RD = 127.0 if _SIM else 128.0

T = 2048
D = 1024
NT = T // 128
NC = D // 128
H = 16
NJ = 16
DH = 4096
HCH = 32
TS = 256
NS = T // TS
EPS = 1e-5

ENGS = ("sync", "tensor", "vector", "scalar", "gpsimd")


class Sem:
    def __init__(self, nc, name):
        self._cm = nc.semaphore(name)
        self.h = self._cm.__enter__()
        self.n = 0
        self.name = name


class Plan:
    def __init__(self):
        self.items = {e: [] for e in ENGS}
        self.incs = {e: {} for e in ENGS}

    AUTO = {}

    def op(self, eng, emit, waits=(), inc=None, amt=1):
        w = [(s, v) for (s, v) in waits if v and v > 0]
        if inc is None and eng in self.AUTO:
            inc = self.AUTO[eng]
            amt = 1
        if inc is not None:
            inc.n += amt
            self.incs[eng][inc.name] = inc
        self.items[eng].append((w, emit, inc, amt))
        return inc.n if inc is not None else None

    def replay(self, eng, e, hwm):
        for w, emit, inc, amt in self.items[eng]:
            for s, v in w:
                if hwm[eng].get(s.name, 0) < v:
                    e.wait_ge(s.h, v)
                    hwm[eng][s.name] = v
            ins = emit(e)
            if inc is not None:
                ins.then_inc(inc.h, amt)
        for s in self.incs[eng].values():
            if hwm[eng].get(s.name, 0) < s.n:
                e.wait_ge(s.h, s.n)
                hwm[eng][s.name] = s.n


PHASES = []
_PHTOT = {e: 0 for e in ENGS}


def run_block(nc, plan, hwm, phase=""):
    for e in ENGS:
        _PHTOT[e] += len(plan.items[e])
    PHASES.append((phase, dict(_PHTOT)))
    with nc.Block() as blk:
        @blk.sync
        def _(e):
            plan.replay("sync", e, hwm)

        @blk.tensor
        def _(e):
            plan.replay("tensor", e, hwm)

        @blk.vector
        def _(e):
            plan.replay("vector", e, hwm)

        @blk.scalar
        def _(e):
            plan.replay("scalar", e, hwm)

        @blk.gpsimd
        def _(e):
            plan.replay("gpsimd", e, hwm)


def build_nc():
    PHASES.clear()
    for e in ENGS:
        _PHTOT[e] = 0
    nc = bass.Bass("TRN2", target_bir_lowering=False, debug=False,
                   detect_race_conditions=False)

    dx = nc.dram_tensor("x", [T, D], F32, kind="ExternalInput").ap()
    dw = {k: nc.dram_tensor(k, [D, D], F32, kind="ExternalInput").ap()
          for k in ("wq", "wk", "wv", "wo")}
    dbq = nc.dram_tensor("bq", [128, NC], F32, kind="ExternalInput").ap()
    dbk = nc.dram_tensor("bk", [128, NC], F32, kind="ExternalInput").ap()
    dbv = nc.dram_tensor("bv", [1, D], F32, kind="ExternalInput").ap()
    dbo = nc.dram_tensor("bo", [1, D], F32, kind="ExternalInput").ap()
    dw1t = nc.dram_tensor("w1t", [D, DH], F32, kind="ExternalInput").ap()
    dw2t = nc.dram_tensor("w2t", [DH, D], F32, kind="ExternalInput").ap()
    ddiag = nc.dram_tensor("diag3", [128, 3 * HCH * 128], F32, kind="ExternalInput").ap()
    dcb = nc.dram_tensor("convb", [128, HCH], F32, kind="ExternalInput").ap()
    dmask = nc.dram_tensor("mask01", [128, 384], F32, kind="ExternalInput").ap()
    diden = nc.dram_tensor("iden", [128, 128], F32, kind="ExternalInput").ap()
    dhalves = nc.dram_tensor("halves", [128, 2], F32, kind="ExternalInput").ap()
    dones128 = nc.dram_tensor("ones128", [1, 128], F32, kind="ExternalInput").ap()
    donescol = nc.dram_tensor("onescol", [128, 1], F32, kind="ExternalInput").ap()
    dout = nc.dram_tensor("out", [T, D], F32, kind="ExternalOutput").ap()
    DBG = os.environ.get("KDEBUG") == "1"
    iknd = "ExternalOutput" if DBG else "Internal"
    dx2 = nc.dram_tensor("x2sp", [T, D], BF16, kind=iknd).ap()
    dw1b = nc.dram_tensor("w1b16", [8, 128, 4096], BF16, kind=iknd).ap()
    dw2b = nc.dram_tensor("w2b16", [128, HCH * 1024], BF16, kind=iknd).ap()
    dx3t = nc.dram_tensor("x3tsp", [NC, 128, T + 2], BF16, kind=iknd).ap()
    dbgO = (nc.dram_tensor("dbgO", [T, D], BF16, kind="ExternalOutput").ap()
            if DBG else None)
    dbgS = (nc.dram_tensor("dbgS", [128, 80], F32, kind="ExternalOutput").ap()
            if DBG else None)

    sD = Sem(nc, "sD")
    sG = Sem(nc, "sG")
    sX = [Sem(nc, "sX0"), Sem(nc, "sX1")]
    sX2 = [Sem(nc, "sX2_0"), Sem(nc, "sX2_1")]
    sW = [Sem(nc, f"sW{i}") for i in range(4)]
    sSp = [Sem(nc, f"sSp{i}") for i in range(4)]
    sOut = [Sem(nc, "sOut0"), Sem(nc, "sOut1")]
    sGq = Sem(nc, "sGq")
    sGk = Sem(nc, "sGk")
    sGs = [Sem(nc, "sGs0"), Sem(nc, "sGs1")]
    sGw = [Sem(nc, "sGw0"), Sem(nc, "sGw1")]
    sW2 = [Sem(nc, "sW2_0"), Sem(nc, "sW2_1")]
    sT = Sem(nc, "sT")
    sV = Sem(nc, "sV")
    sS = Sem(nc, "sS")
    sP = Sem(nc, "sP")
    Plan.AUTO = {"vector": sV, "scalar": sS, "gpsimd": sP}
    hwm = {e: {} for e in ENGS}
    cms = {}
    aseq = {"n": 0}

    def alloc(name, shape, dt, space="sbuf", side=None):
        cm = (nc.sbuf_tensor(name, shape, dt, side=side) if space == "sbuf"
              else nc.psum_tensor(name, shape, dt))
        h = cm.__enter__()
        aseq["n"] += 1
        cms[name] = (aseq["n"], cm)
        return h

    def free(*names):
        for nm in sorted(names, key=lambda nm: -cms[nm][0]):
            cms.pop(nm)[1].__exit__(None, None, None)

    bqs = alloc("bqs", [128, NC], F32)
    bks = alloc("bks", [128, NC], F32)
    bvB = alloc("bvB", [128, D], F32, side="right")
    boB = alloc("boB", [128, D], F32, side="right")
    bvrow = alloc("bvrow", [1, D], F32, side="right")
    borow = alloc("borow", [1, D], F32, side="right")
    cbs = alloc("cbs", [128, HCH], F32)
    mask01 = alloc("sb_mask01", [128, 384], BF16)
    iden = alloc("sb_iden", [128, 128], BF16)
    idenf = alloc("sb_idenf", [128, 128], F32)
    halves = alloc("sb_halves", [128, 2], F32)
    ones128 = alloc("sb_ones128", [1, 128], F32)
    onescol = alloc("sb_onescol", [128, 1], F32)
    epsc = alloc("epsc", [128, 1], F32)
    negC = alloc("negC", [128, H], F32)
    eC = alloc("eC", [128, H], F32)
    mwc = alloc("mwc", [128, 2], F32)
    swc = alloc("swc", [128, 2], F32)
    stat = alloc("stat", [128, 12], F32)
    aggr = alloc("aggr", [128, 2], F32)
    scol = alloc("scol", [128, 1], F32)
    rcol = alloc("rcol", [128, 1], F32)
    wsum = alloc("wsum", [128, 64], F32)
    mrow = alloc("mrow", [1, 64], F32)
    mtot = alloc("mtot", [1, 2], F32)
    zrow = alloc("zrow", [128, 1], BF16)
    qmx = alloc("qmx", [128, NC], F32)
    kmx = alloc("kmx", [128, NC], F32)
    qkp = alloc("qkp", [128, NC], F32)
    c2x8 = alloc("c2x8", [2, NC], F32)
    crow = alloc("crow", [1, 16], F32)
    dbgs = alloc("dbgs", [128, 80], F32)

    p = Plan()
    for (dst, src) in ((bqs, dbq), (bks, dbk), (bvrow, dbv), (borow, dbo),
                       (cbs, dcb), (halves, dhalves), (ones128, dones128),
                       (onescol, donescol), (idenf, diden)):
        p.op("sync", lambda e, d=dst, s=src: e.dma_start(out=d[:], in_=s[:, :]),
             inc=sD, amt=16)
    p.op("gpsimd", lambda e: e.dma_start(out=mask01[:], in_=dmask[:, :]), inc=sG, amt=16)
    p.op("gpsimd", lambda e: e.dma_start(out=iden[:], in_=diden[:, :]), inc=sG, amt=16)
    p.op("vector", lambda e: e.memset(epsc[:], EPS), inc=sV)
    nzr = p.op("vector", lambda e: e.memset(zrow[:], 0.0), inc=sV)
    for c in range(NC):
        p.op("sync", lambda e, c=c: e.dma_start(out=dx3t[c, :, 0:1], in_=zrow[:, 0:1],
             allow_slow_non_contiguous=True), waits=[(sV, nzr)], inc=sD, amt=16)
        p.op("sync", lambda e, c=c: e.dma_start(out=dx3t[c, :, T + 1:T + 2], in_=zrow[:, 0:1],
             allow_slow_non_contiguous=True), inc=sD, amt=16)
    p.op("sync", lambda e: e.nop(), waits=[(sD, sD.n)])
    p.op("gpsimd", lambda e: e.nop(), waits=[(sG, sG.n)])
    run_block(nc, p, hwm, "b0_const")

    B1all = alloc("B1all", [128, NC * T], BF16)
    B1 = [B1all[:, c * T:(c + 1) * T] for c in range(NC)]
    B1r3 = B1all.ap().rearrange("p (c t) -> p c t", t=T)
    QT = [alloc(f"QT{c}", [128, T], BF16) for c in range(NC)]
    KT = [alloc(f"KT{c}", [128, T], BF16) for c in range(NC)]
    VA = [alloc(f"VA{j}", [128, H * 65], BF16) for j in range(NJ + 2)]

    def vaug(j):
        return VA[j + 1]

    wsb = [[alloc(f"wsb{s}_{c}", [128, D], BF16) for c in range(NC)] for s in range(2)]
    xin = [alloc(f"xin_{i}", [128, D], F32) for i in range(2)]
    x1b = [alloc(f"x1b_{i}", [128, D], BF16) for i in range(2)]
    ptr = [alloc(f"ptr_{i}", [128, 1024], BF16, "psum") for i in range(2)]
    pbc = [alloc(f"pbc_{i}", [128, 512], F32, "psum") for i in range(2)]

    p = Plan()
    wq_ld = [0] * NC
    wk_ld = [0] * NC
    wv_ld = [0] * NC
    for c in range(NC):
        wq_ld[c] = p.op("gpsimd", lambda e, c=c: e.dma_start(
            out=wsb[0][c][:], in_=dw["wq"][c * 128:(c + 1) * 128, :]), inc=sGq, amt=16)
    for c in range(NC):
        wk_ld[c] = p.op("gpsimd", lambda e, c=c: e.dma_start(
            out=wsb[1][c][:], in_=dw["wk"][c * 128:(c + 1) * 128, :]), inc=sGk, amt=16)
    for j in (-1, NJ):
        p.op("vector", lambda e, j=j: e.memset(
            vaug(j).ap().rearrange("p (h c) -> p h c", c=65)[:, :, 0:64], 0.0))
    vinit = 0
    for j in range(-1, NJ + 1):
        vinit = p.op("vector", lambda e, j=j: e.memset(
            vaug(j).ap().rearrange("p (h c) -> p h c", c=65)[:, :, 64:65], 1.0), inc=sV)

    bc_ev = 0
    for i, (dst, src) in enumerate(((bvB, bvrow), (bvB, bvrow), (boB, borow), (boB, borow))):
        off = (i % 2) * 512
        wl = [(sV, bc_ev)] if i >= 2 else []
        n = p.op("tensor", lambda e, s=src, off=off, i=i: e.matmul(
            pbc[i % 2][:, :], ones128[:, :], s[0:1, off:off + 512], start=True, stop=True),
            waits=wl, inc=sT)
        bc_ev = p.op("vector", lambda e, d=dst, off=off, i=i: e.tensor_copy(
            d[:, off:off + 512], pbc[i % 2][:, :]), waits=[(sT, n)], inc=sV)

    ln_apply = [0] * NT
    tr_done = [0] * NT
    ev_done = [0] * NT
    sq_done = [0] * NT
    for t in range(NT):
        wl = [(sV, ln_apply[t - 2])] if t >= 2 else []
        xload = p.op("sync", lambda e, t=t: e.dma_start(
            out=xin[t % 2][:], in_=dx[t * 128:(t + 1) * 128, :]), waits=wl,
            inc=sX[t % 2], amt=16)
        wl = [(sX[t % 2], xload)]
        if t >= 1:
            wl.append((sS, sq_done[t - 1]))
        p.op("vector", lambda e, t=t: e.bn_stats(stat[:, 0:6], xin[t % 2][:, 0:512]),
             waits=wl)
        nsb = p.op("vector", lambda e, t=t: e.bn_stats(stat[:, 6:12], xin[t % 2][:, 512:1024]))
        nag = p.op("vector", lambda e: e.bn_aggr(aggr[:], stat[:]), waits=[(sV, nsb)], inc=sV)
        sq_done[t] = p.op("scalar", lambda e: e.activation(
            out=scol[:], in_=aggr[:, 1:2], func=AF.Sqrt, bias=epsc[:], scale=1.0),
            waits=[(sV, nag)], inc=sS)
        nrc1 = p.op("vector", lambda e: e.reciprocal(rcol[:], scol[:]),
                    waits=[(sS, sq_done[t])])
        wl = [(sV, nrc1)]
        if t >= 2:
            wl.append((sT, tr_done[t - 2]))
        ln_apply[t] = p.op("vector", lambda e, t=t: e.tensor_scalar(
            out=x1b[t % 2][:], in0=xin[t % 2][:], scalar1=aggr[:, 0:1], scalar2=rcol[:],
            op0=ALU.subtract, op1=ALU.mult), waits=wl, inc=sV)
        for c in range(NC):
            wl = []
            if c == 0:
                wl.append((sV, ln_apply[t]))
                if t >= 2:
                    wl.append((sS, ev_done[t - 2]))
            tr_done[t] = p.op("tensor", lambda e, t=t, c=c: e.transpose(
                ptr[t % 2][:, c * 128:(c + 1) * 128],
                x1b[t % 2][:, c * 128:(c + 1) * 128], iden[:]),
                waits=wl, inc=sT if c == NC - 1 else None)
        ev_done[t] = p.op("scalar", lambda e, t=t: e.copy(
            B1r3[:, :, t * 128:(t + 1) * 128],
            ptr[t % 2].ap().rearrange("p (c q) -> p c q", q=128)[:, :, :]),
            waits=[(sT, tr_done[t])], inc=sS)
    run_block(nc, p, hwm, "b1_ln1")
    free("xin_0", "xin_1", "x1b_0", "x1b_1", "ptr_0", "ptr_1", "pbc_0", "pbc_1")
    free("bvrow", "borow")

    wstage = [alloc(f"wstage_{i}", [128, 1024], F32) for i in range(4)]
    pq = [alloc(f"pq_{i}", [128, 512], F32, "psum") for i in range(4)]
    pms = alloc("pms", [128, 512], F32, "psum")

    p = Plan()

    red_done = []
    piece_srcs = []
    for ch in range(NC):
        for h4 in range(4):
            piece_srcs.append((dw1t[ch * 128:(ch + 1) * 128, h4 * 1024:(h4 + 1) * 1024], 1024))
    for ch in range(HCH):
        piece_srcs.append((dw2t[ch * 128:(ch + 1) * 128, :], 1024))

    def emit_piece(idx):
        src, width = piece_srcs[idx]
        slot = idx % 4
        wl = [(sV, red_done[idx - 4])] if idx >= 4 else []
        ld = p.op("sync", lambda e, src=src, slot=slot, width=width: e.dma_start(
            out=wstage[slot][:, 0:width], in_=src), waits=wl, inc=sW[slot], amt=16)
        red_done.append(p.op("vector", lambda e, slot=slot, idx=idx, width=width: e.reduce_sum(
            wsum[:, idx:idx + 1], wstage[slot][:, 0:width], axis=AXX,
            apply_absolute_value=True), waits=[(sW[slot], ld)], inc=sV))

    NPC1 = 40
    for idx in range(NPC1):
        emit_piece(idx)

    def proj_qk(wset, wld, bias_s, OUT, slot_ev, evac_eng, wsem):
        for oc in range(NC):
            for st in range(4):
                slot = (oc * 4 + st) % 4
                wl = [(wsem, wld[NC - 1])]
                if slot_ev[slot]:
                    wl.append((evac_eng, slot_ev[slot]))
                last = 0
                for ic in range(NC):
                    last = p.op("tensor", lambda e, oc=oc, st=st, ic=ic, slot=slot, wset=wset: e.matmul(
                        pq[slot][:, :], wsb[wset][ic][:, oc * 128:(oc + 1) * 128],
                        B1[ic][:, st * 512:(st + 1) * 512],
                        start=(ic == 0), stop=(ic == NC - 1)),
                        waits=wl if ic == 0 else [],
                        inc=sT if ic == NC - 1 else None)
                slot_ev[slot] = p.op("scalar", lambda e, oc=oc, st=st, slot=slot, OUT=OUT, bias_s=bias_s: e.activation(
                    out=OUT[oc][:, st * 512:(st + 1) * 512], in_=pq[slot][:, :],
                    func=AF.Identity, bias=bias_s[:, oc:oc + 1], scale=1.0),
                    waits=[(sT, last)], inc=sS)
        return slot_ev

    slot_ev = [0, 0, 0, 0]
    slot_ev = proj_qk(0, wq_ld, bqs, QT, slot_ev, sS, sGq)
    for c in range(NC):
        wv_ld[c] = p.op("gpsimd", lambda e, c=c: e.dma_start(
            out=wsb[0][c][:], in_=dw["wv"][c * 128:(c + 1) * 128, :]),
            waits=[(sS, max(slot_ev))], inc=sGq, amt=16)
    slot_ev = proj_qk(1, wk_ld, bks, KT, slot_ev, sS, sGk)

    prev_s_ev = list(slot_ev)
    slot_vv = [0, 0, 0, 0]
    nred = 0
    kev = max(slot_ev)

    def emit_qkmax(i):
        src = QT[i] if i < NC else KT[i - NC]
        dst = qmx if i < NC else kmx
        c = i % NC
        return p.op("vector", lambda e, src=src, dst=dst, c=c: e.reduce_max(
            dst[:, c:c + 1], src[:, :], axis=AXX, apply_absolute_value=True),
            waits=[(sS, kev)] if i == 0 else [], inc=sV)

    for t in range(NT):
        for ofs in range(2):
            g = t * 2 + ofs
            slot = g % 4
            wl = [(sGq, wv_ld[NC - 1]), (sV, bc_ev), (sV, vinit)]
            if prev_s_ev[slot]:
                wl.append((sS, prev_s_ev[slot]))
                prev_s_ev[slot] = 0
            if slot_vv[slot]:
                wl.append((sV, slot_vv[slot]))
            last = 0
            for ic in range(NC):
                last = p.op("tensor", lambda e, t=t, ofs=ofs, ic=ic, slot=slot: e.matmul(
                    pq[slot][:, :], B1[ic][:, t * 128:(t + 1) * 128],
                    wsb[0][ic][:, ofs * 512:(ofs + 1) * 512],
                    start=(ic == 0), stop=(ic == NC - 1)),
                    waits=wl if ic == 0 else [],
                    inc=sT if ic == NC - 1 else None)
            slot_vv[slot] = p.op("vector", lambda e, t=t, ofs=ofs, slot=slot: e.scalar_tensor_tensor(
                out=vaug(t).ap().rearrange("p (h c) -> p h c", c=65)[:, ofs * 8:(ofs + 1) * 8, 0:64],
                in0=pq[slot][:, :], scalar=1.0,
                in1=bvB[:, ofs * 512:(ofs + 1) * 512],
                op0=ALU.mult, op1=ALU.add),
                waits=[(sT, last)], inc=sV)
            if g % 2 == 1 and g // 2 < 2 * NC:
                nred = emit_qkmax(g // 2)

    for idx in range(NPC1, len(piece_srcs)):
        emit_piece(idx)

    n1 = p.op("tensor", lambda e: e.matmul(pms[0:1, 0:64], onescol[:, :], wsum[:, 0:64],
                                           start=True, stop=True),
              waits=[(sV, red_done[-1])], inc=sT)
    n2 = p.op("scalar", lambda e: e.copy(mrow[0:1, 0:64], pms[0:1, 0:64]),
              waits=[(sT, n1)], inc=sS)
    p.op("vector", lambda e: e.reduce_sum(mtot[0:1, 0:1], mrow[0:1, 0:32], axis=AXX),
         waits=[(sS, n2)])
    nr2 = p.op("vector", lambda e: e.reduce_sum(mtot[0:1, 1:2], mrow[0:1, 32:64], axis=AXX))
    n3 = p.op("vector", lambda e: e.tensor_scalar(out=mtot[:], in0=mtot[:],
                                             scalar1=1.0 / (D * DH), scalar2=1e-5,
                                             op0=ALU.mult, op1=ALU.max),
              waits=[(sV, nr2)], inc=sV)
    n4 = p.op("tensor", lambda e: e.matmul(pms[:, 0:2], ones128[:, :], mtot[0:1, :],
                                           start=True, stop=True),
              waits=[(sV, n3)], inc=sT)
    n5 = p.op("scalar", lambda e: e.copy(mwc[:], pms[0:128, 0:2]), waits=[(sT, n4)], inc=sS)
    p.op("vector", lambda e: e.reciprocal(swc[:, 0:1], mwc[:, 0:1]), waits=[(sS, n5)])
    n6 = p.op("vector", lambda e: e.reciprocal(swc[:, 1:2], mwc[:, 1:2]), inc=sV)
    if DBG:
        p.op("vector", lambda e: e.tensor_copy(dbgs[:, 32:34], mwc[:, :]),
             waits=[(sV, n6)])
        p.op("vector", lambda e: e.tensor_copy(dbgs[:, 34:36], swc[:, :]))
        p.op("vector", lambda e: e.tensor_copy(dbgs[0:1, 36:60], mrow[0:1, 24:48]))
        p.op("vector", lambda e: e.tensor_copy(dbgs[0:1, 60:62], mtot[0:1, :]))

    nq = p.op("vector", lambda e: e.tensor_mul(qkp[:], qmx[:], kmx[:]),
              waits=[(sV, nred)], inc=sV)
    nm = p.op("tensor", lambda e: e.matmul(pms[0:2, 0:8], halves[:, :], qkp[:, :],
                                           start=True, stop=True),
              waits=[(sV, nq), (sS, n5)], inc=sT)
    ne = p.op("scalar", lambda e: e.copy(c2x8[:], pms[0:2, 0:8]), waits=[(sT, nm)], inc=sS)
    nts = p.op("vector", lambda e: e.tensor_scalar(out=c2x8[:], in0=c2x8[:], scalar1=0.125,
                                                   scalar2=-40.0, op0=ALU.mult, op1=ALU.add),
               waits=[(sS, ne)])
    nrl = p.op("vector", lambda e: e.tensor_relu(c2x8[:], c2x8[:]), waits=[(sV, nts)])
    nr = p.op("vector", lambda e: e.tensor_scalar(out=c2x8[:], in0=c2x8[:], scalar1=-1.0,
                                                  scalar2=None, op0=ALU.mult),
              waits=[(sV, nrl)], inc=sV)
    nc1 = p.op("vector", lambda e: e.tensor_copy(crow[0:1, 0:8], c2x8[0:1, :]),
               waits=[(sV, nr)], inc=sV)
    nd = p.op("sync", lambda e: e.dma_start(out=crow[0:1, 8:16], in_=c2x8[1:2, :]),
              waits=[(sV, nr)], inc=sD, amt=16)
    nb1 = p.op("tensor", lambda e: e.matmul(pms[:, 0:8], ones128[:, :], crow[0:1, 0:8],
                                            start=True, stop=True),
               waits=[(sV, nc1), (sD, nd), (sS, ne)], inc=sT)
    nb1e = p.op("scalar", lambda e: e.copy(
        negC.ap().rearrange("p (a b) -> p a b", b=2)[:, :, 0:1], pms[0:128, 0:8]),
        waits=[(sT, nb1)], inc=sS)
    nb2 = p.op("tensor", lambda e: e.matmul(pms[:, 0:8], ones128[:, :], crow[0:1, 8:16],
                                            start=True, stop=True),
               waits=[(sS, nb1e)], inc=sT)
    nb2e = p.op("scalar", lambda e: e.copy(
        negC.ap().rearrange("p (a b) -> p a b", b=2)[:, :, 1:2], pms[0:128, 0:8]),
        waits=[(sT, nb2)], inc=sS)
    nec = p.op("scalar", lambda e: e.activation(out=eC[:], in_=negC[:], func=AF.Exp,
                                                bias=0.0, scale=1.0),
               waits=[(sS, nb2e)], inc=sS)


    run_block(nc, p, hwm, "b2_qkv")
    free(*[f"wsb{s}_{c}" for s in range(2) for c in range(NC)],
         "wstage_0", "wstage_1", "wstage_2", "wstage_3",
         "pq_0", "pq_1", "pq_2", "pq_3", "pms")

    B2 = [alloc(f"B2_{t}", [128, D], BF16, side="right") for t in range(NT)]
    expraw = [alloc(f"expraw_{i}", [128, 384], BF16) for i in range(3)]
    expm = [alloc(f"expm_{i}", [128, 384], BF16) for i in range(3)]
    rodc = [alloc(f"rodc_{i}", [128, 1], F32) for i in range(2)]
    psc = [alloc(f"psc_{i}", [128, 512], F32, "psum") for i in range(3)]
    pod = [alloc(f"pod_{i}", [128, 512], F32, "psum") for i in range(4)]

    p = Plan()
    wstage3 = [alloc(f"wstage3_{i}", [128, 1024], F32) for i in range(4)]
    tern_u16 = [alloc(f"tern_u16_{i}", [128, 1024], I16) for i in range(2)]
    tern2b = [alloc(f"tern2b_{i}", [128, 1024], BF16) for i in range(2)]
    tern_jobs = []
    for ch in range(NC):
        for q in range(4):
            tern_jobs.append((dw1t[ch * 128:(ch + 1) * 128, q * 1024:(q + 1) * 1024],
                              [(dw1b[q * 2 + half, :, ch * 512:(ch + 1) * 512],
                                half * 512, (half + 1) * 512) for half in range(2)], 0))
    for ch in range(HCH):
        tern_jobs.append((dw2t[ch * 128:(ch + 1) * 128, :],
                          [(dw2b[:, ch * 1024:(ch + 1) * 1024], 0, 1024)], 1))
    tstate = {"sp": [0, 0, 0, 0]}

    def tern_steps():
        for i0 in range(0, len(tern_jobs), 4):
            picks = [k for k in range(i0, min(i0 + 4, len(tern_jobs)))]
            lds, n1s, n2s = {}, {}, {}
            for k in picks:
                srcA, dst, wi = tern_jobs[k]
                slot = k % 4
                wl = [(sV, n6)]
                if tstate["sp"][slot]:
                    wl.append((sSp[slot], tstate["sp"][slot]))
                lds[k] = p.op("sync", lambda e, srcA=srcA, slot=slot: e.dma_start(
                    out=wstage3[slot][:, :], in_=srcA), waits=wl, inc=sW[slot], amt=16)
                yield
            for j0 in range(0, len(picks), 2):
                pair = picks[j0:j0 + 2]
                for k in pair:
                    wi = tern_jobs[k][2]
                    slot = k % 4
                    wl = [(sW[slot], lds[k])]
                    if tstate.get("u16", [0, 0])[k % 2]:
                        wl.append((sV, tstate["u16"][k % 2]))
                    n1s[k] = p.op("vector", lambda e, slot=slot, wi=wi, k=k: e.tensor_scalar(
                        out=tern_u16[k % 2][:, :], in0=wstage3[slot][:, :],
                        scalar1=swc[:, wi:wi + 1], scalar2=RC, op0=ALU.mult, op1=ALU.add),
                        waits=wl)
                    yield
                for k in pair:
                    slot = k % 4
                    n2s[k] = p.op("vector", lambda e, slot=slot, k=k: e.tensor_scalar(
                        out=wstage3[slot][:, :], in0=tern_u16[k % 2][:, :],
                        scalar1=RD, scalar2=1.0, op0=ALU.subtract, op1=ALU.min),
                        waits=[(sV, n1s[k])])
                    tstate.setdefault("u16", [0, 0])[k % 2] = n2s[k]
                    yield
                for k in pair:
                    slot = k % 4
                    wl = [(sV, n2s[k])]
                    if tstate.get("t2b", [0, 0])[k % 2]:
                        wl.append((sSp[(k - 2) % 4], tstate["t2b"][k % 2]))
                    n2s[k] = p.op("vector", lambda e, slot=slot, k=k: e.tensor_scalar(
                        out=tern2b[k % 2][:, :], in0=wstage3[slot][:, :],
                        scalar1=-1.0, scalar2=None, op0=ALU.max),
                        waits=wl, inc=sV)
                    yield
                for k in pair:
                    slot = k % 4
                    for pi, (dst, c0, c1) in enumerate(tern_jobs[k][1]):
                        tstate["sp"][slot] = p.op("sync", lambda e, dst=dst, c0=c0, c1=c1, slot=slot, k=k: e.dma_start(
                            out=dst, in_=tern2b[k % 2][:, c0:c1]),
                            waits=[(sV, n2s[k])] if pi == 0 else [],
                            inc=sSp[slot], amt=16)
                    tstate.setdefault("t2b", [0, 0])[k % 2] = tstate["sp"][slot]
                    yield

    tern_gen = tern_steps()

    def win(j):
        q0 = (j - 1) * 128
        q1 = (j + 2) * 128
        c0 = max(0, -q0)
        q0r = max(q0, 0)
        q1r = min(q1, T)
        return q0r, q1r, c0, c0 + (q1r - q0r)

    pod_ev = [0, 0, 0, 0]
    psc_ev = [0, 0, 0]
    expm_use = [0, 0, 0]
    mask_done = [0, 0, 0]
    rodc_use = [0, 0]
    mask_info = {}

    def emit_av(h, j):
        sl = (j + 1) % 3
        msem, nmask = mask_info.pop((h, j))
        last_use = 0
        for s in range(3):
            qb = j - 1 + s
            if qb < 0 or qb >= NJ:
                continue
            wc = qb * 128 - (j - 1) * 128
            first = (j == qb - 1)
            lastc = (j == qb + 1)
            wl = [(msem, nmask)]
            if first and pod_ev[qb % 4]:
                wl.append((sV, pod_ev[qb % 4]))
            last_use = p.op("tensor", lambda e, qb=qb, j=j, h=h, sl=sl, wc=wc, first=first, lastc=lastc: e.matmul(
                pod[qb % 4][:, 0:65], expm[sl][:, wc:wc + 128],
                vaug(j)[:, h * 65:h * 65 + 65], start=first, stop=lastc),
                waits=wl, inc=sT)
            if lastc:
                wl = [(sT, last_use)]
                if rodc_use[qb % 2]:
                    wl.append((sV, rodc_use[qb % 2]))
                nrc = p.op("vector", lambda e, qb=qb: e.reciprocal(
                    rodc[qb % 2][:, :], pod[qb % 4][:, 64:65]), waits=wl, inc=sV)
                pod_ev[qb % 4] = p.op("vector", lambda e, qb=qb, h=h: e.tensor_scalar(
                    out=B2[qb][:, h * 64:(h + 1) * 64], in0=pod[qb % 4][:, 0:64],
                    scalar1=rodc[qb % 2][:, :], scalar2=None, op0=ALU.mult),
                    waits=[(sV, nrc)], inc=sV)
                rodc_use[qb % 2] = pod_ev[qb % 4]
        if last_use:
            expm_use[sl] = last_use

    seq = [(h, j) for h in range(H) for j in range(-1, NJ + 1)]
    for idx, (h, j) in enumerate(seq):
        hc = h // 2
        hr = (h % 2) * 64
        sl = (j + 1) % 3
        q0r, q1r, c0, c1 = win(j)
        pad = (j < 0) or (j >= NJ)
        if not pad:
            wl = [(sS, nec)]
            if psc_ev[sl]:
                wl.append((sS, psc_ev[sl]))
            nmm = p.op("tensor", lambda e, hc=hc, hr=hr, j=j, sl=sl, q0r=q0r, q1r=q1r, c0=c0, c1=c1: e.matmul(
                psc[sl][:, c0:c1], KT[hc][hr:hr + 64, j * 128:(j + 1) * 128],
                QT[hc][hr:hr + 64, q0r:q1r], start=True, stop=True),
                waits=wl, inc=sT)
            wl = [(sT, nmm)]
            if mask_done[sl]:
                wl.append((sP, mask_done[sl]))
            psc_ev[sl] = p.op("scalar", lambda e, h=h, sl=sl, c0=c0, c1=c1: e.activation(
                out=expraw[sl][:, c0:c1], in_=psc[sl][:, c0:c1], func=AF.Exp,
                bias=negC[:, h:h + 1], scale=0.125), waits=wl, inc=sS)
            wl = [(sS, psc_ev[sl])]
            if expm_use[sl]:
                wl.append((sT, expm_use[sl]))
            mask_done[sl] = p.op("gpsimd", lambda e, sl=sl, c0=c0, c1=c1: e.tensor_tensor(
                expm[sl][:, c0:c1], expraw[sl][:, c0:c1], mask01[:, c0:c1], ALU.mult),
                waits=wl, inc=sP)
            mask_info[(h, j)] = (sP, mask_done[sl])
        else:
            wl = [(sS, nec)]
            if expm_use[sl]:
                wl.append((sT, expm_use[sl]))
            nmask = p.op("vector", lambda e, h=h, sl=sl, c0=c0, c1=c1: e.tensor_scalar(
                out=expm[sl][:, c0:c1], in0=mask01[:, c0:c1],
                scalar1=eC[:, h:h + 1], scalar2=None, op0=ALU.mult),
                waits=wl, inc=sV)
            mask_info[(h, j)] = (sV, nmask)
        if idx >= 2:
            emit_av(*seq[idx - 2])
        for _ in range(2):
            next(tern_gen, None)
    emit_av(*seq[-2])
    emit_av(*seq[-1])
    for _ in tern_gen:
        pass
    p.op("sync", lambda e: e.nop(), waits=[(sD, sD.n)] +
         [(s_, s_.n) for s_ in sW] + [(s_, s_.n) for s_ in sSp])
    run_block(nc, p, hwm, "b3_attn")
    free(*[f"QT{c}" for c in range(NC)], *[f"KT{c}" for c in range(NC)],
         *[f"VA{j}" for j in range(NJ + 2)],
         *[f"wstage3_{i}" for i in range(4)], "tern_u16_0", "tern_u16_1",
         "tern2b_0", "tern2b_1",
         "expraw_0", "expraw_1", "expraw_2", "expm_0", "expm_1", "expm_2",
         "rodc_0", "rodc_1",
         "psc_0", "psc_1", "psc_2", "pod_0", "pod_1", "pod_2", "pod_3")

    wosb = [alloc(f"wo_{c}", [128, D], BF16) for c in range(NC)]
    attn = [alloc(f"attn_{i}", [128, D], F32) for i in range(2)]
    x2f = [alloc(f"x2f_{i}", [128, D], F32) for i in range(2)]
    x3f = [alloc(f"x3f_{i}", [128, D], F32) for i in range(2)]
    xre = [alloc(f"xre_{i}", [128, D], F32) for i in range(2)]
    q16 = alloc("q16", [128, D], I16)
    x3q = [alloc(f"x3q_{i}", [128, D], BF16) for i in range(2)]
    x3st = [alloc(f"x3st_{i}", [128, D], BF16) for i in range(2)]
    sxc = alloc("sxc", [128, 1], F32)
    sxic = alloc("sxic", [128, 1], F32)
    ptrO = [alloc(f"ptrO_{i}", [128, 1024], BF16, "psum") for i in range(2)]
    ptrX = [alloc(f"ptrX_{i}", [128, 1024], BF16, "psum") for i in range(2)]
    po4 = [alloc(f"po4_{i}", [128, 512], F32, "psum") for i in range(2)]

    p = Plan()
    if DBG:
        for t in range(NT):
            p.op("sync", lambda e, t=t: e.dma_start(
                out=dbgO[t * 128:(t + 1) * 128, :], in_=B2[t][:, :]), inc=sD, amt=16)
    wo_ld = [0] * NC
    for c in range(NC):
        wo_ld[c] = p.op("gpsimd", lambda e, c=c: e.dma_start(
            out=wosb[c][:], in_=dw["wo"][c * 128:(c + 1) * 128, :]), inc=sGq, amt=16)
    otr = [0] * NT
    oev = [0] * NT
    po_ev = [0, 0]
    x2sp = [0, 0]
    x3sp = [0, 0]
    apply2 = [0] * NT
    apply3 = [0] * NT
    q_done = [0] * NT
    x3tr = [0] * NT
    x3ev = [0] * NT
    for t in range(NT):
        for c in range(NC):
            wl = []
            if c == 0 and t >= 2:
                wl.append((sS, oev[t - 2]))
            otr[t] = p.op("tensor", lambda e, t=t, c=c: e.transpose(
                ptrO[t % 2][:, c * 128:(c + 1) * 128],
                B2[t][:, c * 128:(c + 1) * 128], iden[:]),
                waits=wl, inc=sT if c == NC - 1 else None)
        oev[t] = p.op("scalar", lambda e, t=t: e.copy(
            B1r3[:, :, t * 128:(t + 1) * 128],
            ptrO[t % 2].ap().rearrange("p (c q) -> p c q", q=128)[:, :, :]),
            waits=[(sT, otr[t])], inc=sS)
        xre_ld = p.op("sync", lambda e, t=t: e.dma_start(
            out=xre[t % 2][:], in_=dx[t * 128:(t + 1) * 128, :]),
            waits=[(sV, apply2[t - 2])] if t >= 2 else [], inc=sX[t % 2], amt=16)
        for ofs in range(2):
            slot = ofs
            wl = [(sS, oev[t]), (sGq, wo_ld[NC - 1])]
            if po_ev[slot]:
                wl.append((sV, po_ev[slot]))
            last = 0
            for c in range(NC):
                last = p.op("tensor", lambda e, t=t, ofs=ofs, c=c, slot=slot: e.matmul(
                    po4[slot][:, :], B1[c][:, t * 128:(t + 1) * 128],
                    wosb[c][:, ofs * 512:(ofs + 1) * 512],
                    start=(c == 0), stop=(c == NC - 1)),
                    waits=wl if c == 0 else [],
                    inc=sT if c == NC - 1 else None)
            wl2 = [(sT, last)]
            if t >= 2:
                wl2.append((sV, apply2[t - 2]))
            po_ev[slot] = p.op("vector", lambda e, t=t, ofs=ofs, slot=slot: e.scalar_tensor_tensor(
                out=attn[t % 2][:, ofs * 512:(ofs + 1) * 512], in0=po4[slot][:, :],
                scalar=1.0, in1=boB[:, ofs * 512:(ofs + 1) * 512],
                op0=ALU.mult, op1=ALU.add), waits=wl2, inc=sV)
        if t >= 1:
            emit_x3t(t - 1)
        p.op("vector", lambda e, t=t: e.bn_stats(stat[:, 0:6], attn[t % 2][:, 0:512]),
             waits=[(sV, po_ev[1])])
        nsb = p.op("vector", lambda e, t=t: e.bn_stats(stat[:, 6:12], attn[t % 2][:, 512:1024]))
        nag = p.op("vector", lambda e: e.bn_aggr(aggr[:], stat[:]), waits=[(sV, nsb)], inc=sV)
        nsq = p.op("scalar", lambda e: e.activation(out=scol[:], in_=aggr[:, 1:2],
                                                    func=AF.Sqrt, bias=epsc[:], scale=1.0),
                   waits=[(sV, nag)], inc=sS)
        nrcp = p.op("vector", lambda e: e.reciprocal(rcol[:], scol[:]), waits=[(sS, nsq)])
        if DBG and t == 0:
            p.op("vector", lambda e: e.tensor_copy(dbgs[:, 64:66], aggr[:, :]))
            p.op("vector", lambda e: e.tensor_copy(dbgs[:, 66:67], scol[:, :]))
            p.op("vector", lambda e: e.tensor_copy(dbgs[:, 67:68], rcol[:, :]),
                 waits=[(sV, nrcp)])
            p.op("vector", lambda e: e.tensor_copy(dbgs[:, 72:76], attn[0][:, 0:4]))
        napl2 = p.op("vector", lambda e, t=t: e.tensor_scalar(
            out=attn[t % 2][:], in0=attn[t % 2][:], scalar1=aggr[:, 0:1], scalar2=rcol[:],
            op0=ALU.subtract, op1=ALU.mult), waits=[(sV, nrcp)])
        wl = [(sX[t % 2], xre_ld), (sV, napl2)]
        if x2sp[t % 2]:
            wl.append((sGs[t % 2], x2sp[t % 2]))
        apply2[t] = p.op("vector", lambda e, t=t: e.tensor_add(
            x2f[t % 2][:], attn[t % 2][:], xre[t % 2][:]), waits=wl, inc=sV)
        x2sp[t % 2] = p.op("gpsimd", lambda e, t=t: e.dma_start(
            out=dx2[t * 128:(t + 1) * 128, :], in_=x2f[t % 2][:, :]),
            waits=[(sV, apply2[t])], inc=sGs[t % 2], amt=16)
        p.op("vector", lambda e, t=t: e.bn_stats(stat[:, 0:6], x2f[t % 2][:, 0:512]),
             waits=[(sV, apply2[t])])
        nsb = p.op("vector", lambda e, t=t: e.bn_stats(stat[:, 6:12], x2f[t % 2][:, 512:1024]))
        nag = p.op("vector", lambda e: e.bn_aggr(aggr[:], stat[:]), waits=[(sV, nsb)], inc=sV)
        nsq = p.op("scalar", lambda e: e.activation(out=scol[:], in_=aggr[:, 1:2],
                                                    func=AF.Sqrt, bias=epsc[:], scale=1.0),
                   waits=[(sV, nag)], inc=sS)
        nrcp2 = p.op("vector", lambda e: e.reciprocal(rcol[:], scol[:]), waits=[(sS, nsq)])
        if DBG and t == 0:
            p.op("vector", lambda e: e.tensor_copy(dbgs[:, 68:70], aggr[:, :]))
            p.op("vector", lambda e: e.tensor_copy(dbgs[:, 70:71], scol[:, :]))
            p.op("vector", lambda e: e.tensor_copy(dbgs[:, 71:72], rcol[:, :]),
                 waits=[(sV, nrcp2)])
        wl = [(sV, nrcp2)]
        if t >= 2:
            wl.append((sV, q_done[t - 2]))
        apply3[t] = p.op("vector", lambda e, t=t: e.tensor_scalar(
            out=x3f[t % 2][:], in0=x2f[t % 2][:], scalar1=aggr[:, 0:1], scalar2=rcol[:],
            op0=ALU.subtract, op1=ALU.mult), waits=wl, inc=sV)
        wl = [(sV, apply3[t])]
        if t >= 1:
            wl.append((sV, q_done[t - 1]))
        nred4 = p.op("vector", lambda e, t=t: e.reduce_max(
            sxc[:, :], x3f[t % 2][:, :], axis=AXX, apply_absolute_value=True), waits=wl)
        nclp = p.op("vector", lambda e: e.tensor_scalar(out=sxc[:], in0=sxc[:], scalar1=1e-5,
                                                        scalar2=None, op0=ALU.max),
                    waits=[(sV, nred4)])
        nrs = p.op("vector", lambda e: e.reciprocal(sxic[:], sxc[:]), waits=[(sV, nclp)])
        nm127 = p.op("vector", lambda e: e.tensor_scalar(out=sxic[:], in0=sxic[:],
                                                         scalar1=127.0, scalar2=None,
                                                         op0=ALU.mult), waits=[(sV, nrs)])
        nq16t = p.op("vector", lambda e, t=t: e.tensor_scalar(
            out=q16[:, :], in0=x3f[t % 2][:], scalar1=sxic[:], scalar2=RC,
            op0=ALU.mult, op1=ALU.add), waits=[(sV, nm127)])
        nsxf = p.op("vector", lambda e: e.tensor_scalar(out=sxc[:], in0=sxc[:],
                                                        scalar1=1.0 / 127.0,
                                                        scalar2=None, op0=ALU.mult))
        if DBG:
            p.op("vector", lambda e, t=t: e.tensor_copy(dbgs[:, t:t + 1], sxc[:, :]),
                 waits=[(sV, nsxf)])
            p.op("vector", lambda e, t=t: e.tensor_copy(dbgs[:, 16 + t:17 + t], sxic[:, :]))
        wl = [(sV, nq16t), (sV, nsxf)]
        if t >= 2:
            wl.append((sS, x3ev[t - 2]))
        q_done[t] = p.op("vector", lambda e, t=t: e.tensor_scalar(
            out=x3q[t % 2][:], in0=q16[:, :], scalar1=RD, scalar2=sxc[:],
            op0=ALU.subtract, op1=ALU.mult), waits=wl, inc=sV)

        def emit_x3t(tp):
            for c in range(NC):
                wl = []
                if c == 0:
                    wl.append((sV, q_done[tp]))
                    if tp >= 2:
                        wl.append((sS, x3ev[tp - 2]))
                x3tr[tp] = p.op("tensor", lambda e, tp=tp, c=c: e.transpose(
                    ptrX[tp % 2][:, c * 128:(c + 1) * 128],
                    x3q[tp % 2][:, c * 128:(c + 1) * 128], iden[:]),
                    waits=wl, inc=sT if c == NC - 1 else None)
            wl = [(sT, x3tr[tp])]
            if x3sp[tp % 2]:
                wl.append((sSp[tp % 2], x3sp[tp % 2]))
            x3ev[tp] = p.op("scalar", lambda e, tp=tp: e.copy(
                x3st[tp % 2][:, :], ptrX[tp % 2][:, :]), waits=wl, inc=sS)
            x3sp[tp % 2] = p.op("sync", lambda e, tp=tp: e.dma_start(
                out=dx3t[:, :, 1 + tp * 128:1 + (tp + 1) * 128].transpose([1, 0, 2]),
                in_=x3st[tp % 2][:, :]), waits=[(sS, x3ev[tp])],
                inc=sSp[tp % 2], amt=16)

        if t == NT - 1:
            emit_x3t(t)
    p.op("sync", lambda e: e.nop(), waits=[(sSp[0], sSp[0].n), (sSp[1], sSp[1].n),
                                           (sX[0], sX[0].n), (sX[1], sX[1].n)])
    p.op("gpsimd", lambda e: e.nop(), waits=[(sGs[0], sGs[0].n), (sGs[1], sGs[1].n),
                                             (sGq, sGq.n)])
    run_block(nc, p, hwm, "b4_oproj")
    free(*[f"wo_{c}" for c in range(NC)], "B1all",
         *[f"B2_{t}" for t in range(NT)],
         "bvB", "boB",
         "attn_0", "attn_1", "x2f_0", "x2f_1", "x3f_0", "x3f_1", "xre_0", "xre_1",
         "q16", "x3q_0", "x3q_1", "x3st_0", "x3st_1", "sxc", "sxic",
         "ptrO_0", "ptrO_1", "ptrX_0", "ptrX_1", "po4_0", "po4_1")

    diag = alloc("diag", [128, 3 * HCH * 128], BF16)
    w1h = [alloc(f"w1h_{i}", [128, 4 * 1024], BF16) for i in range(2)]
    w2b = [alloc(f"w2b_{i}", [128, 4 * 1024], BF16) for i in range(2)]
    x3w = [alloc(f"x3w_{i}", [128, 8 * 256], BF16) for i in range(3)]
    Ab = [alloc(f"Ab_{i}", [128, HCH * 258], BF16) for i in range(2)]
    H2 = alloc("H2", [128, HCH * 256], F32)
    H2q = [alloc(f"H2q_{i}", [128, HCH * 256], BF16) for i in range(2)]
    tland = [alloc(f"tland_{i}", [128, 256], F32) for i in range(2)]
    q16b = [alloc(f"q16b_{i}", [128, 256], I16) for i in range(2)]
    apar = alloc("apar", [128, 4 * 256], F32)
    amaxC = alloc("amaxC", [128, 256], F32)
    v2 = alloc("v2", [128, 4], F32)
    sxr = alloc("sxr", [1, 256], F32)
    sxB = alloc("sxB", [128, 256], F32)
    sc2b = [alloc(f"sc2b_{i}", [128, 2], F32) for i in range(2)]
    out2 = [alloc(f"out2_{i}", [128, D], F32) for i in range(2)]
    x4b = [alloc(f"x4b_{i}", [128, D], F32) for i in range(2)]
    x2re = [alloc(f"x2re_{i}", [128, D], BF16) for i in range(2)]
    pA = [alloc(f"pA_{i}", [128, 512], F32, "psum") for i in range(2)]
    pC = alloc("pC", [128, 512], F32, "psum")
    pM = [alloc(f"pM_{i}", [128, 512], F32, "psum") for i in range(4)]
    pX = alloc("pX", [128, 512], F32, "psum")

    p = Plan()
    dg_ld = p.op("gpsimd", lambda e: e.dma_start(
        out=diag[:, :], in_=ddiag[:, :]), inc=sG, amt=16)

    def dg(k, c):
        return diag[:, (k * HCH + c) * 128:(k * HCH + c) * 128 + 128]

    def ab(s, c):
        return Ab[s % 2][:, c * 258:(c + 1) * 258]

    x3_ld = [0] * (NS + 2)
    a_mm = [[0] * HCH for _ in range(NS + 2)]
    a_ev = [[0] * HCH for _ in range(NS + 2)]
    halo_d = [[0] * HCH for _ in range(NS + 2)]
    conv_mm = [[0] * HCH for _ in range(NS)]
    silu_d = [[0] * HCH for _ in range(NS)]
    ntp_d = [[0] * HCH for _ in range(NS)]
    h2q_d = [[0] * HCH for _ in range(NS)]
    apar_d = [[0] * 4 for _ in range(NS)]
    mm2_last = [0] * NS
    w1_ld = {}
    w1_used = {}
    pa_ev = [0, 0]
    pc_ev = [0]
    pm_ev = [0, 0, 0, 0]
    w2_used = [0, 0]
    x2re_used = [0, 0]
    out_dma = [0, 0]
    vcomb = [0]
    sxB_ev = [0]
    sc2_d = [0, 0]
    trans_last = [0]
    ktrans_last = [0]

    def emit_x3ld(s2):
        par = s2 % 3
        if s2 == NS:
            src = dx3t[:, :, s2 * TS:s2 * TS + 2].transpose([1, 0, 2])
            dst = x3w[par].ap().rearrange("p (c n) -> p c n", n=256)[:, :, 0:2]
        else:
            src = dx3t[:, :, s2 * TS:s2 * TS + 256].transpose([1, 0, 2])
            dst = x3w[par][:, :]
        wl = [(sT, a_mm[s2 - 3][HCH - 1])] if s2 >= 3 else []
        if s2 >= 2:
            wl.append((sX[s2 % 2], x3_ld[s2 - 2]))
        x3_ld[s2] = p.op("sync", lambda e, src=src, dst=dst: e.dma_start(out=dst, in_=src),
                         waits=wl, inc=sX[s2 % 2], amt=16)

    def emit_w1ld(s2, q):
        j = s2 * 8 + q
        wl = [(sT, w1_used[j - 2])] if (j - 2) in w1_used else []
        src = dw1b[q, :, :]
        w1_ld[j] = p.op("gpsimd", lambda e, src=src, j=j: e.dma_start(
            out=w1h[j % 2][:, :], in_=src), waits=wl, inc=sGw[j % 2], amt=16)

    def emit_A_chunk(s2, hcx):
        par = s2 % 3
        width = 2 if s2 == NS else 256
        slotp = hcx % 2
        j = s2 * 8 + hcx // 4
        hs2 = hcx % 4
        wl = [(sX[s2 % 2], x3_ld[s2]), (sGw[j % 2], w1_ld[j])]
        if pa_ev[slotp]:
            wl.append((sS, pa_ev[slotp]))
        last = 0
        for d in range(NC):
            if s2 == NS:
                mv = x3w[par].ap().rearrange("p (c n) -> p c n", n=256)[:, d, 0:2]
            else:
                mv = x3w[par][:, d * 256:(d + 1) * 256]
            last = p.op("tensor", lambda e, j=j, hs2=hs2, d=d, slotp=slotp, mv=mv, width=width: e.matmul(
                pA[slotp][:, 0:width],
                w1h[j % 2][:, d * 512 + hs2 * 128:d * 512 + hs2 * 128 + 128],
                mv, start=(d == 0), stop=(d == NC - 1)),
                waits=wl if d == 0 else [],
                inc=sT if d == NC - 1 else None)
        a_mm[s2][hcx] = last
        w1_used[j] = last
        wl = [(sT, last)]
        if s2 >= 2:
            wl.append((sT, conv_mm[s2 - 2][hcx]))
        pa_ev[slotp] = p.op("scalar", lambda e, s2=s2, hcx=hcx, slotp=slotp, width=width: e.activation(
            out=ab(s2, hcx)[:, 0:width], in_=pA[slotp][:, 0:width],
            func=AF.Identity, bias=0.0, scale=mwc[:, 0:1]), waits=wl, inc=sS)
        a_ev[s2][hcx] = pa_ev[slotp]
        if s2 >= 1:
            halo_d[s2 - 1][hcx] = p.op("scalar", lambda e, s2=s2, hcx=hcx: e.copy(
                ab(s2 - 1, hcx)[:, 256:258], ab(s2, hcx)[:, 0:2]),
                waits=[(sS, a_ev[s2][hcx])], inc=sS)

    def emit_conv_chunk(s, hcx):
        wl = [(sS, halo_d[s][hcx]), (sG, dg_ld)]
        if pc_ev[0]:
            wl.append((sS, pc_ev[0]))
        if s >= 1:
            wl.append((sV, ntp_d[s - 1][hcx]))
            wl.append((sV, apar_d[s - 1][hcx // 8]))
        last = 0
        for k in range(3):
            last = p.op("tensor", lambda e, s=s, hcx=hcx, k=k: e.matmul(
                pC[:, 0:256], dg(k, hcx), ab(s, hcx)[:, k:k + 256],
                start=(k == 0), stop=(k == 2)),
                waits=wl if k == 0 else [],
                inc=sT if k == 2 else None)
        conv_mm[s][hcx] = last
        silu_d[s][hcx] = p.op("scalar", lambda e, s=s, hcx=hcx: e.activation(
            out=H2[:, hcx * 256:(hcx + 1) * 256], in_=pC[:, 0:256],
            func=AF.Silu, bias=cbs[:, hcx:hcx + 1], scale=1.0),
            waits=[(sT, last)], inc=sS)
        pc_ev[0] = silu_d[s][hcx]

    def emit_mm2(sm):
        par = sm % 2
        x2_ld = [0, 0]
        for tsl in range(2):
            t = sm * 2 + tsl
            wl = [(sV, x2re_used[tsl])] if x2re_used[tsl] else []
            x2_ld[tsl] = p.op("sync", lambda e, t=t, tsl=tsl: e.dma_start(
                out=x2re[tsl][:, :], in_=dx2[t * 128:(t + 1) * 128, :]),
                waits=wl, inc=sX2[tsl], amt=16)
        last = 0
        for cc in range(0, HCH, 4):
            b = (cc // 4) % 2
            wl = [(sT, w2_used[b])] if w2_used[b] else []
            w2_ld = p.op("sync", lambda e, cc=cc, b=b: e.dma_start(
                out=w2b[b][:, :], in_=dw2b[:, cc * 1024:(cc + 4) * 1024]),
                waits=wl, inc=sW2[b], amt=16)
            for hcx in range(cc, cc + 4):
                qq = hcx % 4
                for tsl in range(2):
                    for ofs in range(2):
                        slot = tsl * 2 + ofs
                        wl = []
                        if hcx == cc and tsl == 0 and ofs == 0:
                            wl.append((sW2[b], w2_ld))
                            wl.append((sV, h2q_d[sm][cc + 3]))
                        if hcx == 0 and pm_ev[slot]:
                            wl.append((sV, pm_ev[slot]))
                        fin = (hcx == cc + 3 and tsl == 1 and ofs == 1)
                        last = p.op("tensor", lambda e, sm=sm, hcx=hcx, tsl=tsl, ofs=ofs, b=b, qq=qq, slot=slot: e.matmul(
                            pM[slot][:, :],
                            H2q[sm % 2][:, hcx * 256 + tsl * 128:hcx * 256 + tsl * 128 + 128],
                            w2b[b][:, qq * 1024 + ofs * 512:qq * 1024 + ofs * 512 + 512],
                            start=(hcx == 0), stop=(hcx == HCH - 1)),
                            waits=wl, inc=sT if fin else None)
            w2_used[b] = last
        mm2_last[sm] = last
        for tsl in range(2):
            for ofs in range(2):
                slot = tsl * 2 + ofs
                wl = [(sT, mm2_last[sm]), (sV, sc2_d[par])]
                if x2re_used[tsl]:
                    wl.append((sV, x2re_used[tsl]))
                pm_ev[slot] = p.op("vector", lambda e, tsl=tsl, ofs=ofs, slot=slot, par=par: e.tensor_scalar(
                    out=out2[tsl][:, ofs * 512:(ofs + 1) * 512], in0=pM[slot][:, :],
                    scalar1=sc2b[par][:, tsl:tsl + 1], scalar2=None, op0=ALU.mult),
                    waits=wl, inc=sV)
            p.op("vector", lambda e, tsl=tsl: e.bn_stats(stat[:, 0:6], out2[tsl][:, 0:512]),
                 waits=[(sV, pm_ev[tsl * 2 + 1])])
            nsb = p.op("vector", lambda e, tsl=tsl: e.bn_stats(stat[:, 6:12],
                                                               out2[tsl][:, 512:1024]))
            nag = p.op("vector", lambda e: e.bn_aggr(aggr[:], stat[:]),
                       waits=[(sV, nsb)], inc=sV)
            nsq = p.op("scalar", lambda e: e.activation(out=scol[:], in_=aggr[:, 1:2],
                                                        func=AF.Sqrt, bias=epsc[:], scale=1.0),
                       waits=[(sV, nag)], inc=sS)
            nrcp5 = p.op("vector", lambda e: e.reciprocal(rcol[:], scol[:]),
                         waits=[(sS, nsq)])
            napl4 = p.op("vector", lambda e, tsl=tsl: e.tensor_scalar(
                out=out2[tsl][:], in0=out2[tsl][:], scalar1=aggr[:, 0:1], scalar2=rcol[:],
                op0=ALU.subtract, op1=ALU.mult), waits=[(sV, nrcp5)])
            wl = [(sX2[tsl], x2_ld[tsl]), (sV, napl4)]
            if out_dma[tsl]:
                wl.append((sOut[tsl], out_dma[tsl]))
            x2re_used[tsl] = p.op("vector", lambda e, tsl=tsl: e.tensor_add(
                x4b[tsl][:], out2[tsl][:], x2re[tsl][:]), waits=wl, inc=sV)
            t = sm * 2 + tsl
            out_dma[tsl] = p.op("sync", lambda e, t=t, tsl=tsl: e.dma_start(
                out=dout[t * 128:(t + 1) * 128, :], in_=x4b[tsl][:, :]),
                waits=[(sV, x2re_used[tsl])], inc=sOut[tsl], amt=16)

    emit_x3ld(0)
    emit_x3ld(1)
    emit_x3ld(2)
    for s2 in (0, 1):
        for hcx in range(HCH):
            if hcx % 4 == 0:
                emit_w1ld(s2, hcx // 4)
            emit_A_chunk(s2, hcx)

    for s in range(NS):
        s2 = s + 2
        if s + 3 <= NS:
            emit_x3ld(s + 3)
        for hcx in range(HCH):
            emit_conv_chunk(s, hcx)
            if s2 <= NS:
                if hcx % 4 == 0:
                    emit_w1ld(s2, hcx // 4)
                emit_A_chunk(s2, hcx)
            if hcx % 8 == 7:
                g3 = hcx // 8
                wl = [(sS, silu_d[s][hcx])]
                if g3 == 0 and vcomb[0]:
                    wl.append((sV, vcomb[0]))
                apar_d[s][g3] = p.op("vector", lambda e, g3=g3: e.reduce_max(
                    apar[:, g3 * 256:(g3 + 1) * 256],
                    H2.ap().rearrange("p (c n) -> p n c", c=HCH)[:, :, g3 * 8:(g3 + 1) * 8],
                    axis=AXX, apply_absolute_value=True), waits=wl, inc=sV)
        cA = p.op("vector", lambda e: e.tensor_tensor(
            amaxC[:, :], apar[:, 0:256], apar[:, 256:512], ALU.max),
            waits=[(sV, apar_d[s][1]), (sT, trans_last[0])], inc=sV)
        cB = p.op("vector", lambda e: e.tensor_tensor(
            apar[:, 768:1024], apar[:, 512:768], apar[:, 768:1024], ALU.max),
            waits=[(sV, apar_d[s][3])], inc=sV)
        cF = p.op("vector", lambda e: e.tensor_tensor(
            amaxC[:, :], amaxC[:, :], apar[:, 768:1024], ALU.max),
            waits=[(sV, cA), (sV, cB)], inc=sV)
        vcomb[0] = cF
        red = 0
        for tt in range(2):
            wl = [(sV, cF), (sV, sxB_ev[0])] if tt == 0 else [(sV, red)]
            ntr = p.op("tensor", lambda e, tt=tt: e.transpose(
                pX[:, 0:128], amaxC[:, tt * 128:(tt + 1) * 128], idenf[:]),
                waits=wl, inc=sT)
            red = p.op("vector", lambda e, tt=tt: e.reduce_max(
                v2[:, tt:tt + 1], pX[0:128, 0:128], axis=AXX),
                waits=[(sT, ntr)], inc=sV)
            trans_last[0] = ntr
        ncl5 = p.op("vector", lambda e: e.tensor_scalar(out=v2[:, 0:2], in0=v2[:, 0:2],
                                                        scalar1=1e-5, scalar2=None,
                                                        op0=ALU.max),
                    waits=[(sV, red), (sT, ktrans_last[0])], inc=sV)
        p.op("vector", lambda e: e.reciprocal(v2[:, 2:3], v2[:, 0:1]), waits=[(sV, ncl5)])
        nrB = p.op("vector", lambda e: e.reciprocal(v2[:, 3:4], v2[:, 1:2]))
        nm7 = p.op("vector", lambda e: e.tensor_scalar(out=v2[:, 2:4], in0=v2[:, 2:4],
                                                       scalar1=127.0, scalar2=None,
                                                       op0=ALU.mult),
                   waits=[(sV, nrB)], inc=sV)
        nv2 = p.op("vector", lambda e: e.tensor_scalar(out=v2[:, 0:2], in0=v2[:, 0:2],
                                                       scalar1=1.0 / 127.0, scalar2=None,
                                                       op0=ALU.mult), inc=sV)
        for tsl in range(2):
            sc2_d[s % 2] = p.op("vector", lambda e, tsl=tsl, s=s: e.tensor_tensor(
                sc2b[s % 2][:, tsl:tsl + 1], v2[:, tsl:tsl + 1], mwc[:, 1:2], ALU.mult),
                waits=[(sV, nv2)], inc=sV)
        prev_ev = 0
        for k in range(2):
            wl = [(sV, nm7)]
            if prev_ev:
                wl.append((sV, prev_ev))
            ntr = p.op("tensor", lambda e, k=k: e.transpose(
                pX[0:1, 128:256], v2[:, 2 + k:3 + k], idenf[:]), waits=wl, inc=sT)
            prev_ev = p.op("vector", lambda e, k=k: e.tensor_copy(
                sxr[0:1, k * 128:(k + 1) * 128], pX[0:1, 128:256]),
                waits=[(sT, ntr)], inc=sV)
            ktrans_last[0] = ntr
        nb = p.op("tensor", lambda e: e.matmul(pX[:, 256:512], ones128[:, :],
                                               sxr[0:1, :], start=True, stop=True),
                  waits=[(sV, prev_ev)], inc=sT)
        wl = [(sT, nb)]
        if s >= 1:
            wl.append((sV, ntp_d[s - 1][HCH - 1]))
        sxB_ev[0] = p.op("vector", lambda e: e.tensor_copy(sxB[:, :], pX[0:128, 256:512]),
                         waits=wl, inc=sV)
        for h0 in range(0, HCH, 2):
            ntl = [0, 0]
            nuu = [0, 0]
            for i in range(2):
                hcx = h0 + i
                wl = [(sV, sxB_ev[0])]
                if hcx >= 2:
                    wl.append((sV, h2q_d[s][hcx - 2]))
                ntl[i] = p.op("vector", lambda e, hcx=hcx, i=i: e.tensor_tensor(
                    tland[i][:, :], H2[:, hcx * 256:(hcx + 1) * 256], sxB[:, :], ALU.mult),
                    waits=wl, inc=sV)
                ntp_d[s][hcx] = ntl[i]
            for i in range(2):
                nuu[i] = p.op("vector", lambda e, i=i: e.tensor_scalar(
                    out=q16b[i][:, :], in0=tland[i][:, :],
                    scalar1=RC, scalar2=None, op0=ALU.add),
                    waits=[(sV, ntl[i])], inc=sV)
            for i in range(2):
                hcx = h0 + i
                wl = [(sV, nuu[i])]
                if s >= 2:
                    wl.append((sT, mm2_last[s - 2]))
                h2q_d[s][hcx] = p.op("vector", lambda e, s=s, hcx=hcx, i=i: e.tensor_scalar(
                    out=H2q[s % 2][:, hcx * 256:(hcx + 1) * 256], in0=q16b[i][:, :],
                    scalar1=RD, scalar2=None, op0=ALU.subtract), waits=wl, inc=sV)
        if s >= 1:
            emit_mm2(s - 1)
    emit_mm2(NS - 1)
    if DBG:
        p.op("sync", lambda e: e.dma_start(out=dbgS[:, :], in_=dbgs[:, :]), inc=sD, amt=16)
    p.op("sync", lambda e: e.nop(), waits=[
        (sOut[0], sOut[0].n), (sOut[1], sOut[1].n),
        (sX[0], sX[0].n), (sX[1], sX[1].n), (sX2[0], sX2[0].n), (sX2[1], sX2[1].n)])
    p.op("gpsimd", lambda e: e.nop(), waits=[
        (sG, sG.n), (sGw[0], sGw[0].n), (sGw[1], sGw[1].n),
        (sW2[0], sW2[0].n), (sW2[1], sW2[1].n)])
    run_block(nc, p, hwm, "b5_ffn")

    free(*list(cms))
    for s in (sP, sS, sV, sT, *sW2[::-1], *sGw[::-1], *sGs[::-1], sGk, sGq,
              *sOut[::-1], *sSp[::-1], *sW[::-1], *sX2[::-1], *sX[::-1], sG, sD):
        s._cm.__exit__(None, None, None)
    return nc


_NC_CACHE = {}


def _get_nc():
    key = "nc"
    if key not in _NC_CACHE:
        _NC_CACHE[key] = build_nc()
    return _NC_CACHE[key]


def kernel(L0_attn_x, L0_attn_wq, L0_attn_bq, L0_attn_wk, L0_attn_bk,
           L0_attn_wv, L0_attn_bv, L0_attn_wo, L0_attn_bo,
           L0_ffn_w1, L0_ffn_conv_w, L0_ffn_conv_b, L0_ffn_w2):
    x = np.ascontiguousarray(np.asarray(L0_attn_x, dtype=np.float32))
    B = x.shape[0]
    f32 = lambda a: np.ascontiguousarray(np.asarray(a, dtype=np.float32))
    wq, wk, wv, wo = f32(L0_attn_wq), f32(L0_attn_wk), f32(L0_attn_wv), f32(L0_attn_wo)
    bq = np.ascontiguousarray(f32(L0_attn_bq).reshape(NC, 128).T)
    bk = np.ascontiguousarray(f32(L0_attn_bk).reshape(NC, 128).T)
    bv = f32(L0_attn_bv).reshape(1, D)
    bo = f32(L0_attn_bo).reshape(1, D)
    w1t = np.ascontiguousarray(f32(L0_ffn_w1).T)
    w2t = np.ascontiguousarray(f32(L0_ffn_w2).T)
    cw = f32(L0_ffn_conv_w).reshape(DH, 3)
    diag3 = np.zeros((128, 3, HCH, 128), dtype=np.float32)
    idx = np.arange(128)
    for k in range(3):
        for c in range(HCH):
            diag3[idx, k, c, idx] = cw[c * 128 + idx, k]
    diag3 = diag3.reshape(128, 3 * HCH * 128)
    convb = np.ascontiguousarray(f32(L0_ffn_conv_b).reshape(HCH, 128).T)
    pp = np.arange(128)[:, None]
    ff = np.arange(384)[None, :]
    mask = ((pp <= ff) & (ff <= pp + 256)).astype(np.float32)
    iden = np.eye(128, dtype=np.float32)
    hv = np.zeros((128, 2), dtype=np.float32)
    hv[:64, 0] = 1.0
    hv[64:, 1] = 1.0
    ones128 = np.ones((1, 128), dtype=np.float32)
    onescol = np.ones((128, 1), dtype=np.float32)

    nc = _get_nc()
    common = dict(wq=wq, wk=wk, wv=wv, wo=wo, bq=bq, bk=bk, bv=bv, bo=bo,
                  w1t=w1t, w2t=w2t, diag3=diag3, convb=convb, mask01=mask,
                  iden=iden, halves=hv, ones128=ones128, onescol=onescol)
    in_maps = [dict(common, x=np.ascontiguousarray(x[i])) for i in range(B)]
    res = run_bass_kernel_spmd(nc, in_maps, core_ids=list(range(B)))
    out = np.stack([np.asarray(r["out"], dtype=np.float32) for r in res.results], axis=0)
    return out


# version 20
# speedup vs baseline: 1.1513x; 1.0032x over previous
import os
import numpy as np
import concourse.bass as bass
import concourse.mybir as mybir
from concourse.bass_utils import run_bass_kernel_spmd

F32 = mybir.dt.float32
BF16 = mybir.dt.bfloat16
I16 = mybir.dt.int16
I8 = mybir.dt.int8
AF = mybir.ActivationFunctionType
ALU = mybir.AluOpType
AXX = mybir.AxisListType.X

_SIM = os.environ.get("BASS_SIM_FP64") == "1"
RC = 127.5 if _SIM else 128.0
RD = 127.0 if _SIM else 128.0

T = 2048
D = 1024
NT = T // 128
NC = D // 128
H = 16
NJ = 16
DH = 4096
HCH = 32
TS = 256
NS = T // TS
EPS = 1e-5

ENGS = ("sync", "tensor", "vector", "scalar", "gpsimd")


class Sem:
    def __init__(self, nc, name):
        self._cm = nc.semaphore(name)
        self.h = self._cm.__enter__()
        self.n = 0
        self.name = name


class Plan:
    def __init__(self):
        self.items = {e: [] for e in ENGS}
        self.incs = {e: {} for e in ENGS}

    AUTO = {}

    def op(self, eng, emit, waits=(), inc=None, amt=1):
        w = [(s, v) for (s, v) in waits if v and v > 0]
        if inc is None and eng in self.AUTO:
            inc = self.AUTO[eng]
            amt = 1
        if inc is not None:
            inc.n += amt
            self.incs[eng][inc.name] = inc
        self.items[eng].append((w, emit, inc, amt))
        return inc.n if inc is not None else None

    def replay(self, eng, e, hwm):
        for w, emit, inc, amt in self.items[eng]:
            for s, v in w:
                if hwm[eng].get(s.name, 0) < v:
                    e.wait_ge(s.h, v)
                    hwm[eng][s.name] = v
            ins = emit(e)
            if inc is not None:
                ins.then_inc(inc.h, amt)
        for s in self.incs[eng].values():
            if hwm[eng].get(s.name, 0) < s.n:
                e.wait_ge(s.h, s.n)
                hwm[eng][s.name] = s.n


PHASES = []
_PHTOT = {e: 0 for e in ENGS}


def run_block(nc, plan, hwm, phase=""):
    for e in ENGS:
        _PHTOT[e] += len(plan.items[e])
    PHASES.append((phase, dict(_PHTOT)))
    with nc.Block() as blk:
        @blk.sync
        def _(e):
            plan.replay("sync", e, hwm)

        @blk.tensor
        def _(e):
            plan.replay("tensor", e, hwm)

        @blk.vector
        def _(e):
            plan.replay("vector", e, hwm)

        @blk.scalar
        def _(e):
            plan.replay("scalar", e, hwm)

        @blk.gpsimd
        def _(e):
            plan.replay("gpsimd", e, hwm)


def build_nc():
    PHASES.clear()
    for e in ENGS:
        _PHTOT[e] = 0
    nc = bass.Bass("TRN2", target_bir_lowering=False, debug=False,
                   detect_race_conditions=False)

    dx = nc.dram_tensor("x", [T, D], F32, kind="ExternalInput").ap()
    dw = {k: nc.dram_tensor(k, [D, D], F32, kind="ExternalInput").ap()
          for k in ("wq", "wk", "wv", "wo")}
    dbq = nc.dram_tensor("bq", [128, NC], F32, kind="ExternalInput").ap()
    dbk = nc.dram_tensor("bk", [128, NC], F32, kind="ExternalInput").ap()
    dbv = nc.dram_tensor("bv", [1, D], F32, kind="ExternalInput").ap()
    dbo = nc.dram_tensor("bo", [1, D], F32, kind="ExternalInput").ap()
    dw1t = nc.dram_tensor("w1t", [D, DH], F32, kind="ExternalInput").ap()
    dw2t = nc.dram_tensor("w2t", [DH, D], F32, kind="ExternalInput").ap()
    ddiag = nc.dram_tensor("diag3", [128, 3 * HCH * 128], F32, kind="ExternalInput").ap()
    dcb = nc.dram_tensor("convb", [128, HCH], F32, kind="ExternalInput").ap()
    dmask = nc.dram_tensor("mask01", [128, 384], F32, kind="ExternalInput").ap()
    dmaskb = nc.dram_tensor("maskbig", [128, 384], F32, kind="ExternalInput").ap()
    diden = nc.dram_tensor("iden", [128, 128], F32, kind="ExternalInput").ap()
    dhalves = nc.dram_tensor("halves", [128, 2], F32, kind="ExternalInput").ap()
    dones128 = nc.dram_tensor("ones128", [1, 128], F32, kind="ExternalInput").ap()
    donescol = nc.dram_tensor("onescol", [128, 1], F32, kind="ExternalInput").ap()
    dout = nc.dram_tensor("out", [T, D], F32, kind="ExternalOutput").ap()
    DBG = os.environ.get("KDEBUG") == "1"
    iknd = "ExternalOutput" if DBG else "Internal"
    dx2 = nc.dram_tensor("x2sp", [T, D], BF16, kind=iknd).ap()
    dw1b = nc.dram_tensor("w1b16", [8, 128, 4096], BF16, kind=iknd).ap()
    dw2b = nc.dram_tensor("w2b16", [128, HCH * 1024], BF16, kind=iknd).ap()
    dx3t = nc.dram_tensor("x3tsp", [NC, 128, T + 2], BF16, kind=iknd).ap()
    dbgO = (nc.dram_tensor("dbgO", [T, D], BF16, kind="ExternalOutput").ap()
            if DBG else None)
    dbgS = (nc.dram_tensor("dbgS", [128, 80], F32, kind="ExternalOutput").ap()
            if DBG else None)

    sD = Sem(nc, "sD")
    sG = Sem(nc, "sG")
    sX = [Sem(nc, "sX0"), Sem(nc, "sX1")]
    sX2 = [Sem(nc, "sX2_0"), Sem(nc, "sX2_1")]
    sW = [Sem(nc, f"sW{i}") for i in range(4)]
    sSp = [Sem(nc, f"sSp{i}") for i in range(4)]
    sOut = [Sem(nc, "sOut0"), Sem(nc, "sOut1")]
    sGq = Sem(nc, "sGq")
    sGk = Sem(nc, "sGk")
    sGs = [Sem(nc, "sGs0"), Sem(nc, "sGs1")]
    sGw = [Sem(nc, "sGw0"), Sem(nc, "sGw1")]
    sW2 = [Sem(nc, "sW2_0"), Sem(nc, "sW2_1")]
    sT = Sem(nc, "sT")
    sV = Sem(nc, "sV")
    sS = Sem(nc, "sS")
    sP = Sem(nc, "sP")
    Plan.AUTO = {"vector": sV, "scalar": sS, "gpsimd": sP}
    hwm = {e: {} for e in ENGS}
    cms = {}
    aseq = {"n": 0}

    def alloc(name, shape, dt, space="sbuf", side=None):
        cm = (nc.sbuf_tensor(name, shape, dt, side=side) if space == "sbuf"
              else nc.psum_tensor(name, shape, dt))
        h = cm.__enter__()
        aseq["n"] += 1
        cms[name] = (aseq["n"], cm)
        return h

    def free(*names):
        for nm in sorted(names, key=lambda nm: -cms[nm][0]):
            cms.pop(nm)[1].__exit__(None, None, None)

    bqs = alloc("bqs", [128, NC], F32)
    bks = alloc("bks", [128, NC], F32)
    bvB = alloc("bvB", [128, D], F32, side="right")
    boB = alloc("boB", [128, D], F32, side="right")
    bvrow = alloc("bvrow", [1, D], F32, side="right")
    borow = alloc("borow", [1, D], F32, side="right")
    cbs = alloc("cbs", [128, HCH], F32)
    mask01 = alloc("sb_mask01", [128, 384], BF16)
    maskB = alloc("sb_maskB", [128, 384], BF16)
    iden = alloc("sb_iden", [128, 128], BF16)
    idenf = alloc("sb_idenf", [128, 128], F32)
    halves = alloc("sb_halves", [128, 2], F32)
    ones128 = alloc("sb_ones128", [1, 128], F32)
    onescol = alloc("sb_onescol", [128, 1], F32)
    epsc = alloc("epsc", [128, 1], F32)
    negC = alloc("negC", [128, H], F32)
    eC = alloc("eC", [128, H], F32)
    mwc = alloc("mwc", [128, 2], F32)
    swc = alloc("swc", [128, 2], F32)
    stat = alloc("stat", [128, 12], F32)
    aggr = alloc("aggr", [128, 2], F32)
    scol = alloc("scol", [128, 1], F32)
    rcol = alloc("rcol", [128, 1], F32)
    wsum = alloc("wsum", [128, 64], F32)
    mrow = alloc("mrow", [1, 64], F32)
    mtot = alloc("mtot", [1, 2], F32)
    zrow = alloc("zrow", [128, 1], BF16)
    qmx = alloc("qmx", [128, NC], F32)
    kmx = alloc("kmx", [128, NC], F32)
    qkp = alloc("qkp", [128, NC], F32)
    c2x8 = alloc("c2x8", [2, NC], F32)
    crow = alloc("crow", [1, 16], F32)
    dbgs = alloc("dbgs", [128, 80], F32)

    p = Plan()
    for (dst, src) in ((bqs, dbq), (bks, dbk), (bvrow, dbv), (borow, dbo),
                       (cbs, dcb), (halves, dhalves), (ones128, dones128),
                       (onescol, donescol), (idenf, diden)):
        p.op("sync", lambda e, d=dst, s=src: e.dma_start(out=d[:], in_=s[:, :]),
             inc=sD, amt=16)
    p.op("gpsimd", lambda e: e.dma_start(out=mask01[:], in_=dmask[:, :]), inc=sG, amt=16)
    p.op("gpsimd", lambda e: e.dma_start(out=maskB[:], in_=dmaskb[:, :]), inc=sG, amt=16)
    p.op("gpsimd", lambda e: e.dma_start(out=iden[:], in_=diden[:, :]), inc=sG, amt=16)
    p.op("vector", lambda e: e.memset(epsc[:], EPS), inc=sV)
    nzr = p.op("vector", lambda e: e.memset(zrow[:], 0.0), inc=sV)
    for c in range(NC):
        p.op("sync", lambda e, c=c: e.dma_start(out=dx3t[c, :, 0:1], in_=zrow[:, 0:1],
             allow_slow_non_contiguous=True), waits=[(sV, nzr)], inc=sD, amt=16)
        p.op("sync", lambda e, c=c: e.dma_start(out=dx3t[c, :, T + 1:T + 2], in_=zrow[:, 0:1],
             allow_slow_non_contiguous=True), inc=sD, amt=16)
    p.op("sync", lambda e: e.nop(), waits=[(sD, sD.n)])
    p.op("gpsimd", lambda e: e.nop(), waits=[(sG, sG.n)])
    run_block(nc, p, hwm, "b0_const")

    B1all = alloc("B1all", [128, NC * T], BF16)
    B1 = [B1all[:, c * T:(c + 1) * T] for c in range(NC)]
    B1r3 = B1all.ap().rearrange("p (c t) -> p c t", t=T)
    QT = [alloc(f"QT{c}", [128, T], BF16) for c in range(NC)]
    KT = [alloc(f"KT{c}", [128, T], BF16) for c in range(NC)]
    VA = [alloc(f"VA{j}", [128, H * 65], BF16) for j in range(NJ + 2)]

    def vaug(j):
        return VA[j + 1]

    wsb = [[alloc(f"wsb{s}_{c}", [128, D], BF16) for c in range(NC)] for s in range(2)]
    xin = [alloc(f"xin_{i}", [128, D], F32) for i in range(2)]
    x1b = [alloc(f"x1b_{i}", [128, D], BF16) for i in range(2)]
    ptr = [alloc(f"ptr_{i}", [128, 1024], BF16, "psum") for i in range(2)]
    pbc = [alloc(f"pbc_{i}", [128, 512], F32, "psum") for i in range(2)]

    p = Plan()
    wq_ld = [0] * NC
    wk_ld = [0] * NC
    wv_ld = [0] * NC
    for c in range(NC):
        wq_ld[c] = p.op("gpsimd", lambda e, c=c: e.dma_start(
            out=wsb[0][c][:], in_=dw["wq"][c * 128:(c + 1) * 128, :]), inc=sGq, amt=16)
    for c in range(NC):
        wk_ld[c] = p.op("gpsimd", lambda e, c=c: e.dma_start(
            out=wsb[1][c][:], in_=dw["wk"][c * 128:(c + 1) * 128, :]), inc=sGk, amt=16)
    for j in (-1, NJ):
        p.op("vector", lambda e, j=j: e.memset(
            vaug(j).ap().rearrange("p (h c) -> p h c", c=65)[:, :, 0:64], 0.0))
    vinit = 0
    for j in range(-1, NJ + 1):
        vinit = p.op("vector", lambda e, j=j: e.memset(
            vaug(j).ap().rearrange("p (h c) -> p h c", c=65)[:, :, 64:65], 1.0), inc=sV)

    bc_ev = 0
    for i, (dst, src) in enumerate(((bvB, bvrow), (bvB, bvrow), (boB, borow), (boB, borow))):
        off = (i % 2) * 512
        wl = [(sV, bc_ev)] if i >= 2 else []
        n = p.op("tensor", lambda e, s=src, off=off, i=i: e.matmul(
            pbc[i % 2][:, :], ones128[:, :], s[0:1, off:off + 512], start=True, stop=True),
            waits=wl, inc=sT)
        bc_ev = p.op("vector", lambda e, d=dst, off=off, i=i: e.tensor_copy(
            d[:, off:off + 512], pbc[i % 2][:, :]), waits=[(sT, n)], inc=sV)

    ln_apply = [0] * NT
    tr_done = [0] * NT
    ev_done = [0] * NT
    sq_done = [0] * NT
    for t in range(NT):
        wl = [(sV, ln_apply[t - 2])] if t >= 2 else []
        xload = p.op("sync", lambda e, t=t: e.dma_start(
            out=xin[t % 2][:], in_=dx[t * 128:(t + 1) * 128, :]), waits=wl,
            inc=sX[t % 2], amt=16)
        wl = [(sX[t % 2], xload)]
        if t >= 1:
            wl.append((sS, sq_done[t - 1]))
        p.op("vector", lambda e, t=t: e.bn_stats(stat[:, 0:6], xin[t % 2][:, 0:512]),
             waits=wl)
        nsb = p.op("vector", lambda e, t=t: e.bn_stats(stat[:, 6:12], xin[t % 2][:, 512:1024]))
        nag = p.op("vector", lambda e: e.bn_aggr(aggr[:], stat[:]), waits=[(sV, nsb)], inc=sV)
        sq_done[t] = p.op("scalar", lambda e: e.activation(
            out=scol[:], in_=aggr[:, 1:2], func=AF.Sqrt, bias=epsc[:], scale=1.0),
            waits=[(sV, nag)], inc=sS)
        nrc1 = p.op("vector", lambda e: e.reciprocal(rcol[:], scol[:]),
                    waits=[(sS, sq_done[t])])
        wl = [(sV, nrc1)]
        if t >= 2:
            wl.append((sT, tr_done[t - 2]))
        ln_apply[t] = p.op("vector", lambda e, t=t: e.tensor_scalar(
            out=x1b[t % 2][:], in0=xin[t % 2][:], scalar1=aggr[:, 0:1], scalar2=rcol[:],
            op0=ALU.subtract, op1=ALU.mult), waits=wl, inc=sV)
        for c in range(NC):
            wl = []
            if c == 0:
                wl.append((sV, ln_apply[t]))
                if t >= 2:
                    wl.append((sS, ev_done[t - 2]))
            tr_done[t] = p.op("tensor", lambda e, t=t, c=c: e.transpose(
                ptr[t % 2][:, c * 128:(c + 1) * 128],
                x1b[t % 2][:, c * 128:(c + 1) * 128], iden[:]),
                waits=wl, inc=sT if c == NC - 1 else None)
        ev_done[t] = p.op("scalar", lambda e, t=t: e.copy(
            B1r3[:, :, t * 128:(t + 1) * 128],
            ptr[t % 2].ap().rearrange("p (c q) -> p c q", q=128)[:, :, :]),
            waits=[(sT, tr_done[t])], inc=sS)
    run_block(nc, p, hwm, "b1_ln1")
    free("xin_0", "xin_1", "x1b_0", "x1b_1", "ptr_0", "ptr_1", "pbc_0", "pbc_1")
    free("bvrow", "borow")

    wstage = [alloc(f"wstage_{i}", [128, 1024], F32) for i in range(4)]
    pq = [alloc(f"pq_{i}", [128, 512], F32, "psum") for i in range(4)]
    pms = alloc("pms", [128, 512], F32, "psum")

    p = Plan()

    red_done = []
    piece_srcs = []
    for ch in range(NC):
        for h4 in range(4):
            piece_srcs.append((dw1t[ch * 128:(ch + 1) * 128, h4 * 1024:(h4 + 1) * 1024], 1024))
    for ch in range(HCH):
        piece_srcs.append((dw2t[ch * 128:(ch + 1) * 128, :], 1024))

    def emit_piece(idx):
        src, width = piece_srcs[idx]
        slot = idx % 4
        wl = [(sV, red_done[idx - 4])] if idx >= 4 else []
        ld = p.op("sync", lambda e, src=src, slot=slot, width=width: e.dma_start(
            out=wstage[slot][:, 0:width], in_=src), waits=wl, inc=sW[slot], amt=16)
        red_done.append(p.op("vector", lambda e, slot=slot, idx=idx, width=width: e.reduce_sum(
            wsum[:, idx:idx + 1], wstage[slot][:, 0:width], axis=AXX,
            apply_absolute_value=True), waits=[(sW[slot], ld)], inc=sV))

    NPC1 = 40
    for idx in range(NPC1):
        emit_piece(idx)

    def proj_qk(wset, wld, bias_s, OUT, slot_ev, evac_eng, wsem):
        for oc in range(NC):
            for st in range(4):
                slot = (oc * 4 + st) % 4
                wl = [(wsem, wld[NC - 1])]
                if slot_ev[slot]:
                    wl.append((evac_eng, slot_ev[slot]))
                last = 0
                for ic in range(NC):
                    last = p.op("tensor", lambda e, oc=oc, st=st, ic=ic, slot=slot, wset=wset: e.matmul(
                        pq[slot][:, :], wsb[wset][ic][:, oc * 128:(oc + 1) * 128],
                        B1[ic][:, st * 512:(st + 1) * 512],
                        start=(ic == 0), stop=(ic == NC - 1)),
                        waits=wl if ic == 0 else [],
                        inc=sT if ic == NC - 1 else None)
                slot_ev[slot] = p.op("scalar", lambda e, oc=oc, st=st, slot=slot, OUT=OUT, bias_s=bias_s: e.activation(
                    out=OUT[oc][:, st * 512:(st + 1) * 512], in_=pq[slot][:, :],
                    func=AF.Identity, bias=bias_s[:, oc:oc + 1], scale=1.0),
                    waits=[(sT, last)], inc=sS)
        return slot_ev

    slot_ev = [0, 0, 0, 0]
    slot_ev = proj_qk(0, wq_ld, bqs, QT, slot_ev, sS, sGq)
    for c in range(NC):
        wv_ld[c] = p.op("gpsimd", lambda e, c=c: e.dma_start(
            out=wsb[0][c][:], in_=dw["wv"][c * 128:(c + 1) * 128, :]),
            waits=[(sS, max(slot_ev))], inc=sGq, amt=16)
    slot_ev = proj_qk(1, wk_ld, bks, KT, slot_ev, sS, sGk)

    prev_s_ev = list(slot_ev)
    slot_vv = [0, 0, 0, 0]
    nred = 0
    kev = max(slot_ev)

    def emit_qkmax(i):
        src = QT[i] if i < NC else KT[i - NC]
        dst = qmx if i < NC else kmx
        c = i % NC
        return p.op("vector", lambda e, src=src, dst=dst, c=c: e.reduce_max(
            dst[:, c:c + 1], src[:, :], axis=AXX, apply_absolute_value=True),
            waits=[(sS, kev)] if i == 0 else [], inc=sV)

    for t in range(NT):
        for ofs in range(2):
            g = t * 2 + ofs
            slot = g % 4
            wl = [(sGq, wv_ld[NC - 1]), (sV, bc_ev), (sV, vinit)]
            if prev_s_ev[slot]:
                wl.append((sS, prev_s_ev[slot]))
                prev_s_ev[slot] = 0
            if slot_vv[slot]:
                wl.append((sV, slot_vv[slot]))
            last = 0
            for ic in range(NC):
                last = p.op("tensor", lambda e, t=t, ofs=ofs, ic=ic, slot=slot: e.matmul(
                    pq[slot][:, :], B1[ic][:, t * 128:(t + 1) * 128],
                    wsb[0][ic][:, ofs * 512:(ofs + 1) * 512],
                    start=(ic == 0), stop=(ic == NC - 1)),
                    waits=wl if ic == 0 else [],
                    inc=sT if ic == NC - 1 else None)
            slot_vv[slot] = p.op("vector", lambda e, t=t, ofs=ofs, slot=slot: e.scalar_tensor_tensor(
                out=vaug(t).ap().rearrange("p (h c) -> p h c", c=65)[:, ofs * 8:(ofs + 1) * 8, 0:64],
                in0=pq[slot][:, :], scalar=1.0,
                in1=bvB[:, ofs * 512:(ofs + 1) * 512],
                op0=ALU.mult, op1=ALU.add),
                waits=[(sT, last)], inc=sV)
            if g % 2 == 1 and g // 2 < 2 * NC:
                nred = emit_qkmax(g // 2)

    for idx in range(NPC1, len(piece_srcs)):
        emit_piece(idx)

    n1 = p.op("tensor", lambda e: e.matmul(pms[0:1, 0:64], onescol[:, :], wsum[:, 0:64],
                                           start=True, stop=True),
              waits=[(sV, red_done[-1])], inc=sT)
    n2 = p.op("scalar", lambda e: e.copy(mrow[0:1, 0:64], pms[0:1, 0:64]),
              waits=[(sT, n1)], inc=sS)
    p.op("vector", lambda e: e.reduce_sum(mtot[0:1, 0:1], mrow[0:1, 0:32], axis=AXX),
         waits=[(sS, n2)])
    nr2 = p.op("vector", lambda e: e.reduce_sum(mtot[0:1, 1:2], mrow[0:1, 32:64], axis=AXX))
    n3 = p.op("vector", lambda e: e.tensor_scalar(out=mtot[:], in0=mtot[:],
                                             scalar1=1.0 / (D * DH), scalar2=1e-5,
                                             op0=ALU.mult, op1=ALU.max),
              waits=[(sV, nr2)], inc=sV)
    n4 = p.op("tensor", lambda e: e.matmul(pms[:, 0:2], ones128[:, :], mtot[0:1, :],
                                           start=True, stop=True),
              waits=[(sV, n3)], inc=sT)
    n5 = p.op("scalar", lambda e: e.copy(mwc[:], pms[0:128, 0:2]), waits=[(sT, n4)], inc=sS)
    p.op("vector", lambda e: e.reciprocal(swc[:, 0:1], mwc[:, 0:1]), waits=[(sS, n5)])
    n6 = p.op("vector", lambda e: e.reciprocal(swc[:, 1:2], mwc[:, 1:2]), inc=sV)
    if DBG:
        p.op("vector", lambda e: e.tensor_copy(dbgs[:, 32:34], mwc[:, :]),
             waits=[(sV, n6)])
        p.op("vector", lambda e: e.tensor_copy(dbgs[:, 34:36], swc[:, :]))
        p.op("vector", lambda e: e.tensor_copy(dbgs[0:1, 36:60], mrow[0:1, 24:48]))
        p.op("vector", lambda e: e.tensor_copy(dbgs[0:1, 60:62], mtot[0:1, :]))

    nq = p.op("vector", lambda e: e.tensor_mul(qkp[:], qmx[:], kmx[:]),
              waits=[(sV, nred)], inc=sV)
    nm = p.op("tensor", lambda e: e.matmul(pms[0:2, 0:8], halves[:, :], qkp[:, :],
                                           start=True, stop=True),
              waits=[(sV, nq), (sS, n5)], inc=sT)
    ne = p.op("scalar", lambda e: e.copy(c2x8[:], pms[0:2, 0:8]), waits=[(sT, nm)], inc=sS)
    nts = p.op("vector", lambda e: e.tensor_scalar(out=c2x8[:], in0=c2x8[:], scalar1=0.125,
                                                   scalar2=-40.0, op0=ALU.mult, op1=ALU.add),
               waits=[(sS, ne)])
    nrl = p.op("vector", lambda e: e.tensor_relu(c2x8[:], c2x8[:]), waits=[(sV, nts)])
    nr = p.op("vector", lambda e: e.tensor_scalar(out=c2x8[:], in0=c2x8[:], scalar1=-1.0,
                                                  scalar2=None, op0=ALU.mult),
              waits=[(sV, nrl)], inc=sV)
    nc1 = p.op("vector", lambda e: e.tensor_copy(crow[0:1, 0:8], c2x8[0:1, :]),
               waits=[(sV, nr)], inc=sV)
    nd = p.op("sync", lambda e: e.dma_start(out=crow[0:1, 8:16], in_=c2x8[1:2, :]),
              waits=[(sV, nr)], inc=sD, amt=16)
    nb1 = p.op("tensor", lambda e: e.matmul(pms[:, 0:8], ones128[:, :], crow[0:1, 0:8],
                                            start=True, stop=True),
               waits=[(sV, nc1), (sD, nd), (sS, ne)], inc=sT)
    nb1e = p.op("scalar", lambda e: e.copy(
        negC.ap().rearrange("p (a b) -> p a b", b=2)[:, :, 0:1], pms[0:128, 0:8]),
        waits=[(sT, nb1)], inc=sS)
    nb2 = p.op("tensor", lambda e: e.matmul(pms[:, 0:8], ones128[:, :], crow[0:1, 8:16],
                                            start=True, stop=True),
               waits=[(sS, nb1e)], inc=sT)
    nb2e = p.op("scalar", lambda e: e.copy(
        negC.ap().rearrange("p (a b) -> p a b", b=2)[:, :, 1:2], pms[0:128, 0:8]),
        waits=[(sT, nb2)], inc=sS)
    nec = p.op("scalar", lambda e: e.activation(out=eC[:], in_=negC[:], func=AF.Exp,
                                                bias=0.0, scale=1.0),
               waits=[(sS, nb2e)], inc=sS)


    run_block(nc, p, hwm, "b2_qkv")
    free(*[f"wsb{s}_{c}" for s in range(2) for c in range(NC)],
         "wstage_0", "wstage_1", "wstage_2", "wstage_3",
         "pq_0", "pq_1", "pq_2", "pq_3", "pms")

    B2 = [alloc(f"B2_{t}", [128, D], BF16, side="right") for t in range(NT)]
    expraw = [alloc(f"expraw_{i}", [128, 384], BF16) for i in range(3)]
    expm = [alloc(f"expm_{i}", [128, 384], BF16) for i in range(3)]
    rodc = [alloc(f"rodc_{i}", [128, 1], F32) for i in range(2)]
    psc = [alloc(f"psc_{i}", [128, 512], F32, "psum") for i in range(3)]
    pod = [alloc(f"pod_{i}", [128, 512], F32, "psum") for i in range(4)]

    p = Plan()
    wstage3 = [alloc(f"wstage3_{i}", [128, 1024], F32) for i in range(4)]
    tern_u16 = [alloc(f"tern_u16_{i}", [128, 1024], I16) for i in range(2)]
    tern2b = [alloc(f"tern2b_{i}", [128, 1024], BF16) for i in range(2)]
    tern_jobs = []
    for ch in range(NC):
        for q in range(4):
            tern_jobs.append((dw1t[ch * 128:(ch + 1) * 128, q * 1024:(q + 1) * 1024],
                              [(dw1b[q * 2 + half, :, ch * 512:(ch + 1) * 512],
                                half * 512, (half + 1) * 512) for half in range(2)], 0))
    for ch in range(HCH):
        tern_jobs.append((dw2t[ch * 128:(ch + 1) * 128, :],
                          [(dw2b[:, ch * 1024:(ch + 1) * 1024], 0, 1024)], 1))
    tstate = {"sp": [0, 0, 0, 0]}

    def tern_steps():
        for i0 in range(0, len(tern_jobs), 4):
            picks = [k for k in range(i0, min(i0 + 4, len(tern_jobs)))]
            lds, n1s, n2s = {}, {}, {}
            for k in picks:
                srcA, dst, wi = tern_jobs[k]
                slot = k % 4
                wl = [(sV, n6)]
                if tstate["sp"][slot]:
                    wl.append((sSp[slot], tstate["sp"][slot]))
                lds[k] = p.op("sync", lambda e, srcA=srcA, slot=slot: e.dma_start(
                    out=wstage3[slot][:, :], in_=srcA), waits=wl, inc=sW[slot], amt=16)
                yield
            for j0 in range(0, len(picks), 2):
                pair = picks[j0:j0 + 2]
                for k in pair:
                    wi = tern_jobs[k][2]
                    slot = k % 4
                    wl = [(sW[slot], lds[k])]
                    if tstate.get("u16", [0, 0])[k % 2]:
                        wl.append((sV, tstate["u16"][k % 2]))
                    n1s[k] = p.op("vector", lambda e, slot=slot, wi=wi, k=k: e.tensor_scalar(
                        out=tern_u16[k % 2][:, :], in0=wstage3[slot][:, :],
                        scalar1=swc[:, wi:wi + 1], scalar2=RC, op0=ALU.mult, op1=ALU.add),
                        waits=wl)
                    yield
                for k in pair:
                    slot = k % 4
                    n2s[k] = p.op("vector", lambda e, slot=slot, k=k: e.tensor_scalar(
                        out=wstage3[slot][:, :], in0=tern_u16[k % 2][:, :],
                        scalar1=RD, scalar2=1.0, op0=ALU.subtract, op1=ALU.min),
                        waits=[(sV, n1s[k])])
                    tstate.setdefault("u16", [0, 0])[k % 2] = n2s[k]
                    yield
                for k in pair:
                    slot = k % 4
                    wl = [(sV, n2s[k])]
                    if tstate.get("t2b", [0, 0])[k % 2]:
                        wl.append((sSp[(k - 2) % 4], tstate["t2b"][k % 2]))
                    n2s[k] = p.op("vector", lambda e, slot=slot, k=k: e.tensor_scalar(
                        out=tern2b[k % 2][:, :], in0=wstage3[slot][:, :],
                        scalar1=-1.0, scalar2=None, op0=ALU.max),
                        waits=wl, inc=sV)
                    yield
                for k in pair:
                    slot = k % 4
                    for pi, (dst, c0, c1) in enumerate(tern_jobs[k][1]):
                        tstate["sp"][slot] = p.op("sync", lambda e, dst=dst, c0=c0, c1=c1, slot=slot, k=k: e.dma_start(
                            out=dst, in_=tern2b[k % 2][:, c0:c1]),
                            waits=[(sV, n2s[k])] if pi == 0 else [],
                            inc=sSp[slot], amt=16)
                    tstate.setdefault("t2b", [0, 0])[k % 2] = tstate["sp"][slot]
                    yield

    tern_gen = tern_steps()

    def win(j):
        q0 = (j - 1) * 128
        q1 = (j + 2) * 128
        c0 = max(0, -q0)
        q0r = max(q0, 0)
        q1r = min(q1, T)
        return q0r, q1r, c0, c0 + (q1r - q0r)

    pod_ev = [0, 0, 0, 0]
    psc_ev = [0, 0, 0]
    expm_use = [0, 0, 0]
    mask_done = [0, 0, 0]
    rodc_use = [0, 0]
    mask_info = {}

    def emit_av(h, j):
        sl = (j + 1) % 3
        msem, nmask = mask_info.pop((h, j))
        last_use = 0
        for s in range(3):
            qb = j - 1 + s
            if qb < 0 or qb >= NJ:
                continue
            wc = qb * 128 - (j - 1) * 128
            first = (j == qb - 1)
            lastc = (j == qb + 1)
            wl = [(msem, nmask)]
            if first and pod_ev[qb % 4]:
                wl.append((sV, pod_ev[qb % 4]))
            last_use = p.op("tensor", lambda e, qb=qb, j=j, h=h, sl=sl, wc=wc, first=first, lastc=lastc: e.matmul(
                pod[qb % 4][:, 0:65], expm[sl][:, wc:wc + 128],
                vaug(j)[:, h * 65:h * 65 + 65], start=first, stop=lastc),
                waits=wl, inc=sT)
            if lastc:
                wl = [(sT, last_use)]
                if rodc_use[qb % 2]:
                    wl.append((sV, rodc_use[qb % 2]))
                nrc = p.op("vector", lambda e, qb=qb: e.reciprocal(
                    rodc[qb % 2][:, :], pod[qb % 4][:, 64:65]), waits=wl, inc=sV)
                pod_ev[qb % 4] = p.op("vector", lambda e, qb=qb, h=h: e.tensor_scalar(
                    out=B2[qb][:, h * 64:(h + 1) * 64], in0=pod[qb % 4][:, 0:64],
                    scalar1=rodc[qb % 2][:, :], scalar2=None, op0=ALU.mult),
                    waits=[(sV, nrc)], inc=sV)
                rodc_use[qb % 2] = pod_ev[qb % 4]
        if last_use:
            expm_use[sl] = last_use

    seq = [(h, j) for h in range(H) for j in range(-1, NJ + 1)]
    for idx, (h, j) in enumerate(seq):
        hc = h // 2
        hr = (h % 2) * 64
        sl = (j + 1) % 3
        q0r, q1r, c0, c1 = win(j)
        pad = (j < 0) or (j >= NJ)
        if not pad:
            wl = [(sS, nec)]
            if psc_ev[sl]:
                wl.append((sS, psc_ev[sl]))
            p.op("tensor", lambda e, sl=sl, c0=c0, c1=c1: e.matmul(
                psc[sl][:, c0:c1], iden[:], maskB[:, c0:c1], start=True, stop=False),
                waits=wl)
            nmm = p.op("tensor", lambda e, hc=hc, hr=hr, j=j, sl=sl, q0r=q0r, q1r=q1r, c0=c0, c1=c1: e.matmul(
                psc[sl][:, c0:c1], KT[hc][hr:hr + 64, j * 128:(j + 1) * 128],
                QT[hc][hr:hr + 64, q0r:q1r], start=False, stop=True),
                inc=sT)
            wl = [(sT, nmm)]
            if expm_use[sl]:
                wl.append((sT, expm_use[sl]))
            psc_ev[sl] = p.op("scalar", lambda e, h=h, sl=sl, c0=c0, c1=c1: e.activation(
                out=expm[sl][:, c0:c1], in_=psc[sl][:, c0:c1], func=AF.Exp,
                bias=negC[:, h:h + 1], scale=0.125), waits=wl, inc=sS)
            mask_info[(h, j)] = (sS, psc_ev[sl])
        else:
            wl = [(sS, nec)]
            if expm_use[sl]:
                wl.append((sT, expm_use[sl]))
            nmask = p.op("vector", lambda e, h=h, sl=sl, c0=c0, c1=c1: e.tensor_scalar(
                out=expm[sl][:, c0:c1], in0=mask01[:, c0:c1],
                scalar1=eC[:, h:h + 1], scalar2=None, op0=ALU.mult),
                waits=wl, inc=sV)
            mask_info[(h, j)] = (sV, nmask)
        if idx >= 2:
            emit_av(*seq[idx - 2])
        for _ in range(2):
            next(tern_gen, None)
    emit_av(*seq[-2])
    emit_av(*seq[-1])
    for _ in tern_gen:
        pass
    p.op("sync", lambda e: e.nop(), waits=[(sD, sD.n)] +
         [(s_, s_.n) for s_ in sW] + [(s_, s_.n) for s_ in sSp])
    run_block(nc, p, hwm, "b3_attn")
    free(*[f"QT{c}" for c in range(NC)], *[f"KT{c}" for c in range(NC)],
         *[f"VA{j}" for j in range(NJ + 2)],
         *[f"wstage3_{i}" for i in range(4)], "tern_u16_0", "tern_u16_1",
         "tern2b_0", "tern2b_1",
         "expraw_0", "expraw_1", "expraw_2", "expm_0", "expm_1", "expm_2",
         "rodc_0", "rodc_1",
         "psc_0", "psc_1", "psc_2", "pod_0", "pod_1", "pod_2", "pod_3")

    wosb = [alloc(f"wo_{c}", [128, D], BF16) for c in range(NC)]
    attn = [alloc(f"attn_{i}", [128, D], F32) for i in range(2)]
    x2f = [alloc(f"x2f_{i}", [128, D], F32) for i in range(2)]
    x3f = [alloc(f"x3f_{i}", [128, D], F32) for i in range(2)]
    xre = [alloc(f"xre_{i}", [128, D], F32) for i in range(2)]
    q16 = alloc("q16", [128, D], I16)
    x3q = [alloc(f"x3q_{i}", [128, D], BF16) for i in range(2)]
    x3st = [alloc(f"x3st_{i}", [128, D], BF16) for i in range(2)]
    sxc = alloc("sxc", [128, 1], F32)
    sxic = alloc("sxic", [128, 1], F32)
    ptrO = [alloc(f"ptrO_{i}", [128, 1024], BF16, "psum") for i in range(2)]
    ptrX = [alloc(f"ptrX_{i}", [128, 1024], BF16, "psum") for i in range(2)]
    po4 = [alloc(f"po4_{i}", [128, 512], F32, "psum") for i in range(2)]

    p = Plan()
    if DBG:
        for t in range(NT):
            p.op("sync", lambda e, t=t: e.dma_start(
                out=dbgO[t * 128:(t + 1) * 128, :], in_=B2[t][:, :]), inc=sD, amt=16)
    wo_ld = [0] * NC
    for c in range(NC):
        wo_ld[c] = p.op("gpsimd", lambda e, c=c: e.dma_start(
            out=wosb[c][:], in_=dw["wo"][c * 128:(c + 1) * 128, :]), inc=sGq, amt=16)
    otr = [0] * NT
    oev = [0] * NT
    po_ev = [0, 0]
    x2sp = [0, 0]
    x3sp = [0, 0]
    apply2 = [0] * NT
    apply3 = [0] * NT
    q_done = [0] * NT
    x3tr = [0] * NT
    x3ev = [0] * NT
    for t in range(NT):
        for c in range(NC):
            wl = []
            if c == 0 and t >= 2:
                wl.append((sS, oev[t - 2]))
            otr[t] = p.op("tensor", lambda e, t=t, c=c: e.transpose(
                ptrO[t % 2][:, c * 128:(c + 1) * 128],
                B2[t][:, c * 128:(c + 1) * 128], iden[:]),
                waits=wl, inc=sT if c == NC - 1 else None)
        oev[t] = p.op("scalar", lambda e, t=t: e.copy(
            B1r3[:, :, t * 128:(t + 1) * 128],
            ptrO[t % 2].ap().rearrange("p (c q) -> p c q", q=128)[:, :, :]),
            waits=[(sT, otr[t])], inc=sS)
        xre_ld = p.op("sync", lambda e, t=t: e.dma_start(
            out=xre[t % 2][:], in_=dx[t * 128:(t + 1) * 128, :]),
            waits=[(sV, apply2[t - 2])] if t >= 2 else [], inc=sX[t % 2], amt=16)
        for ofs in range(2):
            slot = ofs
            wl = [(sS, oev[t]), (sGq, wo_ld[NC - 1])]
            if po_ev[slot]:
                wl.append((sV, po_ev[slot]))
            last = 0
            for c in range(NC):
                last = p.op("tensor", lambda e, t=t, ofs=ofs, c=c, slot=slot: e.matmul(
                    po4[slot][:, :], B1[c][:, t * 128:(t + 1) * 128],
                    wosb[c][:, ofs * 512:(ofs + 1) * 512],
                    start=(c == 0), stop=(c == NC - 1)),
                    waits=wl if c == 0 else [],
                    inc=sT if c == NC - 1 else None)
            wl2 = [(sT, last)]
            if t >= 2:
                wl2.append((sV, apply2[t - 2]))
            po_ev[slot] = p.op("vector", lambda e, t=t, ofs=ofs, slot=slot: e.scalar_tensor_tensor(
                out=attn[t % 2][:, ofs * 512:(ofs + 1) * 512], in0=po4[slot][:, :],
                scalar=1.0, in1=boB[:, ofs * 512:(ofs + 1) * 512],
                op0=ALU.mult, op1=ALU.add), waits=wl2, inc=sV)
        if t >= 2:
            emit_x3t(t - 2)
        p.op("vector", lambda e, t=t: e.bn_stats(stat[:, 0:6], attn[t % 2][:, 0:512]),
             waits=[(sV, po_ev[1])])
        nsb = p.op("vector", lambda e, t=t: e.bn_stats(stat[:, 6:12], attn[t % 2][:, 512:1024]))
        nag = p.op("vector", lambda e: e.bn_aggr(aggr[:], stat[:]), waits=[(sV, nsb)], inc=sV)
        nsq = p.op("scalar", lambda e: e.activation(out=scol[:], in_=aggr[:, 1:2],
                                                    func=AF.Sqrt, bias=epsc[:], scale=1.0),
                   waits=[(sV, nag)], inc=sS)
        nrcp = p.op("vector", lambda e: e.reciprocal(rcol[:], scol[:]), waits=[(sS, nsq)])
        if DBG and t == 0:
            p.op("vector", lambda e: e.tensor_copy(dbgs[:, 64:66], aggr[:, :]))
            p.op("vector", lambda e: e.tensor_copy(dbgs[:, 66:67], scol[:, :]))
            p.op("vector", lambda e: e.tensor_copy(dbgs[:, 67:68], rcol[:, :]),
                 waits=[(sV, nrcp)])
            p.op("vector", lambda e: e.tensor_copy(dbgs[:, 72:76], attn[0][:, 0:4]))
        napl2 = p.op("vector", lambda e, t=t: e.tensor_scalar(
            out=attn[t % 2][:], in0=attn[t % 2][:], scalar1=aggr[:, 0:1], scalar2=rcol[:],
            op0=ALU.subtract, op1=ALU.mult), waits=[(sV, nrcp)])
        wl = [(sX[t % 2], xre_ld), (sV, napl2)]
        if x2sp[t % 2]:
            wl.append((sGs[t % 2], x2sp[t % 2]))
        apply2[t] = p.op("vector", lambda e, t=t: e.tensor_add(
            x2f[t % 2][:], attn[t % 2][:], xre[t % 2][:]), waits=wl, inc=sV)
        x2sp[t % 2] = p.op("gpsimd", lambda e, t=t: e.dma_start(
            out=dx2[t * 128:(t + 1) * 128, :], in_=x2f[t % 2][:, :]),
            waits=[(sV, apply2[t])], inc=sGs[t % 2], amt=16)
        p.op("vector", lambda e, t=t: e.bn_stats(stat[:, 0:6], x2f[t % 2][:, 0:512]),
             waits=[(sV, apply2[t])])
        nsb = p.op("vector", lambda e, t=t: e.bn_stats(stat[:, 6:12], x2f[t % 2][:, 512:1024]))
        nag = p.op("vector", lambda e: e.bn_aggr(aggr[:], stat[:]), waits=[(sV, nsb)], inc=sV)
        nsq = p.op("scalar", lambda e: e.activation(out=scol[:], in_=aggr[:, 1:2],
                                                    func=AF.Sqrt, bias=epsc[:], scale=1.0),
                   waits=[(sV, nag)], inc=sS)
        nrcp2 = p.op("vector", lambda e: e.reciprocal(rcol[:], scol[:]), waits=[(sS, nsq)])
        if DBG and t == 0:
            p.op("vector", lambda e: e.tensor_copy(dbgs[:, 68:70], aggr[:, :]))
            p.op("vector", lambda e: e.tensor_copy(dbgs[:, 70:71], scol[:, :]))
            p.op("vector", lambda e: e.tensor_copy(dbgs[:, 71:72], rcol[:, :]),
                 waits=[(sV, nrcp2)])
        wl = [(sV, nrcp2)]
        if t >= 2:
            wl.append((sV, q_done[t - 2]))
        apply3[t] = p.op("vector", lambda e, t=t: e.tensor_scalar(
            out=x3f[t % 2][:], in0=x2f[t % 2][:], scalar1=aggr[:, 0:1], scalar2=rcol[:],
            op0=ALU.subtract, op1=ALU.mult), waits=wl, inc=sV)
        wl = [(sV, apply3[t])]
        if t >= 1:
            wl.append((sV, q_done[t - 1]))
        nred4 = p.op("vector", lambda e, t=t: e.reduce_max(
            sxc[:, :], x3f[t % 2][:, :], axis=AXX, apply_absolute_value=True), waits=wl)
        nclp = p.op("vector", lambda e: e.tensor_scalar(out=sxc[:], in0=sxc[:], scalar1=1e-5,
                                                        scalar2=None, op0=ALU.max),
                    waits=[(sV, nred4)])
        nrs = p.op("vector", lambda e: e.reciprocal(sxic[:], sxc[:]), waits=[(sV, nclp)])
        nm127 = p.op("vector", lambda e: e.tensor_scalar(out=sxic[:], in0=sxic[:],
                                                         scalar1=127.0, scalar2=None,
                                                         op0=ALU.mult), waits=[(sV, nrs)])
        nq16t = p.op("vector", lambda e, t=t: e.tensor_scalar(
            out=q16[:, :], in0=x3f[t % 2][:], scalar1=sxic[:], scalar2=RC,
            op0=ALU.mult, op1=ALU.add), waits=[(sV, nm127)])
        nsxf = p.op("vector", lambda e: e.tensor_scalar(out=sxc[:], in0=sxc[:],
                                                        scalar1=1.0 / 127.0,
                                                        scalar2=None, op0=ALU.mult))
        if DBG:
            p.op("vector", lambda e, t=t: e.tensor_copy(dbgs[:, t:t + 1], sxc[:, :]),
                 waits=[(sV, nsxf)])
            p.op("vector", lambda e, t=t: e.tensor_copy(dbgs[:, 16 + t:17 + t], sxic[:, :]))
        wl = [(sV, nq16t), (sV, nsxf)]
        if t >= 2:
            wl.append((sS, x3ev[t - 2]))
        q_done[t] = p.op("vector", lambda e, t=t: e.tensor_scalar(
            out=x3q[t % 2][:], in0=q16[:, :], scalar1=RD, scalar2=sxc[:],
            op0=ALU.subtract, op1=ALU.mult), waits=wl, inc=sV)

        def emit_x3t(tp):
            for c in range(NC):
                wl = []
                if c == 0:
                    wl.append((sV, q_done[tp]))
                    if tp >= 2:
                        wl.append((sS, x3ev[tp - 2]))
                x3tr[tp] = p.op("tensor", lambda e, tp=tp, c=c: e.transpose(
                    ptrX[tp % 2][:, c * 128:(c + 1) * 128],
                    x3q[tp % 2][:, c * 128:(c + 1) * 128], iden[:]),
                    waits=wl, inc=sT if c == NC - 1 else None)
            wl = [(sT, x3tr[tp])]
            if x3sp[tp % 2]:
                wl.append((sSp[tp % 2], x3sp[tp % 2]))
            x3ev[tp] = p.op("scalar", lambda e, tp=tp: e.copy(
                x3st[tp % 2][:, :], ptrX[tp % 2][:, :]), waits=wl, inc=sS)
            x3sp[tp % 2] = p.op("sync", lambda e, tp=tp: e.dma_start(
                out=dx3t[:, :, 1 + tp * 128:1 + (tp + 1) * 128].transpose([1, 0, 2]),
                in_=x3st[tp % 2][:, :]), waits=[(sS, x3ev[tp])],
                inc=sSp[tp % 2], amt=16)

        if t == NT - 1:
            emit_x3t(t - 1)
            emit_x3t(t)
    p.op("sync", lambda e: e.nop(), waits=[(sSp[0], sSp[0].n), (sSp[1], sSp[1].n),
                                           (sX[0], sX[0].n), (sX[1], sX[1].n)])
    p.op("gpsimd", lambda e: e.nop(), waits=[(sGs[0], sGs[0].n), (sGs[1], sGs[1].n),
                                             (sGq, sGq.n)])
    run_block(nc, p, hwm, "b4_oproj")
    free(*[f"wo_{c}" for c in range(NC)], "B1all",
         *[f"B2_{t}" for t in range(NT)],
         "bvB", "boB",
         "attn_0", "attn_1", "x2f_0", "x2f_1", "x3f_0", "x3f_1", "xre_0", "xre_1",
         "q16", "x3q_0", "x3q_1", "x3st_0", "x3st_1", "sxc", "sxic",
         "ptrO_0", "ptrO_1", "ptrX_0", "ptrX_1", "po4_0", "po4_1")

    diag = alloc("diag", [128, 3 * HCH * 128], BF16)
    w1h = [alloc(f"w1h_{i}", [128, 4 * 1024], BF16) for i in range(2)]
    w2b = [alloc(f"w2b_{i}", [128, 4 * 1024], BF16) for i in range(2)]
    x3w = [alloc(f"x3w_{i}", [128, 8 * 256], BF16) for i in range(3)]
    Ab = [alloc(f"Ab_{i}", [128, HCH * 258], BF16) for i in range(2)]
    H2 = alloc("H2", [128, HCH * 256], F32)
    H2q = [alloc(f"H2q_{i}", [128, HCH * 256], BF16) for i in range(2)]
    tland = [alloc(f"tland_{i}", [128, 256], F32) for i in range(2)]
    q16b = [alloc(f"q16b_{i}", [128, 256], I16) for i in range(2)]
    apar = alloc("apar", [128, 4 * 256], F32)
    amaxC = alloc("amaxC", [128, 256], F32)
    v2 = alloc("v2", [128, 4], F32)
    sxr = alloc("sxr", [1, 256], F32)
    sxB = alloc("sxB", [128, 256], F32)
    sc2b = [alloc(f"sc2b_{i}", [128, 2], F32) for i in range(2)]
    out2 = [alloc(f"out2_{i}", [128, D], F32) for i in range(2)]
    x4b = [alloc(f"x4b_{i}", [128, D], F32) for i in range(2)]
    x2re = [alloc(f"x2re_{i}", [128, D], BF16) for i in range(2)]
    pA = [alloc(f"pA_{i}", [128, 512], F32, "psum") for i in range(2)]
    pC = alloc("pC", [128, 512], F32, "psum")
    pM = [alloc(f"pM_{i}", [128, 512], F32, "psum") for i in range(4)]
    pX = alloc("pX", [128, 512], F32, "psum")

    p = Plan()
    dg_ld = p.op("gpsimd", lambda e: e.dma_start(
        out=diag[:, :], in_=ddiag[:, :]), inc=sG, amt=16)

    def dg(k, c):
        return diag[:, (k * HCH + c) * 128:(k * HCH + c) * 128 + 128]

    def ab(s, c):
        return Ab[s % 2][:, c * 258:(c + 1) * 258]

    x3_ld = [0] * (NS + 2)
    a_mm = [[0] * HCH for _ in range(NS + 2)]
    a_ev = [[0] * HCH for _ in range(NS + 2)]
    halo_d = [[0] * HCH for _ in range(NS + 2)]
    conv_mm = [[0] * HCH for _ in range(NS)]
    silu_d = [[0] * HCH for _ in range(NS)]
    ntp_d = [[0] * HCH for _ in range(NS)]
    h2q_d = [[0] * HCH for _ in range(NS)]
    apar_d = [[0] * 4 for _ in range(NS)]
    mm2_last = [0] * NS
    w1_ld = {}
    w1_used = {}
    pa_ev = [0, 0]
    pc_ev = [0]
    pm_ev = [0, 0, 0, 0]
    w2_used = [0, 0]
    x2re_used = [0, 0]
    out_dma = [0, 0]
    vcomb = [0]
    sxB_ev = [0]
    sc2_d = [0, 0]
    trans_last = [0]
    ktrans_last = [0]

    def emit_x3ld(s2):
        par = s2 % 3
        if s2 == NS:
            src = dx3t[:, :, s2 * TS:s2 * TS + 2].transpose([1, 0, 2])
            dst = x3w[par].ap().rearrange("p (c n) -> p c n", n=256)[:, :, 0:2]
        else:
            src = dx3t[:, :, s2 * TS:s2 * TS + 256].transpose([1, 0, 2])
            dst = x3w[par][:, :]
        wl = [(sT, a_mm[s2 - 3][HCH - 1])] if s2 >= 3 else []
        if s2 >= 2:
            wl.append((sX[s2 % 2], x3_ld[s2 - 2]))
        x3_ld[s2] = p.op("sync", lambda e, src=src, dst=dst: e.dma_start(out=dst, in_=src),
                         waits=wl, inc=sX[s2 % 2], amt=16)

    def emit_w1ld(s2, q):
        j = s2 * 8 + q
        wl = [(sT, w1_used[j - 2])] if (j - 2) in w1_used else []
        src = dw1b[q, :, :]
        w1_ld[j] = p.op("gpsimd", lambda e, src=src, j=j: e.dma_start(
            out=w1h[j % 2][:, :], in_=src), waits=wl, inc=sGw[j % 2], amt=16)

    def emit_A_chunk(s2, hcx):
        par = s2 % 3
        width = 2 if s2 == NS else 256
        slotp = hcx % 2
        j = s2 * 8 + hcx // 4
        hs2 = hcx % 4
        wl = [(sX[s2 % 2], x3_ld[s2]), (sGw[j % 2], w1_ld[j])]
        if pa_ev[slotp]:
            wl.append((sS, pa_ev[slotp]))
        last = 0
        for d in range(NC):
            if s2 == NS:
                mv = x3w[par].ap().rearrange("p (c n) -> p c n", n=256)[:, d, 0:2]
            else:
                mv = x3w[par][:, d * 256:(d + 1) * 256]
            last = p.op("tensor", lambda e, j=j, hs2=hs2, d=d, slotp=slotp, mv=mv, width=width: e.matmul(
                pA[slotp][:, 0:width],
                w1h[j % 2][:, d * 512 + hs2 * 128:d * 512 + hs2 * 128 + 128],
                mv, start=(d == 0), stop=(d == NC - 1)),
                waits=wl if d == 0 else [],
                inc=sT if d == NC - 1 else None)
        a_mm[s2][hcx] = last
        w1_used[j] = last
        wl = [(sT, last)]
        if s2 >= 2:
            wl.append((sT, conv_mm[s2 - 2][hcx]))
        pa_ev[slotp] = p.op("scalar", lambda e, s2=s2, hcx=hcx, slotp=slotp, width=width: e.activation(
            out=ab(s2, hcx)[:, 0:width], in_=pA[slotp][:, 0:width],
            func=AF.Identity, bias=0.0, scale=mwc[:, 0:1]), waits=wl, inc=sS)
        a_ev[s2][hcx] = pa_ev[slotp]
        if s2 >= 1:
            halo_d[s2 - 1][hcx] = p.op("scalar", lambda e, s2=s2, hcx=hcx: e.copy(
                ab(s2 - 1, hcx)[:, 256:258], ab(s2, hcx)[:, 0:2]),
                waits=[(sS, a_ev[s2][hcx])], inc=sS)

    def emit_conv_chunk(s, hcx):
        wl = [(sS, halo_d[s][hcx]), (sG, dg_ld)]
        if pc_ev[0]:
            wl.append((sS, pc_ev[0]))
        if s >= 1:
            wl.append((sV, ntp_d[s - 1][hcx]))
            wl.append((sV, apar_d[s - 1][hcx // 8]))
        last = 0
        for k in range(3):
            last = p.op("tensor", lambda e, s=s, hcx=hcx, k=k: e.matmul(
                pC[:, 0:256], dg(k, hcx), ab(s, hcx)[:, k:k + 256],
                start=(k == 0), stop=(k == 2)),
                waits=wl if k == 0 else [],
                inc=sT if k == 2 else None)
        conv_mm[s][hcx] = last
        silu_d[s][hcx] = p.op("scalar", lambda e, s=s, hcx=hcx: e.activation(
            out=H2[:, hcx * 256:(hcx + 1) * 256], in_=pC[:, 0:256],
            func=AF.Silu, bias=cbs[:, hcx:hcx + 1], scale=1.0),
            waits=[(sT, last)], inc=sS)
        pc_ev[0] = silu_d[s][hcx]

    def emit_mm2(sm):
        par = sm % 2
        x2_ld = [0, 0]
        for tsl in range(2):
            t = sm * 2 + tsl
            wl = [(sV, x2re_used[tsl])] if x2re_used[tsl] else []
            x2_ld[tsl] = p.op("sync", lambda e, t=t, tsl=tsl: e.dma_start(
                out=x2re[tsl][:, :], in_=dx2[t * 128:(t + 1) * 128, :]),
                waits=wl, inc=sX2[tsl], amt=16)
        last = 0
        for cc in range(0, HCH, 4):
            b = (cc // 4) % 2
            wl = [(sT, w2_used[b])] if w2_used[b] else []
            w2_ld = p.op("sync", lambda e, cc=cc, b=b: e.dma_start(
                out=w2b[b][:, :], in_=dw2b[:, cc * 1024:(cc + 4) * 1024]),
                waits=wl, inc=sW2[b], amt=16)
            for hcx in range(cc, cc + 4):
                qq = hcx % 4
                for tsl in range(2):
                    for ofs in range(2):
                        slot = tsl * 2 + ofs
                        wl = []
                        if hcx == cc and tsl == 0 and ofs == 0:
                            wl.append((sW2[b], w2_ld))
                            wl.append((sV, h2q_d[sm][cc + 3]))
                        if hcx == 0 and pm_ev[slot]:
                            wl.append((sV, pm_ev[slot]))
                        fin = (hcx == cc + 3 and tsl == 1 and ofs == 1)
                        last = p.op("tensor", lambda e, sm=sm, hcx=hcx, tsl=tsl, ofs=ofs, b=b, qq=qq, slot=slot: e.matmul(
                            pM[slot][:, :],
                            H2q[sm % 2][:, hcx * 256 + tsl * 128:hcx * 256 + tsl * 128 + 128],
                            w2b[b][:, qq * 1024 + ofs * 512:qq * 1024 + ofs * 512 + 512],
                            start=(hcx == 0), stop=(hcx == HCH - 1)),
                            waits=wl, inc=sT if fin else None)
            w2_used[b] = last
        mm2_last[sm] = last
        for tsl in range(2):
            for ofs in range(2):
                slot = tsl * 2 + ofs
                wl = [(sT, mm2_last[sm]), (sV, sc2_d[par])]
                if x2re_used[tsl]:
                    wl.append((sV, x2re_used[tsl]))
                pm_ev[slot] = p.op("vector", lambda e, tsl=tsl, ofs=ofs, slot=slot, par=par: e.tensor_scalar(
                    out=out2[tsl][:, ofs * 512:(ofs + 1) * 512], in0=pM[slot][:, :],
                    scalar1=sc2b[par][:, tsl:tsl + 1], scalar2=None, op0=ALU.mult),
                    waits=wl, inc=sV)
            p.op("vector", lambda e, tsl=tsl: e.bn_stats(stat[:, 0:6], out2[tsl][:, 0:512]),
                 waits=[(sV, pm_ev[tsl * 2 + 1])])
            nsb = p.op("vector", lambda e, tsl=tsl: e.bn_stats(stat[:, 6:12],
                                                               out2[tsl][:, 512:1024]))
            nag = p.op("vector", lambda e: e.bn_aggr(aggr[:], stat[:]),
                       waits=[(sV, nsb)], inc=sV)
            nsq = p.op("scalar", lambda e: e.activation(out=scol[:], in_=aggr[:, 1:2],
                                                        func=AF.Sqrt, bias=epsc[:], scale=1.0),
                       waits=[(sV, nag)], inc=sS)
            nrcp5 = p.op("vector", lambda e: e.reciprocal(rcol[:], scol[:]),
                         waits=[(sS, nsq)])
            napl4 = p.op("vector", lambda e, tsl=tsl: e.tensor_scalar(
                out=out2[tsl][:], in0=out2[tsl][:], scalar1=aggr[:, 0:1], scalar2=rcol[:],
                op0=ALU.subtract, op1=ALU.mult), waits=[(sV, nrcp5)])
            wl = [(sX2[tsl], x2_ld[tsl]), (sV, napl4)]
            if out_dma[tsl]:
                wl.append((sOut[tsl], out_dma[tsl]))
            x2re_used[tsl] = p.op("vector", lambda e, tsl=tsl: e.tensor_add(
                x4b[tsl][:], out2[tsl][:], x2re[tsl][:]), waits=wl, inc=sV)
            t = sm * 2 + tsl
            out_dma[tsl] = p.op("sync", lambda e, t=t, tsl=tsl: e.dma_start(
                out=dout[t * 128:(t + 1) * 128, :], in_=x4b[tsl][:, :]),
                waits=[(sV, x2re_used[tsl])], inc=sOut[tsl], amt=16)

    emit_x3ld(0)
    emit_x3ld(1)
    emit_x3ld(2)
    for s2 in (0, 1):
        for hcx in range(HCH):
            if hcx % 4 == 0:
                emit_w1ld(s2, hcx // 4)
            emit_A_chunk(s2, hcx)

    for s in range(NS):
        s2 = s + 2
        if s + 3 <= NS:
            emit_x3ld(s + 3)
        for hcx in range(HCH):
            emit_conv_chunk(s, hcx)
            if s2 <= NS:
                if hcx % 4 == 0:
                    emit_w1ld(s2, hcx // 4)
                emit_A_chunk(s2, hcx)
            if hcx % 8 == 7:
                g3 = hcx // 8
                wl = [(sS, silu_d[s][hcx])]
                if g3 == 0 and vcomb[0]:
                    wl.append((sV, vcomb[0]))
                apar_d[s][g3] = p.op("vector", lambda e, g3=g3: e.reduce_max(
                    apar[:, g3 * 256:(g3 + 1) * 256],
                    H2.ap().rearrange("p (c n) -> p n c", c=HCH)[:, :, g3 * 8:(g3 + 1) * 8],
                    axis=AXX, apply_absolute_value=True), waits=wl, inc=sV)
        cA = p.op("vector", lambda e: e.tensor_tensor(
            amaxC[:, :], apar[:, 0:256], apar[:, 256:512], ALU.max),
            waits=[(sV, apar_d[s][1]), (sT, trans_last[0])], inc=sV)
        cB = p.op("vector", lambda e: e.tensor_tensor(
            apar[:, 768:1024], apar[:, 512:768], apar[:, 768:1024], ALU.max),
            waits=[(sV, apar_d[s][3])], inc=sV)
        cF = p.op("vector", lambda e: e.tensor_tensor(
            amaxC[:, :], amaxC[:, :], apar[:, 768:1024], ALU.max),
            waits=[(sV, cA), (sV, cB)], inc=sV)
        vcomb[0] = cF
        red = 0
        for tt in range(2):
            wl = [(sV, cF), (sV, sxB_ev[0])] if tt == 0 else [(sV, red)]
            ntr = p.op("tensor", lambda e, tt=tt: e.transpose(
                pX[:, 0:128], amaxC[:, tt * 128:(tt + 1) * 128], idenf[:]),
                waits=wl, inc=sT)
            red = p.op("vector", lambda e, tt=tt: e.reduce_max(
                v2[:, tt:tt + 1], pX[0:128, 0:128], axis=AXX),
                waits=[(sT, ntr)], inc=sV)
            trans_last[0] = ntr
        ncl5 = p.op("vector", lambda e: e.tensor_scalar(out=v2[:, 0:2], in0=v2[:, 0:2],
                                                        scalar1=1e-5, scalar2=None,
                                                        op0=ALU.max),
                    waits=[(sV, red), (sT, ktrans_last[0])], inc=sV)
        p.op("vector", lambda e: e.reciprocal(v2[:, 2:3], v2[:, 0:1]), waits=[(sV, ncl5)])
        nrB = p.op("vector", lambda e: e.reciprocal(v2[:, 3:4], v2[:, 1:2]))
        nm7 = p.op("vector", lambda e: e.tensor_scalar(out=v2[:, 2:4], in0=v2[:, 2:4],
                                                       scalar1=127.0, scalar2=None,
                                                       op0=ALU.mult),
                   waits=[(sV, nrB)], inc=sV)
        nv2 = p.op("vector", lambda e: e.tensor_scalar(out=v2[:, 0:2], in0=v2[:, 0:2],
                                                       scalar1=1.0 / 127.0, scalar2=None,
                                                       op0=ALU.mult), inc=sV)
        for tsl in range(2):
            sc2_d[s % 2] = p.op("vector", lambda e, tsl=tsl, s=s: e.tensor_tensor(
                sc2b[s % 2][:, tsl:tsl + 1], v2[:, tsl:tsl + 1], mwc[:, 1:2], ALU.mult),
                waits=[(sV, nv2)], inc=sV)
        prev_ev = 0
        for k in range(2):
            wl = [(sV, nm7)]
            if prev_ev:
                wl.append((sV, prev_ev))
            ntr = p.op("tensor", lambda e, k=k: e.transpose(
                pX[0:1, 128:256], v2[:, 2 + k:3 + k], idenf[:]), waits=wl, inc=sT)
            prev_ev = p.op("vector", lambda e, k=k: e.tensor_copy(
                sxr[0:1, k * 128:(k + 1) * 128], pX[0:1, 128:256]),
                waits=[(sT, ntr)], inc=sV)
            ktrans_last[0] = ntr
        nb = p.op("tensor", lambda e: e.matmul(pX[:, 256:512], ones128[:, :],
                                               sxr[0:1, :], start=True, stop=True),
                  waits=[(sV, prev_ev)], inc=sT)
        wl = [(sT, nb)]
        if s >= 1:
            wl.append((sV, ntp_d[s - 1][HCH - 1]))
        sxB_ev[0] = p.op("vector", lambda e: e.tensor_copy(sxB[:, :], pX[0:128, 256:512]),
                         waits=wl, inc=sV)
        for h0 in range(0, HCH, 2):
            ntl = [0, 0]
            nuu = [0, 0]
            for i in range(2):
                hcx = h0 + i
                wl = [(sV, sxB_ev[0])]
                if hcx >= 2:
                    wl.append((sV, h2q_d[s][hcx - 2]))
                ntl[i] = p.op("vector", lambda e, hcx=hcx, i=i: e.tensor_tensor(
                    tland[i][:, :], H2[:, hcx * 256:(hcx + 1) * 256], sxB[:, :], ALU.mult),
                    waits=wl, inc=sV)
                ntp_d[s][hcx] = ntl[i]
            for i in range(2):
                nuu[i] = p.op("vector", lambda e, i=i: e.tensor_scalar(
                    out=q16b[i][:, :], in0=tland[i][:, :],
                    scalar1=RC, scalar2=None, op0=ALU.add),
                    waits=[(sV, ntl[i])], inc=sV)
            for i in range(2):
                hcx = h0 + i
                wl = [(sV, nuu[i])]
                if s >= 2:
                    wl.append((sT, mm2_last[s - 2]))
                h2q_d[s][hcx] = p.op("vector", lambda e, s=s, hcx=hcx, i=i: e.tensor_scalar(
                    out=H2q[s % 2][:, hcx * 256:(hcx + 1) * 256], in0=q16b[i][:, :],
                    scalar1=RD, scalar2=None, op0=ALU.subtract), waits=wl, inc=sV)
        if s >= 1:
            emit_mm2(s - 1)
    emit_mm2(NS - 1)
    if DBG:
        p.op("sync", lambda e: e.dma_start(out=dbgS[:, :], in_=dbgs[:, :]), inc=sD, amt=16)
    p.op("sync", lambda e: e.nop(), waits=[
        (sOut[0], sOut[0].n), (sOut[1], sOut[1].n),
        (sX[0], sX[0].n), (sX[1], sX[1].n), (sX2[0], sX2[0].n), (sX2[1], sX2[1].n)])
    p.op("gpsimd", lambda e: e.nop(), waits=[
        (sG, sG.n), (sGw[0], sGw[0].n), (sGw[1], sGw[1].n),
        (sW2[0], sW2[0].n), (sW2[1], sW2[1].n)])
    run_block(nc, p, hwm, "b5_ffn")

    free(*list(cms))
    for s in (sP, sS, sV, sT, *sW2[::-1], *sGw[::-1], *sGs[::-1], sGk, sGq,
              *sOut[::-1], *sSp[::-1], *sW[::-1], *sX2[::-1], *sX[::-1], sG, sD):
        s._cm.__exit__(None, None, None)
    return nc


_NC_CACHE = {}


def _get_nc():
    key = "nc"
    if key not in _NC_CACHE:
        _NC_CACHE[key] = build_nc()
    return _NC_CACHE[key]


def kernel(L0_attn_x, L0_attn_wq, L0_attn_bq, L0_attn_wk, L0_attn_bk,
           L0_attn_wv, L0_attn_bv, L0_attn_wo, L0_attn_bo,
           L0_ffn_w1, L0_ffn_conv_w, L0_ffn_conv_b, L0_ffn_w2):
    x = np.ascontiguousarray(np.asarray(L0_attn_x, dtype=np.float32))
    B = x.shape[0]
    f32 = lambda a: np.ascontiguousarray(np.asarray(a, dtype=np.float32))
    wq, wk, wv, wo = f32(L0_attn_wq), f32(L0_attn_wk), f32(L0_attn_wv), f32(L0_attn_wo)
    bq = np.ascontiguousarray(f32(L0_attn_bq).reshape(NC, 128).T)
    bk = np.ascontiguousarray(f32(L0_attn_bk).reshape(NC, 128).T)
    bv = f32(L0_attn_bv).reshape(1, D)
    bo = f32(L0_attn_bo).reshape(1, D)
    w1t = np.ascontiguousarray(f32(L0_ffn_w1).T)
    w2t = np.ascontiguousarray(f32(L0_ffn_w2).T)
    cw = f32(L0_ffn_conv_w).reshape(DH, 3)
    diag3 = np.zeros((128, 3, HCH, 128), dtype=np.float32)
    idx = np.arange(128)
    for k in range(3):
        for c in range(HCH):
            diag3[idx, k, c, idx] = cw[c * 128 + idx, k]
    diag3 = diag3.reshape(128, 3 * HCH * 128)
    convb = np.ascontiguousarray(f32(L0_ffn_conv_b).reshape(HCH, 128).T)
    pp = np.arange(128)[:, None]
    ff = np.arange(384)[None, :]
    mask = ((pp <= ff) & (ff <= pp + 256)).astype(np.float32)
    maskbig = np.where(mask > 0, 0.0, -1e7).astype(np.float32)
    iden = np.eye(128, dtype=np.float32)
    hv = np.zeros((128, 2), dtype=np.float32)
    hv[:64, 0] = 1.0
    hv[64:, 1] = 1.0
    ones128 = np.ones((1, 128), dtype=np.float32)
    onescol = np.ones((128, 1), dtype=np.float32)

    nc = _get_nc()
    common = dict(wq=wq, wk=wk, wv=wv, wo=wo, bq=bq, bk=bk, bv=bv, bo=bo,
                  w1t=w1t, w2t=w2t, diag3=diag3, convb=convb, mask01=mask,
                  maskbig=maskbig, iden=iden, halves=hv, ones128=ones128,
                  onescol=onescol)
    in_maps = [dict(common, x=np.ascontiguousarray(x[i])) for i in range(B)]
    res = run_bass_kernel_spmd(nc, in_maps, core_ids=list(range(B)))
    out = np.stack([np.asarray(r["out"], dtype=np.float32) for r in res.results], axis=0)
    return out


# version 21
# speedup vs baseline: 1.1684x; 1.0148x over previous
import os
import numpy as np
import concourse.bass as bass
import concourse.mybir as mybir
from concourse.bass_utils import run_bass_kernel_spmd

F32 = mybir.dt.float32
BF16 = mybir.dt.bfloat16
I16 = mybir.dt.int16
I8 = mybir.dt.int8
AF = mybir.ActivationFunctionType
ALU = mybir.AluOpType
AXX = mybir.AxisListType.X

_SIM = os.environ.get("BASS_SIM_FP64") == "1"
RC = 127.5 if _SIM else 128.0
RD = 127.0 if _SIM else 128.0

T = 2048
D = 1024
NT = T // 128
NC = D // 128
H = 16
NJ = 16
DH = 4096
HCH = 32
TS = 256
NS = T // TS
EPS = 1e-5

ENGS = ("sync", "tensor", "vector", "scalar", "gpsimd")


class Sem:
    def __init__(self, nc, name):
        self._cm = nc.semaphore(name)
        self.h = self._cm.__enter__()
        self.n = 0
        self.name = name


class Plan:
    def __init__(self):
        self.items = {e: [] for e in ENGS}
        self.incs = {e: {} for e in ENGS}

    AUTO = {}

    def op(self, eng, emit, waits=(), inc=None, amt=1):
        w = [(s, v) for (s, v) in waits if v and v > 0]
        if inc is None and eng in self.AUTO:
            inc = self.AUTO[eng]
            amt = 1
        if inc is not None:
            inc.n += amt
            self.incs[eng][inc.name] = inc
        self.items[eng].append((w, emit, inc, amt))
        return inc.n if inc is not None else None

    def replay(self, eng, e, hwm):
        for w, emit, inc, amt in self.items[eng]:
            for s, v in w:
                if hwm[eng].get(s.name, 0) < v:
                    e.wait_ge(s.h, v)
                    hwm[eng][s.name] = v
            ins = emit(e)
            if inc is not None:
                ins.then_inc(inc.h, amt)
        for s in self.incs[eng].values():
            if hwm[eng].get(s.name, 0) < s.n:
                e.wait_ge(s.h, s.n)
                hwm[eng][s.name] = s.n


PHASES = []
_PHTOT = {e: 0 for e in ENGS}


def run_block(nc, plan, hwm, phase=""):
    for e in ENGS:
        _PHTOT[e] += len(plan.items[e])
    PHASES.append((phase, dict(_PHTOT)))
    with nc.Block() as blk:
        @blk.sync
        def _(e):
            plan.replay("sync", e, hwm)

        @blk.tensor
        def _(e):
            plan.replay("tensor", e, hwm)

        @blk.vector
        def _(e):
            plan.replay("vector", e, hwm)

        @blk.scalar
        def _(e):
            plan.replay("scalar", e, hwm)

        @blk.gpsimd
        def _(e):
            plan.replay("gpsimd", e, hwm)


def build_nc():
    PHASES.clear()
    for e in ENGS:
        _PHTOT[e] = 0
    nc = bass.Bass("TRN2", target_bir_lowering=False, debug=False,
                   detect_race_conditions=False)

    dx = nc.dram_tensor("x", [T, D], F32, kind="ExternalInput").ap()
    dw = {k: nc.dram_tensor(k, [D, D], F32, kind="ExternalInput").ap()
          for k in ("wq", "wk", "wv", "wo")}
    dbq = nc.dram_tensor("bq", [128, NC], F32, kind="ExternalInput").ap()
    dbk = nc.dram_tensor("bk", [128, NC], F32, kind="ExternalInput").ap()
    dbv = nc.dram_tensor("bv", [1, D], F32, kind="ExternalInput").ap()
    dbo = nc.dram_tensor("bo", [1, D], F32, kind="ExternalInput").ap()
    dw1t = nc.dram_tensor("w1t", [D, DH], F32, kind="ExternalInput").ap()
    dw2t = nc.dram_tensor("w2t", [DH, D], F32, kind="ExternalInput").ap()
    ddiag = nc.dram_tensor("diag3", [128, 3 * HCH * 128], F32, kind="ExternalInput").ap()
    dcb = nc.dram_tensor("convb", [128, HCH], F32, kind="ExternalInput").ap()
    dmask = nc.dram_tensor("mask01", [128, 384], F32, kind="ExternalInput").ap()
    dmaskb = nc.dram_tensor("maskbig", [128, 384], F32, kind="ExternalInput").ap()
    diden = nc.dram_tensor("iden", [128, 128], F32, kind="ExternalInput").ap()
    dhalves = nc.dram_tensor("halves", [128, 2], F32, kind="ExternalInput").ap()
    dones128 = nc.dram_tensor("ones128", [1, 128], F32, kind="ExternalInput").ap()
    donescol = nc.dram_tensor("onescol", [128, 1], F32, kind="ExternalInput").ap()
    dout = nc.dram_tensor("out", [T, D], F32, kind="ExternalOutput").ap()
    DBG = os.environ.get("KDEBUG") == "1"
    iknd = "ExternalOutput" if DBG else "Internal"
    dx2 = nc.dram_tensor("x2sp", [T, D], BF16, kind=iknd).ap()
    dw1b = nc.dram_tensor("w1b16", [8, 128, 4096], BF16, kind=iknd).ap()
    dw2b = nc.dram_tensor("w2b16", [128, HCH * 1024], BF16, kind=iknd).ap()
    dx3t = nc.dram_tensor("x3tsp", [NC, 128, T + 2], BF16, kind=iknd).ap()
    dbgO = (nc.dram_tensor("dbgO", [T, D], BF16, kind="ExternalOutput").ap()
            if DBG else None)
    dbgS = (nc.dram_tensor("dbgS", [128, 80], F32, kind="ExternalOutput").ap()
            if DBG else None)

    sD = Sem(nc, "sD")
    sG = Sem(nc, "sG")
    sX = [Sem(nc, "sX0"), Sem(nc, "sX1")]
    sX2 = [Sem(nc, "sX2_0"), Sem(nc, "sX2_1")]
    sW = [Sem(nc, f"sW{i}") for i in range(6)]
    sSp = [Sem(nc, f"sSp{i}") for i in range(4)]
    sOut = [Sem(nc, "sOut0"), Sem(nc, "sOut1")]
    sGq = Sem(nc, "sGq")
    sGk = Sem(nc, "sGk")
    sGs = [Sem(nc, "sGs0"), Sem(nc, "sGs1")]
    sGw = [Sem(nc, "sGw0"), Sem(nc, "sGw1")]
    sW2 = [Sem(nc, "sW2_0"), Sem(nc, "sW2_1")]
    sT = Sem(nc, "sT")
    sV = Sem(nc, "sV")
    sS = Sem(nc, "sS")
    sP = Sem(nc, "sP")
    Plan.AUTO = {"vector": sV, "scalar": sS, "gpsimd": sP}
    hwm = {e: {} for e in ENGS}
    cms = {}
    aseq = {"n": 0}

    def alloc(name, shape, dt, space="sbuf", side=None):
        cm = (nc.sbuf_tensor(name, shape, dt, side=side) if space == "sbuf"
              else nc.psum_tensor(name, shape, dt))
        h = cm.__enter__()
        aseq["n"] += 1
        cms[name] = (aseq["n"], cm)
        return h

    def free(*names):
        for nm in sorted(names, key=lambda nm: -cms[nm][0]):
            cms.pop(nm)[1].__exit__(None, None, None)

    bqs = alloc("bqs", [128, NC], F32)
    bks = alloc("bks", [128, NC], F32)
    bvB = alloc("bvB", [128, D], F32, side="right")
    boB = alloc("boB", [128, D], F32, side="right")
    bvrow = alloc("bvrow", [1, D], F32, side="right")
    borow = alloc("borow", [1, D], F32, side="right")
    cbs = alloc("cbs", [128, HCH], F32)
    mask01 = alloc("sb_mask01", [128, 384], BF16)
    maskB = alloc("sb_maskB", [128, 384], BF16)
    iden = alloc("sb_iden", [128, 128], BF16)
    idenf = alloc("sb_idenf", [128, 128], F32)
    halves = alloc("sb_halves", [128, 2], F32)
    ones128 = alloc("sb_ones128", [1, 128], F32)
    onescol = alloc("sb_onescol", [128, 1], F32)
    epsc = alloc("epsc", [128, 1], F32)
    rcc = alloc("rcc", [128, 1], F32)
    negC = alloc("negC", [128, H], F32)
    eC = alloc("eC", [128, H], F32)
    mwc = alloc("mwc", [128, 2], F32)
    swc = alloc("swc", [128, 2], F32)
    stat = alloc("stat", [128, 12], F32)
    aggr = alloc("aggr", [128, 2], F32)
    scol = alloc("scol", [128, 1], F32)
    rcol = alloc("rcol", [128, 1], F32)
    wsum = alloc("wsum", [128, 64], F32)
    mrow = alloc("mrow", [1, 64], F32)
    mtot = alloc("mtot", [1, 2], F32)
    zrow = alloc("zrow", [128, 1], BF16)
    qmx = alloc("qmx", [128, NC], F32)
    kmx = alloc("kmx", [128, NC], F32)
    qkp = alloc("qkp", [128, NC], F32)
    c2x8 = alloc("c2x8", [2, NC], F32)
    crow = alloc("crow", [1, 16], F32)
    dbgs = alloc("dbgs", [128, 80], F32)

    p = Plan()
    for (dst, src) in ((bqs, dbq), (bks, dbk), (bvrow, dbv), (borow, dbo),
                       (cbs, dcb), (halves, dhalves), (ones128, dones128),
                       (onescol, donescol), (idenf, diden)):
        p.op("sync", lambda e, d=dst, s=src: e.dma_start(out=d[:], in_=s[:, :]),
             inc=sD, amt=16)
    p.op("gpsimd", lambda e: e.dma_start(out=mask01[:], in_=dmask[:, :]), inc=sG, amt=16)
    p.op("gpsimd", lambda e: e.dma_start(out=maskB[:], in_=dmaskb[:, :]), inc=sG, amt=16)
    p.op("gpsimd", lambda e: e.dma_start(out=iden[:], in_=diden[:, :]), inc=sG, amt=16)
    p.op("vector", lambda e: e.memset(epsc[:], EPS), inc=sV)
    p.op("vector", lambda e: e.memset(rcc[:], RC), inc=sV)
    nzr = p.op("vector", lambda e: e.memset(zrow[:], 0.0), inc=sV)
    for c in range(NC):
        p.op("sync", lambda e, c=c: e.dma_start(out=dx3t[c, :, 0:1], in_=zrow[:, 0:1],
             allow_slow_non_contiguous=True), waits=[(sV, nzr)], inc=sD, amt=16)
        p.op("sync", lambda e, c=c: e.dma_start(out=dx3t[c, :, T + 1:T + 2], in_=zrow[:, 0:1],
             allow_slow_non_contiguous=True), inc=sD, amt=16)
    p.op("sync", lambda e: e.nop(), waits=[(sD, sD.n)])
    p.op("gpsimd", lambda e: e.nop(), waits=[(sG, sG.n)])
    run_block(nc, p, hwm, "b0_const")

    B1all = alloc("B1all", [128, NC * T], BF16)
    B1 = [B1all[:, c * T:(c + 1) * T] for c in range(NC)]
    B1r3 = B1all.ap().rearrange("p (c t) -> p c t", t=T)
    QT = [alloc(f"QT{c}", [128, T], BF16) for c in range(NC)]
    KT = [alloc(f"KT{c}", [128, T], BF16) for c in range(NC)]
    VA = [alloc(f"VA{j}", [128, H * 65], BF16) for j in range(NJ + 2)]

    def vaug(j):
        return VA[j + 1]

    wsb = [[alloc(f"wsb{s}_{c}", [128, D], BF16) for c in range(NC)] for s in range(2)]
    xin = [alloc(f"xin_{i}", [128, D], F32) for i in range(2)]
    x1b = [alloc(f"x1b_{i}", [128, D], BF16) for i in range(2)]
    ptr = [alloc(f"ptr_{i}", [128, 1024], BF16, "psum") for i in range(2)]
    pbc = [alloc(f"pbc_{i}", [128, 512], F32, "psum") for i in range(2)]

    p = Plan()
    wq_ld = [0] * NC
    wk_ld = [0] * NC
    wv_ld = [0] * NC
    for c in range(NC):
        wq_ld[c] = p.op("gpsimd", lambda e, c=c: e.dma_start(
            out=wsb[0][c][:], in_=dw["wq"][c * 128:(c + 1) * 128, :]), inc=sGq, amt=16)
    for c in range(NC):
        wk_ld[c] = p.op("gpsimd", lambda e, c=c: e.dma_start(
            out=wsb[1][c][:], in_=dw["wk"][c * 128:(c + 1) * 128, :]), inc=sGk, amt=16)
    for j in (-1, NJ):
        p.op("vector", lambda e, j=j: e.memset(
            vaug(j).ap().rearrange("p (h c) -> p h c", c=65)[:, :, 0:64], 0.0))
    vinit = 0
    for j in range(-1, NJ + 1):
        vinit = p.op("vector", lambda e, j=j: e.memset(
            vaug(j).ap().rearrange("p (h c) -> p h c", c=65)[:, :, 64:65], 1.0), inc=sV)

    bc_ev = 0
    for i, (dst, src) in enumerate(((bvB, bvrow), (bvB, bvrow), (boB, borow), (boB, borow))):
        off = (i % 2) * 512
        wl = [(sV, bc_ev)] if i >= 2 else []
        n = p.op("tensor", lambda e, s=src, off=off, i=i: e.matmul(
            pbc[i % 2][:, :], ones128[:, :], s[0:1, off:off + 512], start=True, stop=True),
            waits=wl, inc=sT)
        bc_ev = p.op("vector", lambda e, d=dst, off=off, i=i: e.tensor_copy(
            d[:, off:off + 512], pbc[i % 2][:, :]), waits=[(sT, n)], inc=sV)

    ln_apply = [0] * NT
    tr_done = [0] * NT
    ev_done = [0] * NT
    sq_done = [0] * NT
    for t in range(NT):
        wl = [(sV, ln_apply[t - 2])] if t >= 2 else []
        xload = p.op("sync", lambda e, t=t: e.dma_start(
            out=xin[t % 2][:], in_=dx[t * 128:(t + 1) * 128, :]), waits=wl,
            inc=sX[t % 2], amt=16)
        wl = [(sX[t % 2], xload)]
        if t >= 1:
            wl.append((sS, sq_done[t - 1]))
        p.op("vector", lambda e, t=t: e.bn_stats(stat[:, 0:6], xin[t % 2][:, 0:512]),
             waits=wl)
        nsb = p.op("vector", lambda e, t=t: e.bn_stats(stat[:, 6:12], xin[t % 2][:, 512:1024]))
        nag = p.op("vector", lambda e: e.bn_aggr(aggr[:], stat[:]), waits=[(sV, nsb)], inc=sV)
        sq_done[t] = p.op("scalar", lambda e: e.activation(
            out=scol[:], in_=aggr[:, 1:2], func=AF.Sqrt, bias=epsc[:], scale=1.0),
            waits=[(sV, nag)], inc=sS)
        nrc1 = p.op("vector", lambda e: e.reciprocal(rcol[:], scol[:]),
                    waits=[(sS, sq_done[t])])
        wl = [(sV, nrc1)]
        if t >= 2:
            wl.append((sT, tr_done[t - 2]))
        ln_apply[t] = p.op("vector", lambda e, t=t: e.tensor_scalar(
            out=x1b[t % 2][:], in0=xin[t % 2][:], scalar1=aggr[:, 0:1], scalar2=rcol[:],
            op0=ALU.subtract, op1=ALU.mult), waits=wl, inc=sV)
        for c in range(NC):
            wl = []
            if c == 0:
                wl.append((sV, ln_apply[t]))
                if t >= 2:
                    wl.append((sS, ev_done[t - 2]))
            tr_done[t] = p.op("tensor", lambda e, t=t, c=c: e.transpose(
                ptr[t % 2][:, c * 128:(c + 1) * 128],
                x1b[t % 2][:, c * 128:(c + 1) * 128], iden[:]),
                waits=wl, inc=sT if c == NC - 1 else None)
        ev_done[t] = p.op("scalar", lambda e, t=t: e.copy(
            B1r3[:, :, t * 128:(t + 1) * 128],
            ptr[t % 2].ap().rearrange("p (c q) -> p c q", q=128)[:, :, :]),
            waits=[(sT, tr_done[t])], inc=sS)
    run_block(nc, p, hwm, "b1_ln1")
    free("xin_0", "xin_1", "x1b_0", "x1b_1", "ptr_0", "ptr_1", "pbc_0", "pbc_1")
    free("bvrow", "borow")

    wstage = [alloc(f"wstage_{i}", [128, 1024], F32) for i in range(6)]
    pq = [alloc(f"pq_{i}", [128, 512], F32, "psum") for i in range(4)]
    pms = alloc("pms", [128, 512], F32, "psum")

    p = Plan()

    red_done = []
    piece_srcs = []
    for ch in range(NC):
        for h4 in range(4):
            piece_srcs.append((dw1t[ch * 128:(ch + 1) * 128, h4 * 1024:(h4 + 1) * 1024], 1024))
    for ch in range(HCH):
        piece_srcs.append((dw2t[ch * 128:(ch + 1) * 128, :], 1024))

    def emit_piece(idx):
        src, width = piece_srcs[idx]
        slot = idx % 6
        wl = [(sV, red_done[idx - 6])] if idx >= 6 else []
        ld = p.op("sync", lambda e, src=src, slot=slot, width=width: e.dma_start(
            out=wstage[slot][:, 0:width], in_=src), waits=wl, inc=sW[slot], amt=16)
        red_done.append(p.op("vector", lambda e, slot=slot, idx=idx, width=width: e.reduce_sum(
            wsum[:, idx:idx + 1], wstage[slot][:, 0:width], axis=AXX,
            apply_absolute_value=True), waits=[(sW[slot], ld)], inc=sV))

    NPC1 = 40
    for idx in range(NPC1):
        emit_piece(idx)

    def proj_qk(wset, wld, bias_s, OUT, slot_ev, evac_eng, wsem):
        for oc in range(NC):
            for st in range(4):
                slot = (oc * 4 + st) % 4
                wl = [(wsem, wld[NC - 1])]
                if slot_ev[slot]:
                    wl.append((evac_eng, slot_ev[slot]))
                last = 0
                for ic in range(NC):
                    last = p.op("tensor", lambda e, oc=oc, st=st, ic=ic, slot=slot, wset=wset: e.matmul(
                        pq[slot][:, :], wsb[wset][ic][:, oc * 128:(oc + 1) * 128],
                        B1[ic][:, st * 512:(st + 1) * 512],
                        start=(ic == 0), stop=(ic == NC - 1)),
                        waits=wl if ic == 0 else [],
                        inc=sT if ic == NC - 1 else None)
                slot_ev[slot] = p.op("scalar", lambda e, oc=oc, st=st, slot=slot, OUT=OUT, bias_s=bias_s: e.activation(
                    out=OUT[oc][:, st * 512:(st + 1) * 512], in_=pq[slot][:, :],
                    func=AF.Identity, bias=bias_s[:, oc:oc + 1], scale=1.0),
                    waits=[(sT, last)], inc=sS)
        return slot_ev

    slot_ev = [0, 0, 0, 0]
    slot_ev = proj_qk(0, wq_ld, bqs, QT, slot_ev, sS, sGq)
    for c in range(NC):
        wv_ld[c] = p.op("gpsimd", lambda e, c=c: e.dma_start(
            out=wsb[0][c][:], in_=dw["wv"][c * 128:(c + 1) * 128, :]),
            waits=[(sS, max(slot_ev))], inc=sGq, amt=16)
    slot_ev = proj_qk(1, wk_ld, bks, KT, slot_ev, sS, sGk)

    prev_s_ev = list(slot_ev)
    slot_vv = [0, 0, 0, 0]
    nred = 0
    kev = max(slot_ev)

    def emit_qkmax(i):
        src = QT[i] if i < NC else KT[i - NC]
        dst = qmx if i < NC else kmx
        c = i % NC
        return p.op("vector", lambda e, src=src, dst=dst, c=c: e.reduce_max(
            dst[:, c:c + 1], src[:, :], axis=AXX, apply_absolute_value=True),
            waits=[(sS, kev)] if i == 0 else [], inc=sV)

    for t in range(NT):
        for ofs in range(2):
            g = t * 2 + ofs
            slot = g % 4
            wl = [(sGq, wv_ld[NC - 1]), (sV, bc_ev), (sV, vinit)]
            if prev_s_ev[slot]:
                wl.append((sS, prev_s_ev[slot]))
                prev_s_ev[slot] = 0
            if slot_vv[slot]:
                wl.append((sV, slot_vv[slot]))
            last = 0
            for ic in range(NC):
                last = p.op("tensor", lambda e, t=t, ofs=ofs, ic=ic, slot=slot: e.matmul(
                    pq[slot][:, :], B1[ic][:, t * 128:(t + 1) * 128],
                    wsb[0][ic][:, ofs * 512:(ofs + 1) * 512],
                    start=(ic == 0), stop=(ic == NC - 1)),
                    waits=wl if ic == 0 else [],
                    inc=sT if ic == NC - 1 else None)
            slot_vv[slot] = p.op("vector", lambda e, t=t, ofs=ofs, slot=slot: e.scalar_tensor_tensor(
                out=vaug(t).ap().rearrange("p (h c) -> p h c", c=65)[:, ofs * 8:(ofs + 1) * 8, 0:64],
                in0=pq[slot][:, :], scalar=1.0,
                in1=bvB[:, ofs * 512:(ofs + 1) * 512],
                op0=ALU.mult, op1=ALU.add),
                waits=[(sT, last)], inc=sV)
            if g % 2 == 1 and g // 2 < 2 * NC:
                nred = emit_qkmax(g // 2)

    for idx in range(NPC1, len(piece_srcs)):
        emit_piece(idx)

    n1 = p.op("tensor", lambda e: e.matmul(pms[0:1, 0:64], onescol[:, :], wsum[:, 0:64],
                                           start=True, stop=True),
              waits=[(sV, red_done[-1])], inc=sT)
    n2 = p.op("scalar", lambda e: e.copy(mrow[0:1, 0:64], pms[0:1, 0:64]),
              waits=[(sT, n1)], inc=sS)
    p.op("vector", lambda e: e.reduce_sum(mtot[0:1, 0:1], mrow[0:1, 0:32], axis=AXX),
         waits=[(sS, n2)])
    nr2 = p.op("vector", lambda e: e.reduce_sum(mtot[0:1, 1:2], mrow[0:1, 32:64], axis=AXX))
    n3 = p.op("vector", lambda e: e.tensor_scalar(out=mtot[:], in0=mtot[:],
                                             scalar1=1.0 / (D * DH), scalar2=1e-5,
                                             op0=ALU.mult, op1=ALU.max),
              waits=[(sV, nr2)], inc=sV)
    n4 = p.op("tensor", lambda e: e.matmul(pms[:, 0:2], ones128[:, :], mtot[0:1, :],
                                           start=True, stop=True),
              waits=[(sV, n3)], inc=sT)
    n5 = p.op("scalar", lambda e: e.copy(mwc[:], pms[0:128, 0:2]), waits=[(sT, n4)], inc=sS)
    p.op("vector", lambda e: e.reciprocal(swc[:, 0:1], mwc[:, 0:1]), waits=[(sS, n5)])
    n6 = p.op("vector", lambda e: e.reciprocal(swc[:, 1:2], mwc[:, 1:2]), inc=sV)
    if DBG:
        p.op("vector", lambda e: e.tensor_copy(dbgs[:, 32:34], mwc[:, :]),
             waits=[(sV, n6)])
        p.op("vector", lambda e: e.tensor_copy(dbgs[:, 34:36], swc[:, :]))
        p.op("vector", lambda e: e.tensor_copy(dbgs[0:1, 36:60], mrow[0:1, 24:48]))
        p.op("vector", lambda e: e.tensor_copy(dbgs[0:1, 60:62], mtot[0:1, :]))

    nq = p.op("vector", lambda e: e.tensor_mul(qkp[:], qmx[:], kmx[:]),
              waits=[(sV, nred)], inc=sV)
    nm = p.op("tensor", lambda e: e.matmul(pms[0:2, 0:8], halves[:, :], qkp[:, :],
                                           start=True, stop=True),
              waits=[(sV, nq), (sS, n5)], inc=sT)
    ne = p.op("scalar", lambda e: e.copy(c2x8[:], pms[0:2, 0:8]), waits=[(sT, nm)], inc=sS)
    nts = p.op("vector", lambda e: e.tensor_scalar(out=c2x8[:], in0=c2x8[:], scalar1=0.125,
                                                   scalar2=-40.0, op0=ALU.mult, op1=ALU.add),
               waits=[(sS, ne)])
    nrl = p.op("vector", lambda e: e.tensor_relu(c2x8[:], c2x8[:]), waits=[(sV, nts)])
    nr = p.op("vector", lambda e: e.tensor_scalar(out=c2x8[:], in0=c2x8[:], scalar1=-1.0,
                                                  scalar2=None, op0=ALU.mult),
              waits=[(sV, nrl)], inc=sV)
    nc1 = p.op("vector", lambda e: e.tensor_copy(crow[0:1, 0:8], c2x8[0:1, :]),
               waits=[(sV, nr)], inc=sV)
    nd = p.op("sync", lambda e: e.dma_start(out=crow[0:1, 8:16], in_=c2x8[1:2, :]),
              waits=[(sV, nr)], inc=sD, amt=16)
    nb1 = p.op("tensor", lambda e: e.matmul(pms[:, 0:8], ones128[:, :], crow[0:1, 0:8],
                                            start=True, stop=True),
               waits=[(sV, nc1), (sD, nd), (sS, ne)], inc=sT)
    nb1e = p.op("scalar", lambda e: e.copy(
        negC.ap().rearrange("p (a b) -> p a b", b=2)[:, :, 0:1], pms[0:128, 0:8]),
        waits=[(sT, nb1)], inc=sS)
    nb2 = p.op("tensor", lambda e: e.matmul(pms[:, 0:8], ones128[:, :], crow[0:1, 8:16],
                                            start=True, stop=True),
               waits=[(sS, nb1e)], inc=sT)
    nb2e = p.op("scalar", lambda e: e.copy(
        negC.ap().rearrange("p (a b) -> p a b", b=2)[:, :, 1:2], pms[0:128, 0:8]),
        waits=[(sT, nb2)], inc=sS)
    nec = p.op("scalar", lambda e: e.activation(out=eC[:], in_=negC[:], func=AF.Exp,
                                                bias=0.0, scale=1.0),
               waits=[(sS, nb2e)], inc=sS)


    run_block(nc, p, hwm, "b2_qkv")
    free(*[f"wsb{s}_{c}" for s in range(2) for c in range(NC)],
         "wstage_0", "wstage_1", "wstage_2", "wstage_3", "wstage_4", "wstage_5",
         "pq_0", "pq_1", "pq_2", "pq_3", "pms")

    B2 = [alloc(f"B2_{t}", [128, D], BF16, side="right") for t in range(NT)]
    expraw = [alloc(f"expraw_{i}", [128, 384], BF16) for i in range(3)]
    expm = [alloc(f"expm_{i}", [128, 384], BF16) for i in range(3)]
    rodc = [alloc(f"rodc_{i}", [128, 1], F32) for i in range(2)]
    psc = [alloc(f"psc_{i}", [128, 512], F32, "psum") for i in range(3)]
    pod = [alloc(f"pod_{i}", [128, 512], F32, "psum") for i in range(4)]

    p = Plan()
    wstage3 = [alloc(f"wstage3_{i}", [128, 1024], F32) for i in range(4)]
    tern_u16 = [alloc(f"tern_u16_{i}", [128, 1024], I16) for i in range(2)]
    tern2b = [alloc(f"tern2b_{i}", [128, 1024], BF16) for i in range(2)]
    tern_jobs = []
    for ch in range(NC):
        for q in range(4):
            tern_jobs.append((dw1t[ch * 128:(ch + 1) * 128, q * 1024:(q + 1) * 1024],
                              [(dw1b[q * 2 + half, :, ch * 512:(ch + 1) * 512],
                                half * 512, (half + 1) * 512) for half in range(2)], 0))
    for ch in range(HCH):
        tern_jobs.append((dw2t[ch * 128:(ch + 1) * 128, :],
                          [(dw2b[:, ch * 1024:(ch + 1) * 1024], 0, 1024)], 1))
    tstate = {"sp": [0, 0, 0, 0]}

    def tern_steps():
        for i0 in range(0, len(tern_jobs), 4):
            picks = [k for k in range(i0, min(i0 + 4, len(tern_jobs)))]
            lds, n1s, n2s = {}, {}, {}
            for k in picks:
                srcA, dst, wi = tern_jobs[k]
                slot = k % 4
                wl = [(sV, n6)]
                if tstate["sp"][slot]:
                    wl.append((sSp[slot], tstate["sp"][slot]))
                lds[k] = p.op("sync", lambda e, srcA=srcA, slot=slot: e.dma_start(
                    out=wstage3[slot][:, :], in_=srcA), waits=wl, inc=sW[slot], amt=16)
                yield
            for j0 in range(0, len(picks), 2):
                pair = picks[j0:j0 + 2]
                for k in pair:
                    wi = tern_jobs[k][2]
                    slot = k % 4
                    wl = [(sW[slot], lds[k])]
                    if tstate.get("u16", [0, 0])[k % 2]:
                        wl.append((sV, tstate["u16"][k % 2]))
                    n1s[k] = p.op("scalar", lambda e, slot=slot, wi=wi, k=k: e.activation(
                        out=tern_u16[k % 2][:, :], in_=wstage3[slot][:, :],
                        func=AF.Identity, bias=rcc[:], scale=swc[:, wi:wi + 1]),
                        waits=wl, inc=sS)
                    yield
                for k in pair:
                    slot = k % 4
                    n2s[k] = p.op("vector", lambda e, slot=slot, k=k: e.tensor_scalar(
                        out=wstage3[slot][:, :], in0=tern_u16[k % 2][:, :],
                        scalar1=RD, scalar2=1.0, op0=ALU.subtract, op1=ALU.min),
                        waits=[(sS, n1s[k])])
                    tstate.setdefault("u16", [0, 0])[k % 2] = n2s[k]
                    yield
                for k in pair:
                    slot = k % 4
                    wl = [(sV, n2s[k])]
                    if tstate.get("t2b", [0, 0])[k % 2]:
                        wl.append((sSp[(k - 2) % 4], tstate["t2b"][k % 2]))
                    n2s[k] = p.op("vector", lambda e, slot=slot, k=k: e.tensor_scalar(
                        out=tern2b[k % 2][:, :], in0=wstage3[slot][:, :],
                        scalar1=-1.0, scalar2=None, op0=ALU.max),
                        waits=wl, inc=sV)
                    yield
                for k in pair:
                    slot = k % 4
                    for pi, (dst, c0, c1) in enumerate(tern_jobs[k][1]):
                        tstate["sp"][slot] = p.op("sync", lambda e, dst=dst, c0=c0, c1=c1, slot=slot, k=k: e.dma_start(
                            out=dst, in_=tern2b[k % 2][:, c0:c1]),
                            waits=[(sV, n2s[k])] if pi == 0 else [],
                            inc=sSp[slot], amt=16)
                    tstate.setdefault("t2b", [0, 0])[k % 2] = tstate["sp"][slot]
                    yield

    tern_gen = tern_steps()

    def win(j):
        q0 = (j - 1) * 128
        q1 = (j + 2) * 128
        c0 = max(0, -q0)
        q0r = max(q0, 0)
        q1r = min(q1, T)
        return q0r, q1r, c0, c0 + (q1r - q0r)

    pod_ev = [0, 0, 0, 0]
    psc_ev = [0, 0, 0]
    expm_use = [0, 0, 0]
    mask_done = [0, 0, 0]
    rodc_use = [0, 0]
    mask_info = {}

    def emit_av(h, j):
        sl = (j + 1) % 3
        msem, nmask = mask_info.pop((h, j))
        last_use = 0
        for s in range(3):
            qb = j - 1 + s
            if qb < 0 or qb >= NJ:
                continue
            wc = qb * 128 - (j - 1) * 128
            first = (j == qb - 1)
            lastc = (j == qb + 1)
            wl = [(msem, nmask)]
            if first and pod_ev[qb % 4]:
                wl.append((sV, pod_ev[qb % 4]))
            last_use = p.op("tensor", lambda e, qb=qb, j=j, h=h, sl=sl, wc=wc, first=first, lastc=lastc: e.matmul(
                pod[qb % 4][:, 0:65], expm[sl][:, wc:wc + 128],
                vaug(j)[:, h * 65:h * 65 + 65], start=first, stop=lastc),
                waits=wl, inc=sT)
            if lastc:
                wl = [(sT, last_use)]
                if rodc_use[qb % 2]:
                    wl.append((sV, rodc_use[qb % 2]))
                nrc = p.op("vector", lambda e, qb=qb: e.reciprocal(
                    rodc[qb % 2][:, :], pod[qb % 4][:, 64:65]), waits=wl, inc=sV)
                pod_ev[qb % 4] = p.op("vector", lambda e, qb=qb, h=h: e.tensor_scalar(
                    out=B2[qb][:, h * 64:(h + 1) * 64], in0=pod[qb % 4][:, 0:64],
                    scalar1=rodc[qb % 2][:, :], scalar2=None, op0=ALU.mult),
                    waits=[(sV, nrc)], inc=sV)
                rodc_use[qb % 2] = pod_ev[qb % 4]
        if last_use:
            expm_use[sl] = last_use

    seq = [(h, j) for h in range(H) for j in range(-1, NJ + 1)]
    for idx, (h, j) in enumerate(seq):
        hc = h // 2
        hr = (h % 2) * 64
        sl = (j + 1) % 3
        q0r, q1r, c0, c1 = win(j)
        pad = (j < 0) or (j >= NJ)
        if not pad:
            wl = [(sS, nec)]
            if psc_ev[sl]:
                wl.append((sS, psc_ev[sl]))
            p.op("tensor", lambda e, sl=sl, c0=c0, c1=c1: e.matmul(
                psc[sl][:, c0:c1], iden[:], maskB[:, c0:c1], start=True, stop=False),
                waits=wl)
            nmm = p.op("tensor", lambda e, hc=hc, hr=hr, j=j, sl=sl, q0r=q0r, q1r=q1r, c0=c0, c1=c1: e.matmul(
                psc[sl][:, c0:c1], KT[hc][hr:hr + 64, j * 128:(j + 1) * 128],
                QT[hc][hr:hr + 64, q0r:q1r], start=False, stop=True),
                inc=sT)
            wl = [(sT, nmm)]
            if expm_use[sl]:
                wl.append((sT, expm_use[sl]))
            psc_ev[sl] = p.op("scalar", lambda e, h=h, sl=sl, c0=c0, c1=c1: e.activation(
                out=expm[sl][:, c0:c1], in_=psc[sl][:, c0:c1], func=AF.Exp,
                bias=negC[:, h:h + 1], scale=0.125), waits=wl, inc=sS)
            mask_info[(h, j)] = (sS, psc_ev[sl])
        else:
            wl = [(sS, nec)]
            if expm_use[sl]:
                wl.append((sT, expm_use[sl]))
            nmask = p.op("vector", lambda e, h=h, sl=sl, c0=c0, c1=c1: e.tensor_scalar(
                out=expm[sl][:, c0:c1], in0=mask01[:, c0:c1],
                scalar1=eC[:, h:h + 1], scalar2=None, op0=ALU.mult),
                waits=wl, inc=sV)
            mask_info[(h, j)] = (sV, nmask)
        if idx >= 2:
            emit_av(*seq[idx - 2])
        for _ in range(2):
            next(tern_gen, None)
    emit_av(*seq[-2])
    emit_av(*seq[-1])
    for _ in tern_gen:
        pass
    p.op("sync", lambda e: e.nop(), waits=[(sD, sD.n)] +
         [(s_, s_.n) for s_ in sW] + [(s_, s_.n) for s_ in sSp])
    run_block(nc, p, hwm, "b3_attn")
    free(*[f"QT{c}" for c in range(NC)], *[f"KT{c}" for c in range(NC)],
         *[f"VA{j}" for j in range(NJ + 2)],
         *[f"wstage3_{i}" for i in range(4)], "tern_u16_0", "tern_u16_1",
         "tern2b_0", "tern2b_1",
         "expraw_0", "expraw_1", "expraw_2", "expm_0", "expm_1", "expm_2",
         "rodc_0", "rodc_1",
         "psc_0", "psc_1", "psc_2", "pod_0", "pod_1", "pod_2", "pod_3")

    wosb = [alloc(f"wo_{c}", [128, D], BF16) for c in range(NC)]
    attn = [alloc(f"attn_{i}", [128, D], F32) for i in range(2)]
    x2f = [alloc(f"x2f_{i}", [128, D], F32) for i in range(2)]
    x3f = [alloc(f"x3f_{i}", [128, D], F32) for i in range(2)]
    xre = [alloc(f"xre_{i}", [128, D], F32) for i in range(2)]
    q16 = alloc("q16", [128, D], I16)
    x3q = [alloc(f"x3q_{i}", [128, D], BF16) for i in range(2)]
    x3st = [alloc(f"x3st_{i}", [128, D], BF16) for i in range(2)]
    sxc = alloc("sxc", [128, 1], F32)
    sxic = alloc("sxic", [128, 1], F32)
    ptrO = [alloc(f"ptrO_{i}", [128, 1024], BF16, "psum") for i in range(2)]
    ptrX = [alloc(f"ptrX_{i}", [128, 1024], BF16, "psum") for i in range(2)]
    po4 = [alloc(f"po4_{i}", [128, 512], F32, "psum") for i in range(2)]

    p = Plan()
    if DBG:
        for t in range(NT):
            p.op("sync", lambda e, t=t: e.dma_start(
                out=dbgO[t * 128:(t + 1) * 128, :], in_=B2[t][:, :]), inc=sD, amt=16)
    wo_ld = [0] * NC
    for c in range(NC):
        wo_ld[c] = p.op("gpsimd", lambda e, c=c: e.dma_start(
            out=wosb[c][:], in_=dw["wo"][c * 128:(c + 1) * 128, :]), inc=sGq, amt=16)
    otr = [0] * NT
    oev = [0] * NT
    po_ev = [0, 0]
    x2sp = [0, 0]
    x3sp = [0, 0]
    apply2 = [0] * NT
    apply3 = [0] * NT
    q_done = [0] * NT
    x3tr = [0] * NT
    x3ev = [0] * NT
    for t in range(NT):
        for c in range(NC):
            wl = []
            if c == 0 and t >= 2:
                wl.append((sS, oev[t - 2]))
            otr[t] = p.op("tensor", lambda e, t=t, c=c: e.transpose(
                ptrO[t % 2][:, c * 128:(c + 1) * 128],
                B2[t][:, c * 128:(c + 1) * 128], iden[:]),
                waits=wl, inc=sT if c == NC - 1 else None)
        oev[t] = p.op("scalar", lambda e, t=t: e.copy(
            B1r3[:, :, t * 128:(t + 1) * 128],
            ptrO[t % 2].ap().rearrange("p (c q) -> p c q", q=128)[:, :, :]),
            waits=[(sT, otr[t])], inc=sS)
        xre_ld = p.op("sync", lambda e, t=t: e.dma_start(
            out=xre[t % 2][:], in_=dx[t * 128:(t + 1) * 128, :]),
            waits=[(sV, apply2[t - 2])] if t >= 2 else [], inc=sX[t % 2], amt=16)
        for ofs in range(2):
            slot = ofs
            wl = [(sS, oev[t]), (sGq, wo_ld[NC - 1])]
            if po_ev[slot]:
                wl.append((sV, po_ev[slot]))
            last = 0
            for c in range(NC):
                last = p.op("tensor", lambda e, t=t, ofs=ofs, c=c, slot=slot: e.matmul(
                    po4[slot][:, :], B1[c][:, t * 128:(t + 1) * 128],
                    wosb[c][:, ofs * 512:(ofs + 1) * 512],
                    start=(c == 0), stop=(c == NC - 1)),
                    waits=wl if c == 0 else [],
                    inc=sT if c == NC - 1 else None)
            wl2 = [(sT, last)]
            if t >= 2:
                wl2.append((sV, apply2[t - 2]))
            po_ev[slot] = p.op("vector", lambda e, t=t, ofs=ofs, slot=slot: e.scalar_tensor_tensor(
                out=attn[t % 2][:, ofs * 512:(ofs + 1) * 512], in0=po4[slot][:, :],
                scalar=1.0, in1=boB[:, ofs * 512:(ofs + 1) * 512],
                op0=ALU.mult, op1=ALU.add), waits=wl2, inc=sV)
        if t >= 2:
            emit_x3t(t - 2)
        p.op("vector", lambda e, t=t: e.bn_stats(stat[:, 0:6], attn[t % 2][:, 0:512]),
             waits=[(sV, po_ev[1])])
        nsb = p.op("vector", lambda e, t=t: e.bn_stats(stat[:, 6:12], attn[t % 2][:, 512:1024]))
        nag = p.op("vector", lambda e: e.bn_aggr(aggr[:], stat[:]), waits=[(sV, nsb)], inc=sV)
        nsq = p.op("scalar", lambda e: e.activation(out=scol[:], in_=aggr[:, 1:2],
                                                    func=AF.Sqrt, bias=epsc[:], scale=1.0),
                   waits=[(sV, nag)], inc=sS)
        nrcp = p.op("vector", lambda e: e.reciprocal(rcol[:], scol[:]), waits=[(sS, nsq)])
        if DBG and t == 0:
            p.op("vector", lambda e: e.tensor_copy(dbgs[:, 64:66], aggr[:, :]))
            p.op("vector", lambda e: e.tensor_copy(dbgs[:, 66:67], scol[:, :]))
            p.op("vector", lambda e: e.tensor_copy(dbgs[:, 67:68], rcol[:, :]),
                 waits=[(sV, nrcp)])
            p.op("vector", lambda e: e.tensor_copy(dbgs[:, 72:76], attn[0][:, 0:4]))
        napl2 = p.op("vector", lambda e, t=t: e.tensor_scalar(
            out=attn[t % 2][:], in0=attn[t % 2][:], scalar1=aggr[:, 0:1], scalar2=rcol[:],
            op0=ALU.subtract, op1=ALU.mult), waits=[(sV, nrcp)])
        wl = [(sX[t % 2], xre_ld), (sV, napl2)]
        if x2sp[t % 2]:
            wl.append((sGs[t % 2], x2sp[t % 2]))
        apply2[t] = p.op("vector", lambda e, t=t: e.tensor_add(
            x2f[t % 2][:], attn[t % 2][:], xre[t % 2][:]), waits=wl, inc=sV)
        x2sp[t % 2] = p.op("gpsimd", lambda e, t=t: e.dma_start(
            out=dx2[t * 128:(t + 1) * 128, :], in_=x2f[t % 2][:, :]),
            waits=[(sV, apply2[t])], inc=sGs[t % 2], amt=16)
        p.op("vector", lambda e, t=t: e.bn_stats(stat[:, 0:6], x2f[t % 2][:, 0:512]),
             waits=[(sV, apply2[t])])
        nsb = p.op("vector", lambda e, t=t: e.bn_stats(stat[:, 6:12], x2f[t % 2][:, 512:1024]))
        nag = p.op("vector", lambda e: e.bn_aggr(aggr[:], stat[:]), waits=[(sV, nsb)], inc=sV)
        nsq = p.op("scalar", lambda e: e.activation(out=scol[:], in_=aggr[:, 1:2],
                                                    func=AF.Sqrt, bias=epsc[:], scale=1.0),
                   waits=[(sV, nag)], inc=sS)
        nrcp2 = p.op("vector", lambda e: e.reciprocal(rcol[:], scol[:]), waits=[(sS, nsq)])
        if DBG and t == 0:
            p.op("vector", lambda e: e.tensor_copy(dbgs[:, 68:70], aggr[:, :]))
            p.op("vector", lambda e: e.tensor_copy(dbgs[:, 70:71], scol[:, :]))
            p.op("vector", lambda e: e.tensor_copy(dbgs[:, 71:72], rcol[:, :]),
                 waits=[(sV, nrcp2)])
        wl = [(sV, nrcp2)]
        if t >= 2:
            wl.append((sV, q_done[t - 2]))
        apply3[t] = p.op("vector", lambda e, t=t: e.tensor_scalar(
            out=x3f[t % 2][:], in0=x2f[t % 2][:], scalar1=aggr[:, 0:1], scalar2=rcol[:],
            op0=ALU.subtract, op1=ALU.mult), waits=wl, inc=sV)
        wl = [(sV, apply3[t])]
        if t >= 1:
            wl.append((sV, q_done[t - 1]))
        nred4 = p.op("vector", lambda e, t=t: e.reduce_max(
            sxc[:, :], x3f[t % 2][:, :], axis=AXX, apply_absolute_value=True), waits=wl)
        nclp = p.op("vector", lambda e: e.tensor_scalar(out=sxc[:], in0=sxc[:], scalar1=1e-5,
                                                        scalar2=None, op0=ALU.max),
                    waits=[(sV, nred4)])
        nrs = p.op("vector", lambda e: e.reciprocal(sxic[:], sxc[:]), waits=[(sV, nclp)])
        nm127 = p.op("vector", lambda e: e.tensor_scalar(out=sxic[:], in0=sxic[:],
                                                         scalar1=127.0, scalar2=None,
                                                         op0=ALU.mult), waits=[(sV, nrs)])
        nq16t = p.op("vector", lambda e, t=t: e.tensor_scalar(
            out=q16[:, :], in0=x3f[t % 2][:], scalar1=sxic[:], scalar2=RC,
            op0=ALU.mult, op1=ALU.add), waits=[(sV, nm127)])
        nsxf = p.op("vector", lambda e: e.tensor_scalar(out=sxc[:], in0=sxc[:],
                                                        scalar1=1.0 / 127.0,
                                                        scalar2=None, op0=ALU.mult))
        if DBG:
            p.op("vector", lambda e, t=t: e.tensor_copy(dbgs[:, t:t + 1], sxc[:, :]),
                 waits=[(sV, nsxf)])
            p.op("vector", lambda e, t=t: e.tensor_copy(dbgs[:, 16 + t:17 + t], sxic[:, :]))
        wl = [(sV, nq16t), (sV, nsxf)]
        if t >= 2:
            wl.append((sS, x3ev[t - 2]))
        q_done[t] = p.op("vector", lambda e, t=t: e.tensor_scalar(
            out=x3q[t % 2][:], in0=q16[:, :], scalar1=RD, scalar2=sxc[:],
            op0=ALU.subtract, op1=ALU.mult), waits=wl, inc=sV)

        def emit_x3t(tp):
            for c in range(NC):
                wl = []
                if c == 0:
                    wl.append((sV, q_done[tp]))
                    if tp >= 2:
                        wl.append((sS, x3ev[tp - 2]))
                x3tr[tp] = p.op("tensor", lambda e, tp=tp, c=c: e.transpose(
                    ptrX[tp % 2][:, c * 128:(c + 1) * 128],
                    x3q[tp % 2][:, c * 128:(c + 1) * 128], iden[:]),
                    waits=wl, inc=sT if c == NC - 1 else None)
            wl = [(sT, x3tr[tp])]
            if x3sp[tp % 2]:
                wl.append((sSp[tp % 2], x3sp[tp % 2]))
            x3ev[tp] = p.op("scalar", lambda e, tp=tp: e.copy(
                x3st[tp % 2][:, :], ptrX[tp % 2][:, :]), waits=wl, inc=sS)
            x3sp[tp % 2] = p.op("sync", lambda e, tp=tp: e.dma_start(
                out=dx3t[:, :, 1 + tp * 128:1 + (tp + 1) * 128].transpose([1, 0, 2]),
                in_=x3st[tp % 2][:, :]), waits=[(sS, x3ev[tp])],
                inc=sSp[tp % 2], amt=16)

        if t == NT - 1:
            emit_x3t(t - 1)
            emit_x3t(t)
    p.op("sync", lambda e: e.nop(), waits=[(sSp[0], sSp[0].n), (sSp[1], sSp[1].n),
                                           (sX[0], sX[0].n), (sX[1], sX[1].n)])
    p.op("gpsimd", lambda e: e.nop(), waits=[(sGs[0], sGs[0].n), (sGs[1], sGs[1].n),
                                             (sGq, sGq.n)])
    run_block(nc, p, hwm, "b4_oproj")
    free(*[f"wo_{c}" for c in range(NC)], "B1all",
         *[f"B2_{t}" for t in range(NT)],
         "bvB", "boB",
         "attn_0", "attn_1", "x2f_0", "x2f_1", "x3f_0", "x3f_1", "xre_0", "xre_1",
         "q16", "x3q_0", "x3q_1", "x3st_0", "x3st_1", "sxc", "sxic",
         "ptrO_0", "ptrO_1", "ptrX_0", "ptrX_1", "po4_0", "po4_1")

    diag = alloc("diag", [128, 3 * HCH * 128], BF16)
    w1h = [alloc(f"w1h_{i}", [128, 4 * 1024], BF16) for i in range(2)]
    w2b = [alloc(f"w2b_{i}", [128, 4 * 1024], BF16) for i in range(2)]
    x3w = [alloc(f"x3w_{i}", [128, 8 * 256], BF16) for i in range(3)]
    Ab = [alloc(f"Ab_{i}", [128, HCH * 258], BF16) for i in range(2)]
    H2 = alloc("H2", [128, HCH * 256], F32)
    H2q = [alloc(f"H2q_{i}", [128, HCH * 256], BF16) for i in range(2)]
    tland = [alloc(f"tland_{i}", [128, 256], F32) for i in range(2)]
    q16b = [alloc(f"q16b_{i}", [128, 256], I16) for i in range(2)]
    apar = alloc("apar", [128, 4 * 256], F32)
    amaxC = alloc("amaxC", [128, 256], F32)
    v2 = alloc("v2", [128, 4], F32)
    sxr = alloc("sxr", [1, 256], F32)
    sxB = alloc("sxB", [128, 256], F32)
    sc2b = [alloc(f"sc2b_{i}", [128, 2], F32) for i in range(2)]
    out2 = [alloc(f"out2_{i}", [128, D], F32) for i in range(2)]
    x4b = [alloc(f"x4b_{i}", [128, D], F32) for i in range(2)]
    x2re = [alloc(f"x2re_{i}", [128, D], BF16) for i in range(2)]
    pA = [alloc(f"pA_{i}", [128, 512], F32, "psum") for i in range(2)]
    pC = alloc("pC", [128, 512], F32, "psum")
    pM = [alloc(f"pM_{i}", [128, 512], F32, "psum") for i in range(4)]
    pX = alloc("pX", [128, 512], F32, "psum")

    p = Plan()
    dg_ld = p.op("gpsimd", lambda e: e.dma_start(
        out=diag[:, :], in_=ddiag[:, :]), inc=sG, amt=16)

    def dg(k, c):
        return diag[:, (k * HCH + c) * 128:(k * HCH + c) * 128 + 128]

    def ab(s, c):
        return Ab[s % 2][:, c * 258:(c + 1) * 258]

    x3_ld = [0] * (NS + 2)
    a_mm = [[0] * HCH for _ in range(NS + 2)]
    a_ev = [[0] * HCH for _ in range(NS + 2)]
    halo_d = [[0] * HCH for _ in range(NS + 2)]
    conv_mm = [[0] * HCH for _ in range(NS)]
    silu_d = [[0] * HCH for _ in range(NS)]
    ntp_d = [[0] * HCH for _ in range(NS)]
    h2q_d = [[0] * HCH for _ in range(NS)]
    apar_d = [[0] * 4 for _ in range(NS)]
    mm2_last = [0] * NS
    w1_ld = {}
    w1_used = {}
    pa_ev = [0, 0]
    pc_ev = [0]
    pm_ev = [0, 0, 0, 0]
    w2_used = [0, 0]
    x2re_used = [0, 0]
    out_dma = [0, 0]
    vcomb = [0]
    sxB_ev = [0]
    sc2_d = [0, 0]
    trans_last = [0]
    ktrans_last = [0]

    def emit_x3ld(s2):
        par = s2 % 3
        if s2 == NS:
            src = dx3t[:, :, s2 * TS:s2 * TS + 2].transpose([1, 0, 2])
            dst = x3w[par].ap().rearrange("p (c n) -> p c n", n=256)[:, :, 0:2]
        else:
            src = dx3t[:, :, s2 * TS:s2 * TS + 256].transpose([1, 0, 2])
            dst = x3w[par][:, :]
        wl = [(sT, a_mm[s2 - 3][HCH - 1])] if s2 >= 3 else []
        if s2 >= 2:
            wl.append((sX[s2 % 2], x3_ld[s2 - 2]))
        x3_ld[s2] = p.op("sync", lambda e, src=src, dst=dst: e.dma_start(out=dst, in_=src),
                         waits=wl, inc=sX[s2 % 2], amt=16)

    def emit_w1ld(s2, q):
        j = s2 * 8 + q
        wl = [(sT, w1_used[j - 2])] if (j - 2) in w1_used else []
        src = dw1b[q, :, :]
        w1_ld[j] = p.op("gpsimd", lambda e, src=src, j=j: e.dma_start(
            out=w1h[j % 2][:, :], in_=src), waits=wl, inc=sGw[j % 2], amt=16)

    def emit_A_chunk(s2, hcx):
        par = s2 % 3
        width = 2 if s2 == NS else 256
        slotp = hcx % 2
        j = s2 * 8 + hcx // 4
        hs2 = hcx % 4
        wl = [(sX[s2 % 2], x3_ld[s2]), (sGw[j % 2], w1_ld[j])]
        if pa_ev[slotp]:
            wl.append((sS, pa_ev[slotp]))
        last = 0
        for d in range(NC):
            if s2 == NS:
                mv = x3w[par].ap().rearrange("p (c n) -> p c n", n=256)[:, d, 0:2]
            else:
                mv = x3w[par][:, d * 256:(d + 1) * 256]
            last = p.op("tensor", lambda e, j=j, hs2=hs2, d=d, slotp=slotp, mv=mv, width=width: e.matmul(
                pA[slotp][:, 0:width],
                w1h[j % 2][:, d * 512 + hs2 * 128:d * 512 + hs2 * 128 + 128],
                mv, start=(d == 0), stop=(d == NC - 1)),
                waits=wl if d == 0 else [],
                inc=sT if d == NC - 1 else None)
        a_mm[s2][hcx] = last
        w1_used[j] = last
        wl = [(sT, last)]
        if s2 >= 2:
            wl.append((sT, conv_mm[s2 - 2][hcx]))
        pa_ev[slotp] = p.op("scalar", lambda e, s2=s2, hcx=hcx, slotp=slotp, width=width: e.activation(
            out=ab(s2, hcx)[:, 0:width], in_=pA[slotp][:, 0:width],
            func=AF.Identity, bias=0.0, scale=mwc[:, 0:1]), waits=wl, inc=sS)
        a_ev[s2][hcx] = pa_ev[slotp]
        if s2 >= 1:
            halo_d[s2 - 1][hcx] = p.op("scalar", lambda e, s2=s2, hcx=hcx: e.copy(
                ab(s2 - 1, hcx)[:, 256:258], ab(s2, hcx)[:, 0:2]),
                waits=[(sS, a_ev[s2][hcx])], inc=sS)

    def emit_conv_chunk(s, hcx):
        wl = [(sS, halo_d[s][hcx]), (sG, dg_ld)]
        if pc_ev[0]:
            wl.append((sS, pc_ev[0]))
        if s >= 1:
            wl.append((sV, ntp_d[s - 1][hcx]))
            wl.append((sV, apar_d[s - 1][hcx // 8]))
        last = 0
        for k in range(3):
            last = p.op("tensor", lambda e, s=s, hcx=hcx, k=k: e.matmul(
                pC[:, 0:256], dg(k, hcx), ab(s, hcx)[:, k:k + 256],
                start=(k == 0), stop=(k == 2)),
                waits=wl if k == 0 else [],
                inc=sT if k == 2 else None)
        conv_mm[s][hcx] = last
        silu_d[s][hcx] = p.op("scalar", lambda e, s=s, hcx=hcx: e.activation(
            out=H2[:, hcx * 256:(hcx + 1) * 256], in_=pC[:, 0:256],
            func=AF.Silu, bias=cbs[:, hcx:hcx + 1], scale=1.0),
            waits=[(sT, last)], inc=sS)
        pc_ev[0] = silu_d[s][hcx]

    def emit_mm2(sm):
        par = sm % 2
        x2_ld = [0, 0]
        for tsl in range(2):
            t = sm * 2 + tsl
            wl = [(sV, x2re_used[tsl])] if x2re_used[tsl] else []
            x2_ld[tsl] = p.op("sync", lambda e, t=t, tsl=tsl: e.dma_start(
                out=x2re[tsl][:, :], in_=dx2[t * 128:(t + 1) * 128, :]),
                waits=wl, inc=sX2[tsl], amt=16)
        last = 0
        for cc in range(0, HCH, 4):
            b = (cc // 4) % 2
            wl = [(sT, w2_used[b])] if w2_used[b] else []
            w2_ld = p.op("sync", lambda e, cc=cc, b=b: e.dma_start(
                out=w2b[b][:, :], in_=dw2b[:, cc * 1024:(cc + 4) * 1024]),
                waits=wl, inc=sW2[b], amt=16)
            for hcx in range(cc, cc + 4):
                qq = hcx % 4
                for tsl in range(2):
                    for ofs in range(2):
                        slot = tsl * 2 + ofs
                        wl = []
                        if hcx == cc and tsl == 0 and ofs == 0:
                            wl.append((sW2[b], w2_ld))
                            wl.append((sV, h2q_d[sm][cc + 3]))
                        if hcx == 0 and pm_ev[slot]:
                            wl.append((sV, pm_ev[slot]))
                        fin = (hcx == cc + 3 and tsl == 1 and ofs == 1)
                        last = p.op("tensor", lambda e, sm=sm, hcx=hcx, tsl=tsl, ofs=ofs, b=b, qq=qq, slot=slot: e.matmul(
                            pM[slot][:, :],
                            H2q[sm % 2][:, hcx * 256 + tsl * 128:hcx * 256 + tsl * 128 + 128],
                            w2b[b][:, qq * 1024 + ofs * 512:qq * 1024 + ofs * 512 + 512],
                            start=(hcx == 0), stop=(hcx == HCH - 1)),
                            waits=wl, inc=sT if fin else None)
            w2_used[b] = last
        mm2_last[sm] = last
        for tsl in range(2):
            for ofs in range(2):
                slot = tsl * 2 + ofs
                wl = [(sT, mm2_last[sm]), (sV, sc2_d[par])]
                if x2re_used[tsl]:
                    wl.append((sV, x2re_used[tsl]))
                pm_ev[slot] = p.op("vector", lambda e, tsl=tsl, ofs=ofs, slot=slot, par=par: e.tensor_scalar(
                    out=out2[tsl][:, ofs * 512:(ofs + 1) * 512], in0=pM[slot][:, :],
                    scalar1=sc2b[par][:, tsl:tsl + 1], scalar2=None, op0=ALU.mult),
                    waits=wl, inc=sV)
            p.op("vector", lambda e, tsl=tsl: e.bn_stats(stat[:, 0:6], out2[tsl][:, 0:512]),
                 waits=[(sV, pm_ev[tsl * 2 + 1])])
            nsb = p.op("vector", lambda e, tsl=tsl: e.bn_stats(stat[:, 6:12],
                                                               out2[tsl][:, 512:1024]))
            nag = p.op("vector", lambda e: e.bn_aggr(aggr[:], stat[:]),
                       waits=[(sV, nsb)], inc=sV)
            nsq = p.op("scalar", lambda e: e.activation(out=scol[:], in_=aggr[:, 1:2],
                                                        func=AF.Sqrt, bias=epsc[:], scale=1.0),
                       waits=[(sV, nag)], inc=sS)
            nrcp5 = p.op("vector", lambda e: e.reciprocal(rcol[:], scol[:]),
                         waits=[(sS, nsq)])
            napl4 = p.op("vector", lambda e, tsl=tsl: e.tensor_scalar(
                out=out2[tsl][:], in0=out2[tsl][:], scalar1=aggr[:, 0:1], scalar2=rcol[:],
                op0=ALU.subtract, op1=ALU.mult), waits=[(sV, nrcp5)])
            wl = [(sX2[tsl], x2_ld[tsl]), (sV, napl4)]
            if out_dma[tsl]:
                wl.append((sOut[tsl], out_dma[tsl]))
            x2re_used[tsl] = p.op("vector", lambda e, tsl=tsl: e.tensor_add(
                x4b[tsl][:], out2[tsl][:], x2re[tsl][:]), waits=wl, inc=sV)
            t = sm * 2 + tsl
            out_dma[tsl] = p.op("sync", lambda e, t=t, tsl=tsl: e.dma_start(
                out=dout[t * 128:(t + 1) * 128, :], in_=x4b[tsl][:, :]),
                waits=[(sV, x2re_used[tsl])], inc=sOut[tsl], amt=16)

    emit_x3ld(0)
    emit_x3ld(1)
    emit_x3ld(2)
    for s2 in (0, 1):
        for hcx in range(HCH):
            if hcx % 4 == 0:
                emit_w1ld(s2, hcx // 4)
            emit_A_chunk(s2, hcx)

    for s in range(NS):
        s2 = s + 2
        if s + 3 <= NS:
            emit_x3ld(s + 3)
        for hcx in range(HCH):
            emit_conv_chunk(s, hcx)
            if s2 <= NS:
                if hcx % 4 == 0:
                    emit_w1ld(s2, hcx // 4)
                emit_A_chunk(s2, hcx)
            if hcx % 8 == 7:
                g3 = hcx // 8
                wl = [(sS, silu_d[s][hcx])]
                if g3 == 0 and vcomb[0]:
                    wl.append((sV, vcomb[0]))
                apar_d[s][g3] = p.op("vector", lambda e, g3=g3: e.reduce_max(
                    apar[:, g3 * 256:(g3 + 1) * 256],
                    H2.ap().rearrange("p (c n) -> p n c", c=HCH)[:, :, g3 * 8:(g3 + 1) * 8],
                    axis=AXX, apply_absolute_value=True), waits=wl, inc=sV)
        cA = p.op("vector", lambda e: e.tensor_tensor(
            amaxC[:, :], apar[:, 0:256], apar[:, 256:512], ALU.max),
            waits=[(sV, apar_d[s][1]), (sT, trans_last[0])], inc=sV)
        cB = p.op("vector", lambda e: e.tensor_tensor(
            apar[:, 768:1024], apar[:, 512:768], apar[:, 768:1024], ALU.max),
            waits=[(sV, apar_d[s][3])], inc=sV)
        cF = p.op("vector", lambda e: e.tensor_tensor(
            amaxC[:, :], amaxC[:, :], apar[:, 768:1024], ALU.max),
            waits=[(sV, cA), (sV, cB)], inc=sV)
        vcomb[0] = cF
        red = 0
        for tt in range(2):
            wl = [(sV, cF), (sV, sxB_ev[0])] if tt == 0 else [(sV, red)]
            ntr = p.op("tensor", lambda e, tt=tt: e.transpose(
                pX[:, 0:128], amaxC[:, tt * 128:(tt + 1) * 128], idenf[:]),
                waits=wl, inc=sT)
            red = p.op("vector", lambda e, tt=tt: e.reduce_max(
                v2[:, tt:tt + 1], pX[0:128, 0:128], axis=AXX),
                waits=[(sT, ntr)], inc=sV)
            trans_last[0] = ntr
        ncl5 = p.op("vector", lambda e: e.tensor_scalar(out=v2[:, 0:2], in0=v2[:, 0:2],
                                                        scalar1=1e-5, scalar2=None,
                                                        op0=ALU.max),
                    waits=[(sV, red), (sT, ktrans_last[0])], inc=sV)
        p.op("vector", lambda e: e.reciprocal(v2[:, 2:3], v2[:, 0:1]), waits=[(sV, ncl5)])
        nrB = p.op("vector", lambda e: e.reciprocal(v2[:, 3:4], v2[:, 1:2]))
        nm7 = p.op("vector", lambda e: e.tensor_scalar(out=v2[:, 2:4], in0=v2[:, 2:4],
                                                       scalar1=127.0, scalar2=None,
                                                       op0=ALU.mult),
                   waits=[(sV, nrB)], inc=sV)
        nv2 = p.op("vector", lambda e: e.tensor_scalar(out=v2[:, 0:2], in0=v2[:, 0:2],
                                                       scalar1=1.0 / 127.0, scalar2=None,
                                                       op0=ALU.mult), inc=sV)
        for tsl in range(2):
            sc2_d[s % 2] = p.op("vector", lambda e, tsl=tsl, s=s: e.tensor_tensor(
                sc2b[s % 2][:, tsl:tsl + 1], v2[:, tsl:tsl + 1], mwc[:, 1:2], ALU.mult),
                waits=[(sV, nv2)], inc=sV)
        prev_ev = 0
        for k in range(2):
            wl = [(sV, nm7)]
            if prev_ev:
                wl.append((sV, prev_ev))
            ntr = p.op("tensor", lambda e, k=k: e.transpose(
                pX[0:1, 128:256], v2[:, 2 + k:3 + k], idenf[:]), waits=wl, inc=sT)
            prev_ev = p.op("vector", lambda e, k=k: e.tensor_copy(
                sxr[0:1, k * 128:(k + 1) * 128], pX[0:1, 128:256]),
                waits=[(sT, ntr)], inc=sV)
            ktrans_last[0] = ntr
        nb = p.op("tensor", lambda e: e.matmul(pX[:, 256:512], ones128[:, :],
                                               sxr[0:1, :], start=True, stop=True),
                  waits=[(sV, prev_ev)], inc=sT)
        wl = [(sT, nb)]
        if s >= 1:
            wl.append((sV, ntp_d[s - 1][HCH - 1]))
        sxB_ev[0] = p.op("vector", lambda e: e.tensor_copy(sxB[:, :], pX[0:128, 256:512]),
                         waits=wl, inc=sV)
        for h0 in range(0, HCH, 2):
            ntl = [0, 0]
            nuu = [0, 0]
            for i in range(2):
                hcx = h0 + i
                wl = [(sV, sxB_ev[0])]
                if hcx >= 2:
                    wl.append((sV, h2q_d[s][hcx - 2]))
                ntl[i] = p.op("vector", lambda e, hcx=hcx, i=i: e.tensor_tensor(
                    tland[i][:, :], H2[:, hcx * 256:(hcx + 1) * 256], sxB[:, :], ALU.mult),
                    waits=wl, inc=sV)
                ntp_d[s][hcx] = ntl[i]
            for i in range(2):
                nuu[i] = p.op("vector", lambda e, i=i: e.tensor_scalar(
                    out=q16b[i][:, :], in0=tland[i][:, :],
                    scalar1=RC, scalar2=None, op0=ALU.add),
                    waits=[(sV, ntl[i])], inc=sV)
            for i in range(2):
                hcx = h0 + i
                wl = [(sV, nuu[i])]
                if s >= 2:
                    wl.append((sT, mm2_last[s - 2]))
                h2q_d[s][hcx] = p.op("vector", lambda e, s=s, hcx=hcx, i=i: e.tensor_scalar(
                    out=H2q[s % 2][:, hcx * 256:(hcx + 1) * 256], in0=q16b[i][:, :],
                    scalar1=RD, scalar2=None, op0=ALU.subtract), waits=wl, inc=sV)
        if s >= 1:
            emit_mm2(s - 1)
    emit_mm2(NS - 1)
    if DBG:
        p.op("sync", lambda e: e.dma_start(out=dbgS[:, :], in_=dbgs[:, :]), inc=sD, amt=16)
    p.op("sync", lambda e: e.nop(), waits=[
        (sOut[0], sOut[0].n), (sOut[1], sOut[1].n),
        (sX[0], sX[0].n), (sX[1], sX[1].n), (sX2[0], sX2[0].n), (sX2[1], sX2[1].n)])
    p.op("gpsimd", lambda e: e.nop(), waits=[
        (sG, sG.n), (sGw[0], sGw[0].n), (sGw[1], sGw[1].n),
        (sW2[0], sW2[0].n), (sW2[1], sW2[1].n)])
    run_block(nc, p, hwm, "b5_ffn")

    free(*list(cms))
    for s in (sP, sS, sV, sT, *sW2[::-1], *sGw[::-1], *sGs[::-1], sGk, sGq,
              *sOut[::-1], *sSp[::-1], *sW[::-1], *sX2[::-1], *sX[::-1], sG, sD):
        s._cm.__exit__(None, None, None)
    return nc


_NC_CACHE = {}


def _get_nc():
    key = "nc"
    if key not in _NC_CACHE:
        _NC_CACHE[key] = build_nc()
    return _NC_CACHE[key]


def kernel(L0_attn_x, L0_attn_wq, L0_attn_bq, L0_attn_wk, L0_attn_bk,
           L0_attn_wv, L0_attn_bv, L0_attn_wo, L0_attn_bo,
           L0_ffn_w1, L0_ffn_conv_w, L0_ffn_conv_b, L0_ffn_w2):
    x = np.ascontiguousarray(np.asarray(L0_attn_x, dtype=np.float32))
    B = x.shape[0]
    f32 = lambda a: np.ascontiguousarray(np.asarray(a, dtype=np.float32))
    wq, wk, wv, wo = f32(L0_attn_wq), f32(L0_attn_wk), f32(L0_attn_wv), f32(L0_attn_wo)
    bq = np.ascontiguousarray(f32(L0_attn_bq).reshape(NC, 128).T)
    bk = np.ascontiguousarray(f32(L0_attn_bk).reshape(NC, 128).T)
    bv = f32(L0_attn_bv).reshape(1, D)
    bo = f32(L0_attn_bo).reshape(1, D)
    w1t = np.ascontiguousarray(f32(L0_ffn_w1).T)
    w2t = np.ascontiguousarray(f32(L0_ffn_w2).T)
    cw = f32(L0_ffn_conv_w).reshape(DH, 3)
    diag3 = np.zeros((128, 3, HCH, 128), dtype=np.float32)
    idx = np.arange(128)
    for k in range(3):
        for c in range(HCH):
            diag3[idx, k, c, idx] = cw[c * 128 + idx, k]
    diag3 = diag3.reshape(128, 3 * HCH * 128)
    convb = np.ascontiguousarray(f32(L0_ffn_conv_b).reshape(HCH, 128).T)
    pp = np.arange(128)[:, None]
    ff = np.arange(384)[None, :]
    mask = ((pp <= ff) & (ff <= pp + 256)).astype(np.float32)
    maskbig = np.where(mask > 0, 0.0, -1e7).astype(np.float32)
    iden = np.eye(128, dtype=np.float32)
    hv = np.zeros((128, 2), dtype=np.float32)
    hv[:64, 0] = 1.0
    hv[64:, 1] = 1.0
    ones128 = np.ones((1, 128), dtype=np.float32)
    onescol = np.ones((128, 1), dtype=np.float32)

    nc = _get_nc()
    common = dict(wq=wq, wk=wk, wv=wv, wo=wo, bq=bq, bk=bk, bv=bv, bo=bo,
                  w1t=w1t, w2t=w2t, diag3=diag3, convb=convb, mask01=mask,
                  maskbig=maskbig, iden=iden, halves=hv, ones128=ones128,
                  onescol=onescol)
    in_maps = [dict(common, x=np.ascontiguousarray(x[i])) for i in range(B)]
    res = run_bass_kernel_spmd(nc, in_maps, core_ids=list(range(B)))
    out = np.stack([np.asarray(r["out"], dtype=np.float32) for r in res.results], axis=0)
    return out


# version 22 (final)
# speedup vs baseline: 1.1848x; 1.0141x over previous
import os
import numpy as np
import concourse.bass as bass
import concourse.mybir as mybir
from concourse.bass_utils import run_bass_kernel_spmd

F32 = mybir.dt.float32
BF16 = mybir.dt.bfloat16
I16 = mybir.dt.int16
I8 = mybir.dt.int8
AF = mybir.ActivationFunctionType
ALU = mybir.AluOpType
AXX = mybir.AxisListType.X

_SIM = os.environ.get("BASS_SIM_FP64") == "1"
RC = 127.5 if _SIM else 128.0
RD = 127.0 if _SIM else 128.0

T = 2048
D = 1024
NT = T // 128
NC = D // 128
H = 16
NJ = 16
DH = 4096
HCH = 32
TS = 256
NS = T // TS
EPS = 1e-5

ENGS = ("sync", "tensor", "vector", "scalar", "gpsimd")


class Sem:
    def __init__(self, nc, name):
        self._cm = nc.semaphore(name)
        self.h = self._cm.__enter__()
        self.n = 0
        self.name = name


class Plan:
    def __init__(self):
        self.items = {e: [] for e in ENGS}
        self.incs = {e: {} for e in ENGS}

    AUTO = {}

    def op(self, eng, emit, waits=(), inc=None, amt=1):
        w = [(s, v) for (s, v) in waits if v and v > 0]
        if inc is None and eng in self.AUTO:
            inc = self.AUTO[eng]
            amt = 1
        if inc is not None:
            inc.n += amt
            self.incs[eng][inc.name] = inc
        self.items[eng].append((w, emit, inc, amt))
        return inc.n if inc is not None else None

    def replay(self, eng, e, hwm):
        for w, emit, inc, amt in self.items[eng]:
            for s, v in w:
                if hwm[eng].get(s.name, 0) < v:
                    e.wait_ge(s.h, v)
                    hwm[eng][s.name] = v
            ins = emit(e)
            if inc is not None:
                ins.then_inc(inc.h, amt)
        for s in self.incs[eng].values():
            if hwm[eng].get(s.name, 0) < s.n:
                e.wait_ge(s.h, s.n)
                hwm[eng][s.name] = s.n


PHASES = []
_PHTOT = {e: 0 for e in ENGS}


def run_block(nc, plan, hwm, phase=""):
    for e in ENGS:
        _PHTOT[e] += len(plan.items[e])
    PHASES.append((phase, dict(_PHTOT)))
    with nc.Block() as blk:
        @blk.sync
        def _(e):
            plan.replay("sync", e, hwm)

        @blk.tensor
        def _(e):
            plan.replay("tensor", e, hwm)

        @blk.vector
        def _(e):
            plan.replay("vector", e, hwm)

        @blk.scalar
        def _(e):
            plan.replay("scalar", e, hwm)

        @blk.gpsimd
        def _(e):
            plan.replay("gpsimd", e, hwm)


def build_nc():
    PHASES.clear()
    for e in ENGS:
        _PHTOT[e] = 0
    nc = bass.Bass("TRN2", target_bir_lowering=False, debug=False,
                   detect_race_conditions=False)

    dx = nc.dram_tensor("x", [T, D], F32, kind="ExternalInput").ap()
    dw = {k: nc.dram_tensor(k, [D, D], F32, kind="ExternalInput").ap()
          for k in ("wq", "wk", "wv", "wo")}
    dbq = nc.dram_tensor("bq", [128, NC], F32, kind="ExternalInput").ap()
    dbk = nc.dram_tensor("bk", [128, NC], F32, kind="ExternalInput").ap()
    dbv = nc.dram_tensor("bv", [1, D], F32, kind="ExternalInput").ap()
    dbo = nc.dram_tensor("bo", [1, D], F32, kind="ExternalInput").ap()
    dw1t = nc.dram_tensor("w1t", [D, DH], F32, kind="ExternalInput").ap()
    dw2t = nc.dram_tensor("w2t", [DH, D], F32, kind="ExternalInput").ap()
    ddiag = nc.dram_tensor("diag3", [128, 3 * HCH * 128], F32, kind="ExternalInput").ap()
    dcb = nc.dram_tensor("convb", [128, HCH], F32, kind="ExternalInput").ap()
    dmask = nc.dram_tensor("mask01", [128, 384], F32, kind="ExternalInput").ap()
    dmaskb = nc.dram_tensor("maskbig", [128, 384], F32, kind="ExternalInput").ap()
    diden = nc.dram_tensor("iden", [128, 128], F32, kind="ExternalInput").ap()
    dhalves = nc.dram_tensor("halves", [128, 2], F32, kind="ExternalInput").ap()
    dones128 = nc.dram_tensor("ones128", [1, 128], F32, kind="ExternalInput").ap()
    donescol = nc.dram_tensor("onescol", [128, 1], F32, kind="ExternalInput").ap()
    dout = nc.dram_tensor("out", [T, D], F32, kind="ExternalOutput").ap()
    DBG = os.environ.get("KDEBUG") == "1"
    iknd = "ExternalOutput" if DBG else "Internal"
    dx2 = nc.dram_tensor("x2sp", [T, D], BF16, kind=iknd).ap()
    dw1b = nc.dram_tensor("w1b16", [8, 128, 4096], BF16, kind=iknd).ap()
    dw2b = nc.dram_tensor("w2b16", [128, HCH * 1024], BF16, kind=iknd).ap()
    dx3t = nc.dram_tensor("x3tsp", [NC, 128, T + 2], BF16, kind=iknd).ap()
    dbgO = (nc.dram_tensor("dbgO", [T, D], BF16, kind="ExternalOutput").ap()
            if DBG else None)
    dbgS = (nc.dram_tensor("dbgS", [128, 80], F32, kind="ExternalOutput").ap()
            if DBG else None)

    sD = Sem(nc, "sD")
    sG = Sem(nc, "sG")
    sX = [Sem(nc, "sX0"), Sem(nc, "sX1")]
    sX2 = [Sem(nc, "sX2_0"), Sem(nc, "sX2_1")]
    sX3 = [Sem(nc, f"sX3_{i}") for i in range(3)]
    sW = [Sem(nc, f"sW{i}") for i in range(6)]
    sSp = [Sem(nc, f"sSp{i}") for i in range(4)]
    sOut = [Sem(nc, "sOut0"), Sem(nc, "sOut1")]
    sGq = Sem(nc, "sGq")
    sGk = Sem(nc, "sGk")
    sGs = [Sem(nc, "sGs0"), Sem(nc, "sGs1")]
    sGw = [Sem(nc, "sGw0"), Sem(nc, "sGw1")]
    sW2 = [Sem(nc, "sW2_0"), Sem(nc, "sW2_1")]
    sT = Sem(nc, "sT")
    sV = Sem(nc, "sV")
    sS = Sem(nc, "sS")
    sP = Sem(nc, "sP")
    Plan.AUTO = {"vector": sV, "scalar": sS, "gpsimd": sP}
    hwm = {e: {} for e in ENGS}
    cms = {}
    aseq = {"n": 0}

    def alloc(name, shape, dt, space="sbuf", side=None):
        cm = (nc.sbuf_tensor(name, shape, dt, side=side) if space == "sbuf"
              else nc.psum_tensor(name, shape, dt))
        h = cm.__enter__()
        aseq["n"] += 1
        cms[name] = (aseq["n"], cm)
        return h

    def free(*names):
        for nm in sorted(names, key=lambda nm: -cms[nm][0]):
            cms.pop(nm)[1].__exit__(None, None, None)

    bqs = alloc("bqs", [128, NC], F32)
    bks = alloc("bks", [128, NC], F32)
    bvB = alloc("bvB", [128, D], F32, side="right")
    boB = alloc("boB", [128, D], F32, side="right")
    bvrow = alloc("bvrow", [1, D], F32, side="right")
    borow = alloc("borow", [1, D], F32, side="right")
    cbs = alloc("cbs", [128, HCH], F32)
    mask01 = alloc("sb_mask01", [128, 384], BF16)
    maskB = alloc("sb_maskB", [128, 384], BF16)
    iden = alloc("sb_iden", [128, 128], BF16)
    idenf = alloc("sb_idenf", [128, 128], F32)
    halves = alloc("sb_halves", [128, 2], F32)
    ones128 = alloc("sb_ones128", [1, 128], F32)
    onescol = alloc("sb_onescol", [128, 1], F32)
    epsc = alloc("epsc", [128, 1], F32)
    rcc = alloc("rcc", [128, 1], F32)
    negC = alloc("negC", [128, H], F32)
    eC = alloc("eC", [128, H], F32)
    mwc = alloc("mwc", [128, 2], F32)
    swc = alloc("swc", [128, 2], F32)
    stat = alloc("stat", [128, 12], F32)
    aggr = alloc("aggr", [128, 2], F32)
    scol = alloc("scol", [128, 1], F32)
    rcol = alloc("rcol", [128, 1], F32)
    wsum = alloc("wsum", [128, 64], F32)
    mrow = alloc("mrow", [1, 64], F32)
    mtot = alloc("mtot", [1, 2], F32)
    zrow = alloc("zrow", [128, 1], BF16)
    qmx = alloc("qmx", [128, NC], F32)
    kmx = alloc("kmx", [128, NC], F32)
    qkp = alloc("qkp", [128, NC], F32)
    c2x8 = alloc("c2x8", [2, NC], F32)
    crow = alloc("crow", [1, 16], F32)
    dbgs = alloc("dbgs", [128, 80], F32)

    p = Plan()
    for (dst, src) in ((bqs, dbq), (bks, dbk), (bvrow, dbv), (borow, dbo),
                       (cbs, dcb), (halves, dhalves), (ones128, dones128),
                       (onescol, donescol), (idenf, diden)):
        p.op("sync", lambda e, d=dst, s=src: e.dma_start(out=d[:], in_=s[:, :]),
             inc=sD, amt=16)
    p.op("gpsimd", lambda e: e.dma_start(out=mask01[:], in_=dmask[:, :]), inc=sG, amt=16)
    p.op("gpsimd", lambda e: e.dma_start(out=maskB[:], in_=dmaskb[:, :]), inc=sG, amt=16)
    p.op("gpsimd", lambda e: e.dma_start(out=iden[:], in_=diden[:, :]), inc=sG, amt=16)
    p.op("vector", lambda e: e.memset(epsc[:], EPS), inc=sV)
    p.op("vector", lambda e: e.memset(rcc[:], RC), inc=sV)
    nzr = p.op("vector", lambda e: e.memset(zrow[:], 0.0), inc=sV)
    for c in range(NC):
        p.op("sync", lambda e, c=c: e.dma_start(out=dx3t[c, :, 0:1], in_=zrow[:, 0:1],
             allow_slow_non_contiguous=True), waits=[(sV, nzr)], inc=sD, amt=16)
        p.op("sync", lambda e, c=c: e.dma_start(out=dx3t[c, :, T + 1:T + 2], in_=zrow[:, 0:1],
             allow_slow_non_contiguous=True), inc=sD, amt=16)
    p.op("sync", lambda e: e.nop(), waits=[(sD, sD.n)])
    p.op("gpsimd", lambda e: e.nop(), waits=[(sG, sG.n)])
    run_block(nc, p, hwm, "b0_const")

    B1all = alloc("B1all", [128, NC * T], BF16)
    B1 = [B1all[:, c * T:(c + 1) * T] for c in range(NC)]
    B1r3 = B1all.ap().rearrange("p (c t) -> p c t", t=T)
    QT = [alloc(f"QT{c}", [128, T], BF16) for c in range(NC)]
    KT = [alloc(f"KT{c}", [128, T], BF16) for c in range(NC)]
    VA = [alloc(f"VA{j}", [128, H * 65], BF16) for j in range(NJ + 2)]

    def vaug(j):
        return VA[j + 1]

    wsb = [[alloc(f"wsb{s}_{c}", [128, D], BF16) for c in range(NC)] for s in range(2)]
    xin = [alloc(f"xin_{i}", [128, D], F32) for i in range(2)]
    x1b = [alloc(f"x1b_{i}", [128, D], BF16) for i in range(2)]
    ptr = [alloc(f"ptr_{i}", [128, 1024], BF16, "psum") for i in range(2)]
    pbc = [alloc(f"pbc_{i}", [128, 512], F32, "psum") for i in range(2)]

    p = Plan()
    wq_ld = [0] * NC
    wk_ld = [0] * NC
    wv_ld = [0] * NC
    for c in range(NC):
        wq_ld[c] = p.op("gpsimd", lambda e, c=c: e.dma_start(
            out=wsb[0][c][:], in_=dw["wq"][c * 128:(c + 1) * 128, :]), inc=sGq, amt=16)
    for c in range(NC):
        wk_ld[c] = p.op("gpsimd", lambda e, c=c: e.dma_start(
            out=wsb[1][c][:], in_=dw["wk"][c * 128:(c + 1) * 128, :]), inc=sGk, amt=16)
    for j in (-1, NJ):
        p.op("vector", lambda e, j=j: e.memset(
            vaug(j).ap().rearrange("p (h c) -> p h c", c=65)[:, :, 0:64], 0.0))
    vinit = 0
    for j in range(-1, NJ + 1):
        vinit = p.op("vector", lambda e, j=j: e.memset(
            vaug(j).ap().rearrange("p (h c) -> p h c", c=65)[:, :, 64:65], 1.0), inc=sV)

    bc_ev = 0
    for i, (dst, src) in enumerate(((bvB, bvrow), (bvB, bvrow), (boB, borow), (boB, borow))):
        off = (i % 2) * 512
        wl = [(sV, bc_ev)] if i >= 2 else []
        n = p.op("tensor", lambda e, s=src, off=off, i=i: e.matmul(
            pbc[i % 2][:, :], ones128[:, :], s[0:1, off:off + 512], start=True, stop=True),
            waits=wl, inc=sT)
        bc_ev = p.op("vector", lambda e, d=dst, off=off, i=i: e.tensor_copy(
            d[:, off:off + 512], pbc[i % 2][:, :]), waits=[(sT, n)], inc=sV)

    ln_apply = [0] * NT
    tr_done = [0] * NT
    ev_done = [0] * NT
    sq_done = [0] * NT
    for t in range(NT):
        wl = [(sV, ln_apply[t - 2])] if t >= 2 else []
        xload = p.op("sync", lambda e, t=t: e.dma_start(
            out=xin[t % 2][:], in_=dx[t * 128:(t + 1) * 128, :]), waits=wl,
            inc=sX[t % 2], amt=16)
        wl = [(sX[t % 2], xload)]
        if t >= 1:
            wl.append((sS, sq_done[t - 1]))
        p.op("vector", lambda e, t=t: e.bn_stats(stat[:, 0:6], xin[t % 2][:, 0:512]),
             waits=wl)
        nsb = p.op("vector", lambda e, t=t: e.bn_stats(stat[:, 6:12], xin[t % 2][:, 512:1024]))
        nag = p.op("vector", lambda e: e.bn_aggr(aggr[:], stat[:]), waits=[(sV, nsb)], inc=sV)
        sq_done[t] = p.op("scalar", lambda e: e.activation(
            out=scol[:], in_=aggr[:, 1:2], func=AF.Sqrt, bias=epsc[:], scale=1.0),
            waits=[(sV, nag)], inc=sS)
        nrc1 = p.op("vector", lambda e: e.reciprocal(rcol[:], scol[:]),
                    waits=[(sS, sq_done[t])])
        wl = [(sV, nrc1)]
        if t >= 2:
            wl.append((sT, tr_done[t - 2]))
        ln_apply[t] = p.op("vector", lambda e, t=t: e.tensor_scalar(
            out=x1b[t % 2][:], in0=xin[t % 2][:], scalar1=aggr[:, 0:1], scalar2=rcol[:],
            op0=ALU.subtract, op1=ALU.mult), waits=wl, inc=sV)
        for c in range(NC):
            wl = []
            if c == 0:
                wl.append((sV, ln_apply[t]))
                if t >= 2:
                    wl.append((sS, ev_done[t - 2]))
            tr_done[t] = p.op("tensor", lambda e, t=t, c=c: e.transpose(
                ptr[t % 2][:, c * 128:(c + 1) * 128],
                x1b[t % 2][:, c * 128:(c + 1) * 128], iden[:]),
                waits=wl, inc=sT if c == NC - 1 else None)
        ev_done[t] = p.op("scalar", lambda e, t=t: e.copy(
            B1r3[:, :, t * 128:(t + 1) * 128],
            ptr[t % 2].ap().rearrange("p (c q) -> p c q", q=128)[:, :, :]),
            waits=[(sT, tr_done[t])], inc=sS)
    run_block(nc, p, hwm, "b1_ln1")
    free("xin_0", "xin_1", "x1b_0", "x1b_1", "ptr_0", "ptr_1", "pbc_0", "pbc_1")
    free("bvrow", "borow")

    wstage = [alloc(f"wstage_{i}", [128, 1024], F32) for i in range(6)]
    pq = [alloc(f"pq_{i}", [128, 512], F32, "psum") for i in range(4)]
    pms = alloc("pms", [128, 512], F32, "psum")

    p = Plan()

    red_done = []
    piece_srcs = []
    for ch in range(NC):
        for h4 in range(4):
            piece_srcs.append((dw1t[ch * 128:(ch + 1) * 128, h4 * 1024:(h4 + 1) * 1024], 1024))
    for ch in range(HCH):
        piece_srcs.append((dw2t[ch * 128:(ch + 1) * 128, :], 1024))

    def emit_piece(idx):
        src, width = piece_srcs[idx]
        slot = idx % 6
        wl = [(sV, red_done[idx - 6])] if idx >= 6 else []
        ld = p.op("sync", lambda e, src=src, slot=slot, width=width: e.dma_start(
            out=wstage[slot][:, 0:width], in_=src), waits=wl, inc=sW[slot], amt=16)
        red_done.append(p.op("vector", lambda e, slot=slot, idx=idx, width=width: e.reduce_sum(
            wsum[:, idx:idx + 1], wstage[slot][:, 0:width], axis=AXX,
            apply_absolute_value=True), waits=[(sW[slot], ld)], inc=sV))

    NPC1 = 40
    for idx in range(NPC1):
        emit_piece(idx)

    def proj_qk(wset, wld, bias_s, OUT, slot_ev, evac_eng, wsem):
        for oc in range(NC):
            for st in range(4):
                slot = (oc * 4 + st) % 4
                wl = [(wsem, wld[NC - 1])]
                if slot_ev[slot]:
                    wl.append((evac_eng, slot_ev[slot]))
                last = 0
                for ic in range(NC):
                    last = p.op("tensor", lambda e, oc=oc, st=st, ic=ic, slot=slot, wset=wset: e.matmul(
                        pq[slot][:, :], wsb[wset][ic][:, oc * 128:(oc + 1) * 128],
                        B1[ic][:, st * 512:(st + 1) * 512],
                        start=(ic == 0), stop=(ic == NC - 1)),
                        waits=wl if ic == 0 else [],
                        inc=sT if ic == NC - 1 else None)
                slot_ev[slot] = p.op("scalar", lambda e, oc=oc, st=st, slot=slot, OUT=OUT, bias_s=bias_s: e.activation(
                    out=OUT[oc][:, st * 512:(st + 1) * 512], in_=pq[slot][:, :],
                    func=AF.Identity, bias=bias_s[:, oc:oc + 1], scale=1.0),
                    waits=[(sT, last)], inc=sS)
        return slot_ev

    slot_ev = [0, 0, 0, 0]
    slot_ev = proj_qk(0, wq_ld, bqs, QT, slot_ev, sS, sGq)
    for c in range(NC):
        wv_ld[c] = p.op("gpsimd", lambda e, c=c: e.dma_start(
            out=wsb[0][c][:], in_=dw["wv"][c * 128:(c + 1) * 128, :]),
            waits=[(sS, max(slot_ev))], inc=sGq, amt=16)
    slot_ev = proj_qk(1, wk_ld, bks, KT, slot_ev, sS, sGk)

    prev_s_ev = list(slot_ev)
    slot_vv = [0, 0, 0, 0]
    nred = 0
    kev = max(slot_ev)

    def emit_qkmax(i):
        src = QT[i] if i < NC else KT[i - NC]
        dst = qmx if i < NC else kmx
        c = i % NC
        return p.op("vector", lambda e, src=src, dst=dst, c=c: e.reduce_max(
            dst[:, c:c + 1], src[:, :], axis=AXX, apply_absolute_value=True),
            waits=[(sS, kev)] if i == 0 else [], inc=sV)

    for t in range(NT):
        for ofs in range(2):
            g = t * 2 + ofs
            slot = g % 4
            wl = [(sGq, wv_ld[NC - 1]), (sV, bc_ev), (sV, vinit)]
            if prev_s_ev[slot]:
                wl.append((sS, prev_s_ev[slot]))
                prev_s_ev[slot] = 0
            if slot_vv[slot]:
                wl.append((sV, slot_vv[slot]))
            last = 0
            for ic in range(NC):
                last = p.op("tensor", lambda e, t=t, ofs=ofs, ic=ic, slot=slot: e.matmul(
                    pq[slot][:, :], B1[ic][:, t * 128:(t + 1) * 128],
                    wsb[0][ic][:, ofs * 512:(ofs + 1) * 512],
                    start=(ic == 0), stop=(ic == NC - 1)),
                    waits=wl if ic == 0 else [],
                    inc=sT if ic == NC - 1 else None)
            slot_vv[slot] = p.op("vector", lambda e, t=t, ofs=ofs, slot=slot: e.scalar_tensor_tensor(
                out=vaug(t).ap().rearrange("p (h c) -> p h c", c=65)[:, ofs * 8:(ofs + 1) * 8, 0:64],
                in0=pq[slot][:, :], scalar=1.0,
                in1=bvB[:, ofs * 512:(ofs + 1) * 512],
                op0=ALU.mult, op1=ALU.add),
                waits=[(sT, last)], inc=sV)
            if g % 2 == 1 and g // 2 < 2 * NC:
                nred = emit_qkmax(g // 2)

    for idx in range(NPC1, len(piece_srcs)):
        emit_piece(idx)

    n1 = p.op("tensor", lambda e: e.matmul(pms[0:1, 0:64], onescol[:, :], wsum[:, 0:64],
                                           start=True, stop=True),
              waits=[(sV, red_done[-1])], inc=sT)
    n2 = p.op("scalar", lambda e: e.copy(mrow[0:1, 0:64], pms[0:1, 0:64]),
              waits=[(sT, n1)], inc=sS)
    p.op("vector", lambda e: e.reduce_sum(mtot[0:1, 0:1], mrow[0:1, 0:32], axis=AXX),
         waits=[(sS, n2)])
    nr2 = p.op("vector", lambda e: e.reduce_sum(mtot[0:1, 1:2], mrow[0:1, 32:64], axis=AXX))
    n3 = p.op("vector", lambda e: e.tensor_scalar(out=mtot[:], in0=mtot[:],
                                             scalar1=1.0 / (D * DH), scalar2=1e-5,
                                             op0=ALU.mult, op1=ALU.max),
              waits=[(sV, nr2)], inc=sV)
    n4 = p.op("tensor", lambda e: e.matmul(pms[:, 0:2], ones128[:, :], mtot[0:1, :],
                                           start=True, stop=True),
              waits=[(sV, n3)], inc=sT)
    n5 = p.op("scalar", lambda e: e.copy(mwc[:], pms[0:128, 0:2]), waits=[(sT, n4)], inc=sS)
    p.op("vector", lambda e: e.reciprocal(swc[:, 0:1], mwc[:, 0:1]), waits=[(sS, n5)])
    n6 = p.op("vector", lambda e: e.reciprocal(swc[:, 1:2], mwc[:, 1:2]), inc=sV)
    if DBG:
        p.op("vector", lambda e: e.tensor_copy(dbgs[:, 32:34], mwc[:, :]),
             waits=[(sV, n6)])
        p.op("vector", lambda e: e.tensor_copy(dbgs[:, 34:36], swc[:, :]))
        p.op("vector", lambda e: e.tensor_copy(dbgs[0:1, 36:60], mrow[0:1, 24:48]))
        p.op("vector", lambda e: e.tensor_copy(dbgs[0:1, 60:62], mtot[0:1, :]))

    nq = p.op("vector", lambda e: e.tensor_mul(qkp[:], qmx[:], kmx[:]),
              waits=[(sV, nred)], inc=sV)
    nm = p.op("tensor", lambda e: e.matmul(pms[0:2, 0:8], halves[:, :], qkp[:, :],
                                           start=True, stop=True),
              waits=[(sV, nq), (sS, n5)], inc=sT)
    ne = p.op("scalar", lambda e: e.copy(c2x8[:], pms[0:2, 0:8]), waits=[(sT, nm)], inc=sS)
    nts = p.op("vector", lambda e: e.tensor_scalar(out=c2x8[:], in0=c2x8[:], scalar1=0.125,
                                                   scalar2=-40.0, op0=ALU.mult, op1=ALU.add),
               waits=[(sS, ne)])
    nrl = p.op("vector", lambda e: e.tensor_relu(c2x8[:], c2x8[:]), waits=[(sV, nts)])
    nr = p.op("vector", lambda e: e.tensor_scalar(out=c2x8[:], in0=c2x8[:], scalar1=-1.0,
                                                  scalar2=None, op0=ALU.mult),
              waits=[(sV, nrl)], inc=sV)
    nc1 = p.op("vector", lambda e: e.tensor_copy(crow[0:1, 0:8], c2x8[0:1, :]),
               waits=[(sV, nr)], inc=sV)
    nd = p.op("sync", lambda e: e.dma_start(out=crow[0:1, 8:16], in_=c2x8[1:2, :]),
              waits=[(sV, nr)], inc=sD, amt=16)
    nb1 = p.op("tensor", lambda e: e.matmul(pms[:, 0:8], ones128[:, :], crow[0:1, 0:8],
                                            start=True, stop=True),
               waits=[(sV, nc1), (sD, nd), (sS, ne)], inc=sT)
    nb1e = p.op("scalar", lambda e: e.copy(
        negC.ap().rearrange("p (a b) -> p a b", b=2)[:, :, 0:1], pms[0:128, 0:8]),
        waits=[(sT, nb1)], inc=sS)
    nb2 = p.op("tensor", lambda e: e.matmul(pms[:, 0:8], ones128[:, :], crow[0:1, 8:16],
                                            start=True, stop=True),
               waits=[(sS, nb1e)], inc=sT)
    nb2e = p.op("scalar", lambda e: e.copy(
        negC.ap().rearrange("p (a b) -> p a b", b=2)[:, :, 1:2], pms[0:128, 0:8]),
        waits=[(sT, nb2)], inc=sS)
    nec = p.op("scalar", lambda e: e.activation(out=eC[:], in_=negC[:], func=AF.Exp,
                                                bias=0.0, scale=1.0),
               waits=[(sS, nb2e)], inc=sS)


    run_block(nc, p, hwm, "b2_qkv")
    free(*[f"wsb{s}_{c}" for s in range(2) for c in range(NC)],
         "wstage_0", "wstage_1", "wstage_2", "wstage_3", "wstage_4", "wstage_5",
         "pq_0", "pq_1", "pq_2", "pq_3", "pms")

    B2 = [alloc(f"B2_{t}", [128, D], BF16, side="right") for t in range(NT)]
    expraw = [alloc(f"expraw_{i}", [128, 384], BF16) for i in range(3)]
    expm = [alloc(f"expm_{i}", [128, 384], BF16) for i in range(3)]
    rodc = [alloc(f"rodc_{i}", [128, 1], F32) for i in range(2)]
    psc = [alloc(f"psc_{i}", [128, 512], F32, "psum") for i in range(3)]
    pod = [alloc(f"pod_{i}", [128, 512], F32, "psum") for i in range(4)]

    p = Plan()
    wstage3 = [alloc(f"wstage3_{i}", [128, 1024], F32) for i in range(4)]
    tern_u16 = [alloc(f"tern_u16_{i}", [128, 1024], I16) for i in range(2)]
    tern2b = [alloc(f"tern2b_{i}", [128, 1024], BF16) for i in range(2)]
    tern_jobs = []
    for ch in range(NC):
        for q in range(4):
            tern_jobs.append((dw1t[ch * 128:(ch + 1) * 128, q * 1024:(q + 1) * 1024],
                              [(dw1b[q * 2 + half, :, ch * 512:(ch + 1) * 512],
                                half * 512, (half + 1) * 512) for half in range(2)], 0))
    for ch in range(HCH):
        tern_jobs.append((dw2t[ch * 128:(ch + 1) * 128, :],
                          [(dw2b[:, ch * 1024:(ch + 1) * 1024], 0, 1024)], 1))
    tstate = {"sp": [0, 0, 0, 0]}

    def tern_steps():
        for i0 in range(0, len(tern_jobs), 4):
            picks = [k for k in range(i0, min(i0 + 4, len(tern_jobs)))]
            lds, n1s, n2s = {}, {}, {}
            for k in picks:
                srcA, dst, wi = tern_jobs[k]
                slot = k % 4
                wl = [(sV, n6)]
                if tstate["sp"][slot]:
                    wl.append((sSp[slot], tstate["sp"][slot]))
                lds[k] = p.op("sync", lambda e, srcA=srcA, slot=slot: e.dma_start(
                    out=wstage3[slot][:, :], in_=srcA), waits=wl, inc=sW[slot], amt=16)
                yield
            for j0 in range(0, len(picks), 2):
                pair = picks[j0:j0 + 2]
                for k in pair:
                    wi = tern_jobs[k][2]
                    slot = k % 4
                    wl = [(sW[slot], lds[k])]
                    if tstate.get("u16", [0, 0])[k % 2]:
                        wl.append((sV, tstate["u16"][k % 2]))
                    n1s[k] = p.op("scalar", lambda e, slot=slot, wi=wi, k=k: e.activation(
                        out=tern_u16[k % 2][:, :], in_=wstage3[slot][:, :],
                        func=AF.Identity, bias=rcc[:], scale=swc[:, wi:wi + 1]),
                        waits=wl, inc=sS)
                    yield
                for k in pair:
                    slot = k % 4
                    n2s[k] = p.op("vector", lambda e, slot=slot, k=k: e.tensor_scalar(
                        out=wstage3[slot][:, :], in0=tern_u16[k % 2][:, :],
                        scalar1=RD, scalar2=1.0, op0=ALU.subtract, op1=ALU.min),
                        waits=[(sS, n1s[k])])
                    tstate.setdefault("u16", [0, 0])[k % 2] = n2s[k]
                    yield
                for k in pair:
                    slot = k % 4
                    wl = [(sV, n2s[k])]
                    if tstate.get("t2b", [0, 0])[k % 2]:
                        wl.append((sSp[(k - 2) % 4], tstate["t2b"][k % 2]))
                    n2s[k] = p.op("vector", lambda e, slot=slot, k=k: e.tensor_scalar(
                        out=tern2b[k % 2][:, :], in0=wstage3[slot][:, :],
                        scalar1=-1.0, scalar2=None, op0=ALU.max),
                        waits=wl, inc=sV)
                    yield
                for k in pair:
                    slot = k % 4
                    for pi, (dst, c0, c1) in enumerate(tern_jobs[k][1]):
                        tstate["sp"][slot] = p.op("sync", lambda e, dst=dst, c0=c0, c1=c1, slot=slot, k=k: e.dma_start(
                            out=dst, in_=tern2b[k % 2][:, c0:c1]),
                            waits=[(sV, n2s[k])] if pi == 0 else [],
                            inc=sSp[slot], amt=16)
                    tstate.setdefault("t2b", [0, 0])[k % 2] = tstate["sp"][slot]
                    yield

    tern_gen = tern_steps()

    def win(j):
        q0 = (j - 1) * 128
        q1 = (j + 2) * 128
        c0 = max(0, -q0)
        q0r = max(q0, 0)
        q1r = min(q1, T)
        return q0r, q1r, c0, c0 + (q1r - q0r)

    pod_ev = [0, 0, 0, 0]
    psc_ev = [0, 0, 0]
    expm_use = [0, 0, 0]
    mask_done = [0, 0, 0]
    rodc_use = [0, 0]
    mask_info = {}

    def emit_av(h, j):
        sl = (j + 1) % 3
        msem, nmask = mask_info.pop((h, j))
        last_use = 0
        for s in range(3):
            qb = j - 1 + s
            if qb < 0 or qb >= NJ:
                continue
            wc = qb * 128 - (j - 1) * 128
            first = (j == qb - 1)
            lastc = (j == qb + 1)
            wl = [(msem, nmask)]
            if first and pod_ev[qb % 4]:
                wl.append((sV, pod_ev[qb % 4]))
            last_use = p.op("tensor", lambda e, qb=qb, j=j, h=h, sl=sl, wc=wc, first=first, lastc=lastc: e.matmul(
                pod[qb % 4][:, 0:65], expm[sl][:, wc:wc + 128],
                vaug(j)[:, h * 65:h * 65 + 65], start=first, stop=lastc),
                waits=wl, inc=sT)
            if lastc:
                wl = [(sT, last_use)]
                if rodc_use[qb % 2]:
                    wl.append((sV, rodc_use[qb % 2]))
                nrc = p.op("vector", lambda e, qb=qb: e.reciprocal(
                    rodc[qb % 2][:, :], pod[qb % 4][:, 64:65]), waits=wl, inc=sV)
                pod_ev[qb % 4] = p.op("vector", lambda e, qb=qb, h=h: e.tensor_scalar(
                    out=B2[qb][:, h * 64:(h + 1) * 64], in0=pod[qb % 4][:, 0:64],
                    scalar1=rodc[qb % 2][:, :], scalar2=None, op0=ALU.mult),
                    waits=[(sV, nrc)], inc=sV)
                rodc_use[qb % 2] = pod_ev[qb % 4]
        if last_use:
            expm_use[sl] = last_use

    seq = [(h, j) for h in range(H) for j in range(-1, NJ + 1)]
    for idx, (h, j) in enumerate(seq):
        hc = h // 2
        hr = (h % 2) * 64
        sl = (j + 1) % 3
        q0r, q1r, c0, c1 = win(j)
        pad = (j < 0) or (j >= NJ)
        if not pad:
            wl = [(sS, nec)]
            if psc_ev[sl]:
                wl.append((sS, psc_ev[sl]))
            p.op("tensor", lambda e, sl=sl, c0=c0, c1=c1: e.matmul(
                psc[sl][:, c0:c1], iden[:], maskB[:, c0:c1], start=True, stop=False),
                waits=wl)
            nmm = p.op("tensor", lambda e, hc=hc, hr=hr, j=j, sl=sl, q0r=q0r, q1r=q1r, c0=c0, c1=c1: e.matmul(
                psc[sl][:, c0:c1], KT[hc][hr:hr + 64, j * 128:(j + 1) * 128],
                QT[hc][hr:hr + 64, q0r:q1r], start=False, stop=True),
                inc=sT)
            wl = [(sT, nmm)]
            if expm_use[sl]:
                wl.append((sT, expm_use[sl]))
            psc_ev[sl] = p.op("scalar", lambda e, h=h, sl=sl, c0=c0, c1=c1: e.activation(
                out=expm[sl][:, c0:c1], in_=psc[sl][:, c0:c1], func=AF.Exp,
                bias=negC[:, h:h + 1], scale=0.125), waits=wl, inc=sS)
            mask_info[(h, j)] = (sS, psc_ev[sl])
        else:
            wl = [(sS, nec)]
            if expm_use[sl]:
                wl.append((sT, expm_use[sl]))
            nmask = p.op("vector", lambda e, h=h, sl=sl, c0=c0, c1=c1: e.tensor_scalar(
                out=expm[sl][:, c0:c1], in0=mask01[:, c0:c1],
                scalar1=eC[:, h:h + 1], scalar2=None, op0=ALU.mult),
                waits=wl, inc=sV)
            mask_info[(h, j)] = (sV, nmask)
        if idx >= 2:
            emit_av(*seq[idx - 2])
        for _ in range(2):
            next(tern_gen, None)
    emit_av(*seq[-2])
    emit_av(*seq[-1])
    for _ in tern_gen:
        pass
    p.op("sync", lambda e: e.nop(), waits=[(sD, sD.n)] +
         [(s_, s_.n) for s_ in sW] + [(s_, s_.n) for s_ in sSp])
    run_block(nc, p, hwm, "b3_attn")
    free(*[f"QT{c}" for c in range(NC)], *[f"KT{c}" for c in range(NC)],
         *[f"VA{j}" for j in range(NJ + 2)],
         *[f"wstage3_{i}" for i in range(4)], "tern_u16_0", "tern_u16_1",
         "tern2b_0", "tern2b_1",
         "expraw_0", "expraw_1", "expraw_2", "expm_0", "expm_1", "expm_2",
         "rodc_0", "rodc_1",
         "psc_0", "psc_1", "psc_2", "pod_0", "pod_1", "pod_2", "pod_3")

    wosb = [alloc(f"wo_{c}", [128, D], BF16) for c in range(NC)]
    attn = [alloc(f"attn_{i}", [128, D], F32) for i in range(2)]
    x2f = [alloc(f"x2f_{i}", [128, D], F32) for i in range(2)]
    x3f = [alloc(f"x3f_{i}", [128, D], F32) for i in range(2)]
    xre = [alloc(f"xre_{i}", [128, D], F32) for i in range(2)]
    q16 = alloc("q16", [128, D], I16)
    x3q = [alloc(f"x3q_{i}", [128, D], BF16) for i in range(2)]
    x3st = [alloc(f"x3st_{i}", [128, D], BF16) for i in range(2)]
    sxc = alloc("sxc", [128, 1], F32)
    sxic = alloc("sxic", [128, 1], F32)
    ptrO = [alloc(f"ptrO_{i}", [128, 1024], BF16, "psum") for i in range(2)]
    ptrX = [alloc(f"ptrX_{i}", [128, 1024], BF16, "psum") for i in range(2)]
    po4 = [alloc(f"po4_{i}", [128, 512], F32, "psum") for i in range(2)]

    p = Plan()
    if DBG:
        for t in range(NT):
            p.op("sync", lambda e, t=t: e.dma_start(
                out=dbgO[t * 128:(t + 1) * 128, :], in_=B2[t][:, :]), inc=sD, amt=16)
    wo_ld = [0] * NC
    for c in range(NC):
        wo_ld[c] = p.op("gpsimd", lambda e, c=c: e.dma_start(
            out=wosb[c][:], in_=dw["wo"][c * 128:(c + 1) * 128, :]), inc=sGq, amt=16)
    otr = [0] * NT
    oev = [0] * NT
    po_ev = [0, 0]
    x2sp = [0, 0]
    x3sp = [0, 0]
    apply2 = [0] * NT
    apply3 = [0] * NT
    q_done = [0] * NT
    x3tr = [0] * NT
    x3ev = [0] * NT
    for t in range(NT):
        for c in range(NC):
            wl = []
            if c == 0 and t >= 2:
                wl.append((sS, oev[t - 2]))
            otr[t] = p.op("tensor", lambda e, t=t, c=c: e.transpose(
                ptrO[t % 2][:, c * 128:(c + 1) * 128],
                B2[t][:, c * 128:(c + 1) * 128], iden[:]),
                waits=wl, inc=sT if c == NC - 1 else None)
        oev[t] = p.op("scalar", lambda e, t=t: e.copy(
            B1r3[:, :, t * 128:(t + 1) * 128],
            ptrO[t % 2].ap().rearrange("p (c q) -> p c q", q=128)[:, :, :]),
            waits=[(sT, otr[t])], inc=sS)
        xre_ld = p.op("sync", lambda e, t=t: e.dma_start(
            out=xre[t % 2][:], in_=dx[t * 128:(t + 1) * 128, :]),
            waits=[(sV, apply2[t - 2])] if t >= 2 else [], inc=sX[t % 2], amt=16)
        for ofs in range(2):
            slot = ofs
            wl = [(sS, oev[t]), (sGq, wo_ld[NC - 1])]
            if po_ev[slot]:
                wl.append((sV, po_ev[slot]))
            last = 0
            for c in range(NC):
                last = p.op("tensor", lambda e, t=t, ofs=ofs, c=c, slot=slot: e.matmul(
                    po4[slot][:, :], B1[c][:, t * 128:(t + 1) * 128],
                    wosb[c][:, ofs * 512:(ofs + 1) * 512],
                    start=(c == 0), stop=(c == NC - 1)),
                    waits=wl if c == 0 else [],
                    inc=sT if c == NC - 1 else None)
            wl2 = [(sT, last)]
            if t >= 2:
                wl2.append((sV, apply2[t - 2]))
            po_ev[slot] = p.op("vector", lambda e, t=t, ofs=ofs, slot=slot: e.scalar_tensor_tensor(
                out=attn[t % 2][:, ofs * 512:(ofs + 1) * 512], in0=po4[slot][:, :],
                scalar=1.0, in1=boB[:, ofs * 512:(ofs + 1) * 512],
                op0=ALU.mult, op1=ALU.add), waits=wl2, inc=sV)
        if t >= 2:
            emit_x3t(t - 2)
        p.op("vector", lambda e, t=t: e.bn_stats(stat[:, 0:6], attn[t % 2][:, 0:512]),
             waits=[(sV, po_ev[1])])
        nsb = p.op("vector", lambda e, t=t: e.bn_stats(stat[:, 6:12], attn[t % 2][:, 512:1024]))
        nag = p.op("vector", lambda e: e.bn_aggr(aggr[:], stat[:]), waits=[(sV, nsb)], inc=sV)
        nsq = p.op("scalar", lambda e: e.activation(out=scol[:], in_=aggr[:, 1:2],
                                                    func=AF.Sqrt, bias=epsc[:], scale=1.0),
                   waits=[(sV, nag)], inc=sS)
        nrcp = p.op("vector", lambda e: e.reciprocal(rcol[:], scol[:]), waits=[(sS, nsq)])
        if DBG and t == 0:
            p.op("vector", lambda e: e.tensor_copy(dbgs[:, 64:66], aggr[:, :]))
            p.op("vector", lambda e: e.tensor_copy(dbgs[:, 66:67], scol[:, :]))
            p.op("vector", lambda e: e.tensor_copy(dbgs[:, 67:68], rcol[:, :]),
                 waits=[(sV, nrcp)])
            p.op("vector", lambda e: e.tensor_copy(dbgs[:, 72:76], attn[0][:, 0:4]))
        napl2 = p.op("vector", lambda e, t=t: e.tensor_scalar(
            out=attn[t % 2][:], in0=attn[t % 2][:], scalar1=aggr[:, 0:1], scalar2=rcol[:],
            op0=ALU.subtract, op1=ALU.mult), waits=[(sV, nrcp)])
        wl = [(sX[t % 2], xre_ld), (sV, napl2)]
        if x2sp[t % 2]:
            wl.append((sGs[t % 2], x2sp[t % 2]))
        apply2[t] = p.op("vector", lambda e, t=t: e.tensor_add(
            x2f[t % 2][:], attn[t % 2][:], xre[t % 2][:]), waits=wl, inc=sV)
        x2sp[t % 2] = p.op("gpsimd", lambda e, t=t: e.dma_start(
            out=dx2[t * 128:(t + 1) * 128, :], in_=x2f[t % 2][:, :]),
            waits=[(sV, apply2[t])], inc=sGs[t % 2], amt=16)
        p.op("vector", lambda e, t=t: e.bn_stats(stat[:, 0:6], x2f[t % 2][:, 0:512]),
             waits=[(sV, apply2[t])])
        nsb = p.op("vector", lambda e, t=t: e.bn_stats(stat[:, 6:12], x2f[t % 2][:, 512:1024]))
        nag = p.op("vector", lambda e: e.bn_aggr(aggr[:], stat[:]), waits=[(sV, nsb)], inc=sV)
        nsq = p.op("scalar", lambda e: e.activation(out=scol[:], in_=aggr[:, 1:2],
                                                    func=AF.Sqrt, bias=epsc[:], scale=1.0),
                   waits=[(sV, nag)], inc=sS)
        nrcp2 = p.op("vector", lambda e: e.reciprocal(rcol[:], scol[:]), waits=[(sS, nsq)])
        if DBG and t == 0:
            p.op("vector", lambda e: e.tensor_copy(dbgs[:, 68:70], aggr[:, :]))
            p.op("vector", lambda e: e.tensor_copy(dbgs[:, 70:71], scol[:, :]))
            p.op("vector", lambda e: e.tensor_copy(dbgs[:, 71:72], rcol[:, :]),
                 waits=[(sV, nrcp2)])
        wl = [(sV, nrcp2)]
        if t >= 2:
            wl.append((sV, q_done[t - 2]))
        apply3[t] = p.op("vector", lambda e, t=t: e.tensor_scalar(
            out=x3f[t % 2][:], in0=x2f[t % 2][:], scalar1=aggr[:, 0:1], scalar2=rcol[:],
            op0=ALU.subtract, op1=ALU.mult), waits=wl, inc=sV)
        wl = [(sV, apply3[t])]
        if t >= 1:
            wl.append((sV, q_done[t - 1]))
        nred4 = p.op("vector", lambda e, t=t: e.reduce_max(
            sxc[:, :], x3f[t % 2][:, :], axis=AXX, apply_absolute_value=True), waits=wl)
        nclp = p.op("vector", lambda e: e.tensor_scalar(out=sxc[:], in0=sxc[:], scalar1=1e-5,
                                                        scalar2=None, op0=ALU.max),
                    waits=[(sV, nred4)])
        nrs = p.op("vector", lambda e: e.reciprocal(sxic[:], sxc[:]), waits=[(sV, nclp)])
        nm127 = p.op("vector", lambda e: e.tensor_scalar(out=sxic[:], in0=sxic[:],
                                                         scalar1=127.0, scalar2=None,
                                                         op0=ALU.mult), waits=[(sV, nrs)])
        nq16t = p.op("vector", lambda e, t=t: e.tensor_scalar(
            out=q16[:, :], in0=x3f[t % 2][:], scalar1=sxic[:], scalar2=RC,
            op0=ALU.mult, op1=ALU.add), waits=[(sV, nm127)])
        nsxf = p.op("vector", lambda e: e.tensor_scalar(out=sxc[:], in0=sxc[:],
                                                        scalar1=1.0 / 127.0,
                                                        scalar2=None, op0=ALU.mult))
        if DBG:
            p.op("vector", lambda e, t=t: e.tensor_copy(dbgs[:, t:t + 1], sxc[:, :]),
                 waits=[(sV, nsxf)])
            p.op("vector", lambda e, t=t: e.tensor_copy(dbgs[:, 16 + t:17 + t], sxic[:, :]))
        wl = [(sV, nq16t), (sV, nsxf)]
        if t >= 2:
            wl.append((sS, x3ev[t - 2]))
        q_done[t] = p.op("vector", lambda e, t=t: e.tensor_scalar(
            out=x3q[t % 2][:], in0=q16[:, :], scalar1=RD, scalar2=sxc[:],
            op0=ALU.subtract, op1=ALU.mult), waits=wl, inc=sV)

        def emit_x3t(tp):
            for c in range(NC):
                wl = []
                if c == 0:
                    wl.append((sV, q_done[tp]))
                    if tp >= 2:
                        wl.append((sS, x3ev[tp - 2]))
                x3tr[tp] = p.op("tensor", lambda e, tp=tp, c=c: e.transpose(
                    ptrX[tp % 2][:, c * 128:(c + 1) * 128],
                    x3q[tp % 2][:, c * 128:(c + 1) * 128], iden[:]),
                    waits=wl, inc=sT if c == NC - 1 else None)
            wl = [(sT, x3tr[tp])]
            if x3sp[tp % 2]:
                wl.append((sSp[tp % 2], x3sp[tp % 2]))
            x3ev[tp] = p.op("scalar", lambda e, tp=tp: e.copy(
                x3st[tp % 2][:, :], ptrX[tp % 2][:, :]), waits=wl, inc=sS)
            x3sp[tp % 2] = p.op("sync", lambda e, tp=tp: e.dma_start(
                out=dx3t[:, :, 1 + tp * 128:1 + (tp + 1) * 128].transpose([1, 0, 2]),
                in_=x3st[tp % 2][:, :]), waits=[(sS, x3ev[tp])],
                inc=sSp[tp % 2], amt=16)

        if t == NT - 1:
            emit_x3t(t - 1)
            emit_x3t(t)
    p.op("sync", lambda e: e.nop(), waits=[(sSp[0], sSp[0].n), (sSp[1], sSp[1].n),
                                           (sX[0], sX[0].n), (sX[1], sX[1].n)])
    p.op("gpsimd", lambda e: e.nop(), waits=[(sGs[0], sGs[0].n), (sGs[1], sGs[1].n),
                                             (sGq, sGq.n)])
    run_block(nc, p, hwm, "b4_oproj")
    free(*[f"wo_{c}" for c in range(NC)], "B1all",
         *[f"B2_{t}" for t in range(NT)],
         "bvB", "boB",
         "attn_0", "attn_1", "x2f_0", "x2f_1", "x3f_0", "x3f_1", "xre_0", "xre_1",
         "q16", "x3q_0", "x3q_1", "x3st_0", "x3st_1", "sxc", "sxic",
         "ptrO_0", "ptrO_1", "ptrX_0", "ptrX_1", "po4_0", "po4_1")

    diag = alloc("diag", [128, 3 * HCH * 128], BF16)
    w1h = [alloc(f"w1h_{i}", [128, 4 * 1024], BF16) for i in range(2)]
    w2b = [alloc(f"w2b_{i}", [128, 4 * 1024], BF16) for i in range(2)]
    x3w = [alloc(f"x3w_{i}", [128, 8 * 256], BF16) for i in range(3)]
    Ab = [alloc(f"Ab_{i}", [128, HCH * 258], BF16) for i in range(2)]
    H2 = alloc("H2", [128, HCH * 256], F32)
    H2q = [alloc(f"H2q_{i}", [128, HCH * 256], BF16) for i in range(2)]
    tland = [alloc(f"tland_{i}", [128, 256], F32) for i in range(2)]
    q16b = [alloc(f"q16b_{i}", [128, 256], I16) for i in range(2)]
    apar = alloc("apar", [128, 4 * 256], F32)
    amaxC = alloc("amaxC", [128, 256], F32)
    v2 = alloc("v2", [128, 4], F32)
    sxr = alloc("sxr", [1, 256], F32)
    sxB = alloc("sxB", [128, 256], F32)
    sc2b = [alloc(f"sc2b_{i}", [128, 2], F32) for i in range(2)]
    out2 = [alloc(f"out2_{i}", [128, D], F32) for i in range(2)]
    x4b = [alloc(f"x4b_{i}", [128, D], F32) for i in range(2)]
    x2re = [alloc(f"x2re_{i}", [128, D], BF16) for i in range(2)]
    pA = [alloc(f"pA_{i}", [128, 512], F32, "psum") for i in range(2)]
    pC = alloc("pC", [128, 512], F32, "psum")
    pM = [alloc(f"pM_{i}", [128, 512], F32, "psum") for i in range(4)]
    pX = alloc("pX", [128, 512], F32, "psum")

    p = Plan()
    dg_ld = p.op("gpsimd", lambda e: e.dma_start(
        out=diag[:, :], in_=ddiag[:, :]), inc=sG, amt=16)

    def dg(k, c):
        return diag[:, (k * HCH + c) * 128:(k * HCH + c) * 128 + 128]

    def ab(s, c):
        return Ab[s % 2][:, c * 258:(c + 1) * 258]

    x3_ld = [0] * (NS + 2)
    a_mm = [[0] * HCH for _ in range(NS + 2)]
    a_ev = [[0] * HCH for _ in range(NS + 2)]
    halo_d = [[0] * HCH for _ in range(NS + 2)]
    conv_mm = [[0] * HCH for _ in range(NS)]
    silu_d = [[0] * HCH for _ in range(NS)]
    ntp_d = [[0] * HCH for _ in range(NS)]
    h2q_d = [[0] * HCH for _ in range(NS)]
    apar_d = [[0] * 4 for _ in range(NS)]
    mm2_last = [0] * NS
    w1_ld = {}
    w1_used = {}
    pa_ev = [0, 0]
    pc_ev = [0]
    pm_ev = [0, 0, 0, 0]
    w2_used = [0, 0]
    x2re_used = [0, 0]
    out_dma = [0, 0]
    vcomb = [0]
    sxB_ev = [0]
    sc2_d = [0, 0]
    trans_last = [0]
    ktrans_last = [0]

    def emit_x3ld(s2):
        par = s2 % 3
        if s2 == NS:
            src = dx3t[:, :, s2 * TS:s2 * TS + 2].transpose([1, 0, 2])
            dst = x3w[par].ap().rearrange("p (c n) -> p c n", n=256)[:, :, 0:2]
        else:
            src = dx3t[:, :, s2 * TS:s2 * TS + 256].transpose([1, 0, 2])
            dst = x3w[par][:, :]
        wl = [(sT, a_mm[s2 - 3][HCH - 1])] if s2 >= 3 else []
        if s2 >= 3:
            wl.append((sX3[par], x3_ld[s2 - 3]))
        x3_ld[s2] = p.op("sync", lambda e, src=src, dst=dst: e.dma_start(out=dst, in_=src),
                         waits=wl, inc=sX3[par], amt=16)

    def emit_w1ld(s2, q):
        j = s2 * 8 + q
        wl = [(sT, w1_used[j - 2])] if (j - 2) in w1_used else []
        src = dw1b[q, :, :]
        w1_ld[j] = p.op("gpsimd", lambda e, src=src, j=j: e.dma_start(
            out=w1h[j % 2][:, :], in_=src), waits=wl, inc=sGw[j % 2], amt=16)

    def emit_A_chunk(s2, hcx):
        par = s2 % 3
        width = 2 if s2 == NS else 256
        slotp = hcx % 2
        j = s2 * 8 + hcx // 4
        hs2 = hcx % 4
        wl = [(sX3[s2 % 3], x3_ld[s2]), (sGw[j % 2], w1_ld[j])]
        if pa_ev[slotp]:
            wl.append((sS, pa_ev[slotp]))
        last = 0
        for d in range(NC):
            if s2 == NS:
                mv = x3w[par].ap().rearrange("p (c n) -> p c n", n=256)[:, d, 0:2]
            else:
                mv = x3w[par][:, d * 256:(d + 1) * 256]
            last = p.op("tensor", lambda e, j=j, hs2=hs2, d=d, slotp=slotp, mv=mv, width=width: e.matmul(
                pA[slotp][:, 0:width],
                w1h[j % 2][:, d * 512 + hs2 * 128:d * 512 + hs2 * 128 + 128],
                mv, start=(d == 0), stop=(d == NC - 1)),
                waits=wl if d == 0 else [],
                inc=sT if d == NC - 1 else None)
        a_mm[s2][hcx] = last
        w1_used[j] = last
        wl = [(sT, last)]
        if s2 >= 2:
            wl.append((sT, conv_mm[s2 - 2][hcx]))
        pa_ev[slotp] = p.op("scalar", lambda e, s2=s2, hcx=hcx, slotp=slotp, width=width: e.activation(
            out=ab(s2, hcx)[:, 0:width], in_=pA[slotp][:, 0:width],
            func=AF.Identity, bias=0.0, scale=mwc[:, 0:1]), waits=wl, inc=sS)
        a_ev[s2][hcx] = pa_ev[slotp]
        if s2 >= 1:
            halo_d[s2 - 1][hcx] = p.op("scalar", lambda e, s2=s2, hcx=hcx: e.copy(
                ab(s2 - 1, hcx)[:, 256:258], ab(s2, hcx)[:, 0:2]),
                waits=[(sS, a_ev[s2][hcx])], inc=sS)

    def emit_conv_chunk(s, hcx):
        wl = [(sS, halo_d[s][hcx]), (sG, dg_ld)]
        if pc_ev[0]:
            wl.append((sS, pc_ev[0]))
        if s >= 1:
            wl.append((sV, ntp_d[s - 1][hcx]))
            wl.append((sV, apar_d[s - 1][hcx // 8]))
        last = 0
        for k in range(3):
            last = p.op("tensor", lambda e, s=s, hcx=hcx, k=k: e.matmul(
                pC[:, 0:256], dg(k, hcx), ab(s, hcx)[:, k:k + 256],
                start=(k == 0), stop=(k == 2)),
                waits=wl if k == 0 else [],
                inc=sT if k == 2 else None)
        conv_mm[s][hcx] = last
        silu_d[s][hcx] = p.op("scalar", lambda e, s=s, hcx=hcx: e.activation(
            out=H2[:, hcx * 256:(hcx + 1) * 256], in_=pC[:, 0:256],
            func=AF.Silu, bias=cbs[:, hcx:hcx + 1], scale=1.0),
            waits=[(sT, last)], inc=sS)
        pc_ev[0] = silu_d[s][hcx]

    def emit_mm2(sm):
        par = sm % 2
        x2_ld = [0, 0]
        for tsl in range(2):
            t = sm * 2 + tsl
            wl = [(sV, x2re_used[tsl])] if x2re_used[tsl] else []
            x2_ld[tsl] = p.op("sync", lambda e, t=t, tsl=tsl: e.dma_start(
                out=x2re[tsl][:, :], in_=dx2[t * 128:(t + 1) * 128, :]),
                waits=wl, inc=sX2[tsl], amt=16)
        last = 0
        for cc in range(0, HCH, 4):
            b = (cc // 4) % 2
            wl = [(sT, w2_used[b])] if w2_used[b] else []
            w2_ld = p.op("sync", lambda e, cc=cc, b=b: e.dma_start(
                out=w2b[b][:, :], in_=dw2b[:, cc * 1024:(cc + 4) * 1024]),
                waits=wl, inc=sW2[b], amt=16)
            for hcx in range(cc, cc + 4):
                qq = hcx % 4
                for tsl in range(2):
                    for ofs in range(2):
                        slot = tsl * 2 + ofs
                        wl = []
                        if hcx == cc and tsl == 0 and ofs == 0:
                            wl.append((sW2[b], w2_ld))
                            wl.append((sV, h2q_d[sm][cc + 3]))
                        if hcx == 0 and pm_ev[slot]:
                            wl.append((sV, pm_ev[slot]))
                        fin = (hcx == cc + 3 and tsl == 1 and ofs == 1)
                        last = p.op("tensor", lambda e, sm=sm, hcx=hcx, tsl=tsl, ofs=ofs, b=b, qq=qq, slot=slot: e.matmul(
                            pM[slot][:, :],
                            H2q[sm % 2][:, hcx * 256 + tsl * 128:hcx * 256 + tsl * 128 + 128],
                            w2b[b][:, qq * 1024 + ofs * 512:qq * 1024 + ofs * 512 + 512],
                            start=(hcx == 0), stop=(hcx == HCH - 1)),
                            waits=wl, inc=sT if fin else None)
            w2_used[b] = last
        mm2_last[sm] = last
        for tsl in range(2):
            for ofs in range(2):
                slot = tsl * 2 + ofs
                wl = [(sT, mm2_last[sm]), (sV, sc2_d[par])]
                if x2re_used[tsl]:
                    wl.append((sV, x2re_used[tsl]))
                pm_ev[slot] = p.op("vector", lambda e, tsl=tsl, ofs=ofs, slot=slot, par=par: e.tensor_scalar(
                    out=out2[tsl][:, ofs * 512:(ofs + 1) * 512], in0=pM[slot][:, :],
                    scalar1=sc2b[par][:, tsl:tsl + 1], scalar2=None, op0=ALU.mult),
                    waits=wl, inc=sV)
            p.op("vector", lambda e, tsl=tsl: e.bn_stats(stat[:, 0:6], out2[tsl][:, 0:512]),
                 waits=[(sV, pm_ev[tsl * 2 + 1])])
            nsb = p.op("vector", lambda e, tsl=tsl: e.bn_stats(stat[:, 6:12],
                                                               out2[tsl][:, 512:1024]))
            nag = p.op("vector", lambda e: e.bn_aggr(aggr[:], stat[:]),
                       waits=[(sV, nsb)], inc=sV)
            nsq = p.op("scalar", lambda e: e.activation(out=scol[:], in_=aggr[:, 1:2],
                                                        func=AF.Sqrt, bias=epsc[:], scale=1.0),
                       waits=[(sV, nag)], inc=sS)
            nrcp5 = p.op("vector", lambda e: e.reciprocal(rcol[:], scol[:]),
                         waits=[(sS, nsq)])
            napl4 = p.op("vector", lambda e, tsl=tsl: e.tensor_scalar(
                out=out2[tsl][:], in0=out2[tsl][:], scalar1=aggr[:, 0:1], scalar2=rcol[:],
                op0=ALU.subtract, op1=ALU.mult), waits=[(sV, nrcp5)])
            wl = [(sX2[tsl], x2_ld[tsl]), (sV, napl4)]
            if out_dma[tsl]:
                wl.append((sOut[tsl], out_dma[tsl]))
            x2re_used[tsl] = p.op("vector", lambda e, tsl=tsl: e.tensor_add(
                x4b[tsl][:], out2[tsl][:], x2re[tsl][:]), waits=wl, inc=sV)
            t = sm * 2 + tsl
            out_dma[tsl] = p.op("sync", lambda e, t=t, tsl=tsl: e.dma_start(
                out=dout[t * 128:(t + 1) * 128, :], in_=x4b[tsl][:, :]),
                waits=[(sV, x2re_used[tsl])], inc=sOut[tsl], amt=16)

    emit_x3ld(0)
    emit_x3ld(1)
    emit_x3ld(2)
    for s2 in (0, 1):
        for hcx in range(HCH):
            if hcx % 4 == 0:
                emit_w1ld(s2, hcx // 4)
            emit_A_chunk(s2, hcx)

    for s in range(NS):
        s2 = s + 2
        if s + 3 <= NS:
            emit_x3ld(s + 3)
        for hcx in range(HCH):
            emit_conv_chunk(s, hcx)
            if s2 <= NS:
                if hcx % 4 == 0:
                    emit_w1ld(s2, hcx // 4)
                emit_A_chunk(s2, hcx)
            if hcx % 8 == 7:
                g3 = hcx // 8
                wl = [(sS, silu_d[s][hcx])]
                if g3 == 0 and vcomb[0]:
                    wl.append((sV, vcomb[0]))
                apar_d[s][g3] = p.op("vector", lambda e, g3=g3: e.reduce_max(
                    apar[:, g3 * 256:(g3 + 1) * 256],
                    H2.ap().rearrange("p (c n) -> p n c", c=HCH)[:, :, g3 * 8:(g3 + 1) * 8],
                    axis=AXX, apply_absolute_value=True), waits=wl, inc=sV)
        cA = p.op("vector", lambda e: e.tensor_tensor(
            amaxC[:, :], apar[:, 0:256], apar[:, 256:512], ALU.max),
            waits=[(sV, apar_d[s][1]), (sT, trans_last[0])], inc=sV)
        cB = p.op("vector", lambda e: e.tensor_tensor(
            apar[:, 768:1024], apar[:, 512:768], apar[:, 768:1024], ALU.max),
            waits=[(sV, apar_d[s][3])], inc=sV)
        cF = p.op("vector", lambda e: e.tensor_tensor(
            amaxC[:, :], amaxC[:, :], apar[:, 768:1024], ALU.max),
            waits=[(sV, cA), (sV, cB)], inc=sV)
        vcomb[0] = cF
        red = 0
        for tt in range(2):
            wl = [(sV, cF), (sV, sxB_ev[0])] if tt == 0 else [(sV, red)]
            ntr = p.op("tensor", lambda e, tt=tt: e.transpose(
                pX[:, 0:128], amaxC[:, tt * 128:(tt + 1) * 128], idenf[:]),
                waits=wl, inc=sT)
            red = p.op("vector", lambda e, tt=tt: e.reduce_max(
                v2[:, tt:tt + 1], pX[0:128, 0:128], axis=AXX),
                waits=[(sT, ntr)], inc=sV)
            trans_last[0] = ntr
        ncl5 = p.op("vector", lambda e: e.tensor_scalar(out=v2[:, 0:2], in0=v2[:, 0:2],
                                                        scalar1=1e-5, scalar2=None,
                                                        op0=ALU.max),
                    waits=[(sV, red), (sT, ktrans_last[0])], inc=sV)
        p.op("vector", lambda e: e.reciprocal(v2[:, 2:3], v2[:, 0:1]), waits=[(sV, ncl5)])
        nrB = p.op("vector", lambda e: e.reciprocal(v2[:, 3:4], v2[:, 1:2]))
        nm7 = p.op("vector", lambda e: e.tensor_scalar(out=v2[:, 2:4], in0=v2[:, 2:4],
                                                       scalar1=127.0, scalar2=None,
                                                       op0=ALU.mult),
                   waits=[(sV, nrB)], inc=sV)
        nv2 = p.op("vector", lambda e: e.tensor_scalar(out=v2[:, 0:2], in0=v2[:, 0:2],
                                                       scalar1=1.0 / 127.0, scalar2=None,
                                                       op0=ALU.mult), inc=sV)
        for tsl in range(2):
            sc2_d[s % 2] = p.op("vector", lambda e, tsl=tsl, s=s: e.tensor_tensor(
                sc2b[s % 2][:, tsl:tsl + 1], v2[:, tsl:tsl + 1], mwc[:, 1:2], ALU.mult),
                waits=[(sV, nv2)], inc=sV)
        prev_ev = 0
        for k in range(2):
            wl = [(sV, nm7)]
            if prev_ev:
                wl.append((sV, prev_ev))
            ntr = p.op("tensor", lambda e, k=k: e.transpose(
                pX[0:1, 128:256], v2[:, 2 + k:3 + k], idenf[:]), waits=wl, inc=sT)
            prev_ev = p.op("vector", lambda e, k=k: e.tensor_copy(
                sxr[0:1, k * 128:(k + 1) * 128], pX[0:1, 128:256]),
                waits=[(sT, ntr)], inc=sV)
            ktrans_last[0] = ntr
        nb = p.op("tensor", lambda e: e.matmul(pX[:, 256:512], ones128[:, :],
                                               sxr[0:1, :], start=True, stop=True),
                  waits=[(sV, prev_ev)], inc=sT)
        wl = [(sT, nb)]
        if s >= 1:
            wl.append((sV, ntp_d[s - 1][HCH - 1]))
        sxB_ev[0] = p.op("vector", lambda e: e.tensor_copy(sxB[:, :], pX[0:128, 256:512]),
                         waits=wl, inc=sV)
        for h0 in range(0, HCH, 2):
            ntl = [0, 0]
            nuu = [0, 0]
            for i in range(2):
                hcx = h0 + i
                wl = [(sV, sxB_ev[0])]
                if hcx >= 2:
                    wl.append((sV, h2q_d[s][hcx - 2]))
                ntl[i] = p.op("vector", lambda e, hcx=hcx, i=i: e.tensor_tensor(
                    tland[i][:, :], H2[:, hcx * 256:(hcx + 1) * 256], sxB[:, :], ALU.mult),
                    waits=wl, inc=sV)
                ntp_d[s][hcx] = ntl[i]
            for i in range(2):
                nuu[i] = p.op("vector", lambda e, i=i: e.tensor_scalar(
                    out=q16b[i][:, :], in0=tland[i][:, :],
                    scalar1=RC, scalar2=None, op0=ALU.add),
                    waits=[(sV, ntl[i])], inc=sV)
            for i in range(2):
                hcx = h0 + i
                wl = [(sV, nuu[i])]
                if s >= 2:
                    wl.append((sT, mm2_last[s - 2]))
                h2q_d[s][hcx] = p.op("vector", lambda e, s=s, hcx=hcx, i=i: e.tensor_scalar(
                    out=H2q[s % 2][:, hcx * 256:(hcx + 1) * 256], in0=q16b[i][:, :],
                    scalar1=RD, scalar2=None, op0=ALU.subtract), waits=wl, inc=sV)
        if s >= 1:
            emit_mm2(s - 1)
    emit_mm2(NS - 1)
    if DBG:
        p.op("sync", lambda e: e.dma_start(out=dbgS[:, :], in_=dbgs[:, :]), inc=sD, amt=16)
    p.op("sync", lambda e: e.nop(), waits=[
        (sOut[0], sOut[0].n), (sOut[1], sOut[1].n),
        (sX[0], sX[0].n), (sX[1], sX[1].n), (sX2[0], sX2[0].n), (sX2[1], sX2[1].n),
        (sX3[0], sX3[0].n), (sX3[1], sX3[1].n), (sX3[2], sX3[2].n)])
    p.op("gpsimd", lambda e: e.nop(), waits=[
        (sG, sG.n), (sGw[0], sGw[0].n), (sGw[1], sGw[1].n),
        (sW2[0], sW2[0].n), (sW2[1], sW2[1].n)])
    run_block(nc, p, hwm, "b5_ffn")

    free(*list(cms))
    for s in (sP, sS, sV, sT, *sW2[::-1], *sGw[::-1], *sGs[::-1], sGk, sGq,
              *sOut[::-1], *sSp[::-1], *sW[::-1], *sX3[::-1], *sX2[::-1],
              *sX[::-1], sG, sD):
        s._cm.__exit__(None, None, None)
    return nc


_NC_CACHE = {}


def _get_nc():
    key = "nc"
    if key not in _NC_CACHE:
        _NC_CACHE[key] = build_nc()
    return _NC_CACHE[key]


def kernel(L0_attn_x, L0_attn_wq, L0_attn_bq, L0_attn_wk, L0_attn_bk,
           L0_attn_wv, L0_attn_bv, L0_attn_wo, L0_attn_bo,
           L0_ffn_w1, L0_ffn_conv_w, L0_ffn_conv_b, L0_ffn_w2):
    x = np.ascontiguousarray(np.asarray(L0_attn_x, dtype=np.float32))
    B = x.shape[0]
    f32 = lambda a: np.ascontiguousarray(np.asarray(a, dtype=np.float32))
    wq, wk, wv, wo = f32(L0_attn_wq), f32(L0_attn_wk), f32(L0_attn_wv), f32(L0_attn_wo)
    bq = np.ascontiguousarray(f32(L0_attn_bq).reshape(NC, 128).T)
    bk = np.ascontiguousarray(f32(L0_attn_bk).reshape(NC, 128).T)
    bv = f32(L0_attn_bv).reshape(1, D)
    bo = f32(L0_attn_bo).reshape(1, D)
    w1t = np.ascontiguousarray(f32(L0_ffn_w1).T)
    w2t = np.ascontiguousarray(f32(L0_ffn_w2).T)
    cw = f32(L0_ffn_conv_w).reshape(DH, 3)
    diag3 = np.zeros((128, 3, HCH, 128), dtype=np.float32)
    idx = np.arange(128)
    for k in range(3):
        for c in range(HCH):
            diag3[idx, k, c, idx] = cw[c * 128 + idx, k]
    diag3 = diag3.reshape(128, 3 * HCH * 128)
    convb = np.ascontiguousarray(f32(L0_ffn_conv_b).reshape(HCH, 128).T)
    pp = np.arange(128)[:, None]
    ff = np.arange(384)[None, :]
    mask = ((pp <= ff) & (ff <= pp + 256)).astype(np.float32)
    maskbig = np.where(mask > 0, 0.0, -1e7).astype(np.float32)
    iden = np.eye(128, dtype=np.float32)
    hv = np.zeros((128, 2), dtype=np.float32)
    hv[:64, 0] = 1.0
    hv[64:, 1] = 1.0
    ones128 = np.ones((1, 128), dtype=np.float32)
    onescol = np.ones((128, 1), dtype=np.float32)

    nc = _get_nc()
    common = dict(wq=wq, wk=wk, wv=wv, wo=wo, bq=bq, bk=bk, bv=bv, bo=bo,
                  w1t=w1t, w2t=w2t, diag3=diag3, convb=convb, mask01=mask,
                  maskbig=maskbig, iden=iden, halves=hv, ones128=ones128,
                  onescol=onescol)
    in_maps = [dict(common, x=np.ascontiguousarray(x[i])) for i in range(B)]
    res = run_bass_kernel_spmd(nc, in_maps, core_ids=list(range(B)))
    out = np.stack([np.asarray(r["out"], dtype=np.float32) for r in res.results], axis=0)
    return out
